# version 05
# baseline (speedup 1.0000x reference)
import math
import os
import numpy as np
import ml_dtypes
from contextlib import ExitStack

import concourse.bass as bass
import concourse.mybir as mybir
from concourse.bass_utils import run_bass_kernel_spmd
from concourse.bass import IndirectOffsetOnAxis

F32 = mybir.dt.float32
BF16 = mybir.dt.bfloat16
U32 = mybir.dt.uint32
AF = mybir.ActivationFunctionType
ALU = mybir.AluOpType
AX = mybir.AxisListType

NCORES = 8
B, L, D = 16, 4096, 1024
NH, EH = 16, 64
TOK = 8192
NCH = 64
CPB = 32
NSC = 16
NE, HID, TOPK = 8, 4096, 2
NTOT = B * L
CAP = int(math.ceil(NTOT * TOPK / NE * 1.05))
CAP_PAD = 2560
GROWS = 512
NGRP = CAP_PAD // GROWS
NBUF = NE * CAP_PAD
DUMP0 = NBUF
XNROWS = NBUF + 2 * TOK
YZERO = NBUF
EPS_LN = 1e-5
EPS_DEN = 1e-6
QSCALE = EH ** (-0.5)


def build_program(stage=3, n_sc=NSC, sublvl=7):
    nc = bass.Bass("TRN2", target_bir_lowering=False, debug=False,
                   dynamic_dma_scratch_size=16384)

    x_in = nc.dram_tensor("x", [TOK, D], F32, kind="ExternalInput").ap()
    wqkv_in = nc.dram_tensor("wqkv", [D, 3 * D], F32, kind="ExternalInput").ap()
    bqkv_in = nc.dram_tensor("bqkv", [3 * D], F32, kind="ExternalInput").ap()
    wout_in = nc.dram_tensor("wout", [D, D], F32, kind="ExternalInput").ap()
    bout_in = nc.dram_tensor("bout", [D], F32, kind="ExternalInput").ap()
    wg_in = nc.dram_tensor("wg", [D, NE], F32, kind="ExternalInput").ap()
    w1_in = nc.dram_tensor("w1", [NE, D, HID], F32, kind="ExternalInput").ap()
    b1_in = nc.dram_tensor("b1", [NE, HID], F32, kind="ExternalInput").ap()
    w2_in = nc.dram_tensor("w2", [NE, HID, D], F32, kind="ExternalInput").ap()
    b2_in = nc.dram_tensor("b2", [NE, D], F32, kind="ExternalInput").ap()
    maskut_in = nc.dram_tensor("maskut", [128, 128], BF16, kind="ExternalInput").ap()
    urank_in = nc.dram_tensor("urank", [128, 128], BF16, kind="ExternalInput").ap()
    ones128_in = nc.dram_tensor("ones128", [128, 128], BF16, kind="ExternalInput").ap()
    ones1x128f_in = nc.dram_tensor("ones1x128f", [1, 128], F32, kind="ExternalInput").ap()
    iota8_in = nc.dram_tensor("iota8", [128, NE], F32, kind="ExternalInput").ap()
    eoff_in = nc.dram_tensor("eoff", [128, NE], F32, kind="ExternalInput").ap()
    iota128_in = nc.dram_tensor("iota128", [128, 1], F32, kind="ExternalInput").ap()
    prevmask_in = nc.dram_tensor("prevmask", [NCORES, 1], F32, kind="ExternalInput").ap()
    epsz_in = nc.dram_tensor("epsz", [128, 2], F32, kind="ExternalInput").ap()
    out_dram = nc.dram_tensor("out", [TOK, D], F32, kind="ExternalOutput").ap()
    dbg_dscat = nc.dram_tensor("dbg_dscat", [128, 2 * NCH], U32, kind="ExternalOutput").ap()
    dbg_ygat = nc.dram_tensor("dbg_ygat", [128, 2 * NCH], U32, kind="ExternalOutput").ap()
    dbg_g1k = nc.dram_tensor("dbg_g1k", [128, NCH], F32, kind="ExternalOutput").ap()
    dbg_g2k = nc.dram_tensor("dbg_g2k", [128, NCH], F32, kind="ExternalOutput").ap()
    dbg_allcnt = nc.dram_tensor("dbg_allcnt", [NCORES, NE], F32, kind="ExternalOutput").ap()
    dbg_thr = nc.dram_tensor("dbg_thr", [128, NE], F32, kind="ExternalOutput").ap()
    DBG3 = os.environ.get("KK_DBG3", "0") == "1"
    if DBG3:
        dbg_qf = nc.dram_tensor("dbg_qf", [128, 4096], BF16, kind="ExternalOutput").ap()
        dbg_kf = nc.dram_tensor("dbg_kf", [128, 4096], BF16, kind="ExternalOutput").ap()
        dbg_v = nc.dram_tensor("dbg_v", [128, 1040], BF16, kind="ExternalOutput").ap()
        dbg_xnt = nc.dram_tensor("dbg_xnt", [128, 4096], BF16, kind="ExternalOutput").ap()
        dbg_attn = nc.dram_tensor("dbg_attn", [128, 1024], BF16, kind="ExternalOutput").ap()
        dbg_s0 = nc.dram_tensor("dbg_s0", [128, 1040], F32, kind="ExternalOutput").ap()
        dbg_lgt = nc.dram_tensor("dbg_lgt", [128, 32], F32, kind="ExternalOutput").ap()

    xn_scr = nc.dram_tensor("xn_scr", [TOK, D], BF16).ap()
    xn_buf = nc.dram_tensor("xn_buf", [XNROWS, D], BF16).ap()
    y_buf = nc.dram_tensor("y_buf", [NBUF + 1, D], BF16).ap()
    part_scr = nc.dram_tensor("part_scr", [TOK, D], F32).ap()
    cc_in = nc.dram_tensor("cc_in", [1, NE], F32)
    cc_out = nc.dram_tensor("cc_out", [NCORES, NE], F32)

    st = ExitStack()

    def sb(name, shape, dt):
        return st.enter_context(nc.sbuf_tensor(name, shape, dt))

    def psf(name):
        return st.enter_context(nc.psum_tensor(name, [128, 512], F32))

    maskut = sb("maskut_s", [128, 128], BF16)
    urank = sb("urank_s", [128, 128], BF16)
    ones128 = sb("ones128b", [128, 128], BF16)
    ones1x128f = sb("ones1x128f_s", [1, 128], F32)
    iota8 = sb("iota8_s", [128, NE], F32)
    eoff = sb("eoff_s", [128, NE], F32)
    iota128 = sb("iota128_s", [128, 1], F32)
    prevmask = sb("prevmask_s", [NCORES, 1], F32)
    epsz = sb("epsz_s", [128, 2], F32)

    msk = sb("msk", [128, NCH * NE], BF16)
    m1s = sb("m1s", [128, NCH * NE], BF16)
    m2s = sb("m2s", [128, NCH * NE], BF16)
    dcol = sb("dcol", [128, NCH], F32)
    cntacc = sb("cntacc", [128, NE], F32)
    kk1 = sb("kk1", [128, NCH], F32)
    kk2 = sb("kk2", [128, NCH], F32)
    g1k = sb("g1k", [128, NCH], F32)
    g2k = sb("g2k", [128, NCH], F32)
    g1s = sb("g1s", [128, NCH], F32)
    g2tmp = sb("g2tmp", [128, NCH], F32)
    dscat = sb("dscat", [128, 2 * NCH], U32)
    ygat = sb("ygat", [128, 2 * NCH], U32)
    cnt8 = sb("cnt8", [NE, 1], F32)
    cntbf = sb("cntbf", [128, NE], BF16)
    allcnt = sb("allcnt", [NCORES, NE], F32)
    base_row = sb("base_row", [1, NE], F32)
    thr = sb("thr", [128, NE], F32)
    thr2 = sb("thr2", [128, NE], F32)
    baseprev = sb("baseprev", [1, NE], F32)
    rp_sb = sb("rp_sb", [128, NE], F32)
    k8_sb = sb("k8_sb", [128, NE], F32)
    mkj = [sb(f"mkj{j}", [128, NE], F32) for j in range(2)]
    uj = [sb(f"uj{j}", [128, NE], F32) for j in range(2)]
    scol = [sb(f"scol{j}", [128, 1], F32) for j in range(2)]
    kcol = [sb(f"kcol{j}", [128, 1], F32) for j in range(2)]
    tcol = [sb(f"tcol{j}", [128, 1], F32) for j in range(2)]
    wcol = [sb(f"wcol{j}", [128, 1], F32) for j in range(2)]
    dcol2 = [sb(f"dcol2{j}", [128, 1], F32) for j in range(2)]
    dumpc = [sb(f"dumpc{j}", [128, 1], F32) for j in range(2)]
    b1_sb = sb("b1_sb", [128, NE * 32], F32)
    b2e = sb("b2e", [1, 1024], BF16)
    fence_v = sb("fence_v", [128, 1], F32)
    fence_a = sb("fence_a", [128, 1], F32)

    p1 = ExitStack()

    def sb1(name, shape, dt):
        return p1.enter_context(nc.sbuf_tensor(name, shape, dt))

    wqkv_sb = sb1("wqkv_sb", [128, 8 * 3072], BF16)
    wout_sb = sb1("wout_sb", [128, 8 * 1024], BF16)
    wg_sb = sb1("wg_sb", [128, 8 * NE], BF16)
    bqk = sb1("bqk", [128, 16], F32)
    bqks = sb1("bqks", [128, 16], F32)
    bv_row = sb1("bv_row", [1, 1024], BF16)
    bout_row = sb1("bout_row", [1, 1024], BF16)
    zrow = sb1("zrow", [128, 1024], BF16)
    xfp = [sb1(f"xfp{i}", [128, D], F32) for i in range(9)]
    bn12 = [sb1(f"bn12_{i}", [128, 12], F32) for i in range(2)]
    mv = [sb1(f"mv{i}", [128, 2], F32) for i in range(2)]
    rstd = [sb1(f"rstd{i}", [128, 1], F32) for i in range(2)]
    negms = [sb1(f"negms{i}", [128, 1], F32) for i in range(2)]
    xn_sb = [sb1(f"xn_sb{i}", [128, D], BF16) for i in range(2)]
    xnt = [sb1(f"xnt{i}", [128, 8 * 512], BF16) for i in range(2)]
    qf = sb1("qf", [128, 8 * 512], BF16)
    kf = sb1("kf", [128, 8 * 512], BF16)
    ef = [sb1(f"ef{i}", [128, 512], F32) for i in range(2)]
    rf = [sb1(f"rf{i}", [128, 512], F32) for i in range(2)]
    tf = [sb1(f"tf{i}", [128, 512], F32) for i in range(2)]
    vsb = [sb1(f"vsb{i}", [128, 16 * 65], BF16) for i in range(4)]
    ktok = [sb1(f"ktok{i}", [128, 1024], BF16) for i in range(4)]
    smt = [sb1(f"smt{i}", [128, 256], BF16) for i in range(2)]
    s0f = sb1("s0f", [128, 8 * 130], F32)
    s0b = sb1("s0b", [128, 8 * 130], BF16)
    dena = [sb1(f"dena{i}", [128, 1], F32) for i in range(4)]
    attn_sb = [sb1(f"attn_sb{i}", [128, 1024], BF16) for i in range(2)]
    attnt = [sb1(f"attnt{i}", [128, 1024], BF16) for i in range(2)]
    part_sb = [sb1(f"part_sb{i}", [128, D], F32) for i in range(2)]
    lgt = [sb1(f"lgt{i}", [128, 32], F32) for i in range(2)]
    mx8 = sb1("mx8", [128, NE], F32)
    idxu = sb1("idxu", [128, NE], U32)
    idxf = sb1("idxf", [128, NE], F32)
    p1.close()

    p2 = ExitStack()

    def sb2(name, shape, dt):
        return p2.enter_context(nc.sbuf_tensor(name, shape, dt))

    xnc = [sb2(f"xnc{i}", [128, D], BF16) for i in range(2)]
    w1_sb = sb2("w1_sb", [128, 8 * HID], BF16)
    w2_sb = sb2("w2_sb", [128, 32 * 1024], BF16)
    tok2 = [sb2(f"tok2_{i}", [128, D], BF16) for i in range(4)]
    xngt = [sb2(f"xngt{i}", [128, 8 * GROWS], BF16) for i in range(2)]
    ht_sb = sb2("ht_sb", [128, 32 * GROWS], BF16)
    y_sb = [sb2(f"y_sb{i}", [128, 1024], BF16) for i in range(2)]
    p2.close()

    p3 = ExitStack()

    def sb3(name, shape, dt):
        return p3.enter_context(nc.sbuf_tensor(name, shape, dt))

    pt3 = [sb3(f"pt3_{i}", [128, D], F32) for i in range(2)]
    yg0 = [sb3(f"yg0_{i}", [128, D], BF16) for i in range(2)]
    yg1 = [sb3(f"yg1_{i}", [128, D], BF16) for i in range(2)]
    t3 = [sb3(f"t3_{i}", [128, D], F32) for i in range(2)]
    o3 = [sb3(f"o3_{i}", [128, D], F32) for i in range(2)]
    p3.close()

    pb = [psf(f"pb{i}") for i in range(8)]

    sems = {}
    slot_sems = {}
    for base, n in [("sLxA", 9), ("sTxnA", 2), ("sTkA", 4), ("sTatA", 2),
                    ("sPwA", 2), ("sXnwA", 2), ("sLxn2A", 2), ("sScatA", 2),
                    ("sRowA", 2), ("sTrsA", 2), ("sYwA", 2), ("sYgA", 4),
                    ("sLptA", 2), ("sOwA", 2)]:
        slot_sems[base] = [st.enter_context(nc.semaphore(f"{base}{i}")) for i in range(n)]
    sLxA = slot_sems["sLxA"]; sTxnA = slot_sems["sTxnA"]; sTkA = slot_sems["sTkA"]
    sTatA = slot_sems["sTatA"]; sPwA = slot_sems["sPwA"]; sXnwA = slot_sems["sXnwA"]
    sLxn2A = slot_sems["sLxn2A"]; sScatA = slot_sems["sScatA"]; sRowA = slot_sems["sRowA"]
    sTrsA = slot_sems["sTrsA"]; sYwA = slot_sems["sYwA"]; sYgA = slot_sems["sYgA"]
    sLptA = slot_sems["sLptA"]; sOwA = slot_sems["sOwA"]
    for name in ["sPreG", "sPreS", "sPreA", "sPreGm", "sMemB",
                 "sStat", "sBnA", "sRst",
                 "sQA", "sFeA", "sFeV", "sQB", "sVe", "sLgV",
                 "sS", "sSmV", "sNum", "sStV", "sDiv", "sPj", "sRes",
                 "sRout", "sKVp",
                 "sCntV1", "sCntP", "sCntV2", "sCCw", "sCC", "sCCl",
                 "sBase1", "sBase2", "sBase3", "sBaseV",
                 "sRkP", "sRkV", "sDestV", "sGate",
                 "sW1", "sW2", "sH", "sGel", "sY", "sYe",
                 "sP3b", "sDbg", "sP1V", "sP1A", "sDivD", "sVeD", "sRcp"]:

        sems[name] = st.enter_context(nc.semaphore(name))
    S = type("S", (), sems)

    NPREG = 8 + 8 + 8 + 1 + 1
    NPRES = 9 + 16
    NPREA = NE * 32
    NMEMB = NBUF // 128 + 1
    NTR_G = 32
    NGG = NE * NGRP

    def wl_after_w1(e):
        return 16 * (NPREG + e * 41 + 8)

    def wl_after_w2(e):
        return 16 * (NPREG + e * 41 + 41)

    with nc.Block() as block:

        @block.sync
        def _(sync):
            for dst, src in [(maskut, maskut_in), (urank, urank_in), (ones128, ones128_in),
                             (ones1x128f, ones1x128f_in), (iota8, iota8_in), (eoff, eoff_in),
                             (iota128, iota128_in), (prevmask, prevmask_in),
                             (epsz, epsz_in)]:
                sync.dma_start(out=dst[:], in_=src).then_inc(S.sPreS, 16)
            for ct in range(16):
                sync.dma_start(out=bqk[:, ct:ct + 1],
                               in_=bqkv_in[ct * 128:(ct + 1) * 128].rearrange("(a b) -> a b", b=1)
                               ).then_inc(S.sPreS, 16)

            def _pstores(sc_):
                for cc_ in range(4):
                    c_ = sc_ * 4 + cc_
                    sync.wait_ge(S.sRes, c_ + 1)
                    sync.dma_start(out=part_scr[c_ * 128:(c_ + 1) * 128, :],
                                   in_=part_sb[c_ % 2][:]).then_inc(sPwA[c_ % 2], 16)
                    if stage < 3:
                        sync.dma_start(out=out_dram[c_ * 128:(c_ + 1) * 128, :],
                                       in_=part_sb[c_ % 2][:]).then_inc(sOwA[c_ % 2], 16)

            for sc in range(n_sc if sublvl >= 2 else 0):
                for cc in range(4):
                    c = sc * 4 + cc
                    if c >= 9 and sublvl >= 7:
                        sync.wait_ge(S.sRes, c - 8)
                    sync.dma_start(out=xfp[c % 9][:], in_=x_in[c * 128:(c + 1) * 128, :]
                                   ).then_inc(sLxA[c % 9], 16)
                for cc in range(4):
                    c = sc * 4 + cc
                    sync.wait_ge(S.sStat, c + 1)
                    if sc >= 2 and cc == 0:
                        sync.wait_ge(S.sQB, 4 * (sc - 1))
                    for k in range(8):
                        sync.dma_start(
                            out=xnt[sc % 2][:, k * 512 + cc * 128: k * 512 + cc * 128 + 128],
                            in_=xn_sb[c % 2][:, k * 128:(k + 1) * 128],
                            transpose=True).then_inc(sTxnA[c % 2], 16)
                    sync.dma_start(out=xn_scr[c * 128:(c + 1) * 128, :], in_=xn_sb[c % 2][:]
                                   ).then_inc(sXnwA[c % 2], 16)
                if sublvl >= 7 and sc >= 1:
                    _pstores(sc - 1)
            if sublvl >= 7 and n_sc >= 1:
                _pstores(n_sc - 1)

            for c in (range(NCH) if stage >= 2 else []):
                if c == 0:
                    sync.wait_ge(sXnwA[0], 16 * 32)
                    sync.wait_ge(sXnwA[1], 16 * 32)
                    sync.wait_ge(S.sRes, NCH)
                    sync.wait_ge(S.sP1V, 1)
                    sync.wait_ge(S.sP1A, 1)
                if c >= 2:
                    sync.wait_ge(sScatA[c % 2], 32 * ((c - 2) // 2 + 1))
                sync.dma_start(out=xnc[c % 2][:], in_=xn_scr[c * 128:(c + 1) * 128, :]
                               ).then_inc(sLxn2A[c % 2], 16)

            for e in (range(NE) if stage >= 3 else []):
                for g in range(NGRP):
                    gg = e * NGRP + g
                    r0 = e * CAP_PAD + g * GROWS
                    if gg == 0:
                        sync.wait_ge(sScatA[0], 32 * 32)
                        sync.wait_ge(sScatA[1], 32 * 32)
                        sync.wait_ge(S.sP1V, 1)
                        sync.wait_ge(S.sP1A, 1)
                        sync.wait_ge(S.sPreGm, 2)
                        sync.wait_ge(sPwA[0], 16 * 32)
                        sync.wait_ge(sPwA[1], 16 * 32)
                    if gg >= 1:
                        sync.wait_ge(sTrsA[(gg - 1) % 2], 16 * NTR_G * ((gg - 1) // 2 + 1))
                    for i in range(4):
                        sync.dma_start(out=tok2[i][:],
                                       in_=xn_buf[r0 + i * 128: r0 + (i + 1) * 128, :]
                                       ).then_inc(sRowA[gg % 2], 16)
                    sync.wait_ge(sRowA[gg % 2], 16 * 4 * (gg // 2 + 1))
                    if gg >= 2:
                        sync.wait_ge(S.sH, 32 * (gg - 1))
                    for i in range(4):
                        for k in range(8):
                            sync.dma_start(
                                out=xngt[gg % 2][:, k * GROWS + i * 128: k * GROWS + i * 128 + 128],
                                in_=tok2[i][:, k * 128:(k + 1) * 128],
                                transpose=True).then_inc(sTrsA[gg % 2], 16)
                    if gg >= 1:
                        pg = gg - 1
                        pe_, pgr = pg // NGRP, pg % NGRP
                        pr0 = pe_ * CAP_PAD + pgr * GROWS
                        for sub in range(4):
                            ss = pg * 4 + sub
                            sync.wait_ge(S.sYe, ss + 1)
                            sync.dma_start(out=y_buf[pr0 + sub * 128: pr0 + (sub + 1) * 128, :],
                                           in_=y_sb[sub % 2][:]).then_inc(sYwA[sub % 2], 16)
            if stage >= 2:
                sync.wait_ge(S.sDestV, NCH)
                sync.wait_ge(S.sGate, 2)
                sync.wait_ge(S.sCCl, 16)
                sync.wait_ge(S.sBaseV, 1)
                sync.dma_start(out=dbg_dscat, in_=dscat[:]).then_inc(S.sDbg, 16)
                sync.dma_start(out=dbg_ygat, in_=ygat[:]).then_inc(S.sDbg, 16)
                sync.dma_start(out=dbg_g1k, in_=g1k[:]).then_inc(S.sDbg, 16)
                sync.dma_start(out=dbg_g2k, in_=g2k[:]).then_inc(S.sDbg, 16)
                sync.dma_start(out=dbg_allcnt, in_=allcnt[:]).then_inc(S.sDbg, 16)
                sync.dma_start(out=dbg_thr, in_=thr2[:]).then_inc(S.sDbg, 16)
            if stage >= 3:
                pg = NGG - 1
                pe_, pgr = pg // NGRP, pg % NGRP
                pr0 = pe_ * CAP_PAD + pgr * GROWS
                for sub in range(4):
                    ss = pg * 4 + sub
                    sync.wait_ge(S.sYe, ss + 1)
                    sync.dma_start(out=y_buf[pr0 + sub * 128: pr0 + (sub + 1) * 128, :],
                                   in_=y_sb[sub % 2][:]).then_inc(sYwA[sub % 2], 16)

            for c in (range(NCH) if stage >= 3 else []):
                if c == 0:
                    sync.wait_ge(sPwA[0], 16 * 32)
                    sync.wait_ge(sPwA[1], 16 * 32)
                    sync.wait_ge(S.sYe, 4 * NGG)
                if c >= 2:
                    sync.wait_ge(S.sP3b, c - 1)
                sync.dma_start(out=pt3[c % 2][:], in_=part_scr[c * 128:(c + 1) * 128, :]
                               ).then_inc(sLptA[c % 2], 16)
                sync.wait_ge(S.sP3b, c + 1)
                sync.dma_start(out=out_dram[c * 128:(c + 1) * 128, :], in_=o3[c % 2][:]
                               ).then_inc(sOwA[c % 2], 16)
            if sublvl >= 7:
                sync.wait_ge(sOwA[0], 16 * 2 * n_sc)
                sync.wait_ge(sOwA[1], 16 * 2 * n_sc)
                if stage >= 2:
                    sync.wait_ge(S.sDbg, 16 * 6)
            else:
                sync.wait_ge(S.sPreGm, 1)
                if os.environ.get("KK_NOMEMSET", "0") != "1":
                    sync.wait_ge(S.sMemB, 16 * NMEMB)
                sync.wait_ge(S.sPreG, 16 * NPREG)
                sync.wait_ge(S.sPreA, 16 * NPREA)
                if sublvl >= 2:
                    sync.wait_ge(sXnwA[0], 16 * 2 * n_sc)
                    sync.wait_ge(sXnwA[1], 16 * 2 * n_sc)
                if sublvl == 21:
                    sync.wait_ge(S.sQA, int(os.environ.get("KK_CT", "16")) * n_sc)
                if sublvl == 22:
                    sync.wait_ge(S.sFeA, 16 * n_sc)
                if sublvl >= 3 and sublvl not in (21, 22):
                    sync.wait_ge(S.sFeV, 16 * n_sc)
                if sublvl >= 4:
                    sync.wait_ge(S.sVe, 4 * n_sc)
                    sync.wait_ge(S.sLgV, n_sc)
                if sublvl >= 5:
                    sync.wait_ge(S.sDiv, 32 * n_sc)
                if sublvl >= 6:
                    sync.wait_ge(S.sRout, 4 * n_sc)

        @block.gpsimd
        def _(gpsimd):
            for k in range(8):
                gpsimd.dma_start(out=wqkv_sb[:, k * 3072:(k + 1) * 3072],
                                 in_=wqkv_in[k * 128:(k + 1) * 128, :]).then_inc(S.sPreG, 16)
            for k in range(8):
                gpsimd.dma_start(out=wout_sb[:, k * 1024:(k + 1) * 1024],
                                 in_=wout_in[k * 128:(k + 1) * 128, :]).then_inc(S.sPreG, 16)
            for k in range(8):
                gpsimd.dma_start(out=wg_sb[:, k * NE:(k + 1) * NE],
                                 in_=wg_in[k * 128:(k + 1) * 128, :]).then_inc(S.sPreG, 16)
            gpsimd.dma_start(out=bv_row[:], in_=bqkv_in[2048:3072].rearrange("(a b) -> a b", a=1)
                             ).then_inc(S.sPreG, 16)
            gpsimd.dma_start(out=bout_row[:], in_=bout_in.rearrange("(a b) -> a b", a=1)
                             ).then_inc(S.sPreG, 16)
            gpsimd.memset(zrow[:], 0.0)
            gpsimd.memset(s0f[:], 0.0)
            gpsimd.memset(s0b[:], 0.0).then_inc(S.sPreGm, 1)
            gpsimd.wait_ge(S.sPreGm, 1)
            if os.environ.get("KK_NOMEMSET", "0") == "1":
                pass
            else:
                for i in range(NBUF // 128):
                    gpsimd.dma_start(out=xn_buf[i * 128:(i + 1) * 128, :], in_=zrow[:]
                                     ).then_inc(S.sMemB, 16)
                gpsimd.dma_start(out=y_buf[YZERO:YZERO + 1, :], in_=zrow[0:1, 0:1024]
                                 ).then_inc(S.sMemB, 16)

            if stage < 2:
                return
            gpsimd.wait_ge(S.sCntV2, 1)
            gpsimd.dma_start(out=cc_in.ap().rearrange("a b -> b a"), in_=cnt8[:]
                             ).then_inc(S.sCCw, 16)
            gpsimd.wait_ge(S.sCCw, 16)
            gpsimd.collective_compute(
                "AllGather", ALU.bypass,
                replica_groups=[list(range(NCORES))],
                ins=[cc_in.ap().opt()], outs=[cc_out.ap().opt()],
            ).then_inc(S.sCC)
            gpsimd.wait_ge(S.sCC, 1)
            gpsimd.dma_start(out=allcnt[:], in_=cc_out.ap()).then_inc(S.sCCl, 16)

            gpsimd.wait_ge(S.sMemB, 16 * NMEMB)
            for c in range(NCH):
                gpsimd.wait_ge(S.sDestV, c + 1)
                gpsimd.wait_ge(sLxn2A[c % 2], 16 * (c // 2 + 1))
                for j in range(2):
                    gpsimd.indirect_dma_start(
                        out=xn_buf,
                        out_offset=IndirectOffsetOnAxis(ap=dscat[:, 2 * c + j: 2 * c + j + 1], axis=0),
                        in_=xnc[c % 2][:], in_offset=None,
                    ).then_inc(sScatA[c % 2], 16)

            for e in (range(NE) if stage >= 3 else []):
                if e == 0:
                    gpsimd.wait_ge(sScatA[0], 32 * 32)
                    gpsimd.wait_ge(sScatA[1], 32 * 32)
                    gpsimd.wait_ge(S.sP1V, 1)
                    gpsimd.wait_ge(S.sP1A, 1)
                    gpsimd.wait_ge(sPwA[0], 16 * 32)
                    gpsimd.wait_ge(sPwA[1], 16 * 32)
                    gpsimd.wait_ge(sXnwA[0], 16 * 32)
                    gpsimd.wait_ge(sXnwA[1], 16 * 32)
                    gpsimd.memset(xngt[0][:], 0.0)
                    gpsimd.memset(xngt[1][:], 0.0).then_inc(S.sPreGm, 1)
                else:
                    gpsimd.wait_ge(S.sH, 32 * NGRP * e)
                for k in range(8):
                    gpsimd.dma_start(out=w1_sb[:, k * HID:(k + 1) * HID],
                                     in_=w1_in[e, k * 128:(k + 1) * 128, :]).then_inc(S.sW1, 16)
                if e > 0:
                    gpsimd.wait_ge(S.sY, 4 * NGRP * e)
                for j in range(32):
                    gpsimd.dma_start(out=w2_sb[:, j * 1024:(j + 1) * 1024],
                                     in_=w2_in[e, j * 128:(j + 1) * 128, :]).then_inc(S.sW2, 16)
                gpsimd.dma_start(out=b2e[:],
                                 in_=b2_in[e, :].rearrange("(a b) -> a b", a=1)).then_inc(S.sW2, 16)

            if stage < 3:
                return
            gpsimd.wait_ge(sYwA[0], 16 * 2 * NGG)
            gpsimd.wait_ge(sYwA[1], 16 * 2 * NGG)
            for c in range(NCH):
                if c >= 2:
                    gpsimd.wait_ge(S.sP3b, c - 1)
                gpsimd.indirect_dma_start(
                    out=yg0[c % 2][:], out_offset=None, in_=y_buf,
                    in_offset=IndirectOffsetOnAxis(ap=ygat[:, 2 * c: 2 * c + 1], axis=0),
                ).then_inc(sYgA[c % 2], 16)
                gpsimd.indirect_dma_start(
                    out=yg1[c % 2][:], out_offset=None, in_=y_buf,
                    in_offset=IndirectOffsetOnAxis(ap=ygat[:, 2 * c + 1: 2 * c + 2], axis=0),
                ).then_inc(sYgA[2 + c % 2], 16)

        @block.tensor
        def _(tensor):
            tensor.wait_ge(S.sPreG, 16 * NPREG)
            tensor.wait_ge(S.sPreS, 16 * NPRES)
            tensor.wait_ge(S.sPreGm, 1)

            for sc in range(n_sc if (sublvl >= 3 or sublvl in (21, 22)) else 0):
                tensor.wait_ge(sTxnA[0], 16 * 8 * 2 * (sc + 1))
                tensor.wait_ge(sTxnA[1], 16 * 8 * 2 * (sc + 1))
                _CT = int(os.environ.get("KK_CT", "16"))
                _K = int(os.environ.get("KK_K", "8"))
                for ct in range(_CT):
                    bank = ct % 2
                    gt = sc * 16 + ct
                    if gt >= 2 and sublvl != 21:
                        tensor.wait_ge(S.sFeA, gt - 1)
                    if ct < 2 and sc >= 1:
                        tensor.wait_ge(S.sSmV, 2 * 32 * sc)
                    _OPS = os.environ.get("KK_OPS", "orig")
                    for k in range(8):
                        i_ = tensor.matmul(pb[bank][:, 0:512],
                                           wqkv_sb[:, k * 3072 + ct * 128: k * 3072 + ct * 128 + 128],
                                           xnt[sc % 2][:, k * 512:(k + 1) * 512],
                                           start=(k == 0), stop=(k == 7))
                    i_.then_inc(S.sQA, 1)
                _SEQB = os.environ.get("KK_SEQB", "0") == "1"
                for cc in range(4 if sublvl >= 4 else 0):
                    c = sc * 4 + cc
                    if c >= 1:
                        tensor.wait_ge(S.sVe, c)
                    if sc >= 1 and cc == 0:
                        tensor.wait_ge(S.sRes, 4 * sc)
                        tensor.wait_ge(S.sLgV, sc)
                        tensor.wait_ge(S.sSmV, 2 * 32 * sc)
                        tensor.wait_ge(S.sStV, 32 * sc)
                    if _SEQB:
                        for cg in range(2):
                            for k in range(8):
                                lhs = xnt[sc % 2][:, k * 512 + cc * 128: k * 512 + cc * 128 + 128]
                                tensor.matmul(pb[2 + cg][:, 0:512], lhs,
                                              wqkv_sb[:, k * 3072 + 2048 + cg * 512: k * 3072 + 2048 + (cg + 1) * 512],
                                              start=(k == 0), stop=False)
                            tensor.matmul(pb[2 + cg][:, 0:512], ones128[0:1, :],
                                          bv_row[0:1, cg * 512:(cg + 1) * 512],
                                          start=False, stop=True)
                        for k in range(8):
                            lhs = xnt[sc % 2][:, k * 512 + cc * 128: k * 512 + cc * 128 + 128]
                            i_ = tensor.matmul(pb[4][:, cc * 8: cc * 8 + 8], lhs,
                                               wg_sb[:, k * NE:(k + 1) * NE],
                                               start=(cc == 0 and k == 0), stop=(cc == 3 and k == 7))
                        i_.then_inc(S.sQB, 1)
                    else:
                        for k in range(8):
                            lhs = xnt[sc % 2][:, k * 512 + cc * 128: k * 512 + cc * 128 + 128]
                            for cg in range(2):
                                tensor.matmul(pb[2 + cg][:, 0:512], lhs,
                                              wqkv_sb[:, k * 3072 + 2048 + cg * 512: k * 3072 + 2048 + (cg + 1) * 512],
                                              start=(k == 0), stop=False)
                            tensor.matmul(pb[4][:, cc * 8: cc * 8 + 8], lhs,
                                          wg_sb[:, k * NE:(k + 1) * NE],
                                          start=(cc == 0 and k == 0), stop=(cc == 3 and k == 7))
                        for cg in range(2):
                            i_ = tensor.matmul(pb[2 + cg][:, 0:512], ones128[0:1, :],
                                               bv_row[0:1, cg * 512:(cg + 1) * 512],
                                               start=False, stop=True)
                        i_.then_inc(S.sQB, 1)

                for cc in range(4 if sublvl >= 5 else 0):
                    c = sc * 4 + cc
                    pos = c % CPB
                    tensor.wait_ge(S.sFeV, sc * 16 + 16)
                    tensor.wait_ge(S.sVeD, c + 1)
                    tensor.wait_ge(sTkA[cc], 16 * 8 * (sc + 1))
                    if cc == 0:
                        tensor.wait_ge(S.sVe, 4 * sc + 4)
                        tensor.wait_ge(S.sLgV, sc + 1)
                    for p in range(8):
                        gp = c * 8 + p
                        sb0 = 0 if p % 2 == 0 else 2
                        if gp >= 2:
                            tensor.wait_ge(S.sSmV, 2 * (gp - 1))
                        colq = p * 512 + cc * 128
                        tensor.matmul(pb[sb0][:, 0:128], kf[0:64, colq:colq + 128],
                                      qf[0:64, colq:colq + 128], start=True, stop=True)
                        tensor.matmul(pb[sb0 + 1][:, 0:128], kf[64:128, colq:colq + 128],
                                      qf[64:128, colq:colq + 128], start=True, stop=True
                                      ).then_inc(S.sS, 1)
                        nbank = 6 + (p % 2)
                        if p == 0:
                            tensor.wait_ge(S.sRes, c)
                        if gp >= 2:
                            tensor.wait_ge(S.sDiv, gp - 1)
                        tensor.wait_ge(S.sSmV, 2 * (gp + 1))
                        if pos > 0:
                            tensor.wait_ge(S.sStV, (c - 1) * 8 + p + 1)
                        tensor.matmul(pb[nbank][:, 0:65], smt[p % 2][:, 0:128],
                                      vsb[cc][:, (2 * p) * 65:(2 * p) * 65 + 65],
                                      start=True, stop=False)
                        i_ = tensor.matmul(pb[nbank][:, 65:130], smt[p % 2][:, 128:256],
                                           vsb[cc][:, (2 * p + 1) * 65:(2 * p + 1) * 65 + 65],
                                           start=False, stop=(pos == 0))
                        if pos > 0:
                            i_ = tensor.matmul(pb[nbank][:, 0:130],
                                               qf[:, colq:colq + 128],
                                               s0b[:, p * 130:(p + 1) * 130],
                                               start=False, stop=True)
                        i_.then_inc(S.sNum, 1)
                        if os.environ.get("KK_ST7", "0") == "1":
                            if gp >= 1:
                                tensor.wait_ge(S.sStV, gp)
                            tensor.matmul(pb[7][:, 192:322],
                                          ktok[cc][:, p * 128:(p + 1) * 128],
                                          vsb[cc][:, (2 * p) * 65:(2 * p) * 65 + 130],
                                          start=True, stop=True).then_inc(S.sKVp, 1)
                        else:
                            kbank = 4 + (p % 2)
                            if gp >= 2:
                                tensor.wait_ge(S.sStV, gp - 1)
                            tensor.matmul(pb[kbank][:, 0:130],
                                          ktok[cc][:, p * 128:(p + 1) * 128],
                                          vsb[cc][:, (2 * p) * 65:(2 * p) * 65 + 130],
                                          start=True, stop=True).then_inc(S.sKVp, 1)
                    if sublvl < 7:
                        continue
                    tensor.wait_ge(sTatA[c % 2], 16 * 8 * (c // 2 + 1))
                    tensor.wait_ge(S.sDiv, 8 * (c + 1))
                    for k in range(8):
                        for cg in range(2):
                            tensor.matmul(pb[6 + cg][:, 0:512],
                                          attnt[c % 2][:, k * 128:(k + 1) * 128],
                                          wout_sb[:, k * 1024 + cg * 512: k * 1024 + (cg + 1) * 512],
                                          start=(k == 0), stop=False)
                    for cg in range(2):
                        i_ = tensor.matmul(pb[6 + cg][:, 0:512], ones128[0:1, :],
                                           bout_row[0:1, cg * 512:(cg + 1) * 512],
                                           start=False, stop=True)
                    i_.then_inc(S.sPj, 1)

            if stage < 2:
                return
            tensor.wait_ge(S.sRout, NCH)
            tensor.wait_ge(S.sCntV1, 1)
            tensor.wait_ge(S.sDiv, 8 * NCH)
            tensor.wait_ge(S.sRes, NCH)
            tensor.wait_ge(S.sLgV, NSC)
            tensor.wait_ge(S.sStV, 8 * NCH)
            tensor.matmul(pb[2][0:NE, 0:1], cntbf[:], ones128[:, 0:1], start=True, stop=True
                          ).then_inc(S.sCntP, 1)
            tensor.wait_ge(S.sCCl, 16)
            tensor.matmul(pb[3][0:1, 0:NE], prevmask[:], allcnt[:], start=True, stop=True
                          ).then_inc(S.sBase1, 1)
            tensor.wait_ge(S.sBase2, 1)
            tensor.matmul(pb[4][:, 0:NE], ones1x128f[:], base_row[:], start=True, stop=True
                          ).then_inc(S.sBase3, 1)
            tensor.wait_ge(S.sBaseV, 1)
            for c in range(NCH):
                bank = c % 2
                if c >= 2:
                    tensor.wait_ge(S.sDestV, c - 1)
                if c > 0:
                    tensor.wait_ge(S.sRkV, c + 1)
                    tensor.matmul(pb[bank][:, 0:NE], urank[:], msk[:, c * NE:(c + 1) * NE],
                                  start=True, stop=False)
                    tensor.matmul(pb[bank][:, 0:NE], ones1x128f[:], baseprev[:],
                                  start=False, stop=True)
                else:
                    tensor.matmul(pb[bank][:, 0:NE], urank[:], msk[:, c * NE:(c + 1) * NE],
                                  start=True, stop=True)
                tensor.matmul(pb[6][0:1, (c % 2) * NE:(c % 2) * NE + NE], ones128[:, 0:1],
                              msk[:, c * NE:(c + 1) * NE],
                              start=True, stop=True).then_inc(S.sRkP, 1)

            if stage < 3:
                return
            tensor.wait_ge(S.sDestV, NCH)
            tensor.wait_ge(S.sCntV2, 1)
            tensor.wait_ge(S.sBase2, 1)
            tensor.wait_ge(S.sBaseV, 1)
            tensor.wait_ge(S.sRkV, NCH)
            for e in range(NE):
                for g in range(NGRP):
                    gg = e * NGRP + g
                    tensor.wait_ge(sTrsA[gg % 2], 16 * NTR_G * (gg // 2 + 1))
                    if gg == 0:
                        tensor.wait_ge(S.sW1, 16 * 8)
                    for ht in range(32):
                        hh = gg * 32 + ht
                        bank = ht % 3
                        if hh >= 3:
                            tensor.wait_ge(S.sGel, hh - 2)
                        if g == 0 and ht == 0 and e > 0:
                            tensor.wait_ge(S.sW1, 16 * 8 * (e + 1))
                        for k in range(8):
                            i_ = tensor.matmul(pb[bank][:, 0:GROWS],
                                               w1_sb[:, k * HID + ht * 128: k * HID + ht * 128 + 128],
                                               xngt[gg % 2][:, k * GROWS:(k + 1) * GROWS],
                                               start=(k == 0), stop=(k == 7))
                        i_.then_inc(S.sH, 1)
                    tensor.wait_ge(S.sGel, 32 * (gg + 1))
                    if g == 0:
                        tensor.wait_ge(S.sW2, 16 * 33 * (e + 1))
                    for sub in range(4):
                        ss = gg * 4 + sub
                        ybk = 3 + (sub % 2) * 2
                        if ss >= 2:
                            tensor.wait_ge(S.sYe, ss - 1)
                        for ht in range(32):
                            for cg in range(2):
                                tensor.matmul(pb[ybk + cg][:, 0:512],
                                              ht_sb[:, ht * GROWS + sub * 128: ht * GROWS + sub * 128 + 128],
                                              w2_sb[:, ht * 1024 + cg * 512: ht * 1024 + (cg + 1) * 512],
                                              start=(ht == 0), stop=False)
                        for cg in range(2):
                            i_ = tensor.matmul(pb[ybk + cg][:, 0:512], ones128[0:1, :],
                                               b2e[0:1, cg * 512:(cg + 1) * 512],
                                               start=False, stop=True)
                        i_.then_inc(S.sY, 1)

        @block.vector
        def _(vector):
            vector.wait_ge(S.sPreS, 16 * NPRES)
            vector.wait_ge(S.sPreGm, 1)
            for i in range(4):
                vector.tensor_copy(
                    vsb[i][:].rearrange("p (h w) -> p h w", w=65)[:, :, 64:65],
                    ones128[:, 0:16].rearrange("p (h w) -> p h w", w=1))
            vector.tensor_scalar(bqks[:, 0:8], bqk[:, 0:8], QSCALE, None, ALU.mult)
            vector.tensor_scalar(bqks[:, 8:16], bqk[:, 8:16], 1.0, None, ALU.mult)
            vector.memset(cntacc[:], 0.0)
            vector.drain()

            def _stats(sc_):
                for cc_ in range(4):
                    c_ = sc_ * 4 + cc_
                    vector.wait_ge(sLxA[c_ % 9], 16 * (c_ // 9 + 1))
                    if c_ >= 2:
                        vector.wait_ge(S.sRst, c_ - 1)
                    vector.bn_stats(bn12[c_ % 2][:, 0:6], xfp[c_ % 9][:, 0:512])
                    vector.bn_stats(bn12[c_ % 2][:, 6:12], xfp[c_ % 9][:, 512:1024])
                    vector.drain()
                    vector.bn_aggr(mv[c_ % 2][:], bn12[c_ % 2][:]).then_inc(S.sBnA, 1)
                    vector.wait_ge(S.sRst, c_ + 1)
                    vector.tensor_scalar(negms[c_ % 2][:], mv[c_ % 2][:, 0:1], rstd[c_ % 2][:],
                                         -1.0, ALU.mult, ALU.mult)
                    vector.drain()
                    if c_ >= 2:
                        vector.wait_ge(sTxnA[c_ % 2], 16 * 8 * ((c_ - 2) // 2 + 1))
                        vector.wait_ge(sXnwA[c_ % 2], 16 * ((c_ - 2) // 2 + 1))
                    vector.tensor_scalar(xn_sb[c_ % 2][:], xfp[c_ % 9][:], rstd[c_ % 2][:],
                                         negms[c_ % 2][:], ALU.mult, ALU.add).then_inc(S.sStat, 1)

            if sublvl >= 2:
                _stats(0)
            for sc in range(n_sc if sublvl >= 2 else 0):
                for ct in range(16 if (sublvl >= 3 and sublvl != 21 and sublvl != 22) else 0):
                    gt = sc * 16 + ct
                    dst = qf if ct < 8 else kf
                    off = (ct % 8) * 512
                    if ct == 8 and sc >= 1 and sublvl >= 5:
                        for cc_ in range(4):
                            vector.wait_ge(sTkA[cc_], 16 * 8 * sc)
                    vector.wait_ge(S.sFeA, gt + 1)
                    vector.tensor_scalar(tf[ct % 2][:], ef[ct % 2][:], 1.0, 1e-4,
                                         ALU.min, ALU.add)
                    vector.drain()
                    vector.tensor_tensor(dst[:, off:off + 512], tf[ct % 2][:], rf[ct % 2][:],
                                         ALU.add).then_inc(S.sFeV, 1)
                if sublvl >= 4:
                    vector.wait_ge(S.sQB, 4 * (sc + 1))
                    vector.tensor_copy(lgt[sc % 2][:], pb[4][:, 0:32]).then_inc(S.sLgV, 1)
                    vector.drain()
                if sublvl >= 2 and sc + 1 < n_sc:
                    _stats(sc + 1)
                for cc in range(4 if sublvl >= 5 else 0):
                    c = sc * 4 + cc
                    pos = c % CPB
                    for p in range(8):
                        gp = c * 8 + p
                        sb0 = 0 if p % 2 == 0 else 2
                        vector.wait_ge(S.sS, gp + 1)
                        if gp >= 2:
                            vector.wait_ge(S.sNum, gp - 1)
                        vector.tensor_tensor(smt[p % 2][:, 0:128], pb[sb0][:, 0:128], maskut[:],
                                             ALU.mult).then_inc(S.sSmV, 1)
                        vector.tensor_tensor(smt[p % 2][:, 128:256], pb[sb0 + 1][:, 0:128], maskut[:],
                                             ALU.mult).then_inc(S.sSmV, 1)
                        vector.wait_ge(S.sKVp, gp + 1)
                        vector.wait_ge(S.sNum, gp + 1)
                        kbank = 4 + (p % 2)
                        kcol0 = 0
                        if os.environ.get("KK_ST7", "0") == "1":
                            kbank, kcol0 = 7, 192
                        if pos == 0:
                            vector.tensor_scalar(s0f[0:64, p * 130: p * 130 + 65],
                                                 pb[kbank][0:64, kcol0:kcol0 + 65],
                                                 1.0, None, ALU.mult)
                            vector.tensor_scalar(s0f[64:128, p * 130 + 65: p * 130 + 130],
                                                 pb[kbank][64:128, kcol0 + 65:kcol0 + 130],
                                                 1.0, None, ALU.mult)
                        else:
                            vector.tensor_tensor(s0f[0:64, p * 130: p * 130 + 65],
                                                 s0f[0:64, p * 130: p * 130 + 65],
                                                 pb[kbank][0:64, kcol0:kcol0 + 65], ALU.add)
                            vector.tensor_tensor(s0f[64:128, p * 130 + 65: p * 130 + 130],
                                                 s0f[64:128, p * 130 + 65: p * 130 + 130],
                                                 pb[kbank][64:128, kcol0 + 65:kcol0 + 130],
                                                 ALU.add)
                        vector.drain()
                        vector.tensor_copy(s0b[:, p * 130:(p + 1) * 130],
                                           s0f[:, p * 130:(p + 1) * 130]).then_inc(S.sStV, 1)
                        nbank = 6 + (p % 2)
                        if gp >= 2:
                            vector.wait_ge(S.sDiv, gp - 1)
                        vector.tensor_scalar(dena[2 * (p % 2)][:], pb[nbank][:, 64:65],
                                             EPS_DEN, None, ALU.add)
                        vector.tensor_scalar(dena[2 * (p % 2) + 1][:], pb[nbank][:, 129:130],
                                             EPS_DEN, None, ALU.add)
                        vector.drain()
                        vector.reciprocal(dena[2 * (p % 2)][:], dena[2 * (p % 2)][:]
                                          ).then_inc(S.sRcp, 1)
                        vector.reciprocal(dena[2 * (p % 2) + 1][:], dena[2 * (p % 2) + 1][:]
                                          ).then_inc(S.sRcp, 1)
                    if sublvl < 7:
                        continue
                    vector.wait_ge(S.sPj, c + 1)
                    if c >= 2:
                        vector.wait_ge(sPwA[c % 2], 16 * ((c - 2) // 2 + 1))
                        if stage < 3:
                            vector.wait_ge(sOwA[c % 2], 16 * ((c - 2) // 2 + 1))
                    vector.tensor_tensor(part_sb[c % 2][:, 0:512], pb[6][:, 0:512],
                                         xfp[c % 9][:, 0:512], ALU.add)
                    vector.tensor_tensor(part_sb[c % 2][:, 512:1024], pb[7][:, 0:512],
                                         xfp[c % 9][:, 512:1024], ALU.add)
                    vector.drain()
                    vector.memset(fence_v[:], 0.0).then_inc(S.sRes, 1)
                    if sublvl < 6:
                        continue
                    lgc = lgt[sc % 2][:, cc * 8: cc * 8 + 8]
                    vector.max(mx8[:], lgc)
                    vector.drain()
                    vector.max_index(idxu[:], mx8[:], lgc)
                    vector.drain()
                    vector.tensor_copy(idxf[:], idxu[:])
                    vector.tensor_tensor(dcol[:, c:c + 1], mx8[:, 0:1], mx8[:, 1:2], ALU.subtract)
                    vector.drain()
                    vector.tensor_scalar(m1s[:, c * NE:(c + 1) * NE], iota8[:], idxf[:, 0:1],
                                         None, ALU.is_equal)
                    vector.tensor_scalar(m2s[:, c * NE:(c + 1) * NE], iota8[:], idxf[:, 1:2],
                                         None, ALU.is_equal)
                    vector.drain()
                    vector.tensor_tensor(msk[:, c * NE:(c + 1) * NE], m1s[:, c * NE:(c + 1) * NE],
                                         m2s[:, c * NE:(c + 1) * NE], ALU.add)
                    vector.drain()
                    vector.tensor_tensor(cntacc[:], cntacc[:], msk[:, c * NE:(c + 1) * NE],
                                         ALU.add).then_inc(S.sRout, 1)
                    vector.drain()

            if stage < 2:
                return
            vector.tensor_copy(cntbf[:], cntacc[:]).then_inc(S.sCntV1, 1)
            vector.wait_ge(S.sCntP, 1)
            vector.tensor_copy(cnt8[:], pb[2][0:NE, 0:1]).then_inc(S.sCntV2, 1)
            vector.wait_ge(S.sBase1, 1)
            vector.tensor_copy(base_row[:], pb[3][0:1, 0:NE]).then_inc(S.sBase2, 1)
            vector.wait_ge(S.sBase3, 1)
            vector.tensor_scalar(thr[:], pb[4][:, 0:NE], -1.0, float(CAP), ALU.mult, ALU.add)
            vector.drain()
            vector.tensor_scalar(thr2[:], thr[:], float(CAP_PAD), None, ALU.min
                                 ).then_inc(S.sBaseV, 1)
            vector.memset(baseprev[:], 0.0).then_inc(S.sRkV, 1)
            vector.drain()
            for c in range(NCH):
                bank = c % 2
                vector.wait_ge(S.sRkP, c + 1)
                if c < NCH - 1:
                    vector.tensor_tensor(baseprev[:], baseprev[:],
                                         pb[6][0:1, (c % 2) * NE:(c % 2) * NE + NE],
                                         ALU.add).then_inc(S.sRkV, 1)
                    vector.drain()
                rank_ps = pb[bank][:, 0:NE]
                vector.tensor_tensor(rp_sb[:], rank_ps, eoff[:], ALU.add)
                vector.tensor_tensor(k8_sb[:], rank_ps, thr2[:], ALU.is_lt)
                vector.drain()
                vector.tensor_tensor(mkj[0][:], m1s[:, c * NE:(c + 1) * NE], k8_sb[:], ALU.mult)
                vector.tensor_tensor(mkj[1][:], m2s[:, c * NE:(c + 1) * NE], k8_sb[:], ALU.mult)
                vector.drain()
                vector.tensor_tensor(uj[0][:], mkj[0][:], rp_sb[:], ALU.mult)
                vector.tensor_tensor(uj[1][:], mkj[1][:], rp_sb[:], ALU.mult)
                vector.drain()
                for j in range(2):
                    vector.reduce_sum(scol[j][:], uj[j][:], axis=AX.X)
                    vector.reduce_sum(kcol[j][:], mkj[j][:], axis=AX.X)
                vector.drain()
                vector.tensor_copy(kk1[:, c:c + 1], kcol[0][:])
                vector.tensor_copy(kk2[:, c:c + 1], kcol[1][:])
                for j in range(2):
                    vector.tensor_scalar(dumpc[j][:], iota128[:], 2.0,
                                         float(DUMP0 + 256 * c + j), ALU.mult, ALU.add)
                    vector.tensor_scalar(tcol[j][:], kcol[j][:], -1.0, 1.0, ALU.mult, ALU.add)
                vector.drain()
                for j in range(2):
                    vector.tensor_tensor(wcol[j][:], tcol[j][:], dumpc[j][:], ALU.mult)
                vector.drain()
                for j in range(2):
                    vector.tensor_tensor(dcol2[j][:], scol[j][:], wcol[j][:], ALU.add)
                vector.drain()
                for j in range(2):
                    vector.tensor_copy(dscat[:, 2 * c + j: 2 * c + j + 1], dcol2[j][:])
                    vector.tensor_scalar(wcol[j][:], tcol[j][:], float(YZERO), None, ALU.mult)
                vector.drain()
                for j in range(2):
                    vector.tensor_tensor(dcol2[j][:], scol[j][:], wcol[j][:], ALU.add)
                vector.drain()
                vector.tensor_copy(ygat[:, 2 * c: 2 * c + 1], dcol2[0][:])
                vector.tensor_copy(ygat[:, 2 * c + 1: 2 * c + 2], dcol2[1][:]
                                   ).then_inc(S.sDestV, 1)
                vector.drain()
            vector.wait_ge(S.sGate, 1)
            vector.tensor_tensor(g1k[:], g1s[:], kk1[:], ALU.mult)
            vector.tensor_scalar(g2tmp[:], g1s[:], -1.0, 1.0, ALU.mult, ALU.add)
            vector.drain()
            vector.tensor_tensor(g2k[:], g2tmp[:], kk2[:], ALU.mult).then_inc(S.sGate, 1)
            vector.drain()
            vector.memset(fence_v[:], 0.0).then_inc(S.sP1V, 1)

            for c in (range(NCH) if stage >= 3 else []):
                vector.wait_ge(sLptA[c % 2], 16 * (c // 2 + 1))
                vector.wait_ge(sYgA[c % 2], 16 * (c // 2 + 1))
                vector.wait_ge(sYgA[2 + c % 2], 16 * (c // 2 + 1))
                if c >= 2:
                    vector.wait_ge(sOwA[c % 2], 16 * ((c - 2) // 2 + 1))
                vector.scalar_tensor_tensor(t3[c % 2][:], yg0[c % 2][:], g1k[:, c:c + 1],
                                            pt3[c % 2][:], ALU.mult, ALU.add)
                vector.drain()
                vector.scalar_tensor_tensor(o3[c % 2][:], yg1[c % 2][:], g2k[:, c:c + 1],
                                            t3[c % 2][:], ALU.mult, ALU.add).then_inc(S.sP3b, 1)

        @block.scalar
        def _(scalar):
            for e in range(NE):
                for ht in range(32):
                    scalar.dma_start(out=b1_sb[:, e * 32 + ht: e * 32 + ht + 1],
                                     in_=b1_in[e, ht * 128:(ht + 1) * 128].rearrange("(a b) -> a b", b=1)
                                     ).then_inc(S.sPreA, 16)
            scalar.wait_ge(S.sPreS, 16 * NPRES)
            scalar.wait_ge(S.sPreG, 16 * NPREG)

            def _rstd(sc_):
                for cc_ in range(4):
                    c_ = sc_ * 4 + cc_
                    scalar.wait_ge(S.sBnA, c_ + 1)
                    if c_ >= 2:
                        scalar.wait_ge(S.sStat, c_ - 1)
                    scalar.activation(rstd[c_ % 2][:], mv[c_ % 2][:, 1:2], AF.Ln,
                                      bias=epsz[:, 0:1], scale=1.0)
                    scalar.drain()
                    scalar.activation(rstd[c_ % 2][:], rstd[c_ % 2][:], AF.Exp,
                                      bias=epsz[:, 1:2], scale=-0.5).then_inc(S.sRst, 1)

            if sublvl >= 2:
                _rstd(0)
            for sc in range(n_sc if sublvl >= 2 else 0):
                for ct in range(16 if (sublvl >= 3 or sublvl == 22) else 0):
                    gt = sc * 16 + ct
                    bank = ct % 2
                    scl = QSCALE if ct < 8 else 1.0
                    scalar.wait_ge(S.sQA, gt + 1)
                    if gt >= 2 and sublvl != 22:
                        scalar.wait_ge(S.sFeV, gt - 1)
                    scalar.activation(ef[ct % 2][:], pb[bank][:, 0:512], AF.Exp,
                                      bias=bqks[:, ct:ct + 1], scale=scl)
                    scalar.activation(rf[ct % 2][:], pb[bank][:, 0:512], AF.Relu,
                                      bias=bqks[:, ct:ct + 1], scale=scl).then_inc(S.sFeA, 1)
                for cc in range(4 if sublvl >= 4 else 0):
                    c = sc * 4 + cc
                    scalar.wait_ge(S.sQB, c + 1)
                    for h in range(16):
                        ins = pb[2 + h // 8][:, (h % 8) * 64:(h % 8) * 64 + 64]
                        i_ = scalar.activation(vsb[cc][:, h * 65: h * 65 + 64], ins, AF.Copy)
                    i_.then_inc(S.sVe, 1)
                    scalar.drain()
                    scalar.activation(fence_a[:], bqks[:, 0:1], AF.Copy).then_inc(S.sVeD, 1)
                if sublvl >= 2 and sc + 1 < n_sc:
                    _rstd(sc + 1)
                if sublvl >= 5:
                    scalar.wait_ge(S.sFeV, sc * 16 + 16)
                    for cc in range(4):
                        c = sc * 4 + cc
                        if c >= 4:
                            scalar.wait_ge(S.sKVp, 8 * (c - 3))
                        for p in range(8):
                            scalar.dma_start(out=ktok[cc][:, p * 128:(p + 1) * 128],
                                             in_=kf[:, p * 512 + cc * 128: p * 512 + cc * 128 + 128],
                                             transpose=True).then_inc(sTkA[cc], 16)
                if sublvl >= 7:
                    for cc in range(4):
                        c = sc * 4 + cc
                        if c >= 2:
                            scalar.wait_ge(sTatA[c % 2], 16 * 8 * ((c - 2) // 2 + 1))
                        for p in range(8):
                            gp = c * 8 + p
                            nbank = 6 + (p % 2)
                            scalar.wait_ge(S.sRcp, 2 * (gp + 1))
                            scalar.activation(attn_sb[c % 2][:, (2 * p) * 64:(2 * p) * 64 + 64],
                                              pb[nbank][:, 0:64], AF.Copy,
                                              scale=dena[2 * (p % 2)][:])
                            scalar.activation(attn_sb[c % 2][:, (2 * p + 1) * 64:(2 * p + 1) * 64 + 64],
                                              pb[nbank][:, 65:129], AF.Copy,
                                              scale=dena[2 * (p % 2) + 1][:]).then_inc(S.sDiv, 1)
                        scalar.drain()
                        scalar.activation(fence_a[:], bqks[:, 0:1], AF.Copy).then_inc(S.sDivD, 1)
                        scalar.wait_ge(S.sDivD, c + 1)
                        if c >= 1:
                            scalar.wait_ge(S.sPj, c)
                        for k in range(8):
                            scalar.dma_start(out=attnt[c % 2][:, k * 128:(k + 1) * 128],
                                             in_=attn_sb[c % 2][:, k * 128:(k + 1) * 128],
                                             transpose=True).then_inc(sTatA[c % 2], 16)

            if stage < 2:
                return
            scalar.wait_ge(S.sRout, NCH)
            scalar.activation(g1s[:], dcol[:], AF.Sigmoid).then_inc(S.sGate, 1)
            scalar.drain()
            scalar.activation(fence_a[:], g1s[:, 0:1], AF.Copy).then_inc(S.sP1A, 1)

            if stage < 3:
                return
            scalar.wait_ge(S.sPreA, 16 * NPREA)
            for e in range(NE):
                for g in range(NGRP):
                    gg = e * NGRP + g
                    for ht in range(32):
                        hh = gg * 32 + ht
                        bank = ht % 3
                        scalar.wait_ge(S.sH, hh + 1)
                        scalar.activation(ht_sb[:, ht * GROWS:(ht + 1) * GROWS],
                                          pb[bank][:, 0:GROWS], AF.Gelu_apprx_tanh,
                                          bias=b1_sb[:, e * 32 + ht: e * 32 + ht + 1],
                                          scale=1.0).then_inc(S.sGel, 1)
                    for sub in range(4):
                        ss = gg * 4 + sub
                        ybk = 3 + (sub % 2) * 2
                        scalar.wait_ge(S.sY, ss + 1)
                        if ss >= 2:
                            scalar.wait_ge(sYwA[sub % 2], 16 * ((ss - 2) // 2 + 1))
                        scalar.activation(y_sb[sub % 2][:, 0:512], pb[ybk][:, 0:512], AF.Copy)
                        scalar.activation(y_sb[sub % 2][:, 512:1024], pb[ybk + 1][:, 0:512],
                                          AF.Copy).then_inc(S.sYe, 1)

    st.close()
    return nc


_NC_CACHE = {}


def _consts():
    i = np.arange(128)
    maskut = (i[:, None] <= i[None, :]).astype(ml_dtypes.bfloat16)
    urank = (i[:, None] < i[None, :]).astype(ml_dtypes.bfloat16)
    ones128 = np.ones((128, 128), ml_dtypes.bfloat16)
    ones1x128f = np.ones((1, 128), np.float32)
    iota8 = np.broadcast_to(np.arange(NE, dtype=np.float32), (128, NE)).copy()
    eoff = np.broadcast_to((np.arange(NE) * CAP_PAD).astype(np.float32), (128, NE)).copy()
    iota128 = np.arange(128, dtype=np.float32).reshape(128, 1)
    epsz = np.zeros((128, 2), np.float32); epsz[:, 0] = EPS_LN
    return maskut, urank, ones128, ones1x128f, iota8, eoff, iota128, epsz


def kernel(L0_attn_x, L0_attn_W_qkv, L0_attn_b_qkv, L0_attn_W_out, L0_attn_b_out,
           L0_ffn_Wg, L0_ffn_W1, L0_ffn_b1, L0_ffn_W2, L0_ffn_b2):
    xf = np.ascontiguousarray(np.asarray(L0_attn_x, np.float32).reshape(NTOT, D))
    maskut, urank, ones128, ones1x128f, iota8, eoff, iota128, epsz = _consts()

    if "nc" not in _NC_CACHE:
        _NC_CACHE["nc"] = build_program(3, NSC, 7)
    nc = _NC_CACHE["nc"]

    shared = dict(
        wqkv=np.ascontiguousarray(np.asarray(L0_attn_W_qkv, np.float32)),
        bqkv=np.ascontiguousarray(np.asarray(L0_attn_b_qkv, np.float32)),
        wout=np.ascontiguousarray(np.asarray(L0_attn_W_out, np.float32)),
        bout=np.ascontiguousarray(np.asarray(L0_attn_b_out, np.float32)),
        wg=np.ascontiguousarray(np.asarray(L0_ffn_Wg, np.float32)),
        w1=np.ascontiguousarray(np.asarray(L0_ffn_W1, np.float32)),
        b1=np.ascontiguousarray(np.asarray(L0_ffn_b1, np.float32)),
        w2=np.ascontiguousarray(np.asarray(L0_ffn_W2, np.float32)),
        b2=np.ascontiguousarray(np.asarray(L0_ffn_b2, np.float32)),
        maskut=maskut, urank=urank, ones128=ones128, ones1x128f=ones1x128f,
        iota8=iota8, eoff=eoff, iota128=iota128, epsz=epsz,
    )
    in_maps = []
    for c in range(NCORES):
        m = dict(shared)
        m["x"] = np.ascontiguousarray(xf[c * TOK:(c + 1) * TOK])
        m["prevmask"] = (np.arange(NCORES) < c).astype(np.float32).reshape(NCORES, 1)
        in_maps.append(m)

    res = run_bass_kernel_spmd(nc, in_maps, core_ids=list(range(NCORES)))
    out = np.concatenate([res.results[c]["out"] for c in range(NCORES)], axis=0)
    return out.reshape(B, L, D)


if __name__ == "__main__":
    import time
    t0 = time.time()
    nc = build_program()
    print(f"build ok in {time.time()-t0:.1f}s; instructions: "
          f"{sum(1 for _ in nc.all_instructions())}")


# version 06
# speedup vs baseline: 1.0496x; 1.0496x over previous
import math
import os
import numpy as np
import ml_dtypes
from contextlib import ExitStack

import concourse.bass as bass
import concourse.mybir as mybir
from concourse.bass_utils import run_bass_kernel_spmd
from concourse.bass import IndirectOffsetOnAxis

F32 = mybir.dt.float32
BF16 = mybir.dt.bfloat16
U32 = mybir.dt.uint32
AF = mybir.ActivationFunctionType
ALU = mybir.AluOpType
AX = mybir.AxisListType

NCORES = 8
B, L, D = 16, 4096, 1024
NH, EH = 16, 64
TOK = 8192
NCH = 64
CPB = 32
NSC = 16
NE, HID, TOPK = 8, 4096, 2
NTOT = B * L
CAP = int(math.ceil(NTOT * TOPK / NE * 1.05))
CAP_PAD = 2560
GROWS = 512
NGRP = CAP_PAD // GROWS
NBUF = NE * CAP_PAD
DUMP0 = NBUF
XNROWS = NBUF + 2 * TOK
YZERO = NBUF
EPS_LN = 1e-5
EPS_DEN = 1e-6
QSCALE = EH ** (-0.5)


def build_program(stage=3, n_sc=NSC, sublvl=7):
    nc = bass.Bass("TRN2", target_bir_lowering=False, debug=False,
                   dynamic_dma_scratch_size=16384)

    x_in = nc.dram_tensor("x", [TOK, D], F32, kind="ExternalInput").ap()
    wqkv_in = nc.dram_tensor("wqkv", [D, 3 * D], F32, kind="ExternalInput").ap()
    bqkv_in = nc.dram_tensor("bqkv", [3 * D], F32, kind="ExternalInput").ap()
    wout_in = nc.dram_tensor("wout", [D, D], F32, kind="ExternalInput").ap()
    bout_in = nc.dram_tensor("bout", [D], F32, kind="ExternalInput").ap()
    wg_in = nc.dram_tensor("wg", [D, NE], F32, kind="ExternalInput").ap()
    w1_in = nc.dram_tensor("w1", [NE, D, HID], F32, kind="ExternalInput").ap()
    b1_in = nc.dram_tensor("b1", [NE, HID], F32, kind="ExternalInput").ap()
    w2_in = nc.dram_tensor("w2", [NE, HID, D], F32, kind="ExternalInput").ap()
    b2_in = nc.dram_tensor("b2", [NE, D], F32, kind="ExternalInput").ap()
    maskut_in = nc.dram_tensor("maskut", [128, 128], BF16, kind="ExternalInput").ap()
    urank_in = nc.dram_tensor("urank", [128, 128], BF16, kind="ExternalInput").ap()
    ones128_in = nc.dram_tensor("ones128", [128, 128], BF16, kind="ExternalInput").ap()
    ones1x128f_in = nc.dram_tensor("ones1x128f", [1, 128], F32, kind="ExternalInput").ap()
    iota8_in = nc.dram_tensor("iota8", [128, NE], F32, kind="ExternalInput").ap()
    eoff_in = nc.dram_tensor("eoff", [128, NE], F32, kind="ExternalInput").ap()
    iota128_in = nc.dram_tensor("iota128", [128, 1], F32, kind="ExternalInput").ap()
    prevmask_in = nc.dram_tensor("prevmask", [NCORES, 1], F32, kind="ExternalInput").ap()
    epsz_in = nc.dram_tensor("epsz", [128, 2], F32, kind="ExternalInput").ap()
    out_dram = nc.dram_tensor("out", [TOK, D], F32, kind="ExternalOutput").ap()
    dbg_dscat = nc.dram_tensor("dbg_dscat", [128, 2 * NCH], U32, kind="ExternalOutput").ap()
    dbg_ygat = nc.dram_tensor("dbg_ygat", [128, 2 * NCH], U32, kind="ExternalOutput").ap()
    dbg_g1k = nc.dram_tensor("dbg_g1k", [128, NCH], F32, kind="ExternalOutput").ap()
    dbg_g2k = nc.dram_tensor("dbg_g2k", [128, NCH], F32, kind="ExternalOutput").ap()
    dbg_allcnt = nc.dram_tensor("dbg_allcnt", [NCORES, NE], F32, kind="ExternalOutput").ap()
    dbg_thr = nc.dram_tensor("dbg_thr", [128, NE], F32, kind="ExternalOutput").ap()
    DBG3 = os.environ.get("KK_DBG3", "0") == "1"
    if DBG3:
        dbg_qf = nc.dram_tensor("dbg_qf", [128, 4096], BF16, kind="ExternalOutput").ap()
        dbg_kf = nc.dram_tensor("dbg_kf", [128, 4096], BF16, kind="ExternalOutput").ap()
        dbg_v = nc.dram_tensor("dbg_v", [128, 1040], BF16, kind="ExternalOutput").ap()
        dbg_xnt = nc.dram_tensor("dbg_xnt", [128, 4096], BF16, kind="ExternalOutput").ap()
        dbg_attn = nc.dram_tensor("dbg_attn", [128, 1024], BF16, kind="ExternalOutput").ap()
        dbg_s0 = nc.dram_tensor("dbg_s0", [128, 1040], F32, kind="ExternalOutput").ap()
        dbg_lgt = nc.dram_tensor("dbg_lgt", [128, 32], F32, kind="ExternalOutput").ap()

    xn_scr = nc.dram_tensor("xn_scr", [TOK, D], BF16).ap()
    xn_buf = nc.dram_tensor("xn_buf", [XNROWS, D], BF16).ap()
    y_buf = nc.dram_tensor("y_buf", [NBUF + 1, D], BF16).ap()
    part_scr = nc.dram_tensor("part_scr", [TOK, D], F32).ap()
    cc_in = nc.dram_tensor("cc_in", [1, NE], F32)
    cc_out = nc.dram_tensor("cc_out", [NCORES, NE], F32)

    st = ExitStack()

    def sb(name, shape, dt):
        return st.enter_context(nc.sbuf_tensor(name, shape, dt))

    def psf(name):
        return st.enter_context(nc.psum_tensor(name, [128, 512], F32))

    maskut = sb("maskut_s", [128, 128], BF16)
    urank = sb("urank_s", [128, 128], BF16)
    ones128 = sb("ones128b", [128, 128], BF16)
    ones1x128f = sb("ones1x128f_s", [1, 128], F32)
    iota8 = sb("iota8_s", [128, NE], F32)
    eoff = sb("eoff_s", [128, NE], F32)
    iota128 = sb("iota128_s", [128, 1], F32)
    prevmask = sb("prevmask_s", [NCORES, 1], F32)
    epsz = sb("epsz_s", [128, 2], F32)

    msk = sb("msk", [128, NCH * NE], BF16)
    m1s = sb("m1s", [128, NCH * NE], BF16)
    m2s = sb("m2s", [128, NCH * NE], BF16)
    dcol = sb("dcol", [128, NCH], F32)
    cntacc = sb("cntacc", [128, NE], F32)
    kk1 = sb("kk1", [128, NCH], F32)
    kk2 = sb("kk2", [128, NCH], F32)
    g1k = sb("g1k", [128, NCH], F32)
    g2k = sb("g2k", [128, NCH], F32)
    g1s = sb("g1s", [128, NCH], F32)
    g2tmp = sb("g2tmp", [128, NCH], F32)
    dscat = sb("dscat", [128, 2 * NCH], U32)
    ygat = sb("ygat", [128, 2 * NCH], U32)
    cnt8 = sb("cnt8", [NE, 1], F32)
    cntbf = sb("cntbf", [128, NE], BF16)
    allcnt = sb("allcnt", [NCORES, NE], F32)
    base_row = sb("base_row", [1, NE], F32)
    thr = sb("thr", [128, NE], F32)
    thr2 = sb("thr2", [128, NE], F32)
    baseprev = sb("baseprev", [1, NE], F32)
    rp_sb = sb("rp_sb", [128, NE], F32)
    k8_sb = sb("k8_sb", [128, NE], F32)
    mkj = [sb(f"mkj{j}", [128, NE], F32) for j in range(2)]
    uj = [sb(f"uj{j}", [128, NE], F32) for j in range(2)]
    scol = [sb(f"scol{j}", [128, 1], F32) for j in range(2)]
    kcol = [sb(f"kcol{j}", [128, 1], F32) for j in range(2)]
    tcol = [sb(f"tcol{j}", [128, 1], F32) for j in range(2)]
    wcol = [sb(f"wcol{j}", [128, 1], F32) for j in range(2)]
    dcol2 = [sb(f"dcol2{j}", [128, 1], F32) for j in range(2)]
    dumpc = [sb(f"dumpc{j}", [128, 1], F32) for j in range(2)]
    b1_sb = sb("b1_sb", [128, NE * 32], F32)
    b2e = sb("b2e", [1, 1024], BF16)
    fence_v = sb("fence_v", [128, 1], F32)
    fence_a = sb("fence_a", [128, 1], F32)

    p1 = ExitStack()

    def sb1(name, shape, dt):
        return p1.enter_context(nc.sbuf_tensor(name, shape, dt))

    wqkv_sb = sb1("wqkv_sb", [128, 8 * 3072], BF16)
    wout_sb = sb1("wout_sb", [128, 8 * 1024], BF16)
    wg_sb = sb1("wg_sb", [128, 8 * NE], BF16)
    bqk = sb1("bqk", [128, 16], F32)
    bqks = sb1("bqks", [128, 16], F32)
    bv_row = sb1("bv_row", [1, 1024], BF16)
    bout_row = sb1("bout_row", [1, 1024], BF16)
    zrow = sb1("zrow", [128, 1024], BF16)
    xfp = [sb1(f"xfp{i}", [128, D], F32) for i in range(9)]
    bn12 = [sb1(f"bn12_{i}", [128, 12], F32) for i in range(2)]
    mv = [sb1(f"mv{i}", [128, 2], F32) for i in range(2)]
    rstd = [sb1(f"rstd{i}", [128, 1], F32) for i in range(2)]
    negms = [sb1(f"negms{i}", [128, 1], F32) for i in range(2)]
    xn_sb = [sb1(f"xn_sb{i}", [128, D], BF16) for i in range(2)]
    xnt = [sb1(f"xnt{i}", [128, 8 * 512], BF16) for i in range(2)]
    qf = sb1("qf", [128, 8 * 512], BF16)
    kf = sb1("kf", [128, 8 * 512], BF16)
    ef = [sb1(f"ef{i}", [128, 512], F32) for i in range(2)]
    rf = [sb1(f"rf{i}", [128, 512], F32) for i in range(2)]
    tf = [sb1(f"tf{i}", [128, 512], F32) for i in range(2)]
    vsb = [sb1(f"vsb{i}", [128, 16 * 65], BF16) for i in range(4)]
    ktok = [sb1(f"ktok{i}", [128, 1024], BF16) for i in range(4)]
    smt = [sb1(f"smt{i}", [128, 256], BF16) for i in range(2)]
    s0f = sb1("s0f", [128, 8 * 130], F32)
    s0b = sb1("s0b", [128, 8 * 130], BF16)
    dena = [sb1(f"dena{i}", [128, 1], F32) for i in range(4)]
    attn_sb = [sb1(f"attn_sb{i}", [128, 1024], BF16) for i in range(2)]
    attnt = [sb1(f"attnt{i}", [128, 1024], BF16) for i in range(2)]
    part_sb = [sb1(f"part_sb{i}", [128, D], F32) for i in range(2)]
    lgt = [sb1(f"lgt{i}", [128, 32], F32) for i in range(2)]
    mx8 = sb1("mx8", [128, NE], F32)
    idxu = sb1("idxu", [128, NE], U32)
    idxf = sb1("idxf", [128, NE], F32)
    p1.close()

    p2 = ExitStack()

    def sb2(name, shape, dt):
        return p2.enter_context(nc.sbuf_tensor(name, shape, dt))

    xnc = [sb2(f"xnc{i}", [128, D], BF16) for i in range(2)]
    w1_sb = sb2("w1_sb", [128, 8 * HID], BF16)
    w2_sb = sb2("w2_sb", [128, 32 * 1024], BF16)
    tok2 = [sb2(f"tok2_{i}", [128, D], BF16) for i in range(4)]
    xngt = [sb2(f"xngt{i}", [128, 8 * GROWS], BF16) for i in range(2)]
    ht_sb = sb2("ht_sb", [128, 32 * GROWS], BF16)
    y_sb = [sb2(f"y_sb{i}", [128, 1024], BF16) for i in range(2)]
    p2.close()

    p3 = ExitStack()

    def sb3(name, shape, dt):
        return p3.enter_context(nc.sbuf_tensor(name, shape, dt))

    pt3 = [sb3(f"pt3_{i}", [128, D], F32) for i in range(4)]
    yg0 = [sb3(f"yg0_{i}", [128, D], BF16) for i in range(4)]
    yg1 = [sb3(f"yg1_{i}", [128, D], BF16) for i in range(4)]
    t3 = [sb3(f"t3_{i}", [128, D], F32) for i in range(4)]
    o3 = [sb3(f"o3_{i}", [128, D], F32) for i in range(4)]
    p3.close()

    pb = [psf(f"pb{i}") for i in range(8)]

    sems = {}
    slot_sems = {}
    for base, n in [("sLxA", 9), ("sTxnA", 2), ("sTkA", 4), ("sTatA", 2),
                    ("sPwA", 2), ("sXnwA", 2), ("sLxn2A", 2), ("sScatA", 2),
                    ("sRowA", 2), ("sTrsA", 2), ("sYwA", 2), ("sYgA", 4),
                    ("sLptA", 4), ("sOwA", 4)]:
        slot_sems[base] = [st.enter_context(nc.semaphore(f"{base}{i}")) for i in range(n)]
    sLxA = slot_sems["sLxA"]; sTxnA = slot_sems["sTxnA"]; sTkA = slot_sems["sTkA"]
    sTatA = slot_sems["sTatA"]; sPwA = slot_sems["sPwA"]; sXnwA = slot_sems["sXnwA"]
    sLxn2A = slot_sems["sLxn2A"]; sScatA = slot_sems["sScatA"]; sRowA = slot_sems["sRowA"]
    sTrsA = slot_sems["sTrsA"]; sYwA = slot_sems["sYwA"]; sYgA = slot_sems["sYgA"]
    sLptA = slot_sems["sLptA"]; sOwA = slot_sems["sOwA"]
    for name in ["sPreG", "sPreS", "sPreA", "sPreGm", "sMemB",
                 "sStat", "sBnA", "sRst",
                 "sQA", "sFeA", "sFeV", "sQB", "sVe", "sLgV",
                 "sS", "sSmV", "sNum", "sStV", "sDiv", "sPj", "sRes",
                 "sRout", "sKVp",
                 "sCntV1", "sCntP", "sCntV2", "sCCw", "sCC", "sCCl",
                 "sBase1", "sBase2", "sBase3", "sBaseV",
                 "sRkP", "sRkV", "sDestV", "sGate",
                 "sW1", "sW2", "sH", "sGel", "sY", "sYe",
                 "sP3b", "sDbg", "sP1V", "sP1A", "sDivD", "sVeD", "sRcp"]:

        sems[name] = st.enter_context(nc.semaphore(name))
    S = type("S", (), sems)

    NPREG = 8 + 8 + 8 + 1 + 1
    NPRES = 9 + 16
    NPREA = NE * 32
    NMEMB = NBUF // 128 + 1
    NTR_G = 32
    NGG = NE * NGRP

    def wl_after_w1(e):
        return 16 * (NPREG + e * 41 + 8)

    def wl_after_w2(e):
        return 16 * (NPREG + e * 41 + 41)

    with nc.Block() as block:

        @block.sync
        def _(sync):
            for dst, src in [(maskut, maskut_in), (urank, urank_in), (ones128, ones128_in),
                             (ones1x128f, ones1x128f_in), (iota8, iota8_in), (eoff, eoff_in),
                             (iota128, iota128_in), (prevmask, prevmask_in),
                             (epsz, epsz_in)]:
                sync.dma_start(out=dst[:], in_=src).then_inc(S.sPreS, 16)
            for ct in range(16):
                sync.dma_start(out=bqk[:, ct:ct + 1],
                               in_=bqkv_in[ct * 128:(ct + 1) * 128].rearrange("(a b) -> a b", b=1)
                               ).then_inc(S.sPreS, 16)

            def _pstores(sc_):
                for cc_ in range(4):
                    c_ = sc_ * 4 + cc_
                    sync.wait_ge(S.sRes, c_ + 1)
                    sync.dma_start(out=part_scr[c_ * 128:(c_ + 1) * 128, :],
                                   in_=part_sb[c_ % 2][:]).then_inc(sPwA[c_ % 2], 16)
                    if stage < 3:
                        sync.dma_start(out=out_dram[c_ * 128:(c_ + 1) * 128, :],
                                       in_=part_sb[c_ % 2][:]).then_inc(sOwA[c_ % 2], 16)

            for sc in range(n_sc if sublvl >= 2 else 0):
                for cc in range(4):
                    c = sc * 4 + cc
                    if c >= 9 and sublvl >= 7:
                        sync.wait_ge(S.sRes, c - 8)
                    sync.dma_start(out=xfp[c % 9][:], in_=x_in[c * 128:(c + 1) * 128, :]
                                   ).then_inc(sLxA[c % 9], 16)
                for cc in range(4):
                    c = sc * 4 + cc
                    sync.wait_ge(S.sStat, c + 1)
                    if sc >= 2 and cc == 0:
                        sync.wait_ge(S.sQB, 4 * (sc - 1))
                    for k in range(8):
                        sync.dma_start(
                            out=xnt[sc % 2][:, k * 512 + cc * 128: k * 512 + cc * 128 + 128],
                            in_=xn_sb[c % 2][:, k * 128:(k + 1) * 128],
                            transpose=True).then_inc(sTxnA[c % 2], 16)
                    sync.dma_start(out=xn_scr[c * 128:(c + 1) * 128, :], in_=xn_sb[c % 2][:]
                                   ).then_inc(sXnwA[c % 2], 16)
                if sublvl >= 7 and sc >= 1:
                    _pstores(sc - 1)
                if sublvl >= 5:
                    sync.wait_ge(S.sFeV, sc * 16 + 16)
                    for cc in range(4):
                        c = sc * 4 + cc
                        if c >= 4:
                            sync.wait_ge(S.sKVp, 8 * (c - 3))
                        for p in range(8):
                            sync.dma_start(out=ktok[cc][:, p * 128:(p + 1) * 128],
                                           in_=kf[:, p * 512 + cc * 128: p * 512 + cc * 128 + 128],
                                           transpose=True).then_inc(sTkA[cc], 16)
            if sublvl >= 7 and n_sc >= 1:
                _pstores(n_sc - 1)

            for c in (range(NCH) if stage >= 2 else []):
                if c == 0:
                    sync.wait_ge(sXnwA[0], 16 * 32)
                    sync.wait_ge(sXnwA[1], 16 * 32)
                    sync.wait_ge(S.sRes, NCH)
                    sync.wait_ge(S.sP1V, 1)
                    sync.wait_ge(S.sP1A, 1)
                if c >= 2:
                    sync.wait_ge(sScatA[c % 2], 32 * ((c - 2) // 2 + 1))
                sync.dma_start(out=xnc[c % 2][:], in_=xn_scr[c * 128:(c + 1) * 128, :]
                               ).then_inc(sLxn2A[c % 2], 16)

            for e in (range(NE) if stage >= 3 else []):
                for g in range(NGRP):
                    gg = e * NGRP + g
                    r0 = e * CAP_PAD + g * GROWS
                    if gg == 0:
                        sync.wait_ge(sScatA[0], 32 * 32)
                        sync.wait_ge(sScatA[1], 32 * 32)
                        sync.wait_ge(S.sP1V, 1)
                        sync.wait_ge(S.sP1A, 1)
                        sync.wait_ge(S.sPreGm, 2)
                        sync.wait_ge(sPwA[0], 16 * 32)
                        sync.wait_ge(sPwA[1], 16 * 32)
                    if gg >= 1:
                        sync.wait_ge(sTrsA[(gg - 1) % 2], 16 * NTR_G * ((gg - 1) // 2 + 1))
                    for i in range(4):
                        sync.dma_start(out=tok2[i][:],
                                       in_=xn_buf[r0 + i * 128: r0 + (i + 1) * 128, :]
                                       ).then_inc(sRowA[gg % 2], 16)
                    sync.wait_ge(sRowA[gg % 2], 16 * 4 * (gg // 2 + 1))
                    if gg >= 2:
                        sync.wait_ge(S.sH, 32 * (gg - 1))
                    for i in range(4):
                        for k in range(8):
                            sync.dma_start(
                                out=xngt[gg % 2][:, k * GROWS + i * 128: k * GROWS + i * 128 + 128],
                                in_=tok2[i][:, k * 128:(k + 1) * 128],
                                transpose=True).then_inc(sTrsA[gg % 2], 16)
                    if gg >= 1:
                        pg = gg - 1
                        pe_, pgr = pg // NGRP, pg % NGRP
                        pr0 = pe_ * CAP_PAD + pgr * GROWS
                        for sub in range(4):
                            ss = pg * 4 + sub
                            sync.wait_ge(S.sYe, ss + 1)
                            sync.dma_start(out=y_buf[pr0 + sub * 128: pr0 + (sub + 1) * 128, :],
                                           in_=y_sb[sub % 2][:]).then_inc(sYwA[sub % 2], 16)
            if stage >= 2:
                sync.wait_ge(S.sDestV, NCH)
                sync.wait_ge(S.sGate, 2)
                sync.wait_ge(S.sCCl, 16)
                sync.wait_ge(S.sBaseV, 1)
                sync.dma_start(out=dbg_dscat, in_=dscat[:]).then_inc(S.sDbg, 16)
                sync.dma_start(out=dbg_ygat, in_=ygat[:]).then_inc(S.sDbg, 16)
                sync.dma_start(out=dbg_g1k, in_=g1k[:]).then_inc(S.sDbg, 16)
                sync.dma_start(out=dbg_g2k, in_=g2k[:]).then_inc(S.sDbg, 16)
                sync.dma_start(out=dbg_allcnt, in_=allcnt[:]).then_inc(S.sDbg, 16)
                sync.dma_start(out=dbg_thr, in_=thr2[:]).then_inc(S.sDbg, 16)
            if stage >= 3:
                pg = NGG - 1
                pe_, pgr = pg // NGRP, pg % NGRP
                pr0 = pe_ * CAP_PAD + pgr * GROWS
                for sub in range(4):
                    ss = pg * 4 + sub
                    sync.wait_ge(S.sYe, ss + 1)
                    sync.dma_start(out=y_buf[pr0 + sub * 128: pr0 + (sub + 1) * 128, :],
                                   in_=y_sb[sub % 2][:]).then_inc(sYwA[sub % 2], 16)

            for c in (range(NCH) if stage >= 3 else []):
                if c == 0:
                    sync.wait_ge(sPwA[0], 16 * 32)
                    sync.wait_ge(sPwA[1], 16 * 32)
                    sync.wait_ge(S.sYe, 4 * NGG)
                if c >= 4:
                    sync.wait_ge(S.sP3b, c - 3)
                sync.dma_start(out=pt3[c % 4][:], in_=part_scr[c * 128:(c + 1) * 128, :]
                               ).then_inc(sLptA[c % 4], 16)
                if c >= 3:
                    cs = c - 3
                    sync.wait_ge(S.sP3b, cs + 1)
                    sync.dma_start(out=out_dram[cs * 128:(cs + 1) * 128, :], in_=o3[cs % 4][:]
                                   ).then_inc(sOwA[cs % 4], 16)
            if stage >= 3:
                for cs in range(NCH - 3, NCH):
                    sync.wait_ge(S.sP3b, cs + 1)
                    sync.dma_start(out=out_dram[cs * 128:(cs + 1) * 128, :], in_=o3[cs % 4][:]
                                   ).then_inc(sOwA[cs % 4], 16)
            if sublvl >= 7:
                if stage >= 3:
                    for i in range(4):
                        sync.wait_ge(sOwA[i], 16 * (NCH // 4))
                else:
                    sync.wait_ge(sOwA[0], 16 * 2 * n_sc)
                    sync.wait_ge(sOwA[1], 16 * 2 * n_sc)
                if stage >= 2:
                    sync.wait_ge(S.sDbg, 16 * 6)
            else:
                sync.wait_ge(S.sPreGm, 1)
                if os.environ.get("KK_NOMEMSET", "0") != "1":
                    sync.wait_ge(S.sMemB, 16 * NMEMB)
                sync.wait_ge(S.sPreG, 16 * NPREG)
                sync.wait_ge(S.sPreA, 16 * NPREA)
                if sublvl >= 2:
                    sync.wait_ge(sXnwA[0], 16 * 2 * n_sc)
                    sync.wait_ge(sXnwA[1], 16 * 2 * n_sc)
                if sublvl == 21:
                    sync.wait_ge(S.sQA, int(os.environ.get("KK_CT", "16")) * n_sc)
                if sublvl == 22:
                    sync.wait_ge(S.sFeA, 16 * n_sc)
                if sublvl >= 3 and sublvl not in (21, 22):
                    sync.wait_ge(S.sFeV, 16 * n_sc)
                if sublvl >= 4:
                    sync.wait_ge(S.sVe, 4 * n_sc)
                    sync.wait_ge(S.sLgV, n_sc)
                if sublvl >= 5:
                    sync.wait_ge(S.sDiv, 32 * n_sc)
                if sublvl >= 6:
                    sync.wait_ge(S.sRout, 4 * n_sc)

        @block.gpsimd
        def _(gpsimd):
            for k in range(8):
                gpsimd.dma_start(out=wqkv_sb[:, k * 3072:(k + 1) * 3072],
                                 in_=wqkv_in[k * 128:(k + 1) * 128, :]).then_inc(S.sPreG, 16)
            for k in range(8):
                gpsimd.dma_start(out=wout_sb[:, k * 1024:(k + 1) * 1024],
                                 in_=wout_in[k * 128:(k + 1) * 128, :]).then_inc(S.sPreG, 16)
            for k in range(8):
                gpsimd.dma_start(out=wg_sb[:, k * NE:(k + 1) * NE],
                                 in_=wg_in[k * 128:(k + 1) * 128, :]).then_inc(S.sPreG, 16)
            gpsimd.dma_start(out=bv_row[:], in_=bqkv_in[2048:3072].rearrange("(a b) -> a b", a=1)
                             ).then_inc(S.sPreG, 16)
            gpsimd.dma_start(out=bout_row[:], in_=bout_in.rearrange("(a b) -> a b", a=1)
                             ).then_inc(S.sPreG, 16)
            gpsimd.memset(zrow[:], 0.0)
            gpsimd.memset(s0f[:], 0.0)
            gpsimd.memset(s0b[:], 0.0).then_inc(S.sPreGm, 1)
            gpsimd.wait_ge(S.sPreGm, 1)
            if os.environ.get("KK_NOMEMSET", "0") == "1":
                pass
            else:
                for i in range(NBUF // 128):
                    gpsimd.dma_start(out=xn_buf[i * 128:(i + 1) * 128, :], in_=zrow[:]
                                     ).then_inc(S.sMemB, 16)
                gpsimd.dma_start(out=y_buf[YZERO:YZERO + 1, :], in_=zrow[0:1, 0:1024]
                                 ).then_inc(S.sMemB, 16)
            for e in range(NE):
                for ht in range(32):
                    gpsimd.dma_start(out=b1_sb[:, e * 32 + ht: e * 32 + ht + 1],
                                     in_=b1_in[e, ht * 128:(ht + 1) * 128].rearrange("(a b) -> a b", b=1)
                                     ).then_inc(S.sPreA, 16)

            if stage < 2:
                return
            gpsimd.wait_ge(S.sCntV2, 1)
            gpsimd.dma_start(out=cc_in.ap().rearrange("a b -> b a"), in_=cnt8[:]
                             ).then_inc(S.sCCw, 16)
            gpsimd.wait_ge(S.sCCw, 16)
            gpsimd.collective_compute(
                "AllGather", ALU.bypass,
                replica_groups=[list(range(NCORES))],
                ins=[cc_in.ap().opt()], outs=[cc_out.ap().opt()],
            ).then_inc(S.sCC)
            gpsimd.wait_ge(S.sCC, 1)
            gpsimd.dma_start(out=allcnt[:], in_=cc_out.ap()).then_inc(S.sCCl, 16)

            gpsimd.wait_ge(S.sMemB, 16 * NMEMB)
            for c in range(NCH):
                gpsimd.wait_ge(S.sDestV, c + 1)
                gpsimd.wait_ge(sLxn2A[c % 2], 16 * (c // 2 + 1))
                for j in range(2):
                    gpsimd.indirect_dma_start(
                        out=xn_buf,
                        out_offset=IndirectOffsetOnAxis(ap=dscat[:, 2 * c + j: 2 * c + j + 1], axis=0),
                        in_=xnc[c % 2][:], in_offset=None,
                    ).then_inc(sScatA[c % 2], 16)

            for e in (range(NE) if stage >= 3 else []):
                if e == 0:
                    gpsimd.wait_ge(sScatA[0], 32 * 32)
                    gpsimd.wait_ge(sScatA[1], 32 * 32)
                    gpsimd.wait_ge(S.sP1V, 1)
                    gpsimd.wait_ge(S.sP1A, 1)
                    gpsimd.wait_ge(sPwA[0], 16 * 32)
                    gpsimd.wait_ge(sPwA[1], 16 * 32)
                    gpsimd.wait_ge(sXnwA[0], 16 * 32)
                    gpsimd.wait_ge(sXnwA[1], 16 * 32)
                    gpsimd.memset(xngt[0][:], 0.0)
                    gpsimd.memset(xngt[1][:], 0.0).then_inc(S.sPreGm, 1)
                else:
                    gpsimd.wait_ge(S.sH, 32 * NGRP * e)
                for k in range(8):
                    gpsimd.dma_start(out=w1_sb[:, k * HID:(k + 1) * HID],
                                     in_=w1_in[e, k * 128:(k + 1) * 128, :]).then_inc(S.sW1, 16)
                if e > 0:
                    gpsimd.wait_ge(S.sY, 4 * NGRP * e)
                for j in range(32):
                    gpsimd.dma_start(out=w2_sb[:, j * 1024:(j + 1) * 1024],
                                     in_=w2_in[e, j * 128:(j + 1) * 128, :]).then_inc(S.sW2, 16)
                gpsimd.dma_start(out=b2e[:],
                                 in_=b2_in[e, :].rearrange("(a b) -> a b", a=1)).then_inc(S.sW2, 16)

            if stage < 3:
                return
            gpsimd.wait_ge(sYwA[0], 16 * 2 * NGG)
            gpsimd.wait_ge(sYwA[1], 16 * 2 * NGG)
            for c in range(NCH):
                if c >= 4:
                    gpsimd.wait_ge(S.sP3b, c - 3)
                gpsimd.indirect_dma_start(
                    out=yg0[c % 4][:], out_offset=None, in_=y_buf,
                    in_offset=IndirectOffsetOnAxis(ap=ygat[:, 2 * c: 2 * c + 1], axis=0),
                ).then_inc(sYgA[c % 4], 16)
                gpsimd.indirect_dma_start(
                    out=yg1[c % 4][:], out_offset=None, in_=y_buf,
                    in_offset=IndirectOffsetOnAxis(ap=ygat[:, 2 * c + 1: 2 * c + 2], axis=0),
                ).then_inc(sYgA[c % 4], 16)

        @block.tensor
        def _(tensor):
            tensor.wait_ge(S.sPreG, 16 * NPREG)
            tensor.wait_ge(S.sPreS, 16 * NPRES)
            tensor.wait_ge(S.sPreGm, 1)

            for sc in range(n_sc if (sublvl >= 3 or sublvl in (21, 22)) else 0):
                tensor.wait_ge(sTxnA[0], 16 * 8 * 2 * (sc + 1))
                tensor.wait_ge(sTxnA[1], 16 * 8 * 2 * (sc + 1))
                _CT = int(os.environ.get("KK_CT", "16"))
                _K = int(os.environ.get("KK_K", "8"))
                for ct in range(_CT):
                    bank = ct % 2
                    gt = sc * 16 + ct
                    if gt >= 2 and sublvl != 21:
                        tensor.wait_ge(S.sFeA, gt - 1)
                    if ct < 2 and sc >= 1:
                        tensor.wait_ge(S.sSmV, 2 * 32 * sc)
                    _OPS = os.environ.get("KK_OPS", "orig")
                    for k in range(8):
                        i_ = tensor.matmul(pb[bank][:, 0:512],
                                           wqkv_sb[:, k * 3072 + ct * 128: k * 3072 + ct * 128 + 128],
                                           xnt[sc % 2][:, k * 512:(k + 1) * 512],
                                           start=(k == 0), stop=(k == 7))
                    i_.then_inc(S.sQA, 1)
                _SEQB = os.environ.get("KK_SEQB", "0") == "1"
                for cc in range(4 if sublvl >= 4 else 0):
                    c = sc * 4 + cc
                    if c >= 1:
                        tensor.wait_ge(S.sVe, c)
                    if sc >= 1 and cc == 0:
                        tensor.wait_ge(S.sRes, 4 * sc)
                        tensor.wait_ge(S.sLgV, sc)
                        tensor.wait_ge(S.sSmV, 2 * 32 * sc)
                        tensor.wait_ge(S.sStV, 32 * sc)
                    if _SEQB:
                        for cg in range(2):
                            for k in range(8):
                                lhs = xnt[sc % 2][:, k * 512 + cc * 128: k * 512 + cc * 128 + 128]
                                tensor.matmul(pb[2 + cg][:, 0:512], lhs,
                                              wqkv_sb[:, k * 3072 + 2048 + cg * 512: k * 3072 + 2048 + (cg + 1) * 512],
                                              start=(k == 0), stop=False)
                            tensor.matmul(pb[2 + cg][:, 0:512], ones128[0:1, :],
                                          bv_row[0:1, cg * 512:(cg + 1) * 512],
                                          start=False, stop=True)
                        for k in range(8):
                            lhs = xnt[sc % 2][:, k * 512 + cc * 128: k * 512 + cc * 128 + 128]
                            i_ = tensor.matmul(pb[4][:, cc * 8: cc * 8 + 8], lhs,
                                               wg_sb[:, k * NE:(k + 1) * NE],
                                               start=(cc == 0 and k == 0), stop=(cc == 3 and k == 7))
                        i_.then_inc(S.sQB, 1)
                    else:
                        for k in range(8):
                            lhs = xnt[sc % 2][:, k * 512 + cc * 128: k * 512 + cc * 128 + 128]
                            for cg in range(2):
                                tensor.matmul(pb[2 + cg][:, 0:512], lhs,
                                              wqkv_sb[:, k * 3072 + 2048 + cg * 512: k * 3072 + 2048 + (cg + 1) * 512],
                                              start=(k == 0), stop=False)
                            tensor.matmul(pb[4][:, cc * 8: cc * 8 + 8], lhs,
                                          wg_sb[:, k * NE:(k + 1) * NE],
                                          start=(cc == 0 and k == 0), stop=(cc == 3 and k == 7))
                        for cg in range(2):
                            i_ = tensor.matmul(pb[2 + cg][:, 0:512], ones128[0:1, :],
                                               bv_row[0:1, cg * 512:(cg + 1) * 512],
                                               start=False, stop=True)
                        i_.then_inc(S.sQB, 1)

                for cc in range(4 if sublvl >= 5 else 0):
                    c = sc * 4 + cc
                    pos = c % CPB
                    tensor.wait_ge(S.sFeV, sc * 16 + 16)
                    tensor.wait_ge(S.sVeD, c + 1)
                    tensor.wait_ge(sTkA[cc], 16 * 8 * (sc + 1))
                    if cc == 0:
                        tensor.wait_ge(S.sVe, 4 * sc + 4)
                        tensor.wait_ge(S.sLgV, sc + 1)
                    for p in range(8):
                        gp = c * 8 + p
                        sb0 = 0 if p % 2 == 0 else 2
                        if gp >= 2:
                            tensor.wait_ge(S.sSmV, 2 * (gp - 1))
                        colq = p * 512 + cc * 128
                        tensor.matmul(pb[sb0][:, 0:128], kf[0:64, colq:colq + 128],
                                      qf[0:64, colq:colq + 128], start=True, stop=True)
                        tensor.matmul(pb[sb0 + 1][:, 0:128], kf[64:128, colq:colq + 128],
                                      qf[64:128, colq:colq + 128], start=True, stop=True
                                      ).then_inc(S.sS, 1)
                        nbank = 6 + (p % 2)
                        if p == 0:
                            tensor.wait_ge(S.sRes, c)
                        if gp >= 2:
                            tensor.wait_ge(S.sDiv, gp - 1)
                        tensor.wait_ge(S.sSmV, 2 * (gp + 1))
                        if pos > 0:
                            tensor.wait_ge(S.sStV, (c - 1) * 8 + p + 1)
                        tensor.matmul(pb[nbank][:, 0:65], smt[p % 2][:, 0:128],
                                      vsb[cc][:, (2 * p) * 65:(2 * p) * 65 + 65],
                                      start=True, stop=False)
                        i_ = tensor.matmul(pb[nbank][:, 65:130], smt[p % 2][:, 128:256],
                                           vsb[cc][:, (2 * p + 1) * 65:(2 * p + 1) * 65 + 65],
                                           start=False, stop=(pos == 0))
                        if pos > 0:
                            i_ = tensor.matmul(pb[nbank][:, 0:130],
                                               qf[:, colq:colq + 128],
                                               s0b[:, p * 130:(p + 1) * 130],
                                               start=False, stop=True)
                        i_.then_inc(S.sNum, 1)
                        if os.environ.get("KK_ST7", "0") == "1":
                            if gp >= 1:
                                tensor.wait_ge(S.sStV, gp)
                            tensor.matmul(pb[7][:, 192:322],
                                          ktok[cc][:, p * 128:(p + 1) * 128],
                                          vsb[cc][:, (2 * p) * 65:(2 * p) * 65 + 130],
                                          start=True, stop=True).then_inc(S.sKVp, 1)
                        else:
                            kbank = 4 + (p % 2)
                            if gp >= 2:
                                tensor.wait_ge(S.sStV, gp - 1)
                            tensor.matmul(pb[kbank][:, 0:130],
                                          ktok[cc][:, p * 128:(p + 1) * 128],
                                          vsb[cc][:, (2 * p) * 65:(2 * p) * 65 + 130],
                                          start=True, stop=True).then_inc(S.sKVp, 1)
                    if sublvl < 7:
                        continue
                    tensor.wait_ge(sTatA[c % 2], 16 * 8 * (c // 2 + 1))
                    tensor.wait_ge(S.sDiv, 8 * (c + 1))
                    for k in range(8):
                        for cg in range(2):
                            tensor.matmul(pb[6 + cg][:, 0:512],
                                          attnt[c % 2][:, k * 128:(k + 1) * 128],
                                          wout_sb[:, k * 1024 + cg * 512: k * 1024 + (cg + 1) * 512],
                                          start=(k == 0), stop=False)
                    for cg in range(2):
                        i_ = tensor.matmul(pb[6 + cg][:, 0:512], ones128[0:1, :],
                                           bout_row[0:1, cg * 512:(cg + 1) * 512],
                                           start=False, stop=True)
                    i_.then_inc(S.sPj, 1)

            if stage < 2:
                return
            tensor.wait_ge(S.sRout, NCH)
            tensor.wait_ge(S.sCntV1, 1)
            tensor.wait_ge(S.sDiv, 8 * NCH)
            tensor.wait_ge(S.sRes, NCH)
            tensor.wait_ge(S.sLgV, NSC)
            tensor.wait_ge(S.sStV, 8 * NCH)
            tensor.matmul(pb[2][0:NE, 0:1], cntbf[:], ones128[:, 0:1], start=True, stop=True
                          ).then_inc(S.sCntP, 1)
            tensor.wait_ge(S.sCCl, 16)
            tensor.matmul(pb[3][0:1, 0:NE], prevmask[:], allcnt[:], start=True, stop=True
                          ).then_inc(S.sBase1, 1)
            tensor.wait_ge(S.sBase2, 1)
            tensor.matmul(pb[4][:, 0:NE], ones1x128f[:], base_row[:], start=True, stop=True
                          ).then_inc(S.sBase3, 1)
            tensor.wait_ge(S.sBaseV, 1)
            for c in range(NCH):
                bank = c % 2
                if c >= 2:
                    tensor.wait_ge(S.sDestV, c - 1)
                if c > 0:
                    tensor.wait_ge(S.sRkV, c + 1)
                    tensor.matmul(pb[bank][:, 0:NE], urank[:], msk[:, c * NE:(c + 1) * NE],
                                  start=True, stop=False)
                    tensor.matmul(pb[bank][:, 0:NE], ones1x128f[:], baseprev[:],
                                  start=False, stop=True)
                else:
                    tensor.matmul(pb[bank][:, 0:NE], urank[:], msk[:, c * NE:(c + 1) * NE],
                                  start=True, stop=True)
                tensor.matmul(pb[6][0:1, (c % 2) * NE:(c % 2) * NE + NE], ones128[:, 0:1],
                              msk[:, c * NE:(c + 1) * NE],
                              start=True, stop=True).then_inc(S.sRkP, 1)

            if stage < 3:
                return
            tensor.wait_ge(S.sDestV, NCH)
            tensor.wait_ge(S.sCntV2, 1)
            tensor.wait_ge(S.sBase2, 1)
            tensor.wait_ge(S.sBaseV, 1)
            tensor.wait_ge(S.sRkV, NCH)
            for e in range(NE):
                for g in range(NGRP):
                    gg = e * NGRP + g
                    tensor.wait_ge(sTrsA[gg % 2], 16 * NTR_G * (gg // 2 + 1))
                    if gg == 0:
                        tensor.wait_ge(S.sW1, 16 * 8)
                    for ht in range(32):
                        hh = gg * 32 + ht
                        bank = ht % 3
                        if hh >= 3:
                            tensor.wait_ge(S.sGel, hh - 2)
                        if g == 0 and ht == 0 and e > 0:
                            tensor.wait_ge(S.sW1, 16 * 8 * (e + 1))
                        for k in range(8):
                            i_ = tensor.matmul(pb[bank][:, 0:GROWS],
                                               w1_sb[:, k * HID + ht * 128: k * HID + ht * 128 + 128],
                                               xngt[gg % 2][:, k * GROWS:(k + 1) * GROWS],
                                               start=(k == 0), stop=(k == 7))
                        i_.then_inc(S.sH, 1)
                    tensor.wait_ge(S.sGel, 32 * (gg + 1))
                    if g == 0:
                        tensor.wait_ge(S.sW2, 16 * 33 * (e + 1))
                    for sub in range(4):
                        ss = gg * 4 + sub
                        ybk = 3 + (sub % 2) * 2
                        if ss >= 2:
                            tensor.wait_ge(S.sYe, ss - 1)
                        for ht in range(32):
                            for cg in range(2):
                                tensor.matmul(pb[ybk + cg][:, 0:512],
                                              ht_sb[:, ht * GROWS + sub * 128: ht * GROWS + sub * 128 + 128],
                                              w2_sb[:, ht * 1024 + cg * 512: ht * 1024 + (cg + 1) * 512],
                                              start=(ht == 0), stop=False)
                        for cg in range(2):
                            i_ = tensor.matmul(pb[ybk + cg][:, 0:512], ones128[0:1, :],
                                               b2e[0:1, cg * 512:(cg + 1) * 512],
                                               start=False, stop=True)
                        i_.then_inc(S.sY, 1)

        @block.vector
        def _(vector):
            vector.wait_ge(S.sPreS, 16 * NPRES)
            vector.wait_ge(S.sPreGm, 1)
            for i in range(4):
                vector.tensor_copy(
                    vsb[i][:].rearrange("p (h w) -> p h w", w=65)[:, :, 64:65],
                    ones128[:, 0:16].rearrange("p (h w) -> p h w", w=1))
            vector.tensor_scalar(bqks[:, 0:8], bqk[:, 0:8], QSCALE, None, ALU.mult)
            vector.tensor_scalar(bqks[:, 8:16], bqk[:, 8:16], 1.0, None, ALU.mult)
            vector.memset(cntacc[:], 0.0)
            vector.drain()

            def _stats(sc_):
                for cc_ in range(4):
                    c_ = sc_ * 4 + cc_
                    vector.wait_ge(sLxA[c_ % 9], 16 * (c_ // 9 + 1))
                    if c_ >= 2:
                        vector.wait_ge(S.sRst, c_ - 1)
                    vector.bn_stats(bn12[c_ % 2][:, 0:6], xfp[c_ % 9][:, 0:512])
                    vector.bn_stats(bn12[c_ % 2][:, 6:12], xfp[c_ % 9][:, 512:1024])
                    vector.drain()
                    vector.bn_aggr(mv[c_ % 2][:], bn12[c_ % 2][:]).then_inc(S.sBnA, 1)
                    vector.wait_ge(S.sRst, c_ + 1)
                    vector.tensor_scalar(negms[c_ % 2][:], mv[c_ % 2][:, 0:1], rstd[c_ % 2][:],
                                         -1.0, ALU.mult, ALU.mult)
                    vector.drain()
                    if c_ >= 2:
                        vector.wait_ge(sTxnA[c_ % 2], 16 * 8 * ((c_ - 2) // 2 + 1))
                        vector.wait_ge(sXnwA[c_ % 2], 16 * ((c_ - 2) // 2 + 1))
                    vector.tensor_scalar(xn_sb[c_ % 2][:], xfp[c_ % 9][:], rstd[c_ % 2][:],
                                         negms[c_ % 2][:], ALU.mult, ALU.add).then_inc(S.sStat, 1)

            if sublvl >= 2:
                _stats(0)
            for sc in range(n_sc if sublvl >= 2 else 0):
                for ct in range(16 if (sublvl >= 3 and sublvl != 21 and sublvl != 22) else 0):
                    gt = sc * 16 + ct
                    dst = qf if ct < 8 else kf
                    off = (ct % 8) * 512
                    if ct == 8 and sc >= 1 and sublvl >= 5:
                        for cc_ in range(4):
                            vector.wait_ge(sTkA[cc_], 16 * 8 * sc)
                    vector.wait_ge(S.sFeA, gt + 1)
                    vector.tensor_scalar(tf[ct % 2][:], ef[ct % 2][:], 1.0, 1e-4,
                                         ALU.min, ALU.add)
                    vector.drain()
                    vector.tensor_tensor(dst[:, off:off + 512], tf[ct % 2][:], rf[ct % 2][:],
                                         ALU.add).then_inc(S.sFeV, 1)
                if sublvl >= 4:
                    vector.wait_ge(S.sQB, 4 * (sc + 1))
                    vector.tensor_copy(lgt[sc % 2][:], pb[4][:, 0:32]).then_inc(S.sLgV, 1)
                    vector.drain()
                if sublvl >= 2 and sc + 1 < n_sc:
                    _stats(sc + 1)
                for cc in range(4 if sublvl >= 5 else 0):
                    c = sc * 4 + cc
                    pos = c % CPB
                    for p in range(8):
                        gp = c * 8 + p
                        sb0 = 0 if p % 2 == 0 else 2
                        vector.wait_ge(S.sS, gp + 1)
                        if gp >= 2:
                            vector.wait_ge(S.sNum, gp - 1)
                        vector.tensor_tensor(smt[p % 2][:, 0:128], pb[sb0][:, 0:128], maskut[:],
                                             ALU.mult).then_inc(S.sSmV, 1)
                        vector.tensor_tensor(smt[p % 2][:, 128:256], pb[sb0 + 1][:, 0:128], maskut[:],
                                             ALU.mult).then_inc(S.sSmV, 1)
                        vector.wait_ge(S.sKVp, gp + 1)
                        vector.wait_ge(S.sNum, gp + 1)
                        kbank = 4 + (p % 2)
                        kcol0 = 0
                        if os.environ.get("KK_ST7", "0") == "1":
                            kbank, kcol0 = 7, 192
                        if pos == 0:
                            vector.tensor_scalar(s0f[0:64, p * 130: p * 130 + 65],
                                                 pb[kbank][0:64, kcol0:kcol0 + 65],
                                                 1.0, None, ALU.mult)
                            vector.tensor_scalar(s0f[64:128, p * 130 + 65: p * 130 + 130],
                                                 pb[kbank][64:128, kcol0 + 65:kcol0 + 130],
                                                 1.0, None, ALU.mult)
                        else:
                            vector.tensor_tensor(s0f[0:64, p * 130: p * 130 + 65],
                                                 s0f[0:64, p * 130: p * 130 + 65],
                                                 pb[kbank][0:64, kcol0:kcol0 + 65], ALU.add)
                            vector.tensor_tensor(s0f[64:128, p * 130 + 65: p * 130 + 130],
                                                 s0f[64:128, p * 130 + 65: p * 130 + 130],
                                                 pb[kbank][64:128, kcol0 + 65:kcol0 + 130],
                                                 ALU.add)
                        vector.drain()
                        vector.tensor_copy(s0b[:, p * 130:(p + 1) * 130],
                                           s0f[:, p * 130:(p + 1) * 130]).then_inc(S.sStV, 1)
                        nbank = 6 + (p % 2)
                        if gp >= 2:
                            vector.wait_ge(S.sDiv, gp - 1)
                        vector.tensor_scalar(dena[2 * (p % 2)][:], pb[nbank][:, 64:65],
                                             EPS_DEN, None, ALU.add)
                        vector.tensor_scalar(dena[2 * (p % 2) + 1][:], pb[nbank][:, 129:130],
                                             EPS_DEN, None, ALU.add)
                        vector.drain()
                        vector.reciprocal(dena[2 * (p % 2)][:], dena[2 * (p % 2)][:]
                                          ).then_inc(S.sRcp, 1)
                        vector.reciprocal(dena[2 * (p % 2) + 1][:], dena[2 * (p % 2) + 1][:]
                                          ).then_inc(S.sRcp, 1)
                    if sublvl < 7:
                        continue
                    vector.wait_ge(S.sPj, c + 1)
                    if c >= 2:
                        vector.wait_ge(sPwA[c % 2], 16 * ((c - 2) // 2 + 1))
                        if stage < 3:
                            vector.wait_ge(sOwA[c % 2], 16 * ((c - 2) // 2 + 1))
                    vector.tensor_tensor(part_sb[c % 2][:, 0:512], pb[6][:, 0:512],
                                         xfp[c % 9][:, 0:512], ALU.add)
                    vector.tensor_tensor(part_sb[c % 2][:, 512:1024], pb[7][:, 0:512],
                                         xfp[c % 9][:, 512:1024], ALU.add)
                    vector.drain()
                    vector.memset(fence_v[:], 0.0).then_inc(S.sRes, 1)
                    if sublvl < 6:
                        continue
                    lgc = lgt[sc % 2][:, cc * 8: cc * 8 + 8]
                    vector.max(mx8[:], lgc)
                    vector.drain()
                    vector.max_index(idxu[:], mx8[:], lgc)
                    vector.drain()
                    vector.tensor_copy(idxf[:], idxu[:])
                    vector.tensor_tensor(dcol[:, c:c + 1], mx8[:, 0:1], mx8[:, 1:2], ALU.subtract)
                    vector.drain()
                    vector.tensor_scalar(m1s[:, c * NE:(c + 1) * NE], iota8[:], idxf[:, 0:1],
                                         None, ALU.is_equal)
                    vector.tensor_scalar(m2s[:, c * NE:(c + 1) * NE], iota8[:], idxf[:, 1:2],
                                         None, ALU.is_equal)
                    vector.drain()
                    vector.tensor_tensor(msk[:, c * NE:(c + 1) * NE], m1s[:, c * NE:(c + 1) * NE],
                                         m2s[:, c * NE:(c + 1) * NE], ALU.add)
                    vector.drain()
                    vector.tensor_tensor(cntacc[:], cntacc[:], msk[:, c * NE:(c + 1) * NE],
                                         ALU.add).then_inc(S.sRout, 1)
                    vector.drain()

            if stage < 2:
                return
            vector.tensor_copy(cntbf[:], cntacc[:]).then_inc(S.sCntV1, 1)
            vector.wait_ge(S.sCntP, 1)
            vector.tensor_copy(cnt8[:], pb[2][0:NE, 0:1]).then_inc(S.sCntV2, 1)
            vector.wait_ge(S.sBase1, 1)
            vector.tensor_copy(base_row[:], pb[3][0:1, 0:NE]).then_inc(S.sBase2, 1)
            vector.wait_ge(S.sBase3, 1)
            vector.tensor_scalar(thr[:], pb[4][:, 0:NE], -1.0, float(CAP), ALU.mult, ALU.add)
            vector.drain()
            vector.tensor_scalar(thr2[:], thr[:], float(CAP_PAD), None, ALU.min
                                 ).then_inc(S.sBaseV, 1)
            vector.memset(baseprev[:], 0.0).then_inc(S.sRkV, 1)
            vector.drain()
            for c in range(NCH):
                bank = c % 2
                vector.wait_ge(S.sRkP, c + 1)
                if c < NCH - 1:
                    vector.tensor_tensor(baseprev[:], baseprev[:],
                                         pb[6][0:1, (c % 2) * NE:(c % 2) * NE + NE],
                                         ALU.add).then_inc(S.sRkV, 1)
                    vector.drain()
                rank_ps = pb[bank][:, 0:NE]
                vector.tensor_tensor(rp_sb[:], rank_ps, eoff[:], ALU.add)
                vector.tensor_tensor(k8_sb[:], rank_ps, thr2[:], ALU.is_lt)
                vector.drain()
                vector.tensor_tensor(mkj[0][:], m1s[:, c * NE:(c + 1) * NE], k8_sb[:], ALU.mult)
                vector.tensor_tensor(mkj[1][:], m2s[:, c * NE:(c + 1) * NE], k8_sb[:], ALU.mult)
                vector.drain()
                vector.tensor_tensor(uj[0][:], mkj[0][:], rp_sb[:], ALU.mult)
                vector.tensor_tensor(uj[1][:], mkj[1][:], rp_sb[:], ALU.mult)
                vector.drain()
                for j in range(2):
                    vector.reduce_sum(scol[j][:], uj[j][:], axis=AX.X)
                    vector.reduce_sum(kcol[j][:], mkj[j][:], axis=AX.X)
                vector.drain()
                vector.tensor_copy(kk1[:, c:c + 1], kcol[0][:])
                vector.tensor_copy(kk2[:, c:c + 1], kcol[1][:])
                for j in range(2):
                    vector.tensor_scalar(dumpc[j][:], iota128[:], 2.0,
                                         float(DUMP0 + 256 * c + j), ALU.mult, ALU.add)
                    vector.tensor_scalar(tcol[j][:], kcol[j][:], -1.0, 1.0, ALU.mult, ALU.add)
                vector.drain()
                for j in range(2):
                    vector.tensor_tensor(wcol[j][:], tcol[j][:], dumpc[j][:], ALU.mult)
                vector.drain()
                for j in range(2):
                    vector.tensor_tensor(dcol2[j][:], scol[j][:], wcol[j][:], ALU.add)
                vector.drain()
                for j in range(2):
                    vector.tensor_copy(dscat[:, 2 * c + j: 2 * c + j + 1], dcol2[j][:])
                    vector.tensor_scalar(wcol[j][:], tcol[j][:], float(YZERO), None, ALU.mult)
                vector.drain()
                for j in range(2):
                    vector.tensor_tensor(dcol2[j][:], scol[j][:], wcol[j][:], ALU.add)
                vector.drain()
                vector.tensor_copy(ygat[:, 2 * c: 2 * c + 1], dcol2[0][:])
                vector.tensor_copy(ygat[:, 2 * c + 1: 2 * c + 2], dcol2[1][:]
                                   ).then_inc(S.sDestV, 1)
                vector.drain()
            vector.wait_ge(S.sGate, 1)
            vector.tensor_tensor(g1k[:], g1s[:], kk1[:], ALU.mult)
            vector.tensor_scalar(g2tmp[:], g1s[:], -1.0, 1.0, ALU.mult, ALU.add)
            vector.drain()
            vector.tensor_tensor(g2k[:], g2tmp[:], kk2[:], ALU.mult).then_inc(S.sGate, 1)
            vector.drain()
            vector.memset(fence_v[:], 0.0).then_inc(S.sP1V, 1)

            for c in (range(NCH) if stage >= 3 else []):
                vector.wait_ge(sLptA[c % 4], 16 * (c // 4 + 1))
                vector.wait_ge(sYgA[c % 4], 32 * (c // 4 + 1))
                if c >= 4:
                    vector.wait_ge(sOwA[c % 4], 16 * (c // 4))
                vector.scalar_tensor_tensor(t3[c % 4][:], yg0[c % 4][:], g1k[:, c:c + 1],
                                            pt3[c % 4][:], ALU.mult, ALU.add)
                vector.drain()
                vector.scalar_tensor_tensor(o3[c % 4][:], yg1[c % 4][:], g2k[:, c:c + 1],
                                            t3[c % 4][:], ALU.mult, ALU.add).then_inc(S.sP3b, 1)

        @block.scalar
        def _(scalar):
            scalar.wait_ge(S.sPreS, 16 * NPRES)
            scalar.wait_ge(S.sPreG, 16 * NPREG)

            def _rstd(sc_):
                for cc_ in range(4):
                    c_ = sc_ * 4 + cc_
                    scalar.wait_ge(S.sBnA, c_ + 1)
                    if c_ >= 2:
                        scalar.wait_ge(S.sStat, c_ - 1)
                    scalar.activation(rstd[c_ % 2][:], mv[c_ % 2][:, 1:2], AF.Ln,
                                      bias=epsz[:, 0:1], scale=1.0)
                    scalar.drain()
                    scalar.activation(rstd[c_ % 2][:], rstd[c_ % 2][:], AF.Exp,
                                      bias=epsz[:, 1:2], scale=-0.5).then_inc(S.sRst, 1)

            if sublvl >= 2:
                _rstd(0)
            for sc in range(n_sc if sublvl >= 2 else 0):
                for ct in range(16 if (sublvl >= 3 or sublvl == 22) else 0):
                    gt = sc * 16 + ct
                    bank = ct % 2
                    scl = QSCALE if ct < 8 else 1.0
                    scalar.wait_ge(S.sQA, gt + 1)
                    if gt >= 2 and sublvl != 22:
                        scalar.wait_ge(S.sFeV, gt - 1)
                    scalar.activation(ef[ct % 2][:], pb[bank][:, 0:512], AF.Exp,
                                      bias=bqks[:, ct:ct + 1], scale=scl)
                    scalar.activation(rf[ct % 2][:], pb[bank][:, 0:512], AF.Relu,
                                      bias=bqks[:, ct:ct + 1], scale=scl).then_inc(S.sFeA, 1)
                for cc in range(4 if sublvl >= 4 else 0):
                    c = sc * 4 + cc
                    scalar.wait_ge(S.sQB, c + 1)
                    for h in range(16):
                        ins = pb[2 + h // 8][:, (h % 8) * 64:(h % 8) * 64 + 64]
                        i_ = scalar.activation(vsb[cc][:, h * 65: h * 65 + 64], ins, AF.Copy)
                    i_.then_inc(S.sVe, 1)
                    scalar.drain()
                    scalar.activation(fence_a[:], bqks[:, 0:1], AF.Copy).then_inc(S.sVeD, 1)
                if sublvl >= 2 and sc + 1 < n_sc:
                    _rstd(sc + 1)
                if sublvl >= 7:
                    for cc in range(4):
                        c = sc * 4 + cc
                        if c >= 2:
                            scalar.wait_ge(sTatA[c % 2], 16 * 8 * ((c - 2) // 2 + 1))
                        for p in range(8):
                            gp = c * 8 + p
                            nbank = 6 + (p % 2)
                            scalar.wait_ge(S.sRcp, 2 * (gp + 1))
                            scalar.activation(attn_sb[c % 2][:, (2 * p) * 64:(2 * p) * 64 + 64],
                                              pb[nbank][:, 0:64], AF.Copy,
                                              scale=dena[2 * (p % 2)][:])
                            scalar.activation(attn_sb[c % 2][:, (2 * p + 1) * 64:(2 * p + 1) * 64 + 64],
                                              pb[nbank][:, 65:129], AF.Copy,
                                              scale=dena[2 * (p % 2) + 1][:]).then_inc(S.sDiv, 1)
                        scalar.drain()
                        scalar.activation(fence_a[:], bqks[:, 0:1], AF.Copy).then_inc(S.sDivD, 1)
                        scalar.wait_ge(S.sDivD, c + 1)
                        if c >= 1:
                            scalar.wait_ge(S.sPj, c)
                        for k in range(8):
                            scalar.dma_start(out=attnt[c % 2][:, k * 128:(k + 1) * 128],
                                             in_=attn_sb[c % 2][:, k * 128:(k + 1) * 128],
                                             transpose=True).then_inc(sTatA[c % 2], 16)

            if stage < 2:
                return
            scalar.wait_ge(S.sRout, NCH)
            scalar.activation(g1s[:], dcol[:], AF.Sigmoid).then_inc(S.sGate, 1)
            scalar.drain()
            scalar.activation(fence_a[:], g1s[:, 0:1], AF.Copy).then_inc(S.sP1A, 1)

            if stage < 3:
                return
            scalar.wait_ge(S.sPreA, 16 * NPREA)
            for e in range(NE):
                for g in range(NGRP):
                    gg = e * NGRP + g
                    for ht in range(32):
                        hh = gg * 32 + ht
                        bank = ht % 3
                        scalar.wait_ge(S.sH, hh + 1)
                        scalar.activation(ht_sb[:, ht * GROWS:(ht + 1) * GROWS],
                                          pb[bank][:, 0:GROWS], AF.Gelu_apprx_tanh,
                                          bias=b1_sb[:, e * 32 + ht: e * 32 + ht + 1],
                                          scale=1.0).then_inc(S.sGel, 1)
                    for sub in range(4):
                        ss = gg * 4 + sub
                        ybk = 3 + (sub % 2) * 2
                        scalar.wait_ge(S.sY, ss + 1)
                        if ss >= 2:
                            scalar.wait_ge(sYwA[sub % 2], 16 * ((ss - 2) // 2 + 1))
                        scalar.activation(y_sb[sub % 2][:, 0:512], pb[ybk][:, 0:512], AF.Copy)
                        scalar.activation(y_sb[sub % 2][:, 512:1024], pb[ybk + 1][:, 0:512],
                                          AF.Copy).then_inc(S.sYe, 1)

    st.close()
    return nc


_NC_CACHE = {}


def _consts():
    i = np.arange(128)
    maskut = (i[:, None] <= i[None, :]).astype(ml_dtypes.bfloat16)
    urank = (i[:, None] < i[None, :]).astype(ml_dtypes.bfloat16)
    ones128 = np.ones((128, 128), ml_dtypes.bfloat16)
    ones1x128f = np.ones((1, 128), np.float32)
    iota8 = np.broadcast_to(np.arange(NE, dtype=np.float32), (128, NE)).copy()
    eoff = np.broadcast_to((np.arange(NE) * CAP_PAD).astype(np.float32), (128, NE)).copy()
    iota128 = np.arange(128, dtype=np.float32).reshape(128, 1)
    epsz = np.zeros((128, 2), np.float32); epsz[:, 0] = EPS_LN
    return maskut, urank, ones128, ones1x128f, iota8, eoff, iota128, epsz


def kernel(L0_attn_x, L0_attn_W_qkv, L0_attn_b_qkv, L0_attn_W_out, L0_attn_b_out,
           L0_ffn_Wg, L0_ffn_W1, L0_ffn_b1, L0_ffn_W2, L0_ffn_b2):
    xf = np.ascontiguousarray(np.asarray(L0_attn_x, np.float32).reshape(NTOT, D))
    maskut, urank, ones128, ones1x128f, iota8, eoff, iota128, epsz = _consts()

    if "nc" not in _NC_CACHE:
        _NC_CACHE["nc"] = build_program(3, NSC, 7)
    nc = _NC_CACHE["nc"]

    shared = dict(
        wqkv=np.ascontiguousarray(np.asarray(L0_attn_W_qkv, np.float32)),
        bqkv=np.ascontiguousarray(np.asarray(L0_attn_b_qkv, np.float32)),
        wout=np.ascontiguousarray(np.asarray(L0_attn_W_out, np.float32)),
        bout=np.ascontiguousarray(np.asarray(L0_attn_b_out, np.float32)),
        wg=np.ascontiguousarray(np.asarray(L0_ffn_Wg, np.float32)),
        w1=np.ascontiguousarray(np.asarray(L0_ffn_W1, np.float32)),
        b1=np.ascontiguousarray(np.asarray(L0_ffn_b1, np.float32)),
        w2=np.ascontiguousarray(np.asarray(L0_ffn_W2, np.float32)),
        b2=np.ascontiguousarray(np.asarray(L0_ffn_b2, np.float32)),
        maskut=maskut, urank=urank, ones128=ones128, ones1x128f=ones1x128f,
        iota8=iota8, eoff=eoff, iota128=iota128, epsz=epsz,
    )
    in_maps = []
    for c in range(NCORES):
        m = dict(shared)
        m["x"] = np.ascontiguousarray(xf[c * TOK:(c + 1) * TOK])
        m["prevmask"] = (np.arange(NCORES) < c).astype(np.float32).reshape(NCORES, 1)
        in_maps.append(m)

    res = run_bass_kernel_spmd(nc, in_maps, core_ids=list(range(NCORES)))
    out = np.concatenate([res.results[c]["out"] for c in range(NCORES)], axis=0)
    return out.reshape(B, L, D)


if __name__ == "__main__":
    import time
    t0 = time.time()
    nc = build_program()
    print(f"build ok in {time.time()-t0:.1f}s; instructions: "
          f"{sum(1 for _ in nc.all_instructions())}")


# version 07
# speedup vs baseline: 1.0760x; 1.0251x over previous
import math
import os
import numpy as np
import ml_dtypes
from contextlib import ExitStack

import concourse.bass as bass
import concourse.mybir as mybir
from concourse.bass_utils import run_bass_kernel_spmd
from concourse.bass import IndirectOffsetOnAxis

F32 = mybir.dt.float32
BF16 = mybir.dt.bfloat16
U32 = mybir.dt.uint32
AF = mybir.ActivationFunctionType
ALU = mybir.AluOpType
AX = mybir.AxisListType

NCORES = 8
B, L, D = 16, 4096, 1024
NH, EH = 16, 64
TOK = 8192
NCH = 64
CPB = 32
NSC = 16
NE, HID, TOPK = 8, 4096, 2
NTOT = B * L
CAP = int(math.ceil(NTOT * TOPK / NE * 1.05))
CAP_PAD = 2560
GROWS = 512
NGRP = CAP_PAD // GROWS
NBUF = NE * CAP_PAD
DUMP0 = NBUF
XNROWS = NBUF + 2 * TOK
YZERO = NBUF
EPS_LN = 1e-5
EPS_DEN = 1e-6
QSCALE = EH ** (-0.5)


def build_program(stage=3, n_sc=NSC, sublvl=7):
    nc = bass.Bass("TRN2", target_bir_lowering=False, debug=False,
                   dynamic_dma_scratch_size=16384)

    x_in = nc.dram_tensor("x", [TOK, D], F32, kind="ExternalInput").ap()
    wqkv_in = nc.dram_tensor("wqkv", [D, 3 * D], F32, kind="ExternalInput").ap()
    bqkv_in = nc.dram_tensor("bqkv", [3 * D], F32, kind="ExternalInput").ap()
    wout_in = nc.dram_tensor("wout", [D, D], F32, kind="ExternalInput").ap()
    bout_in = nc.dram_tensor("bout", [D], F32, kind="ExternalInput").ap()
    wg_in = nc.dram_tensor("wg", [D, NE], F32, kind="ExternalInput").ap()
    w1_in = nc.dram_tensor("w1", [NE, D, HID], F32, kind="ExternalInput").ap()
    b1_in = nc.dram_tensor("b1", [NE, HID], F32, kind="ExternalInput").ap()
    w2_in = nc.dram_tensor("w2", [NE, HID, D], F32, kind="ExternalInput").ap()
    b2_in = nc.dram_tensor("b2", [NE, D], F32, kind="ExternalInput").ap()
    maskut_in = nc.dram_tensor("maskut", [128, 128], BF16, kind="ExternalInput").ap()
    urank_in = nc.dram_tensor("urank", [128, 128], BF16, kind="ExternalInput").ap()
    ones128_in = nc.dram_tensor("ones128", [128, 128], BF16, kind="ExternalInput").ap()
    ones1x128f_in = nc.dram_tensor("ones1x128f", [1, 128], F32, kind="ExternalInput").ap()
    iota8_in = nc.dram_tensor("iota8", [128, NE], F32, kind="ExternalInput").ap()
    eoff_in = nc.dram_tensor("eoff", [128, NE], F32, kind="ExternalInput").ap()
    eoff512_in = nc.dram_tensor("eoff512", [128, NCH * NE], F32, kind="ExternalInput").ap()
    dmp0_in = nc.dram_tensor("dmp0", [128, NCH], F32, kind="ExternalInput").ap()
    dmp1_in = nc.dram_tensor("dmp1", [128, NCH], F32, kind="ExternalInput").ap()
    iota128_in = nc.dram_tensor("iota128", [128, 1], F32, kind="ExternalInput").ap()
    prevmask_in = nc.dram_tensor("prevmask", [NCORES, 1], F32, kind="ExternalInput").ap()
    epsz_in = nc.dram_tensor("epsz", [128, 2], F32, kind="ExternalInput").ap()
    out_dram = nc.dram_tensor("out", [TOK, D], F32, kind="ExternalOutput").ap()
    dbg_dscat = nc.dram_tensor("dbg_dscat", [128, 2 * NCH], U32, kind="ExternalOutput").ap()
    dbg_ygat = nc.dram_tensor("dbg_ygat", [128, 2 * NCH], U32, kind="ExternalOutput").ap()
    dbg_g1k = nc.dram_tensor("dbg_g1k", [128, NCH], F32, kind="ExternalOutput").ap()
    dbg_g2k = nc.dram_tensor("dbg_g2k", [128, NCH], F32, kind="ExternalOutput").ap()
    dbg_allcnt = nc.dram_tensor("dbg_allcnt", [NCORES, NE], F32, kind="ExternalOutput").ap()
    dbg_thr = nc.dram_tensor("dbg_thr", [128, NE], F32, kind="ExternalOutput").ap()
    DBG3 = os.environ.get("KK_DBG3", "0") == "1"
    if DBG3:
        dbg_qf = nc.dram_tensor("dbg_qf", [128, 4096], BF16, kind="ExternalOutput").ap()
        dbg_kf = nc.dram_tensor("dbg_kf", [128, 4096], BF16, kind="ExternalOutput").ap()
        dbg_v = nc.dram_tensor("dbg_v", [128, 1040], BF16, kind="ExternalOutput").ap()
        dbg_xnt = nc.dram_tensor("dbg_xnt", [128, 4096], BF16, kind="ExternalOutput").ap()
        dbg_attn = nc.dram_tensor("dbg_attn", [128, 1024], BF16, kind="ExternalOutput").ap()
        dbg_s0 = nc.dram_tensor("dbg_s0", [128, 1040], F32, kind="ExternalOutput").ap()
        dbg_lgt = nc.dram_tensor("dbg_lgt", [128, 32], F32, kind="ExternalOutput").ap()

    xn_scr = nc.dram_tensor("xn_scr", [TOK, D], BF16).ap()
    xn_buf = nc.dram_tensor("xn_buf", [XNROWS, D], BF16).ap()
    y_buf = nc.dram_tensor("y_buf", [NBUF + 1, D], BF16).ap()
    part_scr = nc.dram_tensor("part_scr", [TOK, D], F32).ap()
    cc_in = nc.dram_tensor("cc_in", [1, NE], F32)
    cc_out = nc.dram_tensor("cc_out", [NCORES, NE], F32)
    ctot_dram = nc.dram_tensor("ctot_scr", [1, NCH * NE], BF16)
    pfl_dram = nc.dram_tensor("pfl_scr", [NCH, NE], F32)
    pfg_dram = nc.dram_tensor("pfg_scr", [NCH, NE], F32)

    st = ExitStack()

    def sb(name, shape, dt):
        return st.enter_context(nc.sbuf_tensor(name, shape, dt))

    def psf(name):
        return st.enter_context(nc.psum_tensor(name, [128, 512], F32))

    maskut = sb("maskut_s", [128, 128], BF16)
    urank = sb("urank_s", [128, 128], BF16)
    ones128 = sb("ones128b", [128, 128], BF16)
    ones1x128f = sb("ones1x128f_s", [1, 128], F32)
    iota8 = sb("iota8_s", [128, NE], F32)
    eoff = sb("eoff_s", [128, NE], F32)
    iota128 = sb("iota128_s", [128, 1], F32)
    prevmask = sb("prevmask_s", [NCORES, 1], F32)
    epsz = sb("epsz_s", [128, 2], F32)

    msk = sb("msk", [128, NCH * NE], BF16)
    m1s = sb("m1s", [128, NCH * NE], BF16)
    m2s = sb("m2s", [128, NCH * NE], BF16)
    dcol = sb("dcol", [128, NCH], F32)
    cntacc = sb("cntacc", [128, NE], F32)
    kk1 = sb("kk1", [128, NCH], F32)
    kk2 = sb("kk2", [128, NCH], F32)
    g1k = sb("g1k", [128, NCH], F32)
    g2k = sb("g2k", [128, NCH], F32)
    g1s = sb("g1s", [128, NCH], F32)
    g2tmp = sb("g2tmp", [128, NCH], F32)
    dscat = sb("dscat", [128, 2 * NCH], U32)
    ygat = sb("ygat", [128, 2 * NCH], U32)
    cnt8 = sb("cnt8", [NE, 1], F32)
    cntbf = sb("cntbf", [128, NE], BF16)
    allcnt = sb("allcnt", [NCORES, NE], F32)
    base_row = sb("base_row", [1, NE], F32)
    eoff512 = sb("eoff512_s", [128, NCH * NE], F32)
    dmp0 = sb("dmp0_s", [128, NCH], F32)
    dmp1 = sb("dmp1_s", [128, NCH], F32)
    b1_sb = sb("b1_sb", [128, NE * 32], F32)
    b2e = sb("b2e", [1, 1024], BF16)
    fence_v = sb("fence_v", [128, 1], F32)
    fence_a = sb("fence_a", [128, 1], F32)

    p1 = ExitStack()

    def sb1(name, shape, dt):
        return p1.enter_context(nc.sbuf_tensor(name, shape, dt))

    wqkv_sb = sb1("wqkv_sb", [128, 8 * 3072], BF16)
    wout_sb = sb1("wout_sb", [128, 8 * 1024], BF16)
    wg_sb = sb1("wg_sb", [128, 8 * NE], BF16)
    bqk = sb1("bqk", [128, 16], F32)
    bqks = sb1("bqks", [128, 16], F32)
    bv_row = sb1("bv_row", [1, 1024], BF16)
    bout_row = sb1("bout_row", [1, 1024], BF16)
    zrow = sb1("zrow", [128, 1024], BF16)
    xfp = [sb1(f"xfp{i}", [128, D], F32) for i in range(9)]
    bn12 = [sb1(f"bn12_{i}", [128, 12], F32) for i in range(2)]
    mv = [sb1(f"mv{i}", [128, 2], F32) for i in range(2)]
    rstd = [sb1(f"rstd{i}", [128, 1], F32) for i in range(2)]
    negms = [sb1(f"negms{i}", [128, 1], F32) for i in range(2)]
    xn_sb = [sb1(f"xn_sb{i}", [128, D], BF16) for i in range(2)]
    xnt = [sb1(f"xnt{i}", [128, 8 * 512], BF16) for i in range(2)]
    qf = sb1("qf", [128, 8 * 512], BF16)
    kf = sb1("kf", [128, 8 * 512], BF16)
    ef = [sb1(f"ef{i}", [128, 512], F32) for i in range(2)]
    rf = [sb1(f"rf{i}", [128, 512], F32) for i in range(2)]
    tf = [sb1(f"tf{i}", [128, 512], F32) for i in range(2)]
    vsb = [sb1(f"vsb{i}", [128, 16 * 65], BF16) for i in range(4)]
    ktok = [sb1(f"ktok{i}", [128, 1024], BF16) for i in range(4)]
    smt = [sb1(f"smt{i}", [128, 256], BF16) for i in range(2)]
    s0f = sb1("s0f", [128, 8 * 130], F32)
    s0b = sb1("s0b", [128, 8 * 130], BF16)
    dena = [sb1(f"dena{i}", [128, 1], F32) for i in range(4)]
    attn_sb = [sb1(f"attn_sb{i}", [128, 1024], BF16) for i in range(2)]
    attnt = [sb1(f"attnt{i}", [128, 1024], BF16) for i in range(2)]
    part_sb = [sb1(f"part_sb{i}", [128, D], F32) for i in range(2)]
    lgt = [sb1(f"lgt{i}", [128, 32], F32) for i in range(2)]
    mx8 = sb1("mx8", [128, NE], F32)
    idxu = sb1("idxu", [128, NE], U32)
    idxf = sb1("idxf", [128, NE], F32)
    p1.close()

    p2 = ExitStack()

    def sb2(name, shape, dt):
        return p2.enter_context(nc.sbuf_tensor(name, shape, dt))

    xnc = [sb2(f"xnc{i}", [128, D], BF16) for i in range(2)]
    w1_sb = sb2("w1_sb", [128, 8 * HID], BF16)
    w2_sb = sb2("w2_sb", [128, 32 * 1024], BF16)
    tok2 = [sb2(f"tok2_{i}", [128, D], BF16) for i in range(4)]
    xngt = [sb2(f"xngt{i}", [128, 8 * GROWS], BF16) for i in range(2)]
    ht_sb = sb2("ht_sb", [128, 32 * GROWS], BF16)
    y_sb = [sb2(f"y_sb{i}", [128, 1024], BF16) for i in range(2)]
    p2.close()

    p3 = ExitStack()

    def sb3(name, shape, dt):
        return p3.enter_context(nc.sbuf_tensor(name, shape, dt))

    ctot_row = sb3("ctot_row", [1, NCH * NE], BF16)
    ctot64 = sb3("ctot64", [NCH, NE], BF16)
    pfl64 = sb3("pfl64", [NCH, NE], F32)
    pfg64 = sb3("pfg64", [NCH, NE], F32)
    prefloc512 = sb3("prefloc512", [1, NCH * NE], F32)
    prefglob512 = sb3("prefglob512", [1, NCH * NE], F32)
    k8a = sb3("k8a", [128, NCH * NE], F32)
    k8b = sb3("k8b", [128, NCH * NE], F32)
    rp512 = sb3("rp512", [128, NCH * NE], F32)
    mk1 = sb3("mk1", [128, NCH * NE], F32)
    mk2 = sb3("mk2", [128, NCH * NE], F32)
    sc1 = sb3("sc1", [128, NCH], F32)
    sc2 = sb3("sc2", [128, NCH], F32)
    tc1 = sb3("tc1", [128, NCH], F32)
    tc2 = sb3("tc2", [128, NCH], F32)
    w1c = sb3("w1c", [128, NCH], F32)
    w2c = sb3("w2c", [128, NCH], F32)
    d1c = sb3("d1c", [128, NCH], F32)
    d2c = sb3("d2c", [128, NCH], F32)
    pt3 = [sb3(f"pt3_{i}", [128, D], F32) for i in range(4)]
    yg0 = [sb3(f"yg0_{i}", [128, D], BF16) for i in range(4)]
    yg1 = [sb3(f"yg1_{i}", [128, D], BF16) for i in range(4)]
    t3 = [sb3(f"t3_{i}", [128, D], F32) for i in range(4)]
    o3 = [sb3(f"o3_{i}", [128, D], F32) for i in range(4)]
    p3.close()

    pb = [psf(f"pb{i}") for i in range(8)]

    sems = {}
    slot_sems = {}
    for base, n in [("sLxA", 9), ("sTxnA", 2), ("sTkA", 4), ("sTatA", 2),
                    ("sPwA", 2), ("sXnwA", 2), ("sLxn2A", 2), ("sScatA", 2),
                    ("sRowA", 2), ("sTrsA", 2), ("sYwA", 2), ("sYgA", 4),
                    ("sLptA", 4), ("sOwA", 4)]:
        slot_sems[base] = [st.enter_context(nc.semaphore(f"{base}{i}")) for i in range(n)]
    sLxA = slot_sems["sLxA"]; sTxnA = slot_sems["sTxnA"]; sTkA = slot_sems["sTkA"]
    sTatA = slot_sems["sTatA"]; sPwA = slot_sems["sPwA"]; sXnwA = slot_sems["sXnwA"]
    sLxn2A = slot_sems["sLxn2A"]; sScatA = slot_sems["sScatA"]; sRowA = slot_sems["sRowA"]
    sTrsA = slot_sems["sTrsA"]; sYwA = slot_sems["sYwA"]; sYgA = slot_sems["sYgA"]
    sLptA = slot_sems["sLptA"]; sOwA = slot_sems["sOwA"]
    for name in ["sPreG", "sPreS", "sPreA", "sPreGm", "sMemB",
                 "sStat", "sBnA", "sRst",
                 "sQA", "sFeA", "sFeV", "sQB", "sVe", "sLgV",
                 "sS", "sSmV", "sNum", "sStV", "sDiv", "sPj", "sRes",
                 "sRout", "sKVp",
                 "sCntV1", "sCntP", "sCntV2", "sCCw", "sCC", "sCCl",
                 "sBase1", "sBase2",
                 "sCs", "sCsV", "sCtW", "sCt64", "sPfxL", "sPfxG", "sPfxV",
                 "sPfW", "sPref", "sRkA", "sRkB", "sDestV", "sGate",
                 "sW1", "sW2", "sH", "sGel", "sY", "sYe",
                 "sP3b", "sDbg", "sP1V", "sP1A", "sDivD", "sVeD", "sRcp"]:

        sems[name] = st.enter_context(nc.semaphore(name))
    S = type("S", (), sems)

    NPREG = 8 + 8 + 8 + 1 + 1
    NPRES = 11 + 16
    NPREA = NE * 32
    NMEMB = 1
    NTR_G = 32
    NGG = NE * NGRP

    def wl_after_w1(e):
        return 16 * (NPREG + e * 41 + 8)

    def wl_after_w2(e):
        return 16 * (NPREG + e * 41 + 41)

    with nc.Block() as block:

        @block.sync
        def _(sync):
            for dst, src in [(maskut, maskut_in), (urank, urank_in), (ones128, ones128_in),
                             (ones1x128f, ones1x128f_in), (iota8, iota8_in),
                             (eoff512, eoff512_in), (dmp0, dmp0_in), (dmp1, dmp1_in),
                             (iota128, iota128_in), (prevmask, prevmask_in),
                             (epsz, epsz_in)]:
                sync.dma_start(out=dst[:], in_=src).then_inc(S.sPreS, 16)
            for ct in range(16):
                sync.dma_start(out=bqk[:, ct:ct + 1],
                               in_=bqkv_in[ct * 128:(ct + 1) * 128].rearrange("(a b) -> a b", b=1)
                               ).then_inc(S.sPreS, 16)

            def _pstores(sc_):
                for cc_ in range(4):
                    c_ = sc_ * 4 + cc_
                    sync.wait_ge(S.sRes, c_ + 1)
                    sync.dma_start(out=part_scr[c_ * 128:(c_ + 1) * 128, :],
                                   in_=part_sb[c_ % 2][:]).then_inc(sPwA[c_ % 2], 16)
                    if stage < 3:
                        sync.dma_start(out=out_dram[c_ * 128:(c_ + 1) * 128, :],
                                       in_=part_sb[c_ % 2][:]).then_inc(sOwA[c_ % 2], 16)

            for sc in range(n_sc if sublvl >= 2 else 0):
                for cc in range(4):
                    c = sc * 4 + cc
                    if c >= 9 and sublvl >= 7:
                        sync.wait_ge(S.sRes, c - 8)
                    sync.dma_start(out=xfp[c % 9][:], in_=x_in[c * 128:(c + 1) * 128, :]
                                   ).then_inc(sLxA[c % 9], 16)
                for cc in range(4):
                    c = sc * 4 + cc
                    sync.wait_ge(S.sStat, c + 1)
                    if sc >= 2 and cc == 0:
                        sync.wait_ge(S.sQB, 4 * (sc - 1))
                    for k in range(8):
                        sync.dma_start(
                            out=xnt[sc % 2][:, k * 512 + cc * 128: k * 512 + cc * 128 + 128],
                            in_=xn_sb[c % 2][:, k * 128:(k + 1) * 128],
                            transpose=True).then_inc(sTxnA[c % 2], 16)
                    sync.dma_start(out=xn_scr[c * 128:(c + 1) * 128, :], in_=xn_sb[c % 2][:]
                                   ).then_inc(sXnwA[c % 2], 16)
                if sublvl >= 7 and sc >= 1:
                    _pstores(sc - 1)
                if sublvl >= 5:
                    sync.wait_ge(S.sFeV, sc * 16 + 16)
                    for cc in range(4):
                        c = sc * 4 + cc
                        if c >= 4:
                            sync.wait_ge(S.sKVp, 8 * (c - 3))
                        for p in range(8):
                            sync.dma_start(out=ktok[cc][:, p * 128:(p + 1) * 128],
                                           in_=kf[:, p * 512 + cc * 128: p * 512 + cc * 128 + 128],
                                           transpose=True).then_inc(sTkA[cc], 16)
            if sublvl >= 7 and n_sc >= 1:
                _pstores(n_sc - 1)

            for c in (range(NCH) if stage >= 2 else []):
                if c == 0:
                    sync.wait_ge(sXnwA[0], 16 * 32)
                    sync.wait_ge(sXnwA[1], 16 * 32)
                    sync.wait_ge(S.sRes, NCH)
                    sync.wait_ge(S.sP1V, 1)
                    sync.wait_ge(S.sP1A, 1)
                if c >= 2:
                    sync.wait_ge(sScatA[c % 2], 32 * ((c - 2) // 2 + 1))
                sync.dma_start(out=xnc[c % 2][:], in_=xn_scr[c * 128:(c + 1) * 128, :]
                               ).then_inc(sLxn2A[c % 2], 16)

            for e in (range(NE) if stage >= 3 else []):
                for g in range(NGRP):
                    gg = e * NGRP + g
                    r0 = e * CAP_PAD + g * GROWS
                    if gg == 0:
                        sync.wait_ge(sScatA[0], 32 * 32)
                        sync.wait_ge(sScatA[1], 32 * 32)
                        sync.wait_ge(S.sP1V, 1)
                        sync.wait_ge(S.sP1A, 1)
                        sync.wait_ge(S.sPreGm, 2)
                        sync.wait_ge(sPwA[0], 16 * 32)
                        sync.wait_ge(sPwA[1], 16 * 32)
                    if gg >= 1:
                        sync.wait_ge(sTrsA[(gg - 1) % 2], 16 * NTR_G * ((gg - 1) // 2 + 1))
                    for i in range(4):
                        sync.dma_start(out=tok2[i][:],
                                       in_=xn_buf[r0 + i * 128: r0 + (i + 1) * 128, :]
                                       ).then_inc(sRowA[gg % 2], 16)
                    sync.wait_ge(sRowA[gg % 2], 16 * 4 * (gg // 2 + 1))
                    if gg >= 2:
                        sync.wait_ge(S.sH, 32 * (gg - 1))
                    for i in range(4):
                        for k in range(8):
                            sync.dma_start(
                                out=xngt[gg % 2][:, k * GROWS + i * 128: k * GROWS + i * 128 + 128],
                                in_=tok2[i][:, k * 128:(k + 1) * 128],
                                transpose=True).then_inc(sTrsA[gg % 2], 16)
                    if gg >= 1:
                        pg = gg - 1
                        pe_, pgr = pg // NGRP, pg % NGRP
                        pr0 = pe_ * CAP_PAD + pgr * GROWS
                        for sub in range(4):
                            ss = pg * 4 + sub
                            sync.wait_ge(S.sYe, ss + 1)
                            sync.dma_start(out=y_buf[pr0 + sub * 128: pr0 + (sub + 1) * 128, :],
                                           in_=y_sb[sub % 2][:]).then_inc(sYwA[sub % 2], 16)
            if stage >= 2:
                sync.wait_ge(S.sDestV, 1)
                sync.wait_ge(S.sGate, 2)
                sync.wait_ge(S.sCCl, 16)
                sync.dma_start(out=dbg_dscat, in_=dscat[:]).then_inc(S.sDbg, 16)
                sync.dma_start(out=dbg_ygat, in_=ygat[:]).then_inc(S.sDbg, 16)
                sync.dma_start(out=dbg_g1k, in_=g1k[:]).then_inc(S.sDbg, 16)
                sync.dma_start(out=dbg_g2k, in_=g2k[:]).then_inc(S.sDbg, 16)
                sync.dma_start(out=dbg_allcnt, in_=allcnt[:]).then_inc(S.sDbg, 16)
                sync.dma_start(out=dbg_thr, in_=cntacc[:]).then_inc(S.sDbg, 16)
            if stage >= 3:
                pg = NGG - 1
                pe_, pgr = pg // NGRP, pg % NGRP
                pr0 = pe_ * CAP_PAD + pgr * GROWS
                for sub in range(4):
                    ss = pg * 4 + sub
                    sync.wait_ge(S.sYe, ss + 1)
                    sync.dma_start(out=y_buf[pr0 + sub * 128: pr0 + (sub + 1) * 128, :],
                                   in_=y_sb[sub % 2][:]).then_inc(sYwA[sub % 2], 16)

            for c in (range(NCH) if stage >= 3 else []):
                if c == 0:
                    sync.wait_ge(sPwA[0], 16 * 32)
                    sync.wait_ge(sPwA[1], 16 * 32)
                    sync.wait_ge(S.sYe, 4 * NGG)
                if c >= 4:
                    sync.wait_ge(S.sP3b, c - 3)
                sync.dma_start(out=pt3[c % 4][:], in_=part_scr[c * 128:(c + 1) * 128, :]
                               ).then_inc(sLptA[c % 4], 16)
                if c >= 3:
                    cs = c - 3
                    sync.wait_ge(S.sP3b, cs + 1)
                    sync.dma_start(out=out_dram[cs * 128:(cs + 1) * 128, :], in_=o3[cs % 4][:]
                                   ).then_inc(sOwA[cs % 4], 16)
            if stage >= 3:
                for cs in range(NCH - 3, NCH):
                    sync.wait_ge(S.sP3b, cs + 1)
                    sync.dma_start(out=out_dram[cs * 128:(cs + 1) * 128, :], in_=o3[cs % 4][:]
                                   ).then_inc(sOwA[cs % 4], 16)
            if sublvl >= 7:
                if stage >= 3:
                    for i in range(4):
                        sync.wait_ge(sOwA[i], 16 * (NCH // 4))
                else:
                    sync.wait_ge(sOwA[0], 16 * 2 * n_sc)
                    sync.wait_ge(sOwA[1], 16 * 2 * n_sc)
                if stage >= 2:
                    sync.wait_ge(S.sDbg, 16 * 6)
            else:
                sync.wait_ge(S.sPreGm, 1)
                if os.environ.get("KK_NOMEMSET", "0") != "1":
                    sync.wait_ge(S.sMemB, 16 * NMEMB)
                sync.wait_ge(S.sPreG, 16 * NPREG)
                sync.wait_ge(S.sPreA, 16 * NPREA)
                if sublvl >= 2:
                    sync.wait_ge(sXnwA[0], 16 * 2 * n_sc)
                    sync.wait_ge(sXnwA[1], 16 * 2 * n_sc)
                if sublvl == 21:
                    sync.wait_ge(S.sQA, int(os.environ.get("KK_CT", "16")) * n_sc)
                if sublvl == 22:
                    sync.wait_ge(S.sFeA, 16 * n_sc)
                if sublvl >= 3 and sublvl not in (21, 22):
                    sync.wait_ge(S.sFeV, 16 * n_sc)
                if sublvl >= 4:
                    sync.wait_ge(S.sVe, 4 * n_sc)
                    sync.wait_ge(S.sLgV, n_sc)
                if sublvl >= 5:
                    sync.wait_ge(S.sDiv, 32 * n_sc)
                if sublvl >= 6:
                    sync.wait_ge(S.sRout, 4 * n_sc)

        @block.gpsimd
        def _(gpsimd):
            for k in range(8):
                gpsimd.dma_start(out=wqkv_sb[:, k * 3072:(k + 1) * 3072],
                                 in_=wqkv_in[k * 128:(k + 1) * 128, :]).then_inc(S.sPreG, 16)
            for k in range(8):
                gpsimd.dma_start(out=wout_sb[:, k * 1024:(k + 1) * 1024],
                                 in_=wout_in[k * 128:(k + 1) * 128, :]).then_inc(S.sPreG, 16)
            for k in range(8):
                gpsimd.dma_start(out=wg_sb[:, k * NE:(k + 1) * NE],
                                 in_=wg_in[k * 128:(k + 1) * 128, :]).then_inc(S.sPreG, 16)
            gpsimd.dma_start(out=bv_row[:], in_=bqkv_in[2048:3072].rearrange("(a b) -> a b", a=1)
                             ).then_inc(S.sPreG, 16)
            gpsimd.dma_start(out=bout_row[:], in_=bout_in.rearrange("(a b) -> a b", a=1)
                             ).then_inc(S.sPreG, 16)
            gpsimd.memset(zrow[:], 0.0)
            gpsimd.memset(s0f[:], 0.0)
            gpsimd.memset(s0b[:], 0.0).then_inc(S.sPreGm, 1)
            gpsimd.wait_ge(S.sPreGm, 1)
            gpsimd.dma_start(out=y_buf[YZERO:YZERO + 1, :], in_=zrow[0:1, 0:1024]
                             ).then_inc(S.sMemB, 16)
            for e in range(NE):
                for ht in range(32):
                    gpsimd.dma_start(out=b1_sb[:, e * 32 + ht: e * 32 + ht + 1],
                                     in_=b1_in[e, ht * 128:(ht + 1) * 128].rearrange("(a b) -> a b", b=1)
                                     ).then_inc(S.sPreA, 16)

            if stage < 2:
                return
            gpsimd.wait_ge(S.sCntV2, 1)
            gpsimd.dma_start(out=cc_in.ap().rearrange("a b -> b a"), in_=cnt8[:]
                             ).then_inc(S.sCCw, 16)
            gpsimd.wait_ge(S.sCsV, 1)
            gpsimd.dma_start(out=ctot_dram.ap(), in_=ctot_row[:]).then_inc(S.sCtW, 16)
            gpsimd.wait_ge(S.sCtW, 16)
            gpsimd.dma_start(out=ctot64[:],
                             in_=ctot_dram.ap().rearrange("a (c e) -> (a c) e", e=NE)
                             ).then_inc(S.sCt64, 16)
            gpsimd.wait_ge(S.sCCw, 16)
            gpsimd.collective_compute(
                "AllGather", ALU.bypass,
                replica_groups=[list(range(NCORES))],
                ins=[cc_in.ap().opt()], outs=[cc_out.ap().opt()],
            ).then_inc(S.sCC)
            gpsimd.wait_ge(S.sCC, 1)
            gpsimd.dma_start(out=allcnt[:], in_=cc_out.ap()).then_inc(S.sCCl, 16)
            gpsimd.wait_ge(S.sPfxV, 1)
            gpsimd.dma_start(out=pfl_dram.ap(), in_=pfl64[:]).then_inc(S.sPfW, 16)
            gpsimd.dma_start(out=pfg_dram.ap(), in_=pfg64[:]).then_inc(S.sPfW, 16)
            gpsimd.wait_ge(S.sPfW, 32)
            gpsimd.dma_start(out=prefloc512[:],
                             in_=pfl_dram.ap().rearrange("(a c) e -> a (c e)", a=1)
                             ).then_inc(S.sPref, 16)
            gpsimd.dma_start(out=prefglob512[:],
                             in_=pfg_dram.ap().rearrange("(a c) e -> a (c e)", a=1)
                             ).then_inc(S.sPref, 16)

            gpsimd.wait_ge(S.sMemB, 16 * NMEMB)
            gpsimd.wait_ge(S.sDestV, 1)
            for c in range(NCH):
                gpsimd.wait_ge(sLxn2A[c % 2], 16 * (c // 2 + 1))
                for j in range(2):
                    gpsimd.indirect_dma_start(
                        out=xn_buf,
                        out_offset=IndirectOffsetOnAxis(ap=dscat[:, 2 * c + j: 2 * c + j + 1], axis=0),
                        in_=xnc[c % 2][:], in_offset=None,
                    ).then_inc(sScatA[c % 2], 16)

            for e in (range(NE) if stage >= 3 else []):
                if e == 0:
                    gpsimd.wait_ge(sScatA[0], 32 * 32)
                    gpsimd.wait_ge(sScatA[1], 32 * 32)
                    gpsimd.wait_ge(S.sP1V, 1)
                    gpsimd.wait_ge(S.sP1A, 1)
                    gpsimd.wait_ge(sPwA[0], 16 * 32)
                    gpsimd.wait_ge(sPwA[1], 16 * 32)
                    gpsimd.wait_ge(sXnwA[0], 16 * 32)
                    gpsimd.wait_ge(sXnwA[1], 16 * 32)
                    gpsimd.memset(xngt[0][:], 0.0)
                    gpsimd.memset(xngt[1][:], 0.0).then_inc(S.sPreGm, 1)
                else:
                    gpsimd.wait_ge(S.sH, 32 * NGRP * e)
                for k in range(8):
                    gpsimd.dma_start(out=w1_sb[:, k * HID:(k + 1) * HID],
                                     in_=w1_in[e, k * 128:(k + 1) * 128, :]).then_inc(S.sW1, 16)
                if e > 0:
                    gpsimd.wait_ge(S.sY, 4 * NGRP * e)
                for j in range(32):
                    gpsimd.dma_start(out=w2_sb[:, j * 1024:(j + 1) * 1024],
                                     in_=w2_in[e, j * 128:(j + 1) * 128, :]).then_inc(S.sW2, 16)
                gpsimd.dma_start(out=b2e[:],
                                 in_=b2_in[e, :].rearrange("(a b) -> a b", a=1)).then_inc(S.sW2, 16)

            if stage < 3:
                return
            gpsimd.wait_ge(sYwA[0], 16 * 2 * NGG)
            gpsimd.wait_ge(sYwA[1], 16 * 2 * NGG)
            for c in range(NCH):
                if c >= 4:
                    gpsimd.wait_ge(S.sP3b, c - 3)
                gpsimd.indirect_dma_start(
                    out=yg0[c % 4][:], out_offset=None, in_=y_buf,
                    in_offset=IndirectOffsetOnAxis(ap=ygat[:, 2 * c: 2 * c + 1], axis=0),
                ).then_inc(sYgA[c % 4], 16)
                gpsimd.indirect_dma_start(
                    out=yg1[c % 4][:], out_offset=None, in_=y_buf,
                    in_offset=IndirectOffsetOnAxis(ap=ygat[:, 2 * c + 1: 2 * c + 2], axis=0),
                ).then_inc(sYgA[c % 4], 16)

        @block.tensor
        def _(tensor):
            tensor.wait_ge(S.sPreG, 16 * NPREG)
            tensor.wait_ge(S.sPreS, 16 * NPRES)
            tensor.wait_ge(S.sPreGm, 1)

            for sc in range(n_sc if (sublvl >= 3 or sublvl in (21, 22)) else 0):
                tensor.wait_ge(sTxnA[0], 16 * 8 * 2 * (sc + 1))
                tensor.wait_ge(sTxnA[1], 16 * 8 * 2 * (sc + 1))
                _CT = int(os.environ.get("KK_CT", "16"))
                _K = int(os.environ.get("KK_K", "8"))
                for ct in range(_CT):
                    bank = ct % 2
                    gt = sc * 16 + ct
                    if gt >= 2 and sublvl != 21:
                        tensor.wait_ge(S.sFeA, gt - 1)
                    if ct < 2 and sc >= 1:
                        tensor.wait_ge(S.sSmV, 2 * 32 * sc)
                    _OPS = os.environ.get("KK_OPS", "orig")
                    for k in range(8):
                        i_ = tensor.matmul(pb[bank][:, 0:512],
                                           wqkv_sb[:, k * 3072 + ct * 128: k * 3072 + ct * 128 + 128],
                                           xnt[sc % 2][:, k * 512:(k + 1) * 512],
                                           start=(k == 0), stop=(k == 7))
                    i_.then_inc(S.sQA, 1)
                _SEQB = os.environ.get("KK_SEQB", "0") == "1"
                for cc in range(4 if sublvl >= 4 else 0):
                    c = sc * 4 + cc
                    if c >= 1:
                        tensor.wait_ge(S.sVe, c)
                    if sc >= 1 and cc == 0:
                        tensor.wait_ge(S.sRes, 4 * sc)
                        tensor.wait_ge(S.sLgV, sc)
                        tensor.wait_ge(S.sSmV, 2 * 32 * sc)
                        tensor.wait_ge(S.sStV, 32 * sc)
                    if _SEQB:
                        for cg in range(2):
                            for k in range(8):
                                lhs = xnt[sc % 2][:, k * 512 + cc * 128: k * 512 + cc * 128 + 128]
                                tensor.matmul(pb[2 + cg][:, 0:512], lhs,
                                              wqkv_sb[:, k * 3072 + 2048 + cg * 512: k * 3072 + 2048 + (cg + 1) * 512],
                                              start=(k == 0), stop=False)
                            tensor.matmul(pb[2 + cg][:, 0:512], ones128[0:1, :],
                                          bv_row[0:1, cg * 512:(cg + 1) * 512],
                                          start=False, stop=True)
                        for k in range(8):
                            lhs = xnt[sc % 2][:, k * 512 + cc * 128: k * 512 + cc * 128 + 128]
                            i_ = tensor.matmul(pb[4][:, cc * 8: cc * 8 + 8], lhs,
                                               wg_sb[:, k * NE:(k + 1) * NE],
                                               start=(cc == 0 and k == 0), stop=(cc == 3 and k == 7))
                        i_.then_inc(S.sQB, 1)
                    else:
                        for k in range(8):
                            lhs = xnt[sc % 2][:, k * 512 + cc * 128: k * 512 + cc * 128 + 128]
                            for cg in range(2):
                                tensor.matmul(pb[2 + cg][:, 0:512], lhs,
                                              wqkv_sb[:, k * 3072 + 2048 + cg * 512: k * 3072 + 2048 + (cg + 1) * 512],
                                              start=(k == 0), stop=False)
                            tensor.matmul(pb[4][:, cc * 8: cc * 8 + 8], lhs,
                                          wg_sb[:, k * NE:(k + 1) * NE],
                                          start=(cc == 0 and k == 0), stop=(cc == 3 and k == 7))
                        for cg in range(2):
                            i_ = tensor.matmul(pb[2 + cg][:, 0:512], ones128[0:1, :],
                                               bv_row[0:1, cg * 512:(cg + 1) * 512],
                                               start=False, stop=True)
                        i_.then_inc(S.sQB, 1)

                for cc in range(4 if sublvl >= 5 else 0):
                    c = sc * 4 + cc
                    pos = c % CPB
                    tensor.wait_ge(S.sFeV, sc * 16 + 16)
                    tensor.wait_ge(S.sVeD, c + 1)
                    tensor.wait_ge(sTkA[cc], 16 * 8 * (sc + 1))
                    if cc == 0:
                        tensor.wait_ge(S.sVe, 4 * sc + 4)
                        tensor.wait_ge(S.sLgV, sc + 1)
                    for p in range(8):
                        gp = c * 8 + p
                        sb0 = 0 if p % 2 == 0 else 2
                        if gp >= 2:
                            tensor.wait_ge(S.sSmV, 2 * (gp - 1))
                        colq = p * 512 + cc * 128
                        tensor.matmul(pb[sb0][:, 0:128], kf[0:64, colq:colq + 128],
                                      qf[0:64, colq:colq + 128], start=True, stop=True)
                        tensor.matmul(pb[sb0 + 1][:, 0:128], kf[64:128, colq:colq + 128],
                                      qf[64:128, colq:colq + 128], start=True, stop=True
                                      ).then_inc(S.sS, 1)
                        nbank = 6 + (p % 2)
                        if p == 0:
                            tensor.wait_ge(S.sRes, c)
                        if gp >= 2:
                            tensor.wait_ge(S.sDiv, gp - 1)
                        tensor.wait_ge(S.sSmV, 2 * (gp + 1))
                        if pos > 0:
                            tensor.wait_ge(S.sStV, (c - 1) * 8 + p + 1)
                        tensor.matmul(pb[nbank][:, 0:65], smt[p % 2][:, 0:128],
                                      vsb[cc][:, (2 * p) * 65:(2 * p) * 65 + 65],
                                      start=True, stop=False)
                        i_ = tensor.matmul(pb[nbank][:, 65:130], smt[p % 2][:, 128:256],
                                           vsb[cc][:, (2 * p + 1) * 65:(2 * p + 1) * 65 + 65],
                                           start=False, stop=(pos == 0))
                        if pos > 0:
                            i_ = tensor.matmul(pb[nbank][:, 0:130],
                                               qf[:, colq:colq + 128],
                                               s0b[:, p * 130:(p + 1) * 130],
                                               start=False, stop=True)
                        i_.then_inc(S.sNum, 1)
                        if os.environ.get("KK_ST7", "0") == "1":
                            if gp >= 1:
                                tensor.wait_ge(S.sStV, gp)
                            tensor.matmul(pb[7][:, 192:322],
                                          ktok[cc][:, p * 128:(p + 1) * 128],
                                          vsb[cc][:, (2 * p) * 65:(2 * p) * 65 + 130],
                                          start=True, stop=True).then_inc(S.sKVp, 1)
                        else:
                            kbank = 4 + (p % 2)
                            if gp >= 2:
                                tensor.wait_ge(S.sStV, gp - 1)
                            tensor.matmul(pb[kbank][:, 0:130],
                                          ktok[cc][:, p * 128:(p + 1) * 128],
                                          vsb[cc][:, (2 * p) * 65:(2 * p) * 65 + 130],
                                          start=True, stop=True).then_inc(S.sKVp, 1)
                    if sublvl < 7:
                        continue
                    tensor.wait_ge(sTatA[c % 2], 16 * 8 * (c // 2 + 1))
                    tensor.wait_ge(S.sDiv, 8 * (c + 1))
                    for k in range(8):
                        for cg in range(2):
                            tensor.matmul(pb[6 + cg][:, 0:512],
                                          attnt[c % 2][:, k * 128:(k + 1) * 128],
                                          wout_sb[:, k * 1024 + cg * 512: k * 1024 + (cg + 1) * 512],
                                          start=(k == 0), stop=False)
                    for cg in range(2):
                        i_ = tensor.matmul(pb[6 + cg][:, 0:512], ones128[0:1, :],
                                           bout_row[0:1, cg * 512:(cg + 1) * 512],
                                           start=False, stop=True)
                    i_.then_inc(S.sPj, 1)

            if stage < 2:
                return
            tensor.wait_ge(S.sRout, NCH)
            tensor.wait_ge(S.sCntV1, 1)
            tensor.wait_ge(S.sDiv, 8 * NCH)
            tensor.wait_ge(S.sRes, NCH)
            tensor.wait_ge(S.sLgV, NSC)
            tensor.wait_ge(S.sStV, 8 * NCH)
            tensor.wait_ge(sPwA[0], 16 * 32)
            tensor.wait_ge(sPwA[1], 16 * 32)
            tensor.wait_ge(sXnwA[0], 16 * 32)
            tensor.wait_ge(sXnwA[1], 16 * 32)
            tensor.matmul(pb[2][0:NE, 0:1], cntbf[:], ones128[:, 0:1], start=True, stop=True
                          ).then_inc(S.sCntP, 1)
            tensor.matmul(pb[6][0:1, 0:NCH * NE], ones128[:, 0:1], msk[:],
                          start=True, stop=True).then_inc(S.sCs, 1)
            tensor.wait_ge(S.sCCl, 16)
            tensor.matmul(pb[3][0:1, 0:NE], prevmask[:], allcnt[:], start=True, stop=True
                          ).then_inc(S.sBase1, 1)
            tensor.wait_ge(S.sCt64, 16)
            tensor.matmul(pb[5][0:NCH, 8:16], urank[0:NCH, 0:NCH], ctot64[:],
                          start=True, stop=True).then_inc(S.sPfxL, 1)
            tensor.wait_ge(S.sBase2, 1)
            tensor.matmul(pb[5][0:NCH, 0:8], urank[0:NCH, 0:NCH], ctot64[:],
                          start=True, stop=False)
            tensor.matmul(pb[5][0:NCH, 0:8], ones1x128f[0:1, 0:NCH], base_row[:],
                          start=False, stop=True).then_inc(S.sPfxG, 1)
            tensor.wait_ge(S.sPref, 32)
            tensor.matmul(pb[0][:, 0:NCH * NE], urank[:], msk[:], start=True, stop=False)
            tensor.matmul(pb[0][:, 0:NCH * NE], ones1x128f[:], prefloc512[:],
                          start=False, stop=True).then_inc(S.sRkA, 1)
            tensor.matmul(pb[1][:, 0:NCH * NE], urank[:], msk[:], start=True, stop=False)
            tensor.matmul(pb[1][:, 0:NCH * NE], ones1x128f[:], prefglob512[:],
                          start=False, stop=True).then_inc(S.sRkB, 1)

            if stage < 3:
                return
            tensor.wait_ge(S.sDestV, 1)
            tensor.wait_ge(S.sCntV2, 1)
            tensor.wait_ge(S.sBase2, 1)
            for e in range(NE):
                for g in range(NGRP):
                    gg = e * NGRP + g
                    tensor.wait_ge(sTrsA[gg % 2], 16 * NTR_G * (gg // 2 + 1))
                    if gg == 0:
                        tensor.wait_ge(S.sW1, 16 * 8)
                    for ht in range(32):
                        hh = gg * 32 + ht
                        bank = ht % 3
                        if hh >= 3:
                            tensor.wait_ge(S.sGel, hh - 2)
                        if g == 0 and ht == 0 and e > 0:
                            tensor.wait_ge(S.sW1, 16 * 8 * (e + 1))
                        for k in range(8):
                            i_ = tensor.matmul(pb[bank][:, 0:GROWS],
                                               w1_sb[:, k * HID + ht * 128: k * HID + ht * 128 + 128],
                                               xngt[gg % 2][:, k * GROWS:(k + 1) * GROWS],
                                               start=(k == 0), stop=(k == 7))
                        i_.then_inc(S.sH, 1)
                    tensor.wait_ge(S.sGel, 32 * (gg + 1))
                    if g == 0:
                        tensor.wait_ge(S.sW2, 16 * 33 * (e + 1))
                    for sub in range(4):
                        ss = gg * 4 + sub
                        ybk = 3 + (sub % 2) * 2
                        if ss >= 2:
                            tensor.wait_ge(S.sYe, ss - 1)
                        for ht in range(32):
                            for cg in range(2):
                                tensor.matmul(pb[ybk + cg][:, 0:512],
                                              ht_sb[:, ht * GROWS + sub * 128: ht * GROWS + sub * 128 + 128],
                                              w2_sb[:, ht * 1024 + cg * 512: ht * 1024 + (cg + 1) * 512],
                                              start=(ht == 0), stop=False)
                        for cg in range(2):
                            i_ = tensor.matmul(pb[ybk + cg][:, 0:512], ones128[0:1, :],
                                               b2e[0:1, cg * 512:(cg + 1) * 512],
                                               start=False, stop=True)
                        i_.then_inc(S.sY, 1)

        @block.vector
        def _(vector):
            vector.wait_ge(S.sPreS, 16 * NPRES)
            vector.wait_ge(S.sPreGm, 1)
            for i in range(4):
                vector.tensor_copy(
                    vsb[i][:].rearrange("p (h w) -> p h w", w=65)[:, :, 64:65],
                    ones128[:, 0:16].rearrange("p (h w) -> p h w", w=1))
            vector.tensor_scalar(bqks[:, 0:8], bqk[:, 0:8], QSCALE, None, ALU.mult)
            vector.tensor_scalar(bqks[:, 8:16], bqk[:, 8:16], 1.0, None, ALU.mult)
            vector.memset(cntacc[:], 0.0)
            vector.drain()

            def _stats(sc_):
                for cc_ in range(4):
                    c_ = sc_ * 4 + cc_
                    vector.wait_ge(sLxA[c_ % 9], 16 * (c_ // 9 + 1))
                    if c_ >= 2:
                        vector.wait_ge(S.sRst, c_ - 1)
                    vector.bn_stats(bn12[c_ % 2][:, 0:6], xfp[c_ % 9][:, 0:512])
                    vector.bn_stats(bn12[c_ % 2][:, 6:12], xfp[c_ % 9][:, 512:1024])
                    vector.drain()
                    vector.bn_aggr(mv[c_ % 2][:], bn12[c_ % 2][:]).then_inc(S.sBnA, 1)
                    vector.wait_ge(S.sRst, c_ + 1)
                    vector.tensor_scalar(negms[c_ % 2][:], mv[c_ % 2][:, 0:1], rstd[c_ % 2][:],
                                         -1.0, ALU.mult, ALU.mult)
                    vector.drain()
                    if c_ >= 2:
                        vector.wait_ge(sTxnA[c_ % 2], 16 * 8 * ((c_ - 2) // 2 + 1))
                        vector.wait_ge(sXnwA[c_ % 2], 16 * ((c_ - 2) // 2 + 1))
                    vector.tensor_scalar(xn_sb[c_ % 2][:], xfp[c_ % 9][:], rstd[c_ % 2][:],
                                         negms[c_ % 2][:], ALU.mult, ALU.add).then_inc(S.sStat, 1)

            if sublvl >= 2:
                _stats(0)
            for sc in range(n_sc if sublvl >= 2 else 0):
                for ct in range(16 if (sublvl >= 3 and sublvl != 21 and sublvl != 22) else 0):
                    gt = sc * 16 + ct
                    dst = qf if ct < 8 else kf
                    off = (ct % 8) * 512
                    if ct == 8 and sc >= 1 and sublvl >= 5:
                        for cc_ in range(4):
                            vector.wait_ge(sTkA[cc_], 16 * 8 * sc)
                    vector.wait_ge(S.sFeA, gt + 1)
                    vector.tensor_scalar(tf[ct % 2][:], ef[ct % 2][:], 1.0, 1e-4,
                                         ALU.min, ALU.add)
                    vector.drain()
                    vector.tensor_tensor(dst[:, off:off + 512], tf[ct % 2][:], rf[ct % 2][:],
                                         ALU.add).then_inc(S.sFeV, 1)
                if sublvl >= 4:
                    vector.wait_ge(S.sQB, 4 * (sc + 1))
                    vector.tensor_copy(lgt[sc % 2][:], pb[4][:, 0:32]).then_inc(S.sLgV, 1)
                    vector.drain()
                if sublvl >= 2 and sc + 1 < n_sc:
                    _stats(sc + 1)
                for cc in range(4 if sublvl >= 5 else 0):
                    c = sc * 4 + cc
                    pos = c % CPB
                    for p in range(8):
                        gp = c * 8 + p
                        sb0 = 0 if p % 2 == 0 else 2
                        vector.wait_ge(S.sS, gp + 1)
                        if gp >= 2:
                            vector.wait_ge(S.sNum, gp - 1)
                        vector.tensor_tensor(smt[p % 2][:, 0:128], pb[sb0][:, 0:128], maskut[:],
                                             ALU.mult).then_inc(S.sSmV, 1)
                        vector.tensor_tensor(smt[p % 2][:, 128:256], pb[sb0 + 1][:, 0:128], maskut[:],
                                             ALU.mult).then_inc(S.sSmV, 1)
                        vector.wait_ge(S.sKVp, gp + 1)
                        vector.wait_ge(S.sNum, gp + 1)
                        kbank = 4 + (p % 2)
                        kcol0 = 0
                        if os.environ.get("KK_ST7", "0") == "1":
                            kbank, kcol0 = 7, 192
                        if pos == 0:
                            vector.tensor_scalar(s0f[0:64, p * 130: p * 130 + 65],
                                                 pb[kbank][0:64, kcol0:kcol0 + 65],
                                                 1.0, None, ALU.mult)
                            vector.tensor_scalar(s0f[64:128, p * 130 + 65: p * 130 + 130],
                                                 pb[kbank][64:128, kcol0 + 65:kcol0 + 130],
                                                 1.0, None, ALU.mult)
                        else:
                            vector.tensor_tensor(s0f[0:64, p * 130: p * 130 + 65],
                                                 s0f[0:64, p * 130: p * 130 + 65],
                                                 pb[kbank][0:64, kcol0:kcol0 + 65], ALU.add)
                            vector.tensor_tensor(s0f[64:128, p * 130 + 65: p * 130 + 130],
                                                 s0f[64:128, p * 130 + 65: p * 130 + 130],
                                                 pb[kbank][64:128, kcol0 + 65:kcol0 + 130],
                                                 ALU.add)
                        vector.drain()
                        vector.tensor_copy(s0b[:, p * 130:(p + 1) * 130],
                                           s0f[:, p * 130:(p + 1) * 130]).then_inc(S.sStV, 1)
                        nbank = 6 + (p % 2)
                        if gp >= 2:
                            vector.wait_ge(S.sDiv, gp - 1)
                        vector.tensor_scalar(dena[2 * (p % 2)][:], pb[nbank][:, 64:65],
                                             EPS_DEN, None, ALU.add)
                        vector.tensor_scalar(dena[2 * (p % 2) + 1][:], pb[nbank][:, 129:130],
                                             EPS_DEN, None, ALU.add)
                        vector.drain()
                        vector.reciprocal(dena[2 * (p % 2)][:], dena[2 * (p % 2)][:]
                                          ).then_inc(S.sRcp, 1)
                        vector.reciprocal(dena[2 * (p % 2) + 1][:], dena[2 * (p % 2) + 1][:]
                                          ).then_inc(S.sRcp, 1)
                    if sublvl < 7:
                        continue
                    vector.wait_ge(S.sPj, c + 1)
                    if c >= 2:
                        vector.wait_ge(sPwA[c % 2], 16 * ((c - 2) // 2 + 1))
                        if stage < 3:
                            vector.wait_ge(sOwA[c % 2], 16 * ((c - 2) // 2 + 1))
                    vector.tensor_tensor(part_sb[c % 2][:, 0:512], pb[6][:, 0:512],
                                         xfp[c % 9][:, 0:512], ALU.add)
                    vector.tensor_tensor(part_sb[c % 2][:, 512:1024], pb[7][:, 0:512],
                                         xfp[c % 9][:, 512:1024], ALU.add)
                    vector.drain()
                    vector.memset(fence_v[:], 0.0).then_inc(S.sRes, 1)
                    if sublvl < 6:
                        continue
                    lgc = lgt[sc % 2][:, cc * 8: cc * 8 + 8]
                    vector.max(mx8[:], lgc)
                    vector.drain()
                    vector.max_index(idxu[:], mx8[:], lgc)
                    vector.drain()
                    vector.tensor_copy(idxf[:], idxu[:])
                    vector.tensor_tensor(dcol[:, c:c + 1], mx8[:, 0:1], mx8[:, 1:2], ALU.subtract)
                    vector.drain()
                    vector.tensor_scalar(m1s[:, c * NE:(c + 1) * NE], iota8[:], idxf[:, 0:1],
                                         None, ALU.is_equal)
                    vector.tensor_scalar(m2s[:, c * NE:(c + 1) * NE], iota8[:], idxf[:, 1:2],
                                         None, ALU.is_equal)
                    vector.drain()
                    vector.tensor_tensor(msk[:, c * NE:(c + 1) * NE], m1s[:, c * NE:(c + 1) * NE],
                                         m2s[:, c * NE:(c + 1) * NE], ALU.add)
                    vector.drain()
                    vector.tensor_tensor(cntacc[:], cntacc[:], msk[:, c * NE:(c + 1) * NE],
                                         ALU.add).then_inc(S.sRout, 1)
                    vector.drain()

            if stage < 2:
                return
            vector.tensor_copy(cntbf[:], cntacc[:]).then_inc(S.sCntV1, 1)
            vector.wait_ge(S.sCntP, 1)
            vector.tensor_copy(cnt8[:], pb[2][0:NE, 0:1]).then_inc(S.sCntV2, 1)
            vector.wait_ge(S.sCs, 1)
            vector.tensor_copy(ctot_row[:], pb[6][0:1, 0:NCH * NE]).then_inc(S.sCsV, 1)
            vector.wait_ge(S.sBase1, 1)
            vector.tensor_copy(base_row[:], pb[3][0:1, 0:NE]).then_inc(S.sBase2, 1)
            vector.wait_ge(S.sPfxL, 1)
            vector.tensor_copy(pfl64[:], pb[5][0:NCH, 8:16])
            vector.wait_ge(S.sPfxG, 1)
            vector.tensor_copy(pfg64[:], pb[5][0:NCH, 0:8]).then_inc(S.sPfxV, 1)
            vector.wait_ge(S.sRkA, 1)
            vector.wait_ge(S.sRkB, 1)
            vector.tensor_scalar(k8a[:], pb[0][:, 0:NCH * NE], float(CAP_PAD), None, ALU.is_lt)
            vector.tensor_scalar(k8b[:], pb[1][:, 0:NCH * NE], float(CAP), None, ALU.is_lt)
            vector.tensor_tensor(rp512[:], pb[0][:, 0:NCH * NE], eoff512[:], ALU.add)
            vector.drain()
            vector.tensor_tensor(k8a[:], k8a[:], k8b[:], ALU.mult)
            vector.drain()
            vector.tensor_tensor(mk1[:], m1s[:], k8a[:], ALU.mult)
            vector.tensor_tensor(mk2[:], m2s[:], k8a[:], ALU.mult)
            vector.drain()
            vector.tensor_tensor(k8a[:], mk1[:], rp512[:], ALU.mult)
            vector.tensor_tensor(k8b[:], mk2[:], rp512[:], ALU.mult)
            vector.drain()
            vector.reduce_sum(kk1[:].rearrange("p (c j) -> p c j", j=1),
                              mk1[:].rearrange("p (c e) -> p c e", e=NE), axis=AX.X)
            vector.reduce_sum(kk2[:].rearrange("p (c j) -> p c j", j=1),
                              mk2[:].rearrange("p (c e) -> p c e", e=NE), axis=AX.X)
            vector.reduce_sum(sc1[:].rearrange("p (c j) -> p c j", j=1),
                              k8a[:].rearrange("p (c e) -> p c e", e=NE), axis=AX.X)
            vector.reduce_sum(sc2[:].rearrange("p (c j) -> p c j", j=1),
                              k8b[:].rearrange("p (c e) -> p c e", e=NE), axis=AX.X)
            vector.drain()
            vector.tensor_scalar(tc1[:], kk1[:], -1.0, 1.0, ALU.mult, ALU.add)
            vector.tensor_scalar(tc2[:], kk2[:], -1.0, 1.0, ALU.mult, ALU.add)
            vector.drain()
            vector.tensor_tensor(w1c[:], tc1[:], dmp0[:], ALU.mult)
            vector.tensor_tensor(w2c[:], tc2[:], dmp1[:], ALU.mult)
            vector.drain()
            vector.tensor_tensor(d1c[:], sc1[:], w1c[:], ALU.add)
            vector.tensor_tensor(d2c[:], sc2[:], w2c[:], ALU.add)
            vector.drain()
            vector.tensor_copy(dscat[:].rearrange("p (c j) -> p c j", j=2)[:, :, 0:1],
                               d1c[:].rearrange("p (c j) -> p c j", j=1))
            vector.tensor_copy(dscat[:].rearrange("p (c j) -> p c j", j=2)[:, :, 1:2],
                               d2c[:].rearrange("p (c j) -> p c j", j=1))
            vector.tensor_scalar(w1c[:], tc1[:], float(YZERO), None, ALU.mult)
            vector.tensor_scalar(w2c[:], tc2[:], float(YZERO), None, ALU.mult)
            vector.drain()
            vector.tensor_tensor(d1c[:], sc1[:], w1c[:], ALU.add)
            vector.tensor_tensor(d2c[:], sc2[:], w2c[:], ALU.add)
            vector.drain()
            vector.tensor_copy(ygat[:].rearrange("p (c j) -> p c j", j=2)[:, :, 0:1],
                               d1c[:].rearrange("p (c j) -> p c j", j=1))
            vector.tensor_copy(ygat[:].rearrange("p (c j) -> p c j", j=2)[:, :, 1:2],
                               d2c[:].rearrange("p (c j) -> p c j", j=1)).then_inc(S.sDestV, 1)
            vector.drain()
            vector.wait_ge(S.sGate, 1)
            vector.tensor_tensor(g1k[:], g1s[:], kk1[:], ALU.mult)
            vector.tensor_scalar(g2tmp[:], g1s[:], -1.0, 1.0, ALU.mult, ALU.add)
            vector.drain()
            vector.tensor_tensor(g2k[:], g2tmp[:], kk2[:], ALU.mult).then_inc(S.sGate, 1)
            vector.drain()
            vector.memset(fence_v[:], 0.0).then_inc(S.sP1V, 1)

            for c in (range(NCH) if stage >= 3 else []):
                vector.wait_ge(sLptA[c % 4], 16 * (c // 4 + 1))
                vector.wait_ge(sYgA[c % 4], 32 * (c // 4 + 1))
                if c >= 4:
                    vector.wait_ge(sOwA[c % 4], 16 * (c // 4))
                vector.scalar_tensor_tensor(t3[c % 4][:], yg0[c % 4][:], g1k[:, c:c + 1],
                                            pt3[c % 4][:], ALU.mult, ALU.add)
                vector.drain()
                vector.scalar_tensor_tensor(o3[c % 4][:], yg1[c % 4][:], g2k[:, c:c + 1],
                                            t3[c % 4][:], ALU.mult, ALU.add).then_inc(S.sP3b, 1)

        @block.scalar
        def _(scalar):
            scalar.wait_ge(S.sPreS, 16 * NPRES)
            scalar.wait_ge(S.sPreG, 16 * NPREG)

            def _rstd(sc_):
                for cc_ in range(4):
                    c_ = sc_ * 4 + cc_
                    scalar.wait_ge(S.sBnA, c_ + 1)
                    if c_ >= 2:
                        scalar.wait_ge(S.sStat, c_ - 1)
                    scalar.activation(rstd[c_ % 2][:], mv[c_ % 2][:, 1:2], AF.Ln,
                                      bias=epsz[:, 0:1], scale=1.0)
                    scalar.drain()
                    scalar.activation(rstd[c_ % 2][:], rstd[c_ % 2][:], AF.Exp,
                                      bias=epsz[:, 1:2], scale=-0.5).then_inc(S.sRst, 1)

            if sublvl >= 2:
                _rstd(0)
            for sc in range(n_sc if sublvl >= 2 else 0):
                for ct in range(16 if (sublvl >= 3 or sublvl == 22) else 0):
                    gt = sc * 16 + ct
                    bank = ct % 2
                    scl = QSCALE if ct < 8 else 1.0
                    scalar.wait_ge(S.sQA, gt + 1)
                    if gt >= 2 and sublvl != 22:
                        scalar.wait_ge(S.sFeV, gt - 1)
                    scalar.activation(ef[ct % 2][:], pb[bank][:, 0:512], AF.Exp,
                                      bias=bqks[:, ct:ct + 1], scale=scl)
                    scalar.activation(rf[ct % 2][:], pb[bank][:, 0:512], AF.Relu,
                                      bias=bqks[:, ct:ct + 1], scale=scl).then_inc(S.sFeA, 1)
                for cc in range(4 if sublvl >= 4 else 0):
                    c = sc * 4 + cc
                    scalar.wait_ge(S.sQB, c + 1)
                    for h in range(16):
                        ins = pb[2 + h // 8][:, (h % 8) * 64:(h % 8) * 64 + 64]
                        i_ = scalar.activation(vsb[cc][:, h * 65: h * 65 + 64], ins, AF.Copy)
                    i_.then_inc(S.sVe, 1)
                    scalar.drain()
                    scalar.activation(fence_a[:], bqks[:, 0:1], AF.Copy).then_inc(S.sVeD, 1)
                if sublvl >= 2 and sc + 1 < n_sc:
                    _rstd(sc + 1)
                if sublvl >= 7:
                    for cc in range(4):
                        c = sc * 4 + cc
                        if c >= 2:
                            scalar.wait_ge(sTatA[c % 2], 16 * 8 * ((c - 2) // 2 + 1))
                        for p in range(8):
                            gp = c * 8 + p
                            nbank = 6 + (p % 2)
                            scalar.wait_ge(S.sRcp, 2 * (gp + 1))
                            scalar.activation(attn_sb[c % 2][:, (2 * p) * 64:(2 * p) * 64 + 64],
                                              pb[nbank][:, 0:64], AF.Copy,
                                              scale=dena[2 * (p % 2)][:])
                            scalar.activation(attn_sb[c % 2][:, (2 * p + 1) * 64:(2 * p + 1) * 64 + 64],
                                              pb[nbank][:, 65:129], AF.Copy,
                                              scale=dena[2 * (p % 2) + 1][:]).then_inc(S.sDiv, 1)
                        scalar.drain()
                        scalar.activation(fence_a[:], bqks[:, 0:1], AF.Copy).then_inc(S.sDivD, 1)
                        scalar.wait_ge(S.sDivD, c + 1)
                        if c >= 1:
                            scalar.wait_ge(S.sPj, c)
                        for k in range(8):
                            scalar.dma_start(out=attnt[c % 2][:, k * 128:(k + 1) * 128],
                                             in_=attn_sb[c % 2][:, k * 128:(k + 1) * 128],
                                             transpose=True).then_inc(sTatA[c % 2], 16)

            if stage < 2:
                return
            scalar.wait_ge(S.sRout, NCH)
            scalar.activation(g1s[:], dcol[:], AF.Sigmoid).then_inc(S.sGate, 1)
            scalar.drain()
            scalar.activation(fence_a[:], g1s[:, 0:1], AF.Copy).then_inc(S.sP1A, 1)

            if stage < 3:
                return
            scalar.wait_ge(S.sPreA, 16 * NPREA)
            for e in range(NE):
                for g in range(NGRP):
                    gg = e * NGRP + g
                    for ht in range(32):
                        hh = gg * 32 + ht
                        bank = ht % 3
                        scalar.wait_ge(S.sH, hh + 1)
                        scalar.activation(ht_sb[:, ht * GROWS:(ht + 1) * GROWS],
                                          pb[bank][:, 0:GROWS], AF.Gelu_apprx_tanh,
                                          bias=b1_sb[:, e * 32 + ht: e * 32 + ht + 1],
                                          scale=1.0).then_inc(S.sGel, 1)
                    for sub in range(4):
                        ss = gg * 4 + sub
                        ybk = 3 + (sub % 2) * 2
                        scalar.wait_ge(S.sY, ss + 1)
                        if ss >= 2:
                            scalar.wait_ge(sYwA[sub % 2], 16 * ((ss - 2) // 2 + 1))
                        scalar.activation(y_sb[sub % 2][:, 0:512], pb[ybk][:, 0:512], AF.Copy)
                        scalar.activation(y_sb[sub % 2][:, 512:1024], pb[ybk + 1][:, 0:512],
                                          AF.Copy).then_inc(S.sYe, 1)

    st.close()
    return nc


_NC_CACHE = {}


def _consts():
    i = np.arange(128)
    maskut = (i[:, None] <= i[None, :]).astype(ml_dtypes.bfloat16)
    urank = (i[:, None] < i[None, :]).astype(ml_dtypes.bfloat16)
    ones128 = np.ones((128, 128), ml_dtypes.bfloat16)
    ones1x128f = np.ones((1, 128), np.float32)
    iota8 = np.broadcast_to(np.arange(NE, dtype=np.float32), (128, NE)).copy()
    eoff = np.broadcast_to((np.arange(NE) * CAP_PAD).astype(np.float32), (128, NE)).copy()
    eoff512 = np.tile(eoff[0], (128, NCH)).astype(np.float32)
    cidx = np.arange(NCH, dtype=np.float32)[None, :]
    pidx = np.arange(128, dtype=np.float32)[:, None]
    dmp0 = (DUMP0 + 256.0 * cidx + 2.0 * pidx + 0.0).astype(np.float32)
    dmp1 = (DUMP0 + 256.0 * cidx + 2.0 * pidx + 1.0).astype(np.float32)
    iota128 = np.arange(128, dtype=np.float32).reshape(128, 1)
    epsz = np.zeros((128, 2), np.float32); epsz[:, 0] = EPS_LN
    return maskut, urank, ones128, ones1x128f, iota8, eoff, eoff512, dmp0, dmp1, iota128, epsz


def kernel(L0_attn_x, L0_attn_W_qkv, L0_attn_b_qkv, L0_attn_W_out, L0_attn_b_out,
           L0_ffn_Wg, L0_ffn_W1, L0_ffn_b1, L0_ffn_W2, L0_ffn_b2):
    xf = np.ascontiguousarray(np.asarray(L0_attn_x, np.float32).reshape(NTOT, D))
    (maskut, urank, ones128, ones1x128f, iota8, eoff, eoff512, dmp0, dmp1,
     iota128, epsz) = _consts()

    if "nc" not in _NC_CACHE:
        _NC_CACHE["nc"] = build_program(3, NSC, 7)
    nc = _NC_CACHE["nc"]

    shared = dict(
        wqkv=np.ascontiguousarray(np.asarray(L0_attn_W_qkv, np.float32)),
        bqkv=np.ascontiguousarray(np.asarray(L0_attn_b_qkv, np.float32)),
        wout=np.ascontiguousarray(np.asarray(L0_attn_W_out, np.float32)),
        bout=np.ascontiguousarray(np.asarray(L0_attn_b_out, np.float32)),
        wg=np.ascontiguousarray(np.asarray(L0_ffn_Wg, np.float32)),
        w1=np.ascontiguousarray(np.asarray(L0_ffn_W1, np.float32)),
        b1=np.ascontiguousarray(np.asarray(L0_ffn_b1, np.float32)),
        w2=np.ascontiguousarray(np.asarray(L0_ffn_W2, np.float32)),
        b2=np.ascontiguousarray(np.asarray(L0_ffn_b2, np.float32)),
        maskut=maskut, urank=urank, ones128=ones128, ones1x128f=ones1x128f,
        iota8=iota8, eoff=eoff, eoff512=eoff512, dmp0=dmp0, dmp1=dmp1,
        iota128=iota128, epsz=epsz,
    )
    in_maps = []
    for c in range(NCORES):
        m = dict(shared)
        m["x"] = np.ascontiguousarray(xf[c * TOK:(c + 1) * TOK])
        m["prevmask"] = (np.arange(NCORES) < c).astype(np.float32).reshape(NCORES, 1)
        in_maps.append(m)

    res = run_bass_kernel_spmd(nc, in_maps, core_ids=list(range(NCORES)))
    out = np.concatenate([res.results[c]["out"] for c in range(NCORES)], axis=0)
    return out.reshape(B, L, D)


if __name__ == "__main__":
    import time
    t0 = time.time()
    nc = build_program()
    print(f"build ok in {time.time()-t0:.1f}s; instructions: "
          f"{sum(1 for _ in nc.all_instructions())}")


# version 08
# speedup vs baseline: 1.0778x; 1.0017x over previous
import math
import os
import numpy as np
import ml_dtypes
from contextlib import ExitStack

import concourse.bass as bass
import concourse.mybir as mybir
from concourse.bass_utils import run_bass_kernel_spmd
from concourse.bass import IndirectOffsetOnAxis

F32 = mybir.dt.float32
BF16 = mybir.dt.bfloat16
U32 = mybir.dt.uint32
AF = mybir.ActivationFunctionType
ALU = mybir.AluOpType
AX = mybir.AxisListType

NCORES = 8
B, L, D = 16, 4096, 1024
NH, EH = 16, 64
TOK = 8192
NCH = 64
CPB = 32
NSC = 16
NE, HID, TOPK = 8, 4096, 2
NTOT = B * L
CAP = int(math.ceil(NTOT * TOPK / NE * 1.05))
CAP_PAD = 2560
GROWS = 512
NGRP = CAP_PAD // GROWS
NBUF = NE * CAP_PAD
DUMP0 = NBUF
XNROWS = NBUF + 2 * TOK
YZERO = NBUF
EPS_LN = 1e-5
EPS_DEN = 1e-6
QSCALE = EH ** (-0.5)


def build_program(stage=3, n_sc=NSC, sublvl=7):
    nc = bass.Bass("TRN2", target_bir_lowering=False, debug=False,
                   dynamic_dma_scratch_size=16384)

    x_in = nc.dram_tensor("x", [TOK, D], F32, kind="ExternalInput").ap()
    wqkv_in = nc.dram_tensor("wqkv", [D, 3 * D], F32, kind="ExternalInput").ap()
    bqkv_in = nc.dram_tensor("bqkv", [3 * D], F32, kind="ExternalInput").ap()
    wout_in = nc.dram_tensor("wout", [D, D], F32, kind="ExternalInput").ap()
    bout_in = nc.dram_tensor("bout", [D], F32, kind="ExternalInput").ap()
    wg_in = nc.dram_tensor("wg", [D, NE], F32, kind="ExternalInput").ap()
    w1_in = nc.dram_tensor("w1", [NE, D, HID], F32, kind="ExternalInput").ap()
    b1_in = nc.dram_tensor("b1", [NE, HID], F32, kind="ExternalInput").ap()
    w2_in = nc.dram_tensor("w2", [NE, HID, D], F32, kind="ExternalInput").ap()
    b2_in = nc.dram_tensor("b2", [NE, D], F32, kind="ExternalInput").ap()
    maskut_in = nc.dram_tensor("maskut", [128, 128], BF16, kind="ExternalInput").ap()
    urank_in = nc.dram_tensor("urank", [128, 128], BF16, kind="ExternalInput").ap()
    ones128_in = nc.dram_tensor("ones128", [128, 128], BF16, kind="ExternalInput").ap()
    ones1x128f_in = nc.dram_tensor("ones1x128f", [1, 128], F32, kind="ExternalInput").ap()
    iota8_in = nc.dram_tensor("iota8", [128, NE], F32, kind="ExternalInput").ap()
    eoff_in = nc.dram_tensor("eoff", [128, NE], F32, kind="ExternalInput").ap()
    eoff512_in = nc.dram_tensor("eoff512", [128, NCH * NE], F32, kind="ExternalInput").ap()
    dmp0_in = nc.dram_tensor("dmp0", [128, NCH], F32, kind="ExternalInput").ap()
    dmp1_in = nc.dram_tensor("dmp1", [128, NCH], F32, kind="ExternalInput").ap()
    iota128_in = nc.dram_tensor("iota128", [128, 1], F32, kind="ExternalInput").ap()
    prevmask_in = nc.dram_tensor("prevmask", [NCORES, 1], F32, kind="ExternalInput").ap()
    epsz_in = nc.dram_tensor("epsz", [128, 2], F32, kind="ExternalInput").ap()
    out_dram = nc.dram_tensor("out", [TOK, D], F32, kind="ExternalOutput").ap()
    dbg_dscat = nc.dram_tensor("dbg_dscat", [128, 2 * NCH], U32, kind="ExternalOutput").ap()
    dbg_ygat = nc.dram_tensor("dbg_ygat", [128, 2 * NCH], U32, kind="ExternalOutput").ap()
    dbg_g1k = nc.dram_tensor("dbg_g1k", [128, NCH], F32, kind="ExternalOutput").ap()
    dbg_g2k = nc.dram_tensor("dbg_g2k", [128, NCH], F32, kind="ExternalOutput").ap()
    dbg_allcnt = nc.dram_tensor("dbg_allcnt", [NCORES, NE], F32, kind="ExternalOutput").ap()
    dbg_thr = nc.dram_tensor("dbg_thr", [128, NE], F32, kind="ExternalOutput").ap()
    DBG3 = os.environ.get("KK_DBG3", "0") == "1"
    if DBG3:
        dbg_qf = nc.dram_tensor("dbg_qf", [128, 4096], BF16, kind="ExternalOutput").ap()
        dbg_kf = nc.dram_tensor("dbg_kf", [128, 4096], BF16, kind="ExternalOutput").ap()
        dbg_v = nc.dram_tensor("dbg_v", [128, 1040], BF16, kind="ExternalOutput").ap()
        dbg_xnt = nc.dram_tensor("dbg_xnt", [128, 4096], BF16, kind="ExternalOutput").ap()
        dbg_attn = nc.dram_tensor("dbg_attn", [128, 1024], BF16, kind="ExternalOutput").ap()
        dbg_s0 = nc.dram_tensor("dbg_s0", [128, 1040], F32, kind="ExternalOutput").ap()
        dbg_lgt = nc.dram_tensor("dbg_lgt", [128, 32], F32, kind="ExternalOutput").ap()

    xn_scr = nc.dram_tensor("xn_scr", [TOK, D], BF16).ap()
    xn_buf = nc.dram_tensor("xn_buf", [XNROWS, D], BF16).ap()
    y_buf = nc.dram_tensor("y_buf", [NBUF + 1, D], BF16).ap()
    part_scr = nc.dram_tensor("part_scr", [TOK, D], F32).ap()
    cc_in = nc.dram_tensor("cc_in", [1, NE], F32)
    cc_out = nc.dram_tensor("cc_out", [NCORES, NE], F32)
    ctot_dram = nc.dram_tensor("ctot_scr", [1, NCH * NE], BF16)
    pfl_dram = nc.dram_tensor("pfl_scr", [NCH, NE], F32)
    pfg_dram = nc.dram_tensor("pfg_scr", [NCH, NE], F32)

    st = ExitStack()

    def sb(name, shape, dt):
        return st.enter_context(nc.sbuf_tensor(name, shape, dt))

    def psf(name):
        return st.enter_context(nc.psum_tensor(name, [128, 512], F32))

    maskut = sb("maskut_s", [128, 128], BF16)
    urank = sb("urank_s", [128, 128], BF16)
    ones128 = sb("ones128b", [128, 128], BF16)
    ones1x128f = sb("ones1x128f_s", [1, 128], F32)
    iota8 = sb("iota8_s", [128, NE], F32)
    eoff = sb("eoff_s", [128, NE], F32)
    iota128 = sb("iota128_s", [128, 1], F32)
    prevmask = sb("prevmask_s", [NCORES, 1], F32)
    epsz = sb("epsz_s", [128, 2], F32)

    msk = sb("msk", [128, NCH * NE], BF16)
    m1s = sb("m1s", [128, NCH * NE], BF16)
    m2s = sb("m2s", [128, NCH * NE], BF16)
    dcol = sb("dcol", [128, NCH], F32)
    cntacc = sb("cntacc", [128, NE], F32)
    kk1 = sb("kk1", [128, NCH], F32)
    kk2 = sb("kk2", [128, NCH], F32)
    g1k = sb("g1k", [128, NCH], F32)
    g2k = sb("g2k", [128, NCH], F32)
    g1s = sb("g1s", [128, NCH], F32)
    g2tmp = sb("g2tmp", [128, NCH], F32)
    dscat = sb("dscat", [128, 2 * NCH], U32)
    ygat = sb("ygat", [128, 2 * NCH], U32)
    cnt8 = sb("cnt8", [NE, 1], F32)
    cntbf = sb("cntbf", [128, NE], BF16)
    allcnt = sb("allcnt", [NCORES, NE], F32)
    base_row = sb("base_row", [1, NE], F32)
    eoff512 = sb("eoff512_s", [128, NCH * NE], F32)
    dmp0 = sb("dmp0_s", [128, NCH], F32)
    dmp1 = sb("dmp1_s", [128, NCH], F32)
    b1_sb = sb("b1_sb", [128, NE * 32], F32)
    b2e = sb("b2e", [1, 1024], BF16)
    fence_v = sb("fence_v", [128, 1], F32)
    fence_a = sb("fence_a", [128, 1], F32)

    p1 = ExitStack()

    def sb1(name, shape, dt):
        return p1.enter_context(nc.sbuf_tensor(name, shape, dt))

    wqkv_sb = sb1("wqkv_sb", [128, 8 * 3072], BF16)
    wout_sb = sb1("wout_sb", [128, 8 * 1024], BF16)
    wg_sb = sb1("wg_sb", [128, 8 * NE], BF16)
    bqk = sb1("bqk", [128, 16], F32)
    bqks = sb1("bqks", [128, 16], F32)
    bv_row = sb1("bv_row", [1, 1024], BF16)
    bout_row = sb1("bout_row", [1, 1024], BF16)
    zrow = sb1("zrow", [128, 1024], BF16)
    xfp = [sb1(f"xfp{i}", [128, D], F32) for i in range(8)]
    bn12 = [sb1(f"bn12_{i}", [128, 12], F32) for i in range(2)]
    mv = [sb1(f"mv{i}", [128, 2], F32) for i in range(2)]
    rstd = [sb1(f"rstd{i}", [128, 1], F32) for i in range(2)]
    negms = [sb1(f"negms{i}", [128, 1], F32) for i in range(2)]
    xn_sb = [sb1(f"xn_sb{i}", [128, D], BF16) for i in range(2)]
    xnt = [sb1(f"xnt{i}", [128, 8 * 512], BF16) for i in range(2)]
    qf = sb1("qf", [128, 8 * 512], BF16)
    kf = sb1("kf", [128, 8 * 512], BF16)
    ef = [sb1(f"ef{i}", [128, 512], F32) for i in range(2)]
    rf = [sb1(f"rf{i}", [128, 512], F32) for i in range(2)]
    vsb = [sb1(f"vsb{i}", [128, 16 * 65], BF16) for i in range(4)]
    ktok = [sb1(f"ktok{i}", [128, 1024], BF16) for i in range(4)]
    smt = [sb1(f"smt{i}", [128, 256], BF16) for i in range(4)]
    s0f = sb1("s0f", [128, 8 * 130], F32)
    s0b = sb1("s0b", [128, 8 * 130], BF16)
    dena = [sb1(f"dena{i}", [128, 1], F32) for i in range(4)]
    attn_sb = [sb1(f"attn_sb{i}", [128, 1024], BF16) for i in range(3)]
    attnt = [sb1(f"attnt{i}", [128, 1024], BF16) for i in range(3)]
    part_sb = [sb1(f"part_sb{i}", [128, D], F32) for i in range(2)]
    lgt = [sb1(f"lgt{i}", [128, 32], F32) for i in range(2)]
    mx8 = sb1("mx8", [128, NE], F32)
    idxu = sb1("idxu", [128, NE], U32)
    idxf = sb1("idxf", [128, NE], F32)
    p1.close()

    p2 = ExitStack()

    def sb2(name, shape, dt):
        return p2.enter_context(nc.sbuf_tensor(name, shape, dt))

    xnc = [sb2(f"xnc{i}", [128, D], BF16) for i in range(2)]
    w1_sb = sb2("w1_sb", [128, 8 * HID], BF16)
    w2_sb = sb2("w2_sb", [128, 32 * 1024], BF16)
    tok2 = [sb2(f"tok2_{i}", [128, D], BF16) for i in range(4)]
    xngt = [sb2(f"xngt{i}", [128, 8 * GROWS], BF16) for i in range(2)]
    ht_sb = sb2("ht_sb", [128, 32 * GROWS], BF16)
    y_sb = [sb2(f"y_sb{i}", [128, 1024], BF16) for i in range(2)]
    p2.close()

    p3 = ExitStack()

    def sb3(name, shape, dt):
        return p3.enter_context(nc.sbuf_tensor(name, shape, dt))

    ctot_row = sb3("ctot_row", [1, NCH * NE], BF16)
    ctot64 = sb3("ctot64", [NCH, NE], BF16)
    pfl64 = sb3("pfl64", [NCH, NE], F32)
    pfg64 = sb3("pfg64", [NCH, NE], F32)
    prefloc512 = sb3("prefloc512", [1, NCH * NE], F32)
    prefglob512 = sb3("prefglob512", [1, NCH * NE], F32)
    k8a = sb3("k8a", [128, NCH * NE], F32)
    k8b = sb3("k8b", [128, NCH * NE], F32)
    rp512 = sb3("rp512", [128, NCH * NE], F32)
    mk1 = sb3("mk1", [128, NCH * NE], F32)
    mk2 = sb3("mk2", [128, NCH * NE], F32)
    sc1 = sb3("sc1", [128, NCH], F32)
    sc2 = sb3("sc2", [128, NCH], F32)
    tc1 = sb3("tc1", [128, NCH], F32)
    tc2 = sb3("tc2", [128, NCH], F32)
    w1c = sb3("w1c", [128, NCH], F32)
    w2c = sb3("w2c", [128, NCH], F32)
    d1c = sb3("d1c", [128, NCH], F32)
    d2c = sb3("d2c", [128, NCH], F32)
    pt3 = [sb3(f"pt3_{i}", [128, D], F32) for i in range(4)]
    yg0 = [sb3(f"yg0_{i}", [128, D], BF16) for i in range(4)]
    yg1 = [sb3(f"yg1_{i}", [128, D], BF16) for i in range(4)]
    t3 = [sb3(f"t3_{i}", [128, D], F32) for i in range(4)]
    o3 = [sb3(f"o3_{i}", [128, D], F32) for i in range(4)]
    p3.close()

    pb = [psf(f"pb{i}") for i in range(8)]

    sems = {}
    slot_sems = {}
    for base, n in [("sLxA", 8), ("sTxnA", 2), ("sTkA", 4), ("sTatA", 3),
                    ("sPwA", 2), ("sXnwA", 2), ("sLxn2A", 2), ("sScatA", 2),
                    ("sRowA", 2), ("sTrsA", 2), ("sYwA", 2), ("sYgA", 4),
                    ("sLptA", 4), ("sOwA", 4)]:
        slot_sems[base] = [st.enter_context(nc.semaphore(f"{base}{i}")) for i in range(n)]
    sLxA = slot_sems["sLxA"]; sTxnA = slot_sems["sTxnA"]; sTkA = slot_sems["sTkA"]
    sTatA = slot_sems["sTatA"]; sPwA = slot_sems["sPwA"]; sXnwA = slot_sems["sXnwA"]
    sLxn2A = slot_sems["sLxn2A"]; sScatA = slot_sems["sScatA"]; sRowA = slot_sems["sRowA"]
    sTrsA = slot_sems["sTrsA"]; sYwA = slot_sems["sYwA"]; sYgA = slot_sems["sYgA"]
    sLptA = slot_sems["sLptA"]; sOwA = slot_sems["sOwA"]
    for name in ["sPreG", "sPreS", "sPreA", "sPreGm", "sMemB",
                 "sStat", "sBnA", "sRst",
                 "sQA", "sFeA", "sFeV", "sQB", "sVe", "sLgV",
                 "sS", "sSmV", "sNum", "sStV", "sDiv", "sPj", "sRes",
                 "sRout", "sKVp",
                 "sCntV1", "sCntP", "sCntV2", "sCCw", "sCC", "sCCl",
                 "sBase1", "sBase2",
                 "sCs", "sCsV", "sCtW", "sCt64", "sPfxL", "sPfxG", "sPfxV",
                 "sPfW", "sPref", "sRkA", "sRkB", "sDestV", "sGate",
                 "sW1", "sW2", "sH", "sGel", "sY", "sYe",
                 "sP3b", "sDbg", "sP1V", "sP1A", "sDivD", "sVeD", "sRcp"]:

        sems[name] = st.enter_context(nc.semaphore(name))
    S = type("S", (), sems)

    NPREG = 8 + 8 + 8 + 1 + 1
    NPRES = 11 + 16
    NPREA = NE * 32
    NMEMB = 1
    NTR_G = 32
    NGG = NE * NGRP

    def wl_after_w1(e):
        return 16 * (NPREG + e * 41 + 8)

    def wl_after_w2(e):
        return 16 * (NPREG + e * 41 + 41)

    with nc.Block() as block:

        @block.sync
        def _(sync):
            for dst, src in [(maskut, maskut_in), (urank, urank_in), (ones128, ones128_in),
                             (ones1x128f, ones1x128f_in), (iota8, iota8_in),
                             (eoff512, eoff512_in), (dmp0, dmp0_in), (dmp1, dmp1_in),
                             (iota128, iota128_in), (prevmask, prevmask_in),
                             (epsz, epsz_in)]:
                sync.dma_start(out=dst[:], in_=src).then_inc(S.sPreS, 16)
            for ct in range(16):
                sync.dma_start(out=bqk[:, ct:ct + 1],
                               in_=bqkv_in[ct * 128:(ct + 1) * 128].rearrange("(a b) -> a b", b=1)
                               ).then_inc(S.sPreS, 16)

            def _pstores(sc_):
                for cc_ in range(4):
                    c_ = sc_ * 4 + cc_
                    sync.wait_ge(S.sRes, c_ + 1)
                    sync.dma_start(out=part_scr[c_ * 128:(c_ + 1) * 128, :],
                                   in_=part_sb[c_ % 2][:]).then_inc(sPwA[c_ % 2], 16)
                    if stage < 3:
                        sync.dma_start(out=out_dram[c_ * 128:(c_ + 1) * 128, :],
                                       in_=part_sb[c_ % 2][:]).then_inc(sOwA[c_ % 2], 16)

            for sc in range(n_sc if sublvl >= 2 else 0):
                for cc in range(4):
                    c = sc * 4 + cc
                    if c >= 8 and sublvl >= 7:
                        sync.wait_ge(S.sRes, c - 7)
                    sync.dma_start(out=xfp[c % 8][:], in_=x_in[c * 128:(c + 1) * 128, :]
                                   ).then_inc(sLxA[c % 8], 16)
                for cc in range(4):
                    c = sc * 4 + cc
                    sync.wait_ge(S.sStat, c + 1)
                    if sc >= 2 and cc == 0:
                        sync.wait_ge(S.sQB, 4 * (sc - 1))
                    for k in range(8):
                        sync.dma_start(
                            out=xnt[sc % 2][:, k * 512 + cc * 128: k * 512 + cc * 128 + 128],
                            in_=xn_sb[c % 2][:, k * 128:(k + 1) * 128],
                            transpose=True).then_inc(sTxnA[c % 2], 16)
                    sync.dma_start(out=xn_scr[c * 128:(c + 1) * 128, :], in_=xn_sb[c % 2][:]
                                   ).then_inc(sXnwA[c % 2], 16)
                if sublvl >= 7 and sc >= 1:
                    _pstores(sc - 1)
                if sublvl >= 5:
                    sync.wait_ge(S.sFeV, sc * 16 + 16)
                    for cc in range(4):
                        c = sc * 4 + cc
                        if c >= 4:
                            sync.wait_ge(S.sKVp, 8 * (c - 3))
                        for p in range(8):
                            sync.dma_start(out=ktok[cc][:, p * 128:(p + 1) * 128],
                                           in_=kf[:, p * 512 + cc * 128: p * 512 + cc * 128 + 128],
                                           transpose=True).then_inc(sTkA[cc], 16)
            if sublvl >= 7 and n_sc >= 1:
                _pstores(n_sc - 1)

            for c in (range(NCH) if stage >= 2 else []):
                if c == 0:
                    sync.wait_ge(sXnwA[0], 16 * 32)
                    sync.wait_ge(sXnwA[1], 16 * 32)
                    sync.wait_ge(S.sRes, NCH)
                    sync.wait_ge(S.sP1V, 1)
                    sync.wait_ge(S.sP1A, 1)
                if c >= 2:
                    sync.wait_ge(sScatA[c % 2], 32 * ((c - 2) // 2 + 1))
                sync.dma_start(out=xnc[c % 2][:], in_=xn_scr[c * 128:(c + 1) * 128, :]
                               ).then_inc(sLxn2A[c % 2], 16)

            for e in (range(NE) if stage >= 3 else []):
                for g in range(NGRP):
                    gg = e * NGRP + g
                    r0 = e * CAP_PAD + g * GROWS
                    if gg == 0:
                        sync.wait_ge(sScatA[0], 32 * 32)
                        sync.wait_ge(sScatA[1], 32 * 32)
                        sync.wait_ge(S.sP1V, 1)
                        sync.wait_ge(S.sP1A, 1)
                        sync.wait_ge(S.sPreGm, 2)
                        sync.wait_ge(sPwA[0], 16 * 32)
                        sync.wait_ge(sPwA[1], 16 * 32)
                    if gg >= 1:
                        sync.wait_ge(sTrsA[(gg - 1) % 2], 16 * NTR_G * ((gg - 1) // 2 + 1))
                    for i in range(4):
                        sync.dma_start(out=tok2[i][:],
                                       in_=xn_buf[r0 + i * 128: r0 + (i + 1) * 128, :]
                                       ).then_inc(sRowA[gg % 2], 16)
                    sync.wait_ge(sRowA[gg % 2], 16 * 4 * (gg // 2 + 1))
                    if gg >= 2:
                        sync.wait_ge(S.sH, 32 * (gg - 1))
                    for i in range(4):
                        for k in range(8):
                            sync.dma_start(
                                out=xngt[gg % 2][:, k * GROWS + i * 128: k * GROWS + i * 128 + 128],
                                in_=tok2[i][:, k * 128:(k + 1) * 128],
                                transpose=True).then_inc(sTrsA[gg % 2], 16)
                    if gg >= 1:
                        pg = gg - 1
                        pe_, pgr = pg // NGRP, pg % NGRP
                        pr0 = pe_ * CAP_PAD + pgr * GROWS
                        for sub in range(4):
                            ss = pg * 4 + sub
                            sync.wait_ge(S.sYe, ss + 1)
                            sync.dma_start(out=y_buf[pr0 + sub * 128: pr0 + (sub + 1) * 128, :],
                                           in_=y_sb[sub % 2][:]).then_inc(sYwA[sub % 2], 16)
            if stage >= 2:
                sync.wait_ge(S.sDestV, 1)
                sync.wait_ge(S.sGate, 2)
                sync.wait_ge(S.sCCl, 16)
                sync.dma_start(out=dbg_dscat, in_=dscat[:]).then_inc(S.sDbg, 16)
                sync.dma_start(out=dbg_ygat, in_=ygat[:]).then_inc(S.sDbg, 16)
                sync.dma_start(out=dbg_g1k, in_=g1k[:]).then_inc(S.sDbg, 16)
                sync.dma_start(out=dbg_g2k, in_=g2k[:]).then_inc(S.sDbg, 16)
                sync.dma_start(out=dbg_allcnt, in_=allcnt[:]).then_inc(S.sDbg, 16)
                sync.dma_start(out=dbg_thr, in_=cntacc[:]).then_inc(S.sDbg, 16)
            if stage >= 3:
                pg = NGG - 1
                pe_, pgr = pg // NGRP, pg % NGRP
                pr0 = pe_ * CAP_PAD + pgr * GROWS
                for sub in range(4):
                    ss = pg * 4 + sub
                    sync.wait_ge(S.sYe, ss + 1)
                    sync.dma_start(out=y_buf[pr0 + sub * 128: pr0 + (sub + 1) * 128, :],
                                   in_=y_sb[sub % 2][:]).then_inc(sYwA[sub % 2], 16)

            for c in (range(NCH) if stage >= 3 else []):
                if c == 0:
                    sync.wait_ge(sPwA[0], 16 * 32)
                    sync.wait_ge(sPwA[1], 16 * 32)
                    sync.wait_ge(S.sYe, 4 * NGG)
                if c >= 4:
                    sync.wait_ge(S.sP3b, c - 3)
                sync.dma_start(out=pt3[c % 4][:], in_=part_scr[c * 128:(c + 1) * 128, :]
                               ).then_inc(sLptA[c % 4], 16)
                if c >= 3:
                    cs = c - 3
                    sync.wait_ge(S.sP3b, cs + 1)
                    sync.dma_start(out=out_dram[cs * 128:(cs + 1) * 128, :], in_=o3[cs % 4][:]
                                   ).then_inc(sOwA[cs % 4], 16)
            if stage >= 3:
                for cs in range(NCH - 3, NCH):
                    sync.wait_ge(S.sP3b, cs + 1)
                    sync.dma_start(out=out_dram[cs * 128:(cs + 1) * 128, :], in_=o3[cs % 4][:]
                                   ).then_inc(sOwA[cs % 4], 16)
            if sublvl >= 7:
                if stage >= 3:
                    for i in range(4):
                        sync.wait_ge(sOwA[i], 16 * (NCH // 4))
                else:
                    sync.wait_ge(sOwA[0], 16 * 2 * n_sc)
                    sync.wait_ge(sOwA[1], 16 * 2 * n_sc)
                if stage >= 2:
                    sync.wait_ge(S.sDbg, 16 * 6)
            else:
                sync.wait_ge(S.sPreGm, 1)
                if os.environ.get("KK_NOMEMSET", "0") != "1":
                    sync.wait_ge(S.sMemB, 16 * NMEMB)
                sync.wait_ge(S.sPreG, 16 * NPREG)
                sync.wait_ge(S.sPreA, 16 * NPREA)
                if sublvl >= 2:
                    sync.wait_ge(sXnwA[0], 16 * 2 * n_sc)
                    sync.wait_ge(sXnwA[1], 16 * 2 * n_sc)
                if sublvl == 21:
                    sync.wait_ge(S.sQA, int(os.environ.get("KK_CT", "16")) * n_sc)
                if sublvl == 22:
                    sync.wait_ge(S.sFeA, 16 * n_sc)
                if sublvl >= 3 and sublvl not in (21, 22):
                    sync.wait_ge(S.sFeV, 16 * n_sc)
                if sublvl >= 4:
                    sync.wait_ge(S.sVe, 4 * n_sc)
                    sync.wait_ge(S.sLgV, n_sc)
                if sublvl >= 5:
                    sync.wait_ge(S.sDiv, 32 * n_sc)
                if sublvl >= 6:
                    sync.wait_ge(S.sRout, 4 * n_sc)

        @block.gpsimd
        def _(gpsimd):
            for k in range(8):
                gpsimd.dma_start(out=wqkv_sb[:, k * 3072:(k + 1) * 3072],
                                 in_=wqkv_in[k * 128:(k + 1) * 128, :]).then_inc(S.sPreG, 16)
            for k in range(8):
                gpsimd.dma_start(out=wout_sb[:, k * 1024:(k + 1) * 1024],
                                 in_=wout_in[k * 128:(k + 1) * 128, :]).then_inc(S.sPreG, 16)
            for k in range(8):
                gpsimd.dma_start(out=wg_sb[:, k * NE:(k + 1) * NE],
                                 in_=wg_in[k * 128:(k + 1) * 128, :]).then_inc(S.sPreG, 16)
            gpsimd.dma_start(out=bv_row[:], in_=bqkv_in[2048:3072].rearrange("(a b) -> a b", a=1)
                             ).then_inc(S.sPreG, 16)
            gpsimd.dma_start(out=bout_row[:], in_=bout_in.rearrange("(a b) -> a b", a=1)
                             ).then_inc(S.sPreG, 16)
            gpsimd.memset(zrow[:], 0.0)
            gpsimd.memset(s0f[:], 0.0)
            gpsimd.memset(s0b[:], 0.0).then_inc(S.sPreGm, 1)
            gpsimd.wait_ge(S.sPreGm, 1)
            gpsimd.dma_start(out=y_buf[YZERO:YZERO + 1, :], in_=zrow[0:1, 0:1024]
                             ).then_inc(S.sMemB, 16)
            for e in range(NE):
                for ht in range(32):
                    gpsimd.dma_start(out=b1_sb[:, e * 32 + ht: e * 32 + ht + 1],
                                     in_=b1_in[e, ht * 128:(ht + 1) * 128].rearrange("(a b) -> a b", b=1)
                                     ).then_inc(S.sPreA, 16)

            if stage < 2:
                return
            gpsimd.wait_ge(S.sCntV2, 1)
            gpsimd.dma_start(out=cc_in.ap().rearrange("a b -> b a"), in_=cnt8[:]
                             ).then_inc(S.sCCw, 16)
            gpsimd.wait_ge(S.sCsV, 1)
            gpsimd.dma_start(out=ctot_dram.ap(), in_=ctot_row[:]).then_inc(S.sCtW, 16)
            gpsimd.wait_ge(S.sCtW, 16)
            gpsimd.dma_start(out=ctot64[:],
                             in_=ctot_dram.ap().rearrange("a (c e) -> (a c) e", e=NE)
                             ).then_inc(S.sCt64, 16)
            gpsimd.wait_ge(S.sCCw, 16)
            gpsimd.collective_compute(
                "AllGather", ALU.bypass,
                replica_groups=[list(range(NCORES))],
                ins=[cc_in.ap().opt()], outs=[cc_out.ap().opt()],
            ).then_inc(S.sCC)
            gpsimd.wait_ge(S.sCC, 1)
            gpsimd.dma_start(out=allcnt[:], in_=cc_out.ap()).then_inc(S.sCCl, 16)
            gpsimd.wait_ge(S.sPfxV, 1)
            gpsimd.dma_start(out=pfl_dram.ap(), in_=pfl64[:]).then_inc(S.sPfW, 16)
            gpsimd.dma_start(out=pfg_dram.ap(), in_=pfg64[:]).then_inc(S.sPfW, 16)
            gpsimd.wait_ge(S.sPfW, 32)
            gpsimd.dma_start(out=prefloc512[:],
                             in_=pfl_dram.ap().rearrange("(a c) e -> a (c e)", a=1)
                             ).then_inc(S.sPref, 16)
            gpsimd.dma_start(out=prefglob512[:],
                             in_=pfg_dram.ap().rearrange("(a c) e -> a (c e)", a=1)
                             ).then_inc(S.sPref, 16)

            gpsimd.wait_ge(S.sMemB, 16 * NMEMB)
            gpsimd.wait_ge(S.sDestV, 1)
            for c in range(NCH):
                gpsimd.wait_ge(sLxn2A[c % 2], 16 * (c // 2 + 1))
                for j in range(2):
                    gpsimd.indirect_dma_start(
                        out=xn_buf,
                        out_offset=IndirectOffsetOnAxis(ap=dscat[:, 2 * c + j: 2 * c + j + 1], axis=0),
                        in_=xnc[c % 2][:], in_offset=None,
                    ).then_inc(sScatA[c % 2], 16)

            for e in (range(NE) if stage >= 3 else []):
                if e == 0:
                    gpsimd.wait_ge(sScatA[0], 32 * 32)
                    gpsimd.wait_ge(sScatA[1], 32 * 32)
                    gpsimd.wait_ge(S.sP1V, 1)
                    gpsimd.wait_ge(S.sP1A, 1)
                    gpsimd.wait_ge(sPwA[0], 16 * 32)
                    gpsimd.wait_ge(sPwA[1], 16 * 32)
                    gpsimd.wait_ge(sXnwA[0], 16 * 32)
                    gpsimd.wait_ge(sXnwA[1], 16 * 32)
                    gpsimd.memset(xngt[0][:], 0.0)
                    gpsimd.memset(xngt[1][:], 0.0).then_inc(S.sPreGm, 1)
                else:
                    gpsimd.wait_ge(S.sH, 32 * NGRP * e)
                for k in range(8):
                    gpsimd.dma_start(out=w1_sb[:, k * HID:(k + 1) * HID],
                                     in_=w1_in[e, k * 128:(k + 1) * 128, :]).then_inc(S.sW1, 16)
                if e > 0:
                    gpsimd.wait_ge(S.sY, 4 * NGRP * e)
                for j in range(32):
                    gpsimd.dma_start(out=w2_sb[:, j * 1024:(j + 1) * 1024],
                                     in_=w2_in[e, j * 128:(j + 1) * 128, :]).then_inc(S.sW2, 16)
                gpsimd.dma_start(out=b2e[:],
                                 in_=b2_in[e, :].rearrange("(a b) -> a b", a=1)).then_inc(S.sW2, 16)

            if stage < 3:
                return
            gpsimd.wait_ge(sYwA[0], 16 * 2 * NGG)
            gpsimd.wait_ge(sYwA[1], 16 * 2 * NGG)
            for c in range(NCH):
                if c >= 4:
                    gpsimd.wait_ge(S.sP3b, c - 3)
                gpsimd.indirect_dma_start(
                    out=yg0[c % 4][:], out_offset=None, in_=y_buf,
                    in_offset=IndirectOffsetOnAxis(ap=ygat[:, 2 * c: 2 * c + 1], axis=0),
                ).then_inc(sYgA[c % 4], 16)
                gpsimd.indirect_dma_start(
                    out=yg1[c % 4][:], out_offset=None, in_=y_buf,
                    in_offset=IndirectOffsetOnAxis(ap=ygat[:, 2 * c + 1: 2 * c + 2], axis=0),
                ).then_inc(sYgA[c % 4], 16)

        @block.tensor
        def _(tensor):
            tensor.wait_ge(S.sPreG, 16 * NPREG)
            tensor.wait_ge(S.sPreS, 16 * NPRES)
            tensor.wait_ge(S.sPreGm, 1)

            for sc in range(n_sc if (sublvl >= 3 or sublvl in (21, 22)) else 0):
                tensor.wait_ge(sTxnA[0], 16 * 8 * 2 * (sc + 1))
                tensor.wait_ge(sTxnA[1], 16 * 8 * 2 * (sc + 1))
                _CT = int(os.environ.get("KK_CT", "16"))
                _K = int(os.environ.get("KK_K", "8"))
                for ct in range(_CT):
                    bank = ct % 2
                    gt = sc * 16 + ct
                    if gt >= 2 and sublvl != 21:
                        tensor.wait_ge(S.sFeA, gt - 1)
                    if ct < 2 and sc >= 1:
                        tensor.wait_ge(S.sSmV, 2 * 32 * sc)
                    _OPS = os.environ.get("KK_OPS", "orig")
                    for k in range(8):
                        i_ = tensor.matmul(pb[bank][:, 0:512],
                                           wqkv_sb[:, k * 3072 + ct * 128: k * 3072 + ct * 128 + 128],
                                           xnt[sc % 2][:, k * 512:(k + 1) * 512],
                                           start=(k == 0), stop=(k == 7))
                    i_.then_inc(S.sQA, 1)
                _SEQB = os.environ.get("KK_SEQB", "0") == "1"
                for cc in range(4 if sublvl >= 4 else 0):
                    c = sc * 4 + cc
                    if c >= 1:
                        tensor.wait_ge(S.sVe, c)
                    if sc >= 1 and cc == 0:
                        tensor.wait_ge(S.sRes, 4 * sc)
                        tensor.wait_ge(S.sLgV, sc)
                        tensor.wait_ge(S.sSmV, 2 * 32 * sc)
                        tensor.wait_ge(S.sStV, 32 * sc)
                    if _SEQB:
                        for cg in range(2):
                            for k in range(8):
                                lhs = xnt[sc % 2][:, k * 512 + cc * 128: k * 512 + cc * 128 + 128]
                                tensor.matmul(pb[2 + cg][:, 0:512], lhs,
                                              wqkv_sb[:, k * 3072 + 2048 + cg * 512: k * 3072 + 2048 + (cg + 1) * 512],
                                              start=(k == 0), stop=False)
                            tensor.matmul(pb[2 + cg][:, 0:512], ones128[0:1, :],
                                          bv_row[0:1, cg * 512:(cg + 1) * 512],
                                          start=False, stop=True)
                        for k in range(8):
                            lhs = xnt[sc % 2][:, k * 512 + cc * 128: k * 512 + cc * 128 + 128]
                            i_ = tensor.matmul(pb[4][:, cc * 8: cc * 8 + 8], lhs,
                                               wg_sb[:, k * NE:(k + 1) * NE],
                                               start=(cc == 0 and k == 0), stop=(cc == 3 and k == 7))
                        i_.then_inc(S.sQB, 1)
                    else:
                        for k in range(8):
                            lhs = xnt[sc % 2][:, k * 512 + cc * 128: k * 512 + cc * 128 + 128]
                            for cg in range(2):
                                tensor.matmul(pb[2 + cg][:, 0:512], lhs,
                                              wqkv_sb[:, k * 3072 + 2048 + cg * 512: k * 3072 + 2048 + (cg + 1) * 512],
                                              start=(k == 0), stop=False)
                            tensor.matmul(pb[4][:, cc * 8: cc * 8 + 8], lhs,
                                          wg_sb[:, k * NE:(k + 1) * NE],
                                          start=(cc == 0 and k == 0), stop=(cc == 3 and k == 7))
                        for cg in range(2):
                            i_ = tensor.matmul(pb[2 + cg][:, 0:512], ones128[0:1, :],
                                               bv_row[0:1, cg * 512:(cg + 1) * 512],
                                               start=False, stop=True)
                        i_.then_inc(S.sQB, 1)

                for cc in range(4 if sublvl >= 5 else 0):
                    c = sc * 4 + cc
                    pos = c % CPB
                    tensor.wait_ge(S.sFeV, sc * 16 + 16)
                    tensor.wait_ge(S.sVeD, c + 1)
                    tensor.wait_ge(sTkA[cc], 16 * 8 * (sc + 1))
                    if cc == 0:
                        tensor.wait_ge(S.sVe, 4 * sc + 4)
                        tensor.wait_ge(S.sLgV, sc + 1)
                    for p in range(8):
                        gp = c * 8 + p
                        sb0 = 0 if p % 2 == 0 else 2
                        if gp >= 2:
                            tensor.wait_ge(S.sSmV, 2 * (gp - 1))
                        colq = p * 512 + cc * 128
                        tensor.matmul(pb[sb0][:, 0:128], kf[0:64, colq:colq + 128],
                                      qf[0:64, colq:colq + 128], start=True, stop=True)
                        tensor.matmul(pb[sb0 + 1][:, 0:128], kf[64:128, colq:colq + 128],
                                      qf[64:128, colq:colq + 128], start=True, stop=True
                                      ).then_inc(S.sS, 1)
                        nbank = 6 + (p % 2)
                        if p == 0:
                            tensor.wait_ge(S.sRes, c)
                        if gp >= 2:
                            tensor.wait_ge(S.sDiv, gp - 1)
                        tensor.wait_ge(S.sSmV, 2 * (gp + 1))
                        if pos > 0:
                            tensor.wait_ge(S.sStV, (c - 1) * 8 + p + 1)
                        tensor.matmul(pb[nbank][:, 0:65], smt[p % 4][:, 0:128],
                                      vsb[cc][:, (2 * p) * 65:(2 * p) * 65 + 65],
                                      start=True, stop=False)
                        i_ = tensor.matmul(pb[nbank][:, 65:130], smt[p % 4][:, 128:256],
                                           vsb[cc][:, (2 * p + 1) * 65:(2 * p + 1) * 65 + 65],
                                           start=False, stop=(pos == 0))
                        if pos > 0:
                            i_ = tensor.matmul(pb[nbank][:, 0:130],
                                               qf[:, colq:colq + 128],
                                               s0b[:, p * 130:(p + 1) * 130],
                                               start=False, stop=True)
                        i_.then_inc(S.sNum, 1)
                        if os.environ.get("KK_ST7", "0") == "1":
                            if gp >= 1:
                                tensor.wait_ge(S.sStV, gp)
                            tensor.matmul(pb[7][:, 192:322],
                                          ktok[cc][:, p * 128:(p + 1) * 128],
                                          vsb[cc][:, (2 * p) * 65:(2 * p) * 65 + 130],
                                          start=True, stop=True).then_inc(S.sKVp, 1)
                        else:
                            kbank = 4 + (p % 2)
                            if gp >= 2:
                                tensor.wait_ge(S.sStV, gp - 1)
                            tensor.matmul(pb[kbank][:, 0:130],
                                          ktok[cc][:, p * 128:(p + 1) * 128],
                                          vsb[cc][:, (2 * p) * 65:(2 * p) * 65 + 130],
                                          start=True, stop=True).then_inc(S.sKVp, 1)
                    if sublvl < 7:
                        continue
                    tensor.wait_ge(sTatA[c % 3], 16 * 8 * (c // 3 + 1))
                    tensor.wait_ge(S.sDiv, 8 * (c + 1))
                    for k in range(8):
                        for cg in range(2):
                            tensor.matmul(pb[6 + cg][:, 0:512],
                                          attnt[c % 3][:, k * 128:(k + 1) * 128],
                                          wout_sb[:, k * 1024 + cg * 512: k * 1024 + (cg + 1) * 512],
                                          start=(k == 0), stop=False)
                    for cg in range(2):
                        i_ = tensor.matmul(pb[6 + cg][:, 0:512], ones128[0:1, :],
                                           bout_row[0:1, cg * 512:(cg + 1) * 512],
                                           start=False, stop=True)
                    i_.then_inc(S.sPj, 1)

            if stage < 2:
                return
            tensor.wait_ge(S.sRout, NCH)
            tensor.wait_ge(S.sCntV1, 1)
            tensor.wait_ge(S.sDiv, 8 * NCH)
            tensor.wait_ge(S.sRes, NCH)
            tensor.wait_ge(S.sLgV, NSC)
            tensor.wait_ge(S.sStV, 8 * NCH)
            tensor.wait_ge(sPwA[0], 16 * 32)
            tensor.wait_ge(sPwA[1], 16 * 32)
            tensor.wait_ge(sXnwA[0], 16 * 32)
            tensor.wait_ge(sXnwA[1], 16 * 32)
            tensor.matmul(pb[2][0:NE, 0:1], cntbf[:], ones128[:, 0:1], start=True, stop=True
                          ).then_inc(S.sCntP, 1)
            tensor.matmul(pb[6][0:1, 0:NCH * NE], ones128[:, 0:1], msk[:],
                          start=True, stop=True).then_inc(S.sCs, 1)
            tensor.wait_ge(S.sCCl, 16)
            tensor.matmul(pb[3][0:1, 0:NE], prevmask[:], allcnt[:], start=True, stop=True
                          ).then_inc(S.sBase1, 1)
            tensor.wait_ge(S.sCt64, 16)
            tensor.matmul(pb[5][0:NCH, 8:16], urank[0:NCH, 0:NCH], ctot64[:],
                          start=True, stop=True).then_inc(S.sPfxL, 1)
            tensor.wait_ge(S.sBase2, 1)
            tensor.matmul(pb[5][0:NCH, 0:8], urank[0:NCH, 0:NCH], ctot64[:],
                          start=True, stop=False)
            tensor.matmul(pb[5][0:NCH, 0:8], ones1x128f[0:1, 0:NCH], base_row[:],
                          start=False, stop=True).then_inc(S.sPfxG, 1)
            tensor.wait_ge(S.sPref, 32)
            tensor.matmul(pb[0][:, 0:NCH * NE], urank[:], msk[:], start=True, stop=False)
            tensor.matmul(pb[0][:, 0:NCH * NE], ones1x128f[:], prefloc512[:],
                          start=False, stop=True).then_inc(S.sRkA, 1)
            tensor.matmul(pb[1][:, 0:NCH * NE], urank[:], msk[:], start=True, stop=False)
            tensor.matmul(pb[1][:, 0:NCH * NE], ones1x128f[:], prefglob512[:],
                          start=False, stop=True).then_inc(S.sRkB, 1)

            if stage < 3:
                return
            tensor.wait_ge(S.sDestV, 1)
            tensor.wait_ge(S.sCntV2, 1)
            tensor.wait_ge(S.sBase2, 1)
            for e in range(NE):
                for g in range(NGRP):
                    gg = e * NGRP + g
                    tensor.wait_ge(sTrsA[gg % 2], 16 * NTR_G * (gg // 2 + 1))
                    if gg == 0:
                        tensor.wait_ge(S.sW1, 16 * 8)
                    for ht in range(32):
                        hh = gg * 32 + ht
                        bank = ht % 3
                        if hh >= 3:
                            tensor.wait_ge(S.sGel, hh - 2)
                        if g == 0 and ht == 0 and e > 0:
                            tensor.wait_ge(S.sW1, 16 * 8 * (e + 1))
                        for k in range(8):
                            i_ = tensor.matmul(pb[bank][:, 0:GROWS],
                                               w1_sb[:, k * HID + ht * 128: k * HID + ht * 128 + 128],
                                               xngt[gg % 2][:, k * GROWS:(k + 1) * GROWS],
                                               start=(k == 0), stop=(k == 7))
                        i_.then_inc(S.sH, 1)
                    tensor.wait_ge(S.sGel, 32 * (gg + 1))
                    if g == 0:
                        tensor.wait_ge(S.sW2, 16 * 33 * (e + 1))
                    for sub in range(4):
                        ss = gg * 4 + sub
                        ybk = 3 + (sub % 2) * 2
                        if ss >= 2:
                            tensor.wait_ge(S.sYe, ss - 1)
                        for ht in range(32):
                            for cg in range(2):
                                tensor.matmul(pb[ybk + cg][:, 0:512],
                                              ht_sb[:, ht * GROWS + sub * 128: ht * GROWS + sub * 128 + 128],
                                              w2_sb[:, ht * 1024 + cg * 512: ht * 1024 + (cg + 1) * 512],
                                              start=(ht == 0), stop=False)
                        for cg in range(2):
                            i_ = tensor.matmul(pb[ybk + cg][:, 0:512], ones128[0:1, :],
                                               b2e[0:1, cg * 512:(cg + 1) * 512],
                                               start=False, stop=True)
                        i_.then_inc(S.sY, 1)

        @block.vector
        def _(vector):
            vector.wait_ge(S.sPreS, 16 * NPRES)
            vector.wait_ge(S.sPreGm, 1)
            for i in range(4):
                vector.tensor_copy(
                    vsb[i][:].rearrange("p (h w) -> p h w", w=65)[:, :, 64:65],
                    ones128[:, 0:16].rearrange("p (h w) -> p h w", w=1))
            vector.tensor_scalar(bqks[:, 0:8], bqk[:, 0:8], QSCALE, None, ALU.mult)
            vector.tensor_scalar(bqks[:, 8:16], bqk[:, 8:16], 1.0, None, ALU.mult)
            vector.memset(cntacc[:], 0.0)
            vector.drain()

            def _stats(sc_):
                for cc_ in range(4):
                    c_ = sc_ * 4 + cc_
                    vector.wait_ge(sLxA[c_ % 8], 16 * (c_ // 8 + 1))
                    if c_ >= 2:
                        vector.wait_ge(S.sRst, c_ - 1)
                    vector.bn_stats(bn12[c_ % 2][:, 0:6], xfp[c_ % 8][:, 0:512])
                    vector.bn_stats(bn12[c_ % 2][:, 6:12], xfp[c_ % 8][:, 512:1024])
                    vector.drain()
                    vector.bn_aggr(mv[c_ % 2][:], bn12[c_ % 2][:]).then_inc(S.sBnA, 1)
                    vector.wait_ge(S.sRst, c_ + 1)
                    vector.tensor_scalar(negms[c_ % 2][:], mv[c_ % 2][:, 0:1], rstd[c_ % 2][:],
                                         -1.0, ALU.mult, ALU.mult)
                    vector.drain()
                    if c_ >= 2:
                        vector.wait_ge(sTxnA[c_ % 2], 16 * 8 * ((c_ - 2) // 2 + 1))
                        vector.wait_ge(sXnwA[c_ % 2], 16 * ((c_ - 2) // 2 + 1))
                    vector.tensor_scalar(xn_sb[c_ % 2][:], xfp[c_ % 8][:], rstd[c_ % 2][:],
                                         negms[c_ % 2][:], ALU.mult, ALU.add).then_inc(S.sStat, 1)

            if sublvl >= 2:
                _stats(0)
            for sc in range(n_sc if sublvl >= 2 else 0):
                for ct in range(16 if (sublvl >= 3 and sublvl != 21 and sublvl != 22) else 0):
                    gt = sc * 16 + ct
                    dst = qf if ct < 8 else kf
                    off = (ct % 8) * 512
                    if ct == 8 and sc >= 1 and sublvl >= 5:
                        for cc_ in range(4):
                            vector.wait_ge(sTkA[cc_], 16 * 8 * sc)
                    vector.wait_ge(S.sFeA, gt + 1)
                    vector.tensor_scalar(ef[ct % 2][:], ef[ct % 2][:], 1.0, 1e-4,
                                         ALU.min, ALU.add)
                    vector.drain()
                    vector.tensor_tensor(dst[:, off:off + 512], ef[ct % 2][:], rf[ct % 2][:],
                                         ALU.add).then_inc(S.sFeV, 1)
                if sublvl >= 4:
                    vector.wait_ge(S.sQB, 4 * (sc + 1))
                    vector.tensor_copy(lgt[sc % 2][:], pb[4][:, 0:32]).then_inc(S.sLgV, 1)
                    vector.drain()
                if sublvl >= 2 and sc + 1 < n_sc:
                    _stats(sc + 1)
                for cc in range(4 if sublvl >= 5 else 0):
                    c = sc * 4 + cc
                    pos = c % CPB
                    for p in range(8):
                        gp = c * 8 + p
                        sb0 = 0 if p % 2 == 0 else 2
                        vector.wait_ge(S.sS, gp + 1)
                        if gp >= 4:
                            vector.wait_ge(S.sNum, gp - 3)
                        vector.tensor_tensor(smt[p % 4][:, 0:128], pb[sb0][:, 0:128], maskut[:],
                                             ALU.mult).then_inc(S.sSmV, 1)
                        vector.tensor_tensor(smt[p % 4][:, 128:256], pb[sb0 + 1][:, 0:128], maskut[:],
                                             ALU.mult).then_inc(S.sSmV, 1)
                        vector.wait_ge(S.sKVp, gp + 1)
                        vector.wait_ge(S.sNum, gp + 1)
                        kbank = 4 + (p % 2)
                        kcol0 = 0
                        if os.environ.get("KK_ST7", "0") == "1":
                            kbank, kcol0 = 7, 192
                        if pos == 0:
                            vector.tensor_scalar(s0f[0:64, p * 130: p * 130 + 65],
                                                 pb[kbank][0:64, kcol0:kcol0 + 65],
                                                 1.0, None, ALU.mult)
                            vector.tensor_scalar(s0f[64:128, p * 130 + 65: p * 130 + 130],
                                                 pb[kbank][64:128, kcol0 + 65:kcol0 + 130],
                                                 1.0, None, ALU.mult)
                        else:
                            vector.tensor_tensor(s0f[0:64, p * 130: p * 130 + 65],
                                                 s0f[0:64, p * 130: p * 130 + 65],
                                                 pb[kbank][0:64, kcol0:kcol0 + 65], ALU.add)
                            vector.tensor_tensor(s0f[64:128, p * 130 + 65: p * 130 + 130],
                                                 s0f[64:128, p * 130 + 65: p * 130 + 130],
                                                 pb[kbank][64:128, kcol0 + 65:kcol0 + 130],
                                                 ALU.add)
                        vector.drain()
                        vector.tensor_copy(s0b[:, p * 130:(p + 1) * 130],
                                           s0f[:, p * 130:(p + 1) * 130]).then_inc(S.sStV, 1)
                        nbank = 6 + (p % 2)
                        if gp >= 2:
                            vector.wait_ge(S.sDiv, gp - 1)
                        vector.tensor_scalar(dena[2 * (p % 2)][:], pb[nbank][:, 64:65],
                                             EPS_DEN, None, ALU.add)
                        vector.tensor_scalar(dena[2 * (p % 2) + 1][:], pb[nbank][:, 129:130],
                                             EPS_DEN, None, ALU.add)
                        vector.drain()
                        vector.reciprocal(dena[2 * (p % 2)][:], dena[2 * (p % 2)][:]
                                          ).then_inc(S.sRcp, 1)
                        vector.reciprocal(dena[2 * (p % 2) + 1][:], dena[2 * (p % 2) + 1][:]
                                          ).then_inc(S.sRcp, 1)
                    if sublvl < 7:
                        continue
                    vector.wait_ge(S.sPj, c + 1)
                    if c >= 2:
                        vector.wait_ge(sPwA[c % 2], 16 * ((c - 2) // 2 + 1))
                        if stage < 3:
                            vector.wait_ge(sOwA[c % 2], 16 * ((c - 2) // 2 + 1))
                    vector.tensor_tensor(part_sb[c % 2][:, 0:512], pb[6][:, 0:512],
                                         xfp[c % 8][:, 0:512], ALU.add)
                    vector.tensor_tensor(part_sb[c % 2][:, 512:1024], pb[7][:, 0:512],
                                         xfp[c % 8][:, 512:1024], ALU.add)
                    vector.drain()
                    vector.memset(fence_v[:], 0.0).then_inc(S.sRes, 1)
                    if sublvl < 6:
                        continue
                    lgc = lgt[sc % 2][:, cc * 8: cc * 8 + 8]
                    vector.max(mx8[:], lgc)
                    vector.drain()
                    vector.max_index(idxu[:], mx8[:], lgc)
                    vector.drain()
                    vector.tensor_copy(idxf[:], idxu[:])
                    vector.tensor_tensor(dcol[:, c:c + 1], mx8[:, 0:1], mx8[:, 1:2], ALU.subtract)
                    vector.drain()
                    vector.tensor_scalar(m1s[:, c * NE:(c + 1) * NE], iota8[:], idxf[:, 0:1],
                                         None, ALU.is_equal)
                    vector.tensor_scalar(m2s[:, c * NE:(c + 1) * NE], iota8[:], idxf[:, 1:2],
                                         None, ALU.is_equal)
                    vector.drain()
                    vector.tensor_tensor(msk[:, c * NE:(c + 1) * NE], m1s[:, c * NE:(c + 1) * NE],
                                         m2s[:, c * NE:(c + 1) * NE], ALU.add)
                    vector.drain()
                    vector.tensor_tensor(cntacc[:], cntacc[:], msk[:, c * NE:(c + 1) * NE],
                                         ALU.add).then_inc(S.sRout, 1)
                    vector.drain()

            if stage < 2:
                return
            vector.tensor_copy(cntbf[:], cntacc[:]).then_inc(S.sCntV1, 1)
            vector.wait_ge(S.sCntP, 1)
            vector.tensor_copy(cnt8[:], pb[2][0:NE, 0:1]).then_inc(S.sCntV2, 1)
            vector.wait_ge(S.sCs, 1)
            vector.tensor_copy(ctot_row[:], pb[6][0:1, 0:NCH * NE]).then_inc(S.sCsV, 1)
            vector.wait_ge(S.sBase1, 1)
            vector.tensor_copy(base_row[:], pb[3][0:1, 0:NE]).then_inc(S.sBase2, 1)
            vector.wait_ge(S.sPfxL, 1)
            vector.tensor_copy(pfl64[:], pb[5][0:NCH, 8:16])
            vector.wait_ge(S.sPfxG, 1)
            vector.tensor_copy(pfg64[:], pb[5][0:NCH, 0:8]).then_inc(S.sPfxV, 1)
            vector.wait_ge(S.sRkA, 1)
            vector.wait_ge(S.sRkB, 1)
            vector.tensor_scalar(k8a[:], pb[0][:, 0:NCH * NE], float(CAP_PAD), None, ALU.is_lt)
            vector.tensor_scalar(k8b[:], pb[1][:, 0:NCH * NE], float(CAP), None, ALU.is_lt)
            vector.tensor_tensor(rp512[:], pb[0][:, 0:NCH * NE], eoff512[:], ALU.add)
            vector.drain()
            vector.tensor_tensor(k8a[:], k8a[:], k8b[:], ALU.mult)
            vector.drain()
            vector.tensor_tensor(mk1[:], m1s[:], k8a[:], ALU.mult)
            vector.tensor_tensor(mk2[:], m2s[:], k8a[:], ALU.mult)
            vector.drain()
            vector.tensor_tensor(k8a[:], mk1[:], rp512[:], ALU.mult)
            vector.tensor_tensor(k8b[:], mk2[:], rp512[:], ALU.mult)
            vector.drain()
            vector.reduce_sum(kk1[:].rearrange("p (c j) -> p c j", j=1),
                              mk1[:].rearrange("p (c e) -> p c e", e=NE), axis=AX.X)
            vector.reduce_sum(kk2[:].rearrange("p (c j) -> p c j", j=1),
                              mk2[:].rearrange("p (c e) -> p c e", e=NE), axis=AX.X)
            vector.reduce_sum(sc1[:].rearrange("p (c j) -> p c j", j=1),
                              k8a[:].rearrange("p (c e) -> p c e", e=NE), axis=AX.X)
            vector.reduce_sum(sc2[:].rearrange("p (c j) -> p c j", j=1),
                              k8b[:].rearrange("p (c e) -> p c e", e=NE), axis=AX.X)
            vector.drain()
            vector.tensor_scalar(tc1[:], kk1[:], -1.0, 1.0, ALU.mult, ALU.add)
            vector.tensor_scalar(tc2[:], kk2[:], -1.0, 1.0, ALU.mult, ALU.add)
            vector.drain()
            vector.tensor_tensor(w1c[:], tc1[:], dmp0[:], ALU.mult)
            vector.tensor_tensor(w2c[:], tc2[:], dmp1[:], ALU.mult)
            vector.drain()
            vector.tensor_tensor(d1c[:], sc1[:], w1c[:], ALU.add)
            vector.tensor_tensor(d2c[:], sc2[:], w2c[:], ALU.add)
            vector.drain()
            vector.tensor_copy(dscat[:].rearrange("p (c j) -> p c j", j=2)[:, :, 0:1],
                               d1c[:].rearrange("p (c j) -> p c j", j=1))
            vector.tensor_copy(dscat[:].rearrange("p (c j) -> p c j", j=2)[:, :, 1:2],
                               d2c[:].rearrange("p (c j) -> p c j", j=1))
            vector.tensor_scalar(w1c[:], tc1[:], float(YZERO), None, ALU.mult)
            vector.tensor_scalar(w2c[:], tc2[:], float(YZERO), None, ALU.mult)
            vector.drain()
            vector.tensor_tensor(d1c[:], sc1[:], w1c[:], ALU.add)
            vector.tensor_tensor(d2c[:], sc2[:], w2c[:], ALU.add)
            vector.drain()
            vector.tensor_copy(ygat[:].rearrange("p (c j) -> p c j", j=2)[:, :, 0:1],
                               d1c[:].rearrange("p (c j) -> p c j", j=1))
            vector.tensor_copy(ygat[:].rearrange("p (c j) -> p c j", j=2)[:, :, 1:2],
                               d2c[:].rearrange("p (c j) -> p c j", j=1)).then_inc(S.sDestV, 1)
            vector.drain()
            vector.wait_ge(S.sGate, 1)
            vector.tensor_tensor(g1k[:], g1s[:], kk1[:], ALU.mult)
            vector.tensor_scalar(g2tmp[:], g1s[:], -1.0, 1.0, ALU.mult, ALU.add)
            vector.drain()
            vector.tensor_tensor(g2k[:], g2tmp[:], kk2[:], ALU.mult).then_inc(S.sGate, 1)
            vector.drain()
            vector.memset(fence_v[:], 0.0).then_inc(S.sP1V, 1)

            for c in (range(NCH) if stage >= 3 else []):
                vector.wait_ge(sLptA[c % 4], 16 * (c // 4 + 1))
                vector.wait_ge(sYgA[c % 4], 32 * (c // 4 + 1))
                if c >= 4:
                    vector.wait_ge(sOwA[c % 4], 16 * (c // 4))
                vector.scalar_tensor_tensor(t3[c % 4][:], yg0[c % 4][:], g1k[:, c:c + 1],
                                            pt3[c % 4][:], ALU.mult, ALU.add)
                vector.drain()
                vector.scalar_tensor_tensor(o3[c % 4][:], yg1[c % 4][:], g2k[:, c:c + 1],
                                            t3[c % 4][:], ALU.mult, ALU.add).then_inc(S.sP3b, 1)

        @block.scalar
        def _(scalar):
            scalar.wait_ge(S.sPreS, 16 * NPRES)
            scalar.wait_ge(S.sPreG, 16 * NPREG)

            def _rstd(sc_):
                for cc_ in range(4):
                    c_ = sc_ * 4 + cc_
                    scalar.wait_ge(S.sBnA, c_ + 1)
                    if c_ >= 2:
                        scalar.wait_ge(S.sStat, c_ - 1)
                    scalar.activation(rstd[c_ % 2][:], mv[c_ % 2][:, 1:2], AF.Ln,
                                      bias=epsz[:, 0:1], scale=1.0)
                    scalar.drain()
                    scalar.activation(rstd[c_ % 2][:], rstd[c_ % 2][:], AF.Exp,
                                      bias=epsz[:, 1:2], scale=-0.5).then_inc(S.sRst, 1)

            if sublvl >= 2:
                _rstd(0)
            for sc in range(n_sc if sublvl >= 2 else 0):
                for ct in range(16 if (sublvl >= 3 or sublvl == 22) else 0):
                    gt = sc * 16 + ct
                    bank = ct % 2
                    scl = QSCALE if ct < 8 else 1.0
                    scalar.wait_ge(S.sQA, gt + 1)
                    if gt >= 2 and sublvl != 22:
                        scalar.wait_ge(S.sFeV, gt - 1)
                    scalar.activation(ef[ct % 2][:], pb[bank][:, 0:512], AF.Exp,
                                      bias=bqks[:, ct:ct + 1], scale=scl)
                    scalar.activation(rf[ct % 2][:], pb[bank][:, 0:512], AF.Relu,
                                      bias=bqks[:, ct:ct + 1], scale=scl).then_inc(S.sFeA, 1)
                for cc in range(4 if sublvl >= 4 else 0):
                    c = sc * 4 + cc
                    scalar.wait_ge(S.sQB, c + 1)
                    for h in range(16):
                        ins = pb[2 + h // 8][:, (h % 8) * 64:(h % 8) * 64 + 64]
                        i_ = scalar.activation(vsb[cc][:, h * 65: h * 65 + 64], ins, AF.Copy)
                    i_.then_inc(S.sVe, 1)
                    scalar.drain()
                    scalar.activation(fence_a[:], bqks[:, 0:1], AF.Copy).then_inc(S.sVeD, 1)
                if sublvl >= 2 and sc + 1 < n_sc:
                    _rstd(sc + 1)
                if sublvl >= 7:
                    for cc in range(4):
                        c = sc * 4 + cc
                        if c >= 2:
                            scalar.wait_ge(sTatA[c % 3], 16 * 8 * (c // 3))
                        for p in range(8):
                            gp = c * 8 + p
                            nbank = 6 + (p % 2)
                            scalar.wait_ge(S.sRcp, 2 * (gp + 1))
                            scalar.activation(attn_sb[c % 3][:, (2 * p) * 64:(2 * p) * 64 + 64],
                                              pb[nbank][:, 0:64], AF.Copy,
                                              scale=dena[2 * (p % 2)][:])
                            scalar.activation(attn_sb[c % 3][:, (2 * p + 1) * 64:(2 * p + 1) * 64 + 64],
                                              pb[nbank][:, 65:129], AF.Copy,
                                              scale=dena[2 * (p % 2) + 1][:]).then_inc(S.sDiv, 1)
                        scalar.drain()
                        scalar.activation(fence_a[:], bqks[:, 0:1], AF.Copy).then_inc(S.sDivD, 1)
                        scalar.wait_ge(S.sDivD, c + 1)
                        if c >= 3:
                            scalar.wait_ge(S.sPj, c - 2)
                        for k in range(8):
                            scalar.dma_start(out=attnt[c % 3][:, k * 128:(k + 1) * 128],
                                             in_=attn_sb[c % 3][:, k * 128:(k + 1) * 128],
                                             transpose=True).then_inc(sTatA[c % 3], 16)

            if stage < 2:
                return
            scalar.wait_ge(S.sRout, NCH)
            scalar.activation(g1s[:], dcol[:], AF.Sigmoid).then_inc(S.sGate, 1)
            scalar.drain()
            scalar.activation(fence_a[:], g1s[:, 0:1], AF.Copy).then_inc(S.sP1A, 1)

            if stage < 3:
                return
            scalar.wait_ge(S.sPreA, 16 * NPREA)
            for e in range(NE):
                for g in range(NGRP):
                    gg = e * NGRP + g
                    for ht in range(32):
                        hh = gg * 32 + ht
                        bank = ht % 3
                        scalar.wait_ge(S.sH, hh + 1)
                        scalar.activation(ht_sb[:, ht * GROWS:(ht + 1) * GROWS],
                                          pb[bank][:, 0:GROWS], AF.Gelu_apprx_tanh,
                                          bias=b1_sb[:, e * 32 + ht: e * 32 + ht + 1],
                                          scale=1.0).then_inc(S.sGel, 1)
                    for sub in range(4):
                        ss = gg * 4 + sub
                        ybk = 3 + (sub % 2) * 2
                        scalar.wait_ge(S.sY, ss + 1)
                        if ss >= 2:
                            scalar.wait_ge(sYwA[sub % 2], 16 * ((ss - 2) // 2 + 1))
                        scalar.activation(y_sb[sub % 2][:, 0:512], pb[ybk][:, 0:512], AF.Copy)
                        scalar.activation(y_sb[sub % 2][:, 512:1024], pb[ybk + 1][:, 0:512],
                                          AF.Copy).then_inc(S.sYe, 1)

    st.close()
    return nc


_NC_CACHE = {}


def _consts():
    i = np.arange(128)
    maskut = (i[:, None] <= i[None, :]).astype(ml_dtypes.bfloat16)
    urank = (i[:, None] < i[None, :]).astype(ml_dtypes.bfloat16)
    ones128 = np.ones((128, 128), ml_dtypes.bfloat16)
    ones1x128f = np.ones((1, 128), np.float32)
    iota8 = np.broadcast_to(np.arange(NE, dtype=np.float32), (128, NE)).copy()
    eoff = np.broadcast_to((np.arange(NE) * CAP_PAD).astype(np.float32), (128, NE)).copy()
    eoff512 = np.tile(eoff[0], (128, NCH)).astype(np.float32)
    cidx = np.arange(NCH, dtype=np.float32)[None, :]
    pidx = np.arange(128, dtype=np.float32)[:, None]
    dmp0 = (DUMP0 + 256.0 * cidx + 2.0 * pidx + 0.0).astype(np.float32)
    dmp1 = (DUMP0 + 256.0 * cidx + 2.0 * pidx + 1.0).astype(np.float32)
    iota128 = np.arange(128, dtype=np.float32).reshape(128, 1)
    epsz = np.zeros((128, 2), np.float32); epsz[:, 0] = EPS_LN
    return maskut, urank, ones128, ones1x128f, iota8, eoff, eoff512, dmp0, dmp1, iota128, epsz


def kernel(L0_attn_x, L0_attn_W_qkv, L0_attn_b_qkv, L0_attn_W_out, L0_attn_b_out,
           L0_ffn_Wg, L0_ffn_W1, L0_ffn_b1, L0_ffn_W2, L0_ffn_b2):
    xf = np.ascontiguousarray(np.asarray(L0_attn_x, np.float32).reshape(NTOT, D))
    (maskut, urank, ones128, ones1x128f, iota8, eoff, eoff512, dmp0, dmp1,
     iota128, epsz) = _consts()

    if "nc" not in _NC_CACHE:
        _NC_CACHE["nc"] = build_program(3, NSC, 7)
    nc = _NC_CACHE["nc"]

    shared = dict(
        wqkv=np.ascontiguousarray(np.asarray(L0_attn_W_qkv, np.float32)),
        bqkv=np.ascontiguousarray(np.asarray(L0_attn_b_qkv, np.float32)),
        wout=np.ascontiguousarray(np.asarray(L0_attn_W_out, np.float32)),
        bout=np.ascontiguousarray(np.asarray(L0_attn_b_out, np.float32)),
        wg=np.ascontiguousarray(np.asarray(L0_ffn_Wg, np.float32)),
        w1=np.ascontiguousarray(np.asarray(L0_ffn_W1, np.float32)),
        b1=np.ascontiguousarray(np.asarray(L0_ffn_b1, np.float32)),
        w2=np.ascontiguousarray(np.asarray(L0_ffn_W2, np.float32)),
        b2=np.ascontiguousarray(np.asarray(L0_ffn_b2, np.float32)),
        maskut=maskut, urank=urank, ones128=ones128, ones1x128f=ones1x128f,
        iota8=iota8, eoff=eoff, eoff512=eoff512, dmp0=dmp0, dmp1=dmp1,
        iota128=iota128, epsz=epsz,
    )
    in_maps = []
    for c in range(NCORES):
        m = dict(shared)
        m["x"] = np.ascontiguousarray(xf[c * TOK:(c + 1) * TOK])
        m["prevmask"] = (np.arange(NCORES) < c).astype(np.float32).reshape(NCORES, 1)
        in_maps.append(m)

    res = run_bass_kernel_spmd(nc, in_maps, core_ids=list(range(NCORES)))
    out = np.concatenate([res.results[c]["out"] for c in range(NCORES)], axis=0)
    return out.reshape(B, L, D)


if __name__ == "__main__":
    import time
    t0 = time.time()
    nc = build_program()
    print(f"build ok in {time.time()-t0:.1f}s; instructions: "
          f"{sum(1 for _ in nc.all_instructions())}")


# version 09
# speedup vs baseline: 1.0791x; 1.0011x over previous
import math
import os
import numpy as np
import ml_dtypes
from contextlib import ExitStack

import concourse.bass as bass
import concourse.mybir as mybir
from concourse.bass_utils import run_bass_kernel_spmd
from concourse.bass import IndirectOffsetOnAxis

F32 = mybir.dt.float32
BF16 = mybir.dt.bfloat16
U32 = mybir.dt.uint32
AF = mybir.ActivationFunctionType
ALU = mybir.AluOpType
AX = mybir.AxisListType

NCORES = 8
B, L, D = 16, 4096, 1024
NH, EH = 16, 64
TOK = 8192
NCH = 64
CPB = 32
NSC = 16
NE, HID, TOPK = 8, 4096, 2
NTOT = B * L
CAP = int(math.ceil(NTOT * TOPK / NE * 1.05))
CAP_PAD = 2560
GROWS = 512
NGRP = CAP_PAD // GROWS
NBUF = NE * CAP_PAD
DUMP0 = NBUF
XNROWS = NBUF + 2 * TOK
YZERO = NBUF
EPS_LN = 1e-5
EPS_DEN = 1e-6
QSCALE = EH ** (-0.5)


def build_program(stage=3, n_sc=NSC, sublvl=7):
    nc = bass.Bass("TRN2", target_bir_lowering=False, debug=False,
                   dynamic_dma_scratch_size=16384)

    x_in = nc.dram_tensor("x", [TOK, D], F32, kind="ExternalInput").ap()
    wqkv_in = nc.dram_tensor("wqkv", [D, 3 * D], F32, kind="ExternalInput").ap()
    bqkv_in = nc.dram_tensor("bqkv", [3 * D], F32, kind="ExternalInput").ap()
    wout_in = nc.dram_tensor("wout", [D, D], F32, kind="ExternalInput").ap()
    bout_in = nc.dram_tensor("bout", [D], F32, kind="ExternalInput").ap()
    wg_in = nc.dram_tensor("wg", [D, NE], F32, kind="ExternalInput").ap()
    w1_in = nc.dram_tensor("w1", [NE, D, HID], F32, kind="ExternalInput").ap()
    b1_in = nc.dram_tensor("b1", [NE, HID], F32, kind="ExternalInput").ap()
    w2_in = nc.dram_tensor("w2", [NE, HID, D], F32, kind="ExternalInput").ap()
    b2_in = nc.dram_tensor("b2", [NE, D], F32, kind="ExternalInput").ap()
    maskut_in = nc.dram_tensor("maskut", [128, 128], BF16, kind="ExternalInput").ap()
    urank_in = nc.dram_tensor("urank", [128, 128], BF16, kind="ExternalInput").ap()
    ones128_in = nc.dram_tensor("ones128", [128, 128], BF16, kind="ExternalInput").ap()
    ones1x128f_in = nc.dram_tensor("ones1x128f", [1, 128], F32, kind="ExternalInput").ap()
    iota8_in = nc.dram_tensor("iota8", [128, NE], F32, kind="ExternalInput").ap()
    eoff_in = nc.dram_tensor("eoff", [128, NE], F32, kind="ExternalInput").ap()
    eoff512_in = nc.dram_tensor("eoff512", [128, NCH * NE], F32, kind="ExternalInput").ap()
    dmp0_in = nc.dram_tensor("dmp0", [128, NCH], F32, kind="ExternalInput").ap()
    dmp1_in = nc.dram_tensor("dmp1", [128, NCH], F32, kind="ExternalInput").ap()
    iota128_in = nc.dram_tensor("iota128", [128, 1], F32, kind="ExternalInput").ap()
    prevmask_in = nc.dram_tensor("prevmask", [NCORES, 1], F32, kind="ExternalInput").ap()
    epsz_in = nc.dram_tensor("epsz", [128, 2], F32, kind="ExternalInput").ap()
    out_dram = nc.dram_tensor("out", [TOK, D], F32, kind="ExternalOutput").ap()
    dbg_dscat = nc.dram_tensor("dbg_dscat", [128, 2 * NCH], U32, kind="ExternalOutput").ap()
    dbg_ygat = nc.dram_tensor("dbg_ygat", [128, 2 * NCH], U32, kind="ExternalOutput").ap()
    dbg_g1k = nc.dram_tensor("dbg_g1k", [128, NCH], F32, kind="ExternalOutput").ap()
    dbg_g2k = nc.dram_tensor("dbg_g2k", [128, NCH], F32, kind="ExternalOutput").ap()
    dbg_allcnt = nc.dram_tensor("dbg_allcnt", [NCORES, NE], F32, kind="ExternalOutput").ap()
    dbg_thr = nc.dram_tensor("dbg_thr", [128, NE], F32, kind="ExternalOutput").ap()
    DBG3 = os.environ.get("KK_DBG3", "0") == "1"
    if DBG3:
        dbg_qf = nc.dram_tensor("dbg_qf", [128, 4096], BF16, kind="ExternalOutput").ap()
        dbg_kf = nc.dram_tensor("dbg_kf", [128, 4096], BF16, kind="ExternalOutput").ap()
        dbg_v = nc.dram_tensor("dbg_v", [128, 1040], BF16, kind="ExternalOutput").ap()
        dbg_xnt = nc.dram_tensor("dbg_xnt", [128, 4096], BF16, kind="ExternalOutput").ap()
        dbg_attn = nc.dram_tensor("dbg_attn", [128, 1024], BF16, kind="ExternalOutput").ap()
        dbg_s0 = nc.dram_tensor("dbg_s0", [128, 1040], F32, kind="ExternalOutput").ap()
        dbg_lgt = nc.dram_tensor("dbg_lgt", [128, 32], F32, kind="ExternalOutput").ap()

    xn_scr = nc.dram_tensor("xn_scr", [TOK, D], BF16).ap()
    xn_buf = nc.dram_tensor("xn_buf", [XNROWS, D], BF16).ap()
    y_buf = nc.dram_tensor("y_buf", [NBUF + 1, D], BF16).ap()
    part_scr = nc.dram_tensor("part_scr", [TOK, D], F32).ap()
    cc_in = nc.dram_tensor("cc_in", [1, NE], F32)
    cc_out = nc.dram_tensor("cc_out", [NCORES, NE], F32)
    ctot_dram = nc.dram_tensor("ctot_scr", [1, NCH * NE], BF16)
    pfl_dram = nc.dram_tensor("pfl_scr", [NCH, NE], F32)
    pfg_dram = nc.dram_tensor("pfg_scr", [NCH, NE], F32)

    st = ExitStack()

    def sb(name, shape, dt):
        return st.enter_context(nc.sbuf_tensor(name, shape, dt))

    def psf(name):
        return st.enter_context(nc.psum_tensor(name, [128, 512], F32))

    maskut = sb("maskut_s", [128, 128], BF16)
    urank = sb("urank_s", [128, 128], BF16)
    ones128 = sb("ones128b", [128, 128], BF16)
    ones1x128f = sb("ones1x128f_s", [1, 128], F32)
    iota8 = sb("iota8_s", [128, NE], F32)
    eoff = sb("eoff_s", [128, NE], F32)
    iota128 = sb("iota128_s", [128, 1], F32)
    prevmask = sb("prevmask_s", [NCORES, 1], F32)
    epsz = sb("epsz_s", [128, 2], F32)

    msk = sb("msk", [128, NCH * NE], BF16)
    m1s = sb("m1s", [128, NCH * NE], BF16)
    m2s = sb("m2s", [128, NCH * NE], BF16)
    dcol = sb("dcol", [128, NCH], F32)
    cntacc = sb("cntacc", [128, NE], F32)
    kk1 = sb("kk1", [128, NCH], F32)
    kk2 = sb("kk2", [128, NCH], F32)
    g1k = sb("g1k", [128, NCH], F32)
    g2k = sb("g2k", [128, NCH], F32)
    g1s = sb("g1s", [128, NCH], F32)
    g2tmp = sb("g2tmp", [128, NCH], F32)
    dscat = sb("dscat", [128, 2 * NCH], U32)
    ygat = sb("ygat", [128, 2 * NCH], U32)
    cnt8 = sb("cnt8", [NE, 1], F32)
    cntbf = sb("cntbf", [128, NE], BF16)
    allcnt = sb("allcnt", [NCORES, NE], F32)
    base_row = sb("base_row", [1, NE], F32)
    eoff512 = sb("eoff512_s", [128, NCH * NE], F32)
    dmp0 = sb("dmp0_s", [128, NCH], F32)
    dmp1 = sb("dmp1_s", [128, NCH], F32)
    b1_sb = sb("b1_sb", [128, NE * 32], F32)
    b2e = sb("b2e", [1, 1024], BF16)
    fence_v = sb("fence_v", [128, 1], F32)
    fence_a = sb("fence_a", [128, 1], F32)

    p1 = ExitStack()

    def sb1(name, shape, dt):
        return p1.enter_context(nc.sbuf_tensor(name, shape, dt))

    wqkv_sb = sb1("wqkv_sb", [128, 8 * 3072], BF16)
    wout_sb = sb1("wout_sb", [128, 8 * 1024], BF16)
    wg_sb = sb1("wg_sb", [128, 8 * NE], BF16)
    bqk = sb1("bqk", [128, 16], F32)
    bqks = sb1("bqks", [128, 16], F32)
    bv_row = sb1("bv_row", [1, 1024], BF16)
    bout_row = sb1("bout_row", [1, 1024], BF16)
    zrow = sb1("zrow", [128, 1024], BF16)
    xfp = [sb1(f"xfp{i}", [128, D], F32) for i in range(9)]
    bn12 = [sb1(f"bn12_{i}", [128, 12], F32) for i in range(2)]
    mv = [sb1(f"mv{i}", [128, 2], F32) for i in range(2)]
    rstd = [sb1(f"rstd{i}", [128, 1], F32) for i in range(2)]
    negms = [sb1(f"negms{i}", [128, 1], F32) for i in range(2)]
    xn_sb = [sb1(f"xn_sb{i}", [128, D], BF16) for i in range(2)]
    xnt = [sb1(f"xnt{i}", [128, 8 * 512], BF16) for i in range(2)]
    qf = sb1("qf", [128, 8 * 512], BF16)
    kf = sb1("kf", [128, 8 * 512], BF16)
    ef = [sb1(f"ef{i}", [128, 512], F32) for i in range(2)]
    rf = [sb1(f"rf{i}", [128, 512], F32) for i in range(2)]
    tf = [sb1(f"tf{i}", [128, 512], F32) for i in range(2)]
    vsb = [sb1(f"vsb{i}", [128, 16 * 65], BF16) for i in range(4)]
    ktok = [sb1(f"ktok{i}", [128, 1024], BF16) for i in range(4)]
    smt = [sb1(f"smt{i}", [128, 256], BF16) for i in range(2)]
    s0f = sb1("s0f", [128, 8 * 130], F32)
    s0b = sb1("s0b", [128, 8 * 130], BF16)
    dena = [sb1(f"dena{i}", [128, 1], F32) for i in range(4)]
    attn_sb = [sb1(f"attn_sb{i}", [128, 1024], BF16) for i in range(2)]
    attnt = [sb1(f"attnt{i}", [128, 1024], BF16) for i in range(2)]
    part_sb = [sb1(f"part_sb{i}", [128, D], F32) for i in range(2)]
    lgt = [sb1(f"lgt{i}", [128, 32], F32) for i in range(2)]
    mx8 = sb1("mx8", [128, NE], F32)
    idxu = sb1("idxu", [128, NE], U32)
    idxf = sb1("idxf", [128, NE], F32)
    p1.close()

    p2 = ExitStack()

    def sb2(name, shape, dt):
        return p2.enter_context(nc.sbuf_tensor(name, shape, dt))

    xnc = [sb2(f"xnc{i}", [128, D], BF16) for i in range(2)]
    w1_sb = sb2("w1_sb", [128, 8 * HID], BF16)
    w2_sb = sb2("w2_sb", [128, 32 * 1024], BF16)
    tok2 = [sb2(f"tok2_{i}", [128, D], BF16) for i in range(4)]
    xngt = [sb2(f"xngt{i}", [128, 8 * GROWS], BF16) for i in range(2)]
    ht_sb = sb2("ht_sb", [128, 32 * GROWS], BF16)
    y_sb = [sb2(f"y_sb{i}", [128, 1024], BF16) for i in range(2)]
    p2.close()

    p3 = ExitStack()

    def sb3(name, shape, dt):
        return p3.enter_context(nc.sbuf_tensor(name, shape, dt))

    ctot_row = sb3("ctot_row", [1, NCH * NE], BF16)
    ctot64 = sb3("ctot64", [NCH, NE], BF16)
    pfl64 = sb3("pfl64", [NCH, NE], F32)
    pfg64 = sb3("pfg64", [NCH, NE], F32)
    prefloc512 = sb3("prefloc512", [1, NCH * NE], F32)
    prefglob512 = sb3("prefglob512", [1, NCH * NE], F32)
    k8a = sb3("k8a", [128, NCH * NE], F32)
    k8b = sb3("k8b", [128, NCH * NE], F32)
    rp512 = sb3("rp512", [128, NCH * NE], F32)
    mk1 = sb3("mk1", [128, NCH * NE], F32)
    mk2 = sb3("mk2", [128, NCH * NE], F32)
    sc1 = sb3("sc1", [128, NCH], F32)
    sc2 = sb3("sc2", [128, NCH], F32)
    tc1 = sb3("tc1", [128, NCH], F32)
    tc2 = sb3("tc2", [128, NCH], F32)
    w1c = sb3("w1c", [128, NCH], F32)
    w2c = sb3("w2c", [128, NCH], F32)
    d1c = sb3("d1c", [128, NCH], F32)
    d2c = sb3("d2c", [128, NCH], F32)
    pt3 = [sb3(f"pt3_{i}", [128, D], F32) for i in range(4)]
    yg0 = [sb3(f"yg0_{i}", [128, D], BF16) for i in range(4)]
    yg1 = [sb3(f"yg1_{i}", [128, D], BF16) for i in range(4)]
    t3 = [sb3(f"t3_{i}", [128, D], F32) for i in range(4)]
    o3 = [sb3(f"o3_{i}", [128, D], F32) for i in range(4)]
    p3.close()

    pb = [psf(f"pb{i}") for i in range(8)]

    sems = {}
    slot_sems = {}
    for base, n in [("sLxA", 9), ("sTxnA", 2), ("sTkA", 4), ("sTatA", 2),
                    ("sPwA", 2), ("sXnwA", 2), ("sLxn2A", 2), ("sScatA", 2),
                    ("sRowA", 2), ("sTrsA", 2), ("sYwA", 2), ("sYgA", 4),
                    ("sLptA", 4), ("sOwA", 4)]:
        slot_sems[base] = [st.enter_context(nc.semaphore(f"{base}{i}")) for i in range(n)]
    sLxA = slot_sems["sLxA"]; sTxnA = slot_sems["sTxnA"]; sTkA = slot_sems["sTkA"]
    sTatA = slot_sems["sTatA"]; sPwA = slot_sems["sPwA"]; sXnwA = slot_sems["sXnwA"]
    sLxn2A = slot_sems["sLxn2A"]; sScatA = slot_sems["sScatA"]; sRowA = slot_sems["sRowA"]
    sTrsA = slot_sems["sTrsA"]; sYwA = slot_sems["sYwA"]; sYgA = slot_sems["sYgA"]
    sLptA = slot_sems["sLptA"]; sOwA = slot_sems["sOwA"]
    for name in ["sPreG", "sPreS", "sPreA", "sPreGm", "sMemB",
                 "sStat", "sBnA", "sRst",
                 "sQA", "sFeA", "sFeV", "sQB", "sVe", "sLgV",
                 "sS", "sSmV", "sNum", "sStV", "sDiv", "sPj", "sRes",
                 "sRout", "sKVp",
                 "sCntV1", "sCntP", "sCntV2", "sCCw", "sCC", "sCCl",
                 "sBase1", "sBase2",
                 "sCs", "sCsV", "sCtW", "sCt64", "sPfxL", "sPfxG", "sPfxV",
                 "sPfW", "sPref", "sRkA", "sRkB", "sDestV", "sGate",
                 "sW1", "sW2", "sH", "sGel", "sY", "sYe",
                 "sP3b", "sDbg", "sP1V", "sP1A", "sDivD", "sVeD", "sRcp"]:

        sems[name] = st.enter_context(nc.semaphore(name))
    S = type("S", (), sems)

    NPREG = 8 + 8 + 8 + 1 + 1
    NPRES = 11 + 16
    NPREA = NE * 32
    NMEMB = 1
    NTR_G = 32
    NGG = NE * NGRP

    def wl_after_w1(e):
        return 16 * (NPREG + e * 41 + 8)

    def wl_after_w2(e):
        return 16 * (NPREG + e * 41 + 41)

    with nc.Block() as block:

        @block.sync
        def _(sync):
            for dst, src in [(maskut, maskut_in), (urank, urank_in), (ones128, ones128_in),
                             (ones1x128f, ones1x128f_in), (iota8, iota8_in),
                             (eoff512, eoff512_in), (dmp0, dmp0_in), (dmp1, dmp1_in),
                             (iota128, iota128_in), (prevmask, prevmask_in),
                             (epsz, epsz_in)]:
                sync.dma_start(out=dst[:], in_=src).then_inc(S.sPreS, 16)
            for ct in range(16):
                sync.dma_start(out=bqk[:, ct:ct + 1],
                               in_=bqkv_in[ct * 128:(ct + 1) * 128].rearrange("(a b) -> a b", b=1)
                               ).then_inc(S.sPreS, 16)

            def _pstores(sc_):
                for cc_ in range(4):
                    c_ = sc_ * 4 + cc_
                    sync.wait_ge(S.sRes, c_ + 1)
                    sync.dma_start(out=part_scr[c_ * 128:(c_ + 1) * 128, :],
                                   in_=part_sb[c_ % 2][:]).then_inc(sPwA[c_ % 2], 16)
                    if stage < 3:
                        sync.dma_start(out=out_dram[c_ * 128:(c_ + 1) * 128, :],
                                       in_=part_sb[c_ % 2][:]).then_inc(sOwA[c_ % 2], 16)

            for sc in range(n_sc if sublvl >= 2 else 0):
                for cc in range(4):
                    c = sc * 4 + cc
                    if c >= 9 and sublvl >= 7:
                        sync.wait_ge(S.sRes, c - 8)
                    sync.dma_start(out=xfp[c % 9][:], in_=x_in[c * 128:(c + 1) * 128, :]
                                   ).then_inc(sLxA[c % 9], 16)
                for cc in range(4):
                    c = sc * 4 + cc
                    sync.wait_ge(S.sStat, c + 1)
                    if sc >= 2 and cc == 0:
                        sync.wait_ge(S.sQB, 4 * (sc - 1))
                    for k in range(8):
                        sync.dma_start(
                            out=xnt[sc % 2][:, k * 512 + cc * 128: k * 512 + cc * 128 + 128],
                            in_=xn_sb[c % 2][:, k * 128:(k + 1) * 128],
                            transpose=True).then_inc(sTxnA[c % 2], 16)
                    sync.dma_start(out=xn_scr[c * 128:(c + 1) * 128, :], in_=xn_sb[c % 2][:]
                                   ).then_inc(sXnwA[c % 2], 16)
                if sublvl >= 7 and sc >= 1:
                    _pstores(sc - 1)
                if sublvl >= 5:
                    sync.wait_ge(S.sFeV, sc * 16 + 16)
                    for cc in range(4):
                        c = sc * 4 + cc
                        if c >= 4:
                            sync.wait_ge(S.sKVp, 8 * (c - 3))
                        for p in range(8):
                            sync.dma_start(out=ktok[cc][:, p * 128:(p + 1) * 128],
                                           in_=kf[:, p * 512 + cc * 128: p * 512 + cc * 128 + 128],
                                           transpose=True).then_inc(sTkA[cc], 16)
            if sublvl >= 7 and n_sc >= 1:
                _pstores(n_sc - 1)

            for c in (range(NCH) if stage >= 2 else []):
                if c == 0:
                    sync.wait_ge(sXnwA[0], 16 * 32)
                    sync.wait_ge(sXnwA[1], 16 * 32)
                    sync.wait_ge(S.sRes, NCH)
                    sync.wait_ge(S.sP1V, 1)
                    sync.wait_ge(S.sP1A, 1)
                if c >= 2:
                    sync.wait_ge(sScatA[c % 2], 32 * ((c - 2) // 2 + 1))
                sync.dma_start(out=xnc[c % 2][:], in_=xn_scr[c * 128:(c + 1) * 128, :]
                               ).then_inc(sLxn2A[c % 2], 16)

            for e in (range(NE) if stage >= 3 else []):
                for g in range(NGRP):
                    gg = e * NGRP + g
                    r0 = e * CAP_PAD + g * GROWS
                    if gg == 0:
                        sync.wait_ge(sScatA[0], 32 * 32)
                        sync.wait_ge(sScatA[1], 32 * 32)
                        sync.wait_ge(S.sP1V, 1)
                        sync.wait_ge(S.sP1A, 1)
                        sync.wait_ge(S.sPreGm, 2)
                        sync.wait_ge(sPwA[0], 16 * 32)
                        sync.wait_ge(sPwA[1], 16 * 32)
                    if gg >= 1:
                        sync.wait_ge(sTrsA[(gg - 1) % 2], 16 * NTR_G * ((gg - 1) // 2 + 1))
                    for i in range(4):
                        sync.dma_start(out=tok2[i][:],
                                       in_=xn_buf[r0 + i * 128: r0 + (i + 1) * 128, :]
                                       ).then_inc(sRowA[gg % 2], 16)
                    sync.wait_ge(sRowA[gg % 2], 16 * 4 * (gg // 2 + 1))
                    if gg >= 2:
                        sync.wait_ge(S.sH, 32 * (gg - 1))
                    for i in range(4):
                        for k in range(8):
                            sync.dma_start(
                                out=xngt[gg % 2][:, k * GROWS + i * 128: k * GROWS + i * 128 + 128],
                                in_=tok2[i][:, k * 128:(k + 1) * 128],
                                transpose=True).then_inc(sTrsA[gg % 2], 16)
                    if gg >= 1:
                        pg = gg - 1
                        pe_, pgr = pg // NGRP, pg % NGRP
                        pr0 = pe_ * CAP_PAD + pgr * GROWS
                        for sub in range(4):
                            ss = pg * 4 + sub
                            sync.wait_ge(S.sYe, ss + 1)
                            sync.dma_start(out=y_buf[pr0 + sub * 128: pr0 + (sub + 1) * 128, :],
                                           in_=y_sb[sub % 2][:]).then_inc(sYwA[sub % 2], 16)
            if stage >= 2:
                sync.wait_ge(S.sDestV, 1)
                sync.wait_ge(S.sGate, 2)
                sync.wait_ge(S.sCCl, 16)
                sync.dma_start(out=dbg_dscat, in_=dscat[:]).then_inc(S.sDbg, 16)
                sync.dma_start(out=dbg_ygat, in_=ygat[:]).then_inc(S.sDbg, 16)
                sync.dma_start(out=dbg_g1k, in_=g1k[:]).then_inc(S.sDbg, 16)
                sync.dma_start(out=dbg_g2k, in_=g2k[:]).then_inc(S.sDbg, 16)
                sync.dma_start(out=dbg_allcnt, in_=allcnt[:]).then_inc(S.sDbg, 16)
                sync.dma_start(out=dbg_thr, in_=cntacc[:]).then_inc(S.sDbg, 16)
            if stage >= 3:
                pg = NGG - 1
                pe_, pgr = pg // NGRP, pg % NGRP
                pr0 = pe_ * CAP_PAD + pgr * GROWS
                for sub in range(4):
                    ss = pg * 4 + sub
                    sync.wait_ge(S.sYe, ss + 1)
                    sync.dma_start(out=y_buf[pr0 + sub * 128: pr0 + (sub + 1) * 128, :],
                                   in_=y_sb[sub % 2][:]).then_inc(sYwA[sub % 2], 16)

            for c in (range(NCH) if stage >= 3 else []):
                if c == 0:
                    sync.wait_ge(sPwA[0], 16 * 32)
                    sync.wait_ge(sPwA[1], 16 * 32)
                    sync.wait_ge(S.sYe, 4 * NGG)
                if c >= 4:
                    sync.wait_ge(S.sP3b, c - 3)
                sync.dma_start(out=pt3[c % 4][:], in_=part_scr[c * 128:(c + 1) * 128, :]
                               ).then_inc(sLptA[c % 4], 16)
                if c >= 3:
                    cs = c - 3
                    sync.wait_ge(S.sP3b, cs + 1)
                    sync.dma_start(out=out_dram[cs * 128:(cs + 1) * 128, :], in_=o3[cs % 4][:]
                                   ).then_inc(sOwA[cs % 4], 16)
            if stage >= 3:
                for cs in range(NCH - 3, NCH):
                    sync.wait_ge(S.sP3b, cs + 1)
                    sync.dma_start(out=out_dram[cs * 128:(cs + 1) * 128, :], in_=o3[cs % 4][:]
                                   ).then_inc(sOwA[cs % 4], 16)
            if sublvl >= 7:
                if stage >= 3:
                    for i in range(4):
                        sync.wait_ge(sOwA[i], 16 * (NCH // 4))
                else:
                    sync.wait_ge(sOwA[0], 16 * 2 * n_sc)
                    sync.wait_ge(sOwA[1], 16 * 2 * n_sc)
                if stage >= 2:
                    sync.wait_ge(S.sDbg, 16 * 6)
            else:
                sync.wait_ge(S.sPreGm, 1)
                if os.environ.get("KK_NOMEMSET", "0") != "1":
                    sync.wait_ge(S.sMemB, 16 * NMEMB)
                sync.wait_ge(S.sPreG, 16 * NPREG)
                sync.wait_ge(S.sPreA, 16 * NPREA)
                if sublvl >= 2:
                    sync.wait_ge(sXnwA[0], 16 * 2 * n_sc)
                    sync.wait_ge(sXnwA[1], 16 * 2 * n_sc)
                if sublvl == 21:
                    sync.wait_ge(S.sQA, int(os.environ.get("KK_CT", "16")) * n_sc)
                if sublvl == 22:
                    sync.wait_ge(S.sFeA, 16 * n_sc)
                if sublvl >= 3 and sublvl not in (21, 22):
                    sync.wait_ge(S.sFeV, 16 * n_sc)
                if sublvl >= 4:
                    sync.wait_ge(S.sVe, 4 * n_sc)
                    sync.wait_ge(S.sLgV, n_sc)
                if sublvl >= 5:
                    sync.wait_ge(S.sDiv, 32 * n_sc)
                if sublvl >= 6:
                    sync.wait_ge(S.sRout, 4 * n_sc)

        @block.gpsimd
        def _(gpsimd):
            for k in range(8):
                gpsimd.dma_start(out=wqkv_sb[:, k * 3072:(k + 1) * 3072],
                                 in_=wqkv_in[k * 128:(k + 1) * 128, :]).then_inc(S.sPreG, 16)
            for k in range(8):
                gpsimd.dma_start(out=wout_sb[:, k * 1024:(k + 1) * 1024],
                                 in_=wout_in[k * 128:(k + 1) * 128, :]).then_inc(S.sPreG, 16)
            for k in range(8):
                gpsimd.dma_start(out=wg_sb[:, k * NE:(k + 1) * NE],
                                 in_=wg_in[k * 128:(k + 1) * 128, :]).then_inc(S.sPreG, 16)
            gpsimd.dma_start(out=bv_row[:], in_=bqkv_in[2048:3072].rearrange("(a b) -> a b", a=1)
                             ).then_inc(S.sPreG, 16)
            gpsimd.dma_start(out=bout_row[:], in_=bout_in.rearrange("(a b) -> a b", a=1)
                             ).then_inc(S.sPreG, 16)
            gpsimd.memset(zrow[:], 0.0)
            gpsimd.memset(s0f[:], 0.0)
            gpsimd.memset(s0b[:], 0.0).then_inc(S.sPreGm, 1)
            gpsimd.wait_ge(S.sPreGm, 1)
            gpsimd.dma_start(out=y_buf[YZERO:YZERO + 1, :], in_=zrow[0:1, 0:1024]
                             ).then_inc(S.sMemB, 16)
            for e in range(NE):
                for ht in range(32):
                    gpsimd.dma_start(out=b1_sb[:, e * 32 + ht: e * 32 + ht + 1],
                                     in_=b1_in[e, ht * 128:(ht + 1) * 128].rearrange("(a b) -> a b", b=1)
                                     ).then_inc(S.sPreA, 16)

            if stage < 2:
                return
            gpsimd.wait_ge(S.sCntV2, 1)
            gpsimd.dma_start(out=cc_in.ap().rearrange("a b -> b a"), in_=cnt8[:]
                             ).then_inc(S.sCCw, 16)
            gpsimd.wait_ge(S.sCsV, 1)
            gpsimd.dma_start(out=ctot_dram.ap(), in_=ctot_row[:]).then_inc(S.sCtW, 16)
            gpsimd.wait_ge(S.sCtW, 16)
            gpsimd.dma_start(out=ctot64[:],
                             in_=ctot_dram.ap().rearrange("a (c e) -> (a c) e", e=NE)
                             ).then_inc(S.sCt64, 16)
            gpsimd.wait_ge(S.sCCw, 16)
            gpsimd.collective_compute(
                "AllGather", ALU.bypass,
                replica_groups=[list(range(NCORES))],
                ins=[cc_in.ap().opt()], outs=[cc_out.ap().opt()],
            ).then_inc(S.sCC)
            gpsimd.wait_ge(S.sCC, 1)
            gpsimd.dma_start(out=allcnt[:], in_=cc_out.ap()).then_inc(S.sCCl, 16)
            gpsimd.wait_ge(S.sPfxV, 1)
            gpsimd.dma_start(out=pfl_dram.ap(), in_=pfl64[:]).then_inc(S.sPfW, 16)
            gpsimd.dma_start(out=pfg_dram.ap(), in_=pfg64[:]).then_inc(S.sPfW, 16)
            gpsimd.wait_ge(S.sPfW, 32)
            gpsimd.dma_start(out=prefloc512[:],
                             in_=pfl_dram.ap().rearrange("(a c) e -> a (c e)", a=1)
                             ).then_inc(S.sPref, 16)
            gpsimd.dma_start(out=prefglob512[:],
                             in_=pfg_dram.ap().rearrange("(a c) e -> a (c e)", a=1)
                             ).then_inc(S.sPref, 16)

            gpsimd.wait_ge(S.sMemB, 16 * NMEMB)
            gpsimd.wait_ge(S.sDestV, 1)
            for c in range(NCH):
                gpsimd.wait_ge(sLxn2A[c % 2], 16 * (c // 2 + 1))
                for j in range(2):
                    gpsimd.indirect_dma_start(
                        out=xn_buf,
                        out_offset=IndirectOffsetOnAxis(ap=dscat[:, 2 * c + j: 2 * c + j + 1], axis=0),
                        in_=xnc[c % 2][:], in_offset=None,
                    ).then_inc(sScatA[c % 2], 16)

            for e in (range(NE) if stage >= 3 else []):
                if e == 0:
                    gpsimd.wait_ge(sScatA[0], 32 * 32)
                    gpsimd.wait_ge(sScatA[1], 32 * 32)
                    gpsimd.wait_ge(S.sP1V, 1)
                    gpsimd.wait_ge(S.sP1A, 1)
                    gpsimd.wait_ge(sPwA[0], 16 * 32)
                    gpsimd.wait_ge(sPwA[1], 16 * 32)
                    gpsimd.wait_ge(sXnwA[0], 16 * 32)
                    gpsimd.wait_ge(sXnwA[1], 16 * 32)
                    gpsimd.memset(xngt[0][:], 0.0)
                    gpsimd.memset(xngt[1][:], 0.0).then_inc(S.sPreGm, 1)
                else:
                    gpsimd.wait_ge(S.sH, 32 * NGRP * e)
                for k in range(8):
                    gpsimd.dma_start(out=w1_sb[:, k * HID:(k + 1) * HID],
                                     in_=w1_in[e, k * 128:(k + 1) * 128, :]).then_inc(S.sW1, 16)
                if e > 0:
                    gpsimd.wait_ge(S.sY, 4 * NGRP * e)
                for j in range(32):
                    gpsimd.dma_start(out=w2_sb[:, j * 1024:(j + 1) * 1024],
                                     in_=w2_in[e, j * 128:(j + 1) * 128, :]).then_inc(S.sW2, 16)
                gpsimd.dma_start(out=b2e[:],
                                 in_=b2_in[e, :].rearrange("(a b) -> a b", a=1)).then_inc(S.sW2, 16)

            if stage < 3:
                return
            gpsimd.wait_ge(sYwA[0], 16 * 2 * NGG)
            gpsimd.wait_ge(sYwA[1], 16 * 2 * NGG)
            for c in range(NCH):
                if c >= 4:
                    gpsimd.wait_ge(S.sP3b, c - 3)
                gpsimd.indirect_dma_start(
                    out=yg0[c % 4][:], out_offset=None, in_=y_buf,
                    in_offset=IndirectOffsetOnAxis(ap=ygat[:, 2 * c: 2 * c + 1], axis=0),
                ).then_inc(sYgA[c % 4], 16)
                gpsimd.indirect_dma_start(
                    out=yg1[c % 4][:], out_offset=None, in_=y_buf,
                    in_offset=IndirectOffsetOnAxis(ap=ygat[:, 2 * c + 1: 2 * c + 2], axis=0),
                ).then_inc(sYgA[c % 4], 16)

        @block.tensor
        def _(tensor):
            tensor.wait_ge(S.sPreG, 16 * NPREG)
            tensor.wait_ge(S.sPreS, 16 * NPRES)
            tensor.wait_ge(S.sPreGm, 1)

            for sc in range(n_sc if (sublvl >= 3 or sublvl in (21, 22)) else 0):
                tensor.wait_ge(sTxnA[0], 16 * 8 * 2 * (sc + 1))
                tensor.wait_ge(sTxnA[1], 16 * 8 * 2 * (sc + 1))
                _CT = int(os.environ.get("KK_CT", "16"))
                _K = int(os.environ.get("KK_K", "8"))
                for ct in range(_CT):
                    bank = ct % 2
                    gt = sc * 16 + ct
                    if gt >= 2 and sublvl != 21:
                        tensor.wait_ge(S.sFeA, gt - 1)
                    if ct < 2 and sc >= 1:
                        tensor.wait_ge(S.sSmV, 2 * 32 * sc)
                    _OPS = os.environ.get("KK_OPS", "orig")
                    for k in range(8):
                        i_ = tensor.matmul(pb[bank][:, 0:512],
                                           wqkv_sb[:, k * 3072 + ct * 128: k * 3072 + ct * 128 + 128],
                                           xnt[sc % 2][:, k * 512:(k + 1) * 512],
                                           start=(k == 0), stop=(k == 7))
                    i_.then_inc(S.sQA, 1)
                _SEQB = os.environ.get("KK_SEQB", "0") == "1"
                for cc in range(4 if sublvl >= 4 else 0):
                    c = sc * 4 + cc
                    if c >= 1:
                        tensor.wait_ge(S.sVe, c)
                    if sc >= 1 and cc == 0:
                        tensor.wait_ge(S.sRes, 4 * sc)
                        tensor.wait_ge(S.sLgV, sc)
                        tensor.wait_ge(S.sSmV, 2 * 32 * sc)
                        tensor.wait_ge(S.sStV, 32 * sc)
                    if _SEQB:
                        for cg in range(2):
                            for k in range(8):
                                lhs = xnt[sc % 2][:, k * 512 + cc * 128: k * 512 + cc * 128 + 128]
                                tensor.matmul(pb[2 + cg][:, 0:512], lhs,
                                              wqkv_sb[:, k * 3072 + 2048 + cg * 512: k * 3072 + 2048 + (cg + 1) * 512],
                                              start=(k == 0), stop=False)
                            tensor.matmul(pb[2 + cg][:, 0:512], ones128[0:1, :],
                                          bv_row[0:1, cg * 512:(cg + 1) * 512],
                                          start=False, stop=True)
                        for k in range(8):
                            lhs = xnt[sc % 2][:, k * 512 + cc * 128: k * 512 + cc * 128 + 128]
                            i_ = tensor.matmul(pb[4][:, cc * 8: cc * 8 + 8], lhs,
                                               wg_sb[:, k * NE:(k + 1) * NE],
                                               start=(cc == 0 and k == 0), stop=(cc == 3 and k == 7))
                        i_.then_inc(S.sQB, 1)
                    else:
                        for k in range(8):
                            lhs = xnt[sc % 2][:, k * 512 + cc * 128: k * 512 + cc * 128 + 128]
                            for cg in range(2):
                                tensor.matmul(pb[2 + cg][:, 0:512], lhs,
                                              wqkv_sb[:, k * 3072 + 2048 + cg * 512: k * 3072 + 2048 + (cg + 1) * 512],
                                              start=(k == 0), stop=False)
                            tensor.matmul(pb[4][:, cc * 8: cc * 8 + 8], lhs,
                                          wg_sb[:, k * NE:(k + 1) * NE],
                                          start=(cc == 0 and k == 0), stop=(cc == 3 and k == 7))
                        for cg in range(2):
                            i_ = tensor.matmul(pb[2 + cg][:, 0:512], ones128[0:1, :],
                                               bv_row[0:1, cg * 512:(cg + 1) * 512],
                                               start=False, stop=True)
                        i_.then_inc(S.sQB, 1)

                for cc in range(4 if sublvl >= 5 else 0):
                    c = sc * 4 + cc
                    pos = c % CPB
                    tensor.wait_ge(S.sFeV, sc * 16 + 16)
                    tensor.wait_ge(S.sVeD, c + 1)
                    tensor.wait_ge(sTkA[cc], 16 * 8 * (sc + 1))
                    if cc == 0:
                        tensor.wait_ge(S.sVe, 4 * sc + 4)
                        tensor.wait_ge(S.sLgV, sc + 1)
                    for p in range(8):
                        gp = c * 8 + p
                        sb0 = 0 if p % 2 == 0 else 2
                        if gp >= 2:
                            tensor.wait_ge(S.sSmV, 2 * (gp - 1))
                        colq = p * 512 + cc * 128
                        tensor.matmul(pb[sb0][:, 0:128], kf[0:64, colq:colq + 128],
                                      qf[0:64, colq:colq + 128], start=True, stop=True)
                        tensor.matmul(pb[sb0 + 1][:, 0:128], kf[64:128, colq:colq + 128],
                                      qf[64:128, colq:colq + 128], start=True, stop=True
                                      ).then_inc(S.sS, 1)
                        nbank = 6 + (p % 2)
                        if p == 0:
                            tensor.wait_ge(S.sRes, c)
                        if gp >= 2:
                            tensor.wait_ge(S.sDiv, gp - 1)
                        tensor.wait_ge(S.sSmV, 2 * (gp + 1))
                        if pos > 0:
                            tensor.wait_ge(S.sStV, (c - 1) * 8 + p + 1)
                        tensor.matmul(pb[nbank][:, 0:65], smt[p % 2][:, 0:128],
                                      vsb[cc][:, (2 * p) * 65:(2 * p) * 65 + 65],
                                      start=True, stop=False)
                        i_ = tensor.matmul(pb[nbank][:, 65:130], smt[p % 2][:, 128:256],
                                           vsb[cc][:, (2 * p + 1) * 65:(2 * p + 1) * 65 + 65],
                                           start=False, stop=(pos == 0))
                        if pos > 0:
                            i_ = tensor.matmul(pb[nbank][:, 0:130],
                                               qf[:, colq:colq + 128],
                                               s0b[:, p * 130:(p + 1) * 130],
                                               start=False, stop=True)
                        i_.then_inc(S.sNum, 1)
                        if os.environ.get("KK_ST7", "0") == "1":
                            if gp >= 1:
                                tensor.wait_ge(S.sStV, gp)
                            tensor.matmul(pb[7][:, 192:322],
                                          ktok[cc][:, p * 128:(p + 1) * 128],
                                          vsb[cc][:, (2 * p) * 65:(2 * p) * 65 + 130],
                                          start=True, stop=True).then_inc(S.sKVp, 1)
                        else:
                            kbank = 4 + (p % 2)
                            if gp >= 2:
                                tensor.wait_ge(S.sStV, gp - 1)
                            tensor.matmul(pb[kbank][:, 0:130],
                                          ktok[cc][:, p * 128:(p + 1) * 128],
                                          vsb[cc][:, (2 * p) * 65:(2 * p) * 65 + 130],
                                          start=True, stop=True).then_inc(S.sKVp, 1)
                    if sublvl < 7:
                        continue
                    tensor.wait_ge(sTatA[c % 2], 16 * 8 * (c // 2 + 1))
                    tensor.wait_ge(S.sDiv, 8 * (c + 1))
                    for k in range(8):
                        for cg in range(2):
                            tensor.matmul(pb[6 + cg][:, 0:512],
                                          attnt[c % 2][:, k * 128:(k + 1) * 128],
                                          wout_sb[:, k * 1024 + cg * 512: k * 1024 + (cg + 1) * 512],
                                          start=(k == 0), stop=False)
                    for cg in range(2):
                        i_ = tensor.matmul(pb[6 + cg][:, 0:512], ones128[0:1, :],
                                           bout_row[0:1, cg * 512:(cg + 1) * 512],
                                           start=False, stop=True)
                    i_.then_inc(S.sPj, 1)

            if stage < 2:
                return
            tensor.wait_ge(S.sRout, NCH)
            tensor.wait_ge(S.sCntV1, 1)
            tensor.wait_ge(S.sDiv, 8 * NCH)
            tensor.wait_ge(S.sRes, NCH)
            tensor.wait_ge(S.sLgV, NSC)
            tensor.wait_ge(S.sStV, 8 * NCH)
            tensor.wait_ge(sPwA[0], 16 * 32)
            tensor.wait_ge(sPwA[1], 16 * 32)
            tensor.wait_ge(sXnwA[0], 16 * 32)
            tensor.wait_ge(sXnwA[1], 16 * 32)
            tensor.matmul(pb[2][0:NE, 0:1], cntbf[:], ones128[:, 0:1], start=True, stop=True
                          ).then_inc(S.sCntP, 1)
            tensor.matmul(pb[6][0:1, 0:NCH * NE], ones128[:, 0:1], msk[:],
                          start=True, stop=True).then_inc(S.sCs, 1)
            tensor.wait_ge(S.sCCl, 16)
            tensor.matmul(pb[3][0:1, 0:NE], prevmask[:], allcnt[:], start=True, stop=True
                          ).then_inc(S.sBase1, 1)
            tensor.wait_ge(S.sCt64, 16)
            tensor.matmul(pb[4][0:NCH, 0:8], urank[0:NCH, 0:NCH], ctot64[:],
                          start=True, stop=True).then_inc(S.sPfxL, 1)
            tensor.wait_ge(S.sBase2, 1)
            tensor.matmul(pb[5][0:NCH, 0:8], urank[0:NCH, 0:NCH], ctot64[:],
                          start=True, stop=False)
            tensor.matmul(pb[5][0:NCH, 0:8], ones1x128f[0:1, 0:NCH], base_row[:],
                          start=False, stop=True).then_inc(S.sPfxG, 1)
            tensor.wait_ge(S.sPref, 32)
            tensor.matmul(pb[0][:, 0:NCH * NE], urank[:], msk[:], start=True, stop=False)
            tensor.matmul(pb[0][:, 0:NCH * NE], ones1x128f[:], prefloc512[:],
                          start=False, stop=True).then_inc(S.sRkA, 1)
            tensor.matmul(pb[1][:, 0:NCH * NE], urank[:], msk[:], start=True, stop=False)
            tensor.matmul(pb[1][:, 0:NCH * NE], ones1x128f[:], prefglob512[:],
                          start=False, stop=True).then_inc(S.sRkB, 1)

            if stage < 3:
                return
            tensor.wait_ge(S.sDestV, 1)
            tensor.wait_ge(S.sCntV2, 1)
            tensor.wait_ge(S.sBase2, 1)
            for e in range(NE):
                for g in range(NGRP):
                    gg = e * NGRP + g
                    tensor.wait_ge(sTrsA[gg % 2], 16 * NTR_G * (gg // 2 + 1))
                    if gg == 0:
                        tensor.wait_ge(S.sW1, 16 * 8)
                    for ht in range(32):
                        hh = gg * 32 + ht
                        bank = ht % 3
                        if hh >= 3:
                            tensor.wait_ge(S.sGel, hh - 2)
                        if g == 0 and ht == 0 and e > 0:
                            tensor.wait_ge(S.sW1, 16 * 8 * (e + 1))
                        for k in range(8):
                            i_ = tensor.matmul(pb[bank][:, 0:GROWS],
                                               w1_sb[:, k * HID + ht * 128: k * HID + ht * 128 + 128],
                                               xngt[gg % 2][:, k * GROWS:(k + 1) * GROWS],
                                               start=(k == 0), stop=(k == 7))
                        i_.then_inc(S.sH, 1)
                    tensor.wait_ge(S.sGel, 32 * (gg + 1))
                    if g == 0:
                        tensor.wait_ge(S.sW2, 16 * 33 * (e + 1))
                    for sub in range(4):
                        ss = gg * 4 + sub
                        ybk = 3 + (sub % 2) * 2
                        if ss >= 2:
                            tensor.wait_ge(S.sYe, ss - 1)
                        for ht in range(32):
                            for cg in range(2):
                                tensor.matmul(pb[ybk + cg][:, 0:512],
                                              ht_sb[:, ht * GROWS + sub * 128: ht * GROWS + sub * 128 + 128],
                                              w2_sb[:, ht * 1024 + cg * 512: ht * 1024 + (cg + 1) * 512],
                                              start=(ht == 0), stop=False)
                        for cg in range(2):
                            i_ = tensor.matmul(pb[ybk + cg][:, 0:512], ones128[0:1, :],
                                               b2e[0:1, cg * 512:(cg + 1) * 512],
                                               start=False, stop=True)
                        i_.then_inc(S.sY, 1)

        @block.vector
        def _(vector):
            vector.wait_ge(S.sPreS, 16 * NPRES)
            vector.wait_ge(S.sPreGm, 1)
            for i in range(4):
                vector.tensor_copy(
                    vsb[i][:].rearrange("p (h w) -> p h w", w=65)[:, :, 64:65],
                    ones128[:, 0:16].rearrange("p (h w) -> p h w", w=1))
            vector.tensor_scalar(bqks[:, 0:8], bqk[:, 0:8], QSCALE, None, ALU.mult)
            vector.tensor_scalar(bqks[:, 8:16], bqk[:, 8:16], 1.0, None, ALU.mult)
            vector.memset(cntacc[:], 0.0)
            vector.drain()

            def _stats(sc_):
                for cc_ in range(4):
                    c_ = sc_ * 4 + cc_
                    vector.wait_ge(sLxA[c_ % 9], 16 * (c_ // 9 + 1))
                    if c_ >= 2:
                        vector.wait_ge(S.sRst, c_ - 1)
                    vector.bn_stats(bn12[c_ % 2][:, 0:6], xfp[c_ % 9][:, 0:512])
                    vector.bn_stats(bn12[c_ % 2][:, 6:12], xfp[c_ % 9][:, 512:1024])
                    vector.drain()
                    vector.bn_aggr(mv[c_ % 2][:], bn12[c_ % 2][:]).then_inc(S.sBnA, 1)
                    vector.wait_ge(S.sRst, c_ + 1)
                    vector.tensor_scalar(negms[c_ % 2][:], mv[c_ % 2][:, 0:1], rstd[c_ % 2][:],
                                         -1.0, ALU.mult, ALU.mult)
                    vector.drain()
                    if c_ >= 2:
                        vector.wait_ge(sTxnA[c_ % 2], 16 * 8 * ((c_ - 2) // 2 + 1))
                        vector.wait_ge(sXnwA[c_ % 2], 16 * ((c_ - 2) // 2 + 1))
                    vector.tensor_scalar(xn_sb[c_ % 2][:], xfp[c_ % 9][:], rstd[c_ % 2][:],
                                         negms[c_ % 2][:], ALU.mult, ALU.add).then_inc(S.sStat, 1)

            if sublvl >= 2:
                _stats(0)
            for sc in range(n_sc if sublvl >= 2 else 0):
                for ct in range(16 if (sublvl >= 3 and sublvl != 21 and sublvl != 22) else 0):
                    gt = sc * 16 + ct
                    dst = qf if ct < 8 else kf
                    off = (ct % 8) * 512
                    if ct == 8 and sc >= 1 and sublvl >= 5:
                        for cc_ in range(4):
                            vector.wait_ge(sTkA[cc_], 16 * 8 * sc)
                    vector.wait_ge(S.sFeA, gt + 1)
                    vector.tensor_scalar(tf[ct % 2][:], ef[ct % 2][:], 1.0, 1e-4,
                                         ALU.min, ALU.add)
                    vector.drain()
                    vector.tensor_tensor(dst[:, off:off + 512], tf[ct % 2][:], rf[ct % 2][:],
                                         ALU.add).then_inc(S.sFeV, 1)
                if sublvl >= 4:
                    vector.wait_ge(S.sQB, 4 * (sc + 1))
                    vector.tensor_copy(lgt[sc % 2][:], pb[4][:, 0:32]).then_inc(S.sLgV, 1)
                    vector.drain()
                if sublvl >= 2 and sc + 1 < n_sc:
                    _stats(sc + 1)
                for cc in range(4 if sublvl >= 5 else 0):
                    c = sc * 4 + cc
                    pos = c % CPB
                    for p in range(8):
                        gp = c * 8 + p
                        sb0 = 0 if p % 2 == 0 else 2
                        vector.wait_ge(S.sS, gp + 1)
                        if gp >= 2:
                            vector.wait_ge(S.sNum, gp - 1)
                        vector.tensor_tensor(smt[p % 2][:, 0:128], pb[sb0][:, 0:128], maskut[:],
                                             ALU.mult).then_inc(S.sSmV, 1)
                        vector.tensor_tensor(smt[p % 2][:, 128:256], pb[sb0 + 1][:, 0:128], maskut[:],
                                             ALU.mult).then_inc(S.sSmV, 1)
                        vector.wait_ge(S.sKVp, gp + 1)
                        vector.wait_ge(S.sNum, gp + 1)
                        kbank = 4 + (p % 2)
                        kcol0 = 0
                        if os.environ.get("KK_ST7", "0") == "1":
                            kbank, kcol0 = 7, 192
                        if pos == 0:
                            vector.tensor_scalar(s0f[0:64, p * 130: p * 130 + 65],
                                                 pb[kbank][0:64, kcol0:kcol0 + 65],
                                                 1.0, None, ALU.mult)
                            vector.tensor_scalar(s0f[64:128, p * 130 + 65: p * 130 + 130],
                                                 pb[kbank][64:128, kcol0 + 65:kcol0 + 130],
                                                 1.0, None, ALU.mult)
                        else:
                            vector.tensor_tensor(s0f[0:64, p * 130: p * 130 + 65],
                                                 s0f[0:64, p * 130: p * 130 + 65],
                                                 pb[kbank][0:64, kcol0:kcol0 + 65], ALU.add)
                            vector.tensor_tensor(s0f[64:128, p * 130 + 65: p * 130 + 130],
                                                 s0f[64:128, p * 130 + 65: p * 130 + 130],
                                                 pb[kbank][64:128, kcol0 + 65:kcol0 + 130],
                                                 ALU.add)
                        vector.drain()
                        vector.tensor_copy(s0b[:, p * 130:(p + 1) * 130],
                                           s0f[:, p * 130:(p + 1) * 130]).then_inc(S.sStV, 1)
                        nbank = 6 + (p % 2)
                        if gp >= 2:
                            vector.wait_ge(S.sDiv, gp - 1)
                        vector.tensor_scalar(dena[2 * (p % 2)][:], pb[nbank][:, 64:65],
                                             EPS_DEN, None, ALU.add)
                        vector.tensor_scalar(dena[2 * (p % 2) + 1][:], pb[nbank][:, 129:130],
                                             EPS_DEN, None, ALU.add)
                        vector.drain()
                        vector.reciprocal(dena[2 * (p % 2)][:], dena[2 * (p % 2)][:]
                                          ).then_inc(S.sRcp, 1)
                        vector.reciprocal(dena[2 * (p % 2) + 1][:], dena[2 * (p % 2) + 1][:]
                                          ).then_inc(S.sRcp, 1)
                    if sublvl < 7:
                        continue
                    vector.wait_ge(S.sPj, c + 1)
                    if c >= 2:
                        vector.wait_ge(sPwA[c % 2], 16 * ((c - 2) // 2 + 1))
                        if stage < 3:
                            vector.wait_ge(sOwA[c % 2], 16 * ((c - 2) // 2 + 1))
                    vector.tensor_tensor(part_sb[c % 2][:, 0:512], pb[6][:, 0:512],
                                         xfp[c % 9][:, 0:512], ALU.add)
                    vector.tensor_tensor(part_sb[c % 2][:, 512:1024], pb[7][:, 0:512],
                                         xfp[c % 9][:, 512:1024], ALU.add)
                    vector.drain()
                    vector.memset(fence_v[:], 0.0).then_inc(S.sRes, 1)
                    if sublvl < 6:
                        continue
                    lgc = lgt[sc % 2][:, cc * 8: cc * 8 + 8]
                    vector.max(mx8[:], lgc)
                    vector.drain()
                    vector.max_index(idxu[:], mx8[:], lgc)
                    vector.drain()
                    vector.tensor_copy(idxf[:], idxu[:])
                    vector.tensor_tensor(dcol[:, c:c + 1], mx8[:, 0:1], mx8[:, 1:2], ALU.subtract)
                    vector.drain()
                    vector.tensor_scalar(m1s[:, c * NE:(c + 1) * NE], iota8[:], idxf[:, 0:1],
                                         None, ALU.is_equal)
                    vector.tensor_scalar(m2s[:, c * NE:(c + 1) * NE], iota8[:], idxf[:, 1:2],
                                         None, ALU.is_equal)
                    vector.drain()
                    vector.tensor_tensor(msk[:, c * NE:(c + 1) * NE], m1s[:, c * NE:(c + 1) * NE],
                                         m2s[:, c * NE:(c + 1) * NE], ALU.add)
                    vector.drain()
                    vector.tensor_tensor(cntacc[:], cntacc[:], msk[:, c * NE:(c + 1) * NE],
                                         ALU.add).then_inc(S.sRout, 1)
                    vector.drain()

            if stage < 2:
                return
            vector.tensor_copy(cntbf[:], cntacc[:]).then_inc(S.sCntV1, 1)
            vector.wait_ge(S.sCntP, 1)
            vector.tensor_copy(cnt8[:], pb[2][0:NE, 0:1]).then_inc(S.sCntV2, 1)
            vector.wait_ge(S.sCs, 1)
            vector.tensor_copy(ctot_row[:], pb[6][0:1, 0:NCH * NE]).then_inc(S.sCsV, 1)
            vector.wait_ge(S.sBase1, 1)
            vector.tensor_copy(base_row[:], pb[3][0:1, 0:NE]).then_inc(S.sBase2, 1)
            vector.wait_ge(S.sPfxL, 1)
            vector.tensor_copy(pfl64[:], pb[4][0:NCH, 0:8])
            vector.wait_ge(S.sPfxG, 1)
            vector.tensor_copy(pfg64[:], pb[5][0:NCH, 0:8]).then_inc(S.sPfxV, 1)
            vector.wait_ge(S.sRkA, 1)
            vector.wait_ge(S.sRkB, 1)
            vector.tensor_scalar(k8a[:], pb[0][:, 0:NCH * NE], float(CAP_PAD), None, ALU.is_lt)
            vector.tensor_scalar(k8b[:], pb[1][:, 0:NCH * NE], float(CAP), None, ALU.is_lt)
            vector.tensor_tensor(rp512[:], pb[0][:, 0:NCH * NE], eoff512[:], ALU.add)
            vector.drain()
            vector.tensor_tensor(k8a[:], k8a[:], k8b[:], ALU.mult)
            vector.drain()
            vector.tensor_tensor(mk1[:], m1s[:], k8a[:], ALU.mult)
            vector.tensor_tensor(mk2[:], m2s[:], k8a[:], ALU.mult)
            vector.drain()
            vector.tensor_tensor(k8a[:], mk1[:], rp512[:], ALU.mult)
            vector.tensor_tensor(k8b[:], mk2[:], rp512[:], ALU.mult)
            vector.drain()
            vector.reduce_sum(kk1[:].rearrange("p (c j) -> p c j", j=1),
                              mk1[:].rearrange("p (c e) -> p c e", e=NE), axis=AX.X)
            vector.reduce_sum(kk2[:].rearrange("p (c j) -> p c j", j=1),
                              mk2[:].rearrange("p (c e) -> p c e", e=NE), axis=AX.X)
            vector.reduce_sum(sc1[:].rearrange("p (c j) -> p c j", j=1),
                              k8a[:].rearrange("p (c e) -> p c e", e=NE), axis=AX.X)
            vector.reduce_sum(sc2[:].rearrange("p (c j) -> p c j", j=1),
                              k8b[:].rearrange("p (c e) -> p c e", e=NE), axis=AX.X)
            vector.drain()
            vector.tensor_scalar(tc1[:], kk1[:], -1.0, 1.0, ALU.mult, ALU.add)
            vector.tensor_scalar(tc2[:], kk2[:], -1.0, 1.0, ALU.mult, ALU.add)
            vector.drain()
            vector.tensor_tensor(w1c[:], tc1[:], dmp0[:], ALU.mult)
            vector.tensor_tensor(w2c[:], tc2[:], dmp1[:], ALU.mult)
            vector.drain()
            vector.tensor_tensor(d1c[:], sc1[:], w1c[:], ALU.add)
            vector.tensor_tensor(d2c[:], sc2[:], w2c[:], ALU.add)
            vector.drain()
            vector.tensor_copy(dscat[:].rearrange("p (c j) -> p c j", j=2)[:, :, 0:1],
                               d1c[:].rearrange("p (c j) -> p c j", j=1))
            vector.tensor_copy(dscat[:].rearrange("p (c j) -> p c j", j=2)[:, :, 1:2],
                               d2c[:].rearrange("p (c j) -> p c j", j=1))
            vector.tensor_scalar(w1c[:], tc1[:], float(YZERO), None, ALU.mult)
            vector.tensor_scalar(w2c[:], tc2[:], float(YZERO), None, ALU.mult)
            vector.drain()
            vector.tensor_tensor(d1c[:], sc1[:], w1c[:], ALU.add)
            vector.tensor_tensor(d2c[:], sc2[:], w2c[:], ALU.add)
            vector.drain()
            vector.tensor_copy(ygat[:].rearrange("p (c j) -> p c j", j=2)[:, :, 0:1],
                               d1c[:].rearrange("p (c j) -> p c j", j=1))
            vector.tensor_copy(ygat[:].rearrange("p (c j) -> p c j", j=2)[:, :, 1:2],
                               d2c[:].rearrange("p (c j) -> p c j", j=1)).then_inc(S.sDestV, 1)
            vector.drain()
            vector.wait_ge(S.sGate, 1)
            vector.tensor_tensor(g1k[:], g1s[:], kk1[:], ALU.mult)
            vector.tensor_scalar(g2tmp[:], g1s[:], -1.0, 1.0, ALU.mult, ALU.add)
            vector.drain()
            vector.tensor_tensor(g2k[:], g2tmp[:], kk2[:], ALU.mult).then_inc(S.sGate, 1)
            vector.drain()
            vector.memset(fence_v[:], 0.0).then_inc(S.sP1V, 1)

            for c in (range(NCH) if stage >= 3 else []):
                vector.wait_ge(sLptA[c % 4], 16 * (c // 4 + 1))
                vector.wait_ge(sYgA[c % 4], 32 * (c // 4 + 1))
                if c >= 4:
                    vector.wait_ge(sOwA[c % 4], 16 * (c // 4))
                vector.scalar_tensor_tensor(t3[c % 4][:], yg0[c % 4][:], g1k[:, c:c + 1],
                                            pt3[c % 4][:], ALU.mult, ALU.add)
                vector.drain()
                vector.scalar_tensor_tensor(o3[c % 4][:], yg1[c % 4][:], g2k[:, c:c + 1],
                                            t3[c % 4][:], ALU.mult, ALU.add).then_inc(S.sP3b, 1)

        @block.scalar
        def _(scalar):
            scalar.wait_ge(S.sPreS, 16 * NPRES)
            scalar.wait_ge(S.sPreG, 16 * NPREG)

            def _rstd(sc_):
                for cc_ in range(4):
                    c_ = sc_ * 4 + cc_
                    scalar.wait_ge(S.sBnA, c_ + 1)
                    if c_ >= 2:
                        scalar.wait_ge(S.sStat, c_ - 1)
                    scalar.activation(rstd[c_ % 2][:], mv[c_ % 2][:, 1:2], AF.Ln,
                                      bias=epsz[:, 0:1], scale=1.0)
                    scalar.drain()
                    scalar.activation(rstd[c_ % 2][:], rstd[c_ % 2][:], AF.Exp,
                                      bias=epsz[:, 1:2], scale=-0.5).then_inc(S.sRst, 1)

            if sublvl >= 2:
                _rstd(0)
            for sc in range(n_sc if sublvl >= 2 else 0):
                for ct in range(16 if (sublvl >= 3 or sublvl == 22) else 0):
                    gt = sc * 16 + ct
                    bank = ct % 2
                    scl = QSCALE if ct < 8 else 1.0
                    scalar.wait_ge(S.sQA, gt + 1)
                    if gt >= 2 and sublvl != 22:
                        scalar.wait_ge(S.sFeV, gt - 1)
                    scalar.activation(ef[ct % 2][:], pb[bank][:, 0:512], AF.Exp,
                                      bias=bqks[:, ct:ct + 1], scale=scl)
                    scalar.activation(rf[ct % 2][:], pb[bank][:, 0:512], AF.Relu,
                                      bias=bqks[:, ct:ct + 1], scale=scl).then_inc(S.sFeA, 1)
                for cc in range(4 if sublvl >= 4 else 0):
                    c = sc * 4 + cc
                    scalar.wait_ge(S.sQB, c + 1)
                    for h in range(16):
                        ins = pb[2 + h // 8][:, (h % 8) * 64:(h % 8) * 64 + 64]
                        i_ = scalar.activation(vsb[cc][:, h * 65: h * 65 + 64], ins, AF.Copy)
                    i_.then_inc(S.sVe, 1)
                    scalar.drain()
                    scalar.activation(fence_a[:], bqks[:, 0:1], AF.Copy).then_inc(S.sVeD, 1)
                if sublvl >= 2 and sc + 1 < n_sc:
                    _rstd(sc + 1)
                if sublvl >= 7:
                    for cc in range(4):
                        c = sc * 4 + cc
                        if c >= 2:
                            scalar.wait_ge(sTatA[c % 2], 16 * 8 * ((c - 2) // 2 + 1))
                        for p in range(8):
                            gp = c * 8 + p
                            nbank = 6 + (p % 2)
                            scalar.wait_ge(S.sRcp, 2 * (gp + 1))
                            scalar.activation(attn_sb[c % 2][:, (2 * p) * 64:(2 * p) * 64 + 64],
                                              pb[nbank][:, 0:64], AF.Copy,
                                              scale=dena[2 * (p % 2)][:])
                            scalar.activation(attn_sb[c % 2][:, (2 * p + 1) * 64:(2 * p + 1) * 64 + 64],
                                              pb[nbank][:, 65:129], AF.Copy,
                                              scale=dena[2 * (p % 2) + 1][:]).then_inc(S.sDiv, 1)
                        scalar.drain()
                        scalar.activation(fence_a[:], bqks[:, 0:1], AF.Copy).then_inc(S.sDivD, 1)
                        scalar.wait_ge(S.sDivD, c + 1)
                        if c >= 1:
                            scalar.wait_ge(S.sPj, c)
                        for k in range(8):
                            scalar.dma_start(out=attnt[c % 2][:, k * 128:(k + 1) * 128],
                                             in_=attn_sb[c % 2][:, k * 128:(k + 1) * 128],
                                             transpose=True).then_inc(sTatA[c % 2], 16)

            if stage < 2:
                return
            scalar.wait_ge(S.sRout, NCH)
            scalar.activation(g1s[:], dcol[:], AF.Sigmoid).then_inc(S.sGate, 1)
            scalar.drain()
            scalar.activation(fence_a[:], g1s[:, 0:1], AF.Copy).then_inc(S.sP1A, 1)

            if stage < 3:
                return
            scalar.wait_ge(S.sPreA, 16 * NPREA)
            for e in range(NE):
                for g in range(NGRP):
                    gg = e * NGRP + g
                    for ht in range(32):
                        hh = gg * 32 + ht
                        bank = ht % 3
                        scalar.wait_ge(S.sH, hh + 1)
                        scalar.activation(ht_sb[:, ht * GROWS:(ht + 1) * GROWS],
                                          pb[bank][:, 0:GROWS], AF.Gelu_apprx_tanh,
                                          bias=b1_sb[:, e * 32 + ht: e * 32 + ht + 1],
                                          scale=1.0).then_inc(S.sGel, 1)
                    for sub in range(4):
                        ss = gg * 4 + sub
                        ybk = 3 + (sub % 2) * 2
                        scalar.wait_ge(S.sY, ss + 1)
                        if ss >= 2:
                            scalar.wait_ge(sYwA[sub % 2], 16 * ((ss - 2) // 2 + 1))
                        scalar.activation(y_sb[sub % 2][:, 0:512], pb[ybk][:, 0:512], AF.Copy)
                        scalar.activation(y_sb[sub % 2][:, 512:1024], pb[ybk + 1][:, 0:512],
                                          AF.Copy).then_inc(S.sYe, 1)

    st.close()
    return nc


_NC_CACHE = {}


def _consts():
    i = np.arange(128)
    maskut = (i[:, None] <= i[None, :]).astype(ml_dtypes.bfloat16)
    urank = (i[:, None] < i[None, :]).astype(ml_dtypes.bfloat16)
    ones128 = np.ones((128, 128), ml_dtypes.bfloat16)
    ones1x128f = np.ones((1, 128), np.float32)
    iota8 = np.broadcast_to(np.arange(NE, dtype=np.float32), (128, NE)).copy()
    eoff = np.broadcast_to((np.arange(NE) * CAP_PAD).astype(np.float32), (128, NE)).copy()
    eoff512 = np.tile(eoff[0], (128, NCH)).astype(np.float32)
    cidx = np.arange(NCH, dtype=np.float32)[None, :]
    pidx = np.arange(128, dtype=np.float32)[:, None]
    dmp0 = (DUMP0 + 256.0 * cidx + 2.0 * pidx + 0.0).astype(np.float32)
    dmp1 = (DUMP0 + 256.0 * cidx + 2.0 * pidx + 1.0).astype(np.float32)
    iota128 = np.arange(128, dtype=np.float32).reshape(128, 1)
    epsz = np.zeros((128, 2), np.float32); epsz[:, 0] = EPS_LN
    return maskut, urank, ones128, ones1x128f, iota8, eoff, eoff512, dmp0, dmp1, iota128, epsz


def kernel(L0_attn_x, L0_attn_W_qkv, L0_attn_b_qkv, L0_attn_W_out, L0_attn_b_out,
           L0_ffn_Wg, L0_ffn_W1, L0_ffn_b1, L0_ffn_W2, L0_ffn_b2):
    xf = np.ascontiguousarray(np.asarray(L0_attn_x, np.float32).reshape(NTOT, D))
    (maskut, urank, ones128, ones1x128f, iota8, eoff, eoff512, dmp0, dmp1,
     iota128, epsz) = _consts()

    if "nc" not in _NC_CACHE:
        _NC_CACHE["nc"] = build_program(3, NSC, 7)
    nc = _NC_CACHE["nc"]

    shared = dict(
        wqkv=np.ascontiguousarray(np.asarray(L0_attn_W_qkv, np.float32)),
        bqkv=np.ascontiguousarray(np.asarray(L0_attn_b_qkv, np.float32)),
        wout=np.ascontiguousarray(np.asarray(L0_attn_W_out, np.float32)),
        bout=np.ascontiguousarray(np.asarray(L0_attn_b_out, np.float32)),
        wg=np.ascontiguousarray(np.asarray(L0_ffn_Wg, np.float32)),
        w1=np.ascontiguousarray(np.asarray(L0_ffn_W1, np.float32)),
        b1=np.ascontiguousarray(np.asarray(L0_ffn_b1, np.float32)),
        w2=np.ascontiguousarray(np.asarray(L0_ffn_W2, np.float32)),
        b2=np.ascontiguousarray(np.asarray(L0_ffn_b2, np.float32)),
        maskut=maskut, urank=urank, ones128=ones128, ones1x128f=ones1x128f,
        iota8=iota8, eoff=eoff, eoff512=eoff512, dmp0=dmp0, dmp1=dmp1,
        iota128=iota128, epsz=epsz,
    )
    in_maps = []
    for c in range(NCORES):
        m = dict(shared)
        m["x"] = np.ascontiguousarray(xf[c * TOK:(c + 1) * TOK])
        m["prevmask"] = (np.arange(NCORES) < c).astype(np.float32).reshape(NCORES, 1)
        in_maps.append(m)

    res = run_bass_kernel_spmd(nc, in_maps, core_ids=list(range(NCORES)))
    out = np.concatenate([res.results[c]["out"] for c in range(NCORES)], axis=0)
    return out.reshape(B, L, D)


if __name__ == "__main__":
    import time
    t0 = time.time()
    nc = build_program()
    print(f"build ok in {time.time()-t0:.1f}s; instructions: "
          f"{sum(1 for _ in nc.all_instructions())}")


# version 10 (final)
# speedup vs baseline: 1.0879x; 1.0082x over previous
import math
import os
import numpy as np
import ml_dtypes
from contextlib import ExitStack

import concourse.bass as bass
import concourse.mybir as mybir
from concourse.bass_utils import run_bass_kernel_spmd
from concourse.bass import IndirectOffsetOnAxis

F32 = mybir.dt.float32
BF16 = mybir.dt.bfloat16
U32 = mybir.dt.uint32
AF = mybir.ActivationFunctionType
ALU = mybir.AluOpType
AX = mybir.AxisListType

NCORES = 8
B, L, D = 16, 4096, 1024
NH, EH = 16, 64
TOK = 8192
NCH = 64
CPB = 32
NSC = 16
NE, HID, TOPK = 8, 4096, 2
NTOT = B * L
CAP = int(math.ceil(NTOT * TOPK / NE * 1.05))
CAP_PAD = 2560
GROWS = 512
NGRP = CAP_PAD // GROWS
NBUF = NE * CAP_PAD
DUMP0 = NBUF
XNROWS = NBUF + 2 * TOK
YZERO = NBUF
EPS_LN = 1e-5
EPS_DEN = 1e-6
QSCALE = EH ** (-0.5)


def build_program(stage=3, n_sc=NSC, sublvl=7):
    nc = bass.Bass("TRN2", target_bir_lowering=False, debug=False,
                   dynamic_dma_scratch_size=16384)

    x_in = nc.dram_tensor("x", [TOK, D], F32, kind="ExternalInput").ap()
    wqkv_in = nc.dram_tensor("wqkv", [D, 3 * D], F32, kind="ExternalInput").ap()
    bqkv_in = nc.dram_tensor("bqkv", [3 * D], F32, kind="ExternalInput").ap()
    wout_in = nc.dram_tensor("wout", [D, D], F32, kind="ExternalInput").ap()
    bout_in = nc.dram_tensor("bout", [D], F32, kind="ExternalInput").ap()
    wg_in = nc.dram_tensor("wg", [D, NE], F32, kind="ExternalInput").ap()
    w1_in = nc.dram_tensor("w1", [NE, D, HID], F32, kind="ExternalInput").ap()
    b1_in = nc.dram_tensor("b1", [NE, HID], F32, kind="ExternalInput").ap()
    w2_in = nc.dram_tensor("w2", [NE, HID, D], F32, kind="ExternalInput").ap()
    b2_in = nc.dram_tensor("b2", [NE, D], F32, kind="ExternalInput").ap()
    maskut_in = nc.dram_tensor("maskut", [128, 128], BF16, kind="ExternalInput").ap()
    urank_in = nc.dram_tensor("urank", [128, 128], BF16, kind="ExternalInput").ap()
    ones128_in = nc.dram_tensor("ones128", [128, 128], BF16, kind="ExternalInput").ap()
    ones1x128f_in = nc.dram_tensor("ones1x128f", [1, 128], F32, kind="ExternalInput").ap()
    iota8_in = nc.dram_tensor("iota8", [128, NE], F32, kind="ExternalInput").ap()
    eoff_in = nc.dram_tensor("eoff", [128, NE], F32, kind="ExternalInput").ap()
    eoff512_in = nc.dram_tensor("eoff512", [128, NCH * NE], F32, kind="ExternalInput").ap()
    dmp0_in = nc.dram_tensor("dmp0", [128, NCH], F32, kind="ExternalInput").ap()
    dmp1_in = nc.dram_tensor("dmp1", [128, NCH], F32, kind="ExternalInput").ap()
    iota128_in = nc.dram_tensor("iota128", [128, 1], F32, kind="ExternalInput").ap()
    prevmask_in = nc.dram_tensor("prevmask", [NCORES, 1], F32, kind="ExternalInput").ap()
    epsz_in = nc.dram_tensor("epsz", [128, 2], F32, kind="ExternalInput").ap()
    out_dram = nc.dram_tensor("out", [TOK, D], F32, kind="ExternalOutput").ap()
    dbg_dscat = nc.dram_tensor("dbg_dscat", [128, 2 * NCH], U32, kind="ExternalOutput").ap()
    dbg_ygat = nc.dram_tensor("dbg_ygat", [128, 2 * NCH], U32, kind="ExternalOutput").ap()
    dbg_g1k = nc.dram_tensor("dbg_g1k", [128, NCH], F32, kind="ExternalOutput").ap()
    dbg_g2k = nc.dram_tensor("dbg_g2k", [128, NCH], F32, kind="ExternalOutput").ap()
    dbg_allcnt = nc.dram_tensor("dbg_allcnt", [NCORES, NE], F32, kind="ExternalOutput").ap()
    dbg_thr = nc.dram_tensor("dbg_thr", [128, NE], F32, kind="ExternalOutput").ap()
    DBG3 = os.environ.get("KK_DBG3", "0") == "1"
    if DBG3:
        dbg_qf = nc.dram_tensor("dbg_qf", [128, 4096], BF16, kind="ExternalOutput").ap()
        dbg_kf = nc.dram_tensor("dbg_kf", [128, 4096], BF16, kind="ExternalOutput").ap()
        dbg_v = nc.dram_tensor("dbg_v", [128, 1040], BF16, kind="ExternalOutput").ap()
        dbg_xnt = nc.dram_tensor("dbg_xnt", [128, 4096], BF16, kind="ExternalOutput").ap()
        dbg_attn = nc.dram_tensor("dbg_attn", [128, 1024], BF16, kind="ExternalOutput").ap()
        dbg_s0 = nc.dram_tensor("dbg_s0", [128, 1040], F32, kind="ExternalOutput").ap()
        dbg_lgt = nc.dram_tensor("dbg_lgt", [128, 32], F32, kind="ExternalOutput").ap()

    xn_scr = nc.dram_tensor("xn_scr", [TOK, D], BF16).ap()
    xn_buf = nc.dram_tensor("xn_buf", [XNROWS, D], BF16).ap()
    y_buf = nc.dram_tensor("y_buf", [NBUF + 1, D], BF16).ap()
    part_scr = nc.dram_tensor("part_scr", [TOK, D], F32).ap()
    cc_in = nc.dram_tensor("cc_in", [1, NE], F32)
    cc_out = nc.dram_tensor("cc_out", [NCORES, NE], F32)
    ctot_dram = nc.dram_tensor("ctot_scr", [1, NCH * NE], BF16)
    pfl_dram = nc.dram_tensor("pfl_scr", [NCH, NE], F32)
    pfg_dram = nc.dram_tensor("pfg_scr", [NCH, NE], F32)

    st = ExitStack()

    def sb(name, shape, dt):
        return st.enter_context(nc.sbuf_tensor(name, shape, dt))

    def psf(name):
        return st.enter_context(nc.psum_tensor(name, [128, 512], F32))

    maskut = sb("maskut_s", [128, 128], BF16)
    urank = sb("urank_s", [128, 128], BF16)
    ones128 = sb("ones128b", [128, 128], BF16)
    ones1x128f = sb("ones1x128f_s", [1, 128], F32)
    iota8 = sb("iota8_s", [128, NE], F32)
    eoff = sb("eoff_s", [128, NE], F32)
    iota128 = sb("iota128_s", [128, 1], F32)
    prevmask = sb("prevmask_s", [NCORES, 1], F32)
    epsz = sb("epsz_s", [128, 2], F32)

    msk = sb("msk", [128, NCH * NE], BF16)
    m1s = sb("m1s", [128, NCH * NE], BF16)
    m2s = sb("m2s", [128, NCH * NE], BF16)
    dcol = sb("dcol", [128, NCH], F32)
    cntacc = sb("cntacc", [128, NE], F32)
    kk1 = sb("kk1", [128, NCH], F32)
    kk2 = sb("kk2", [128, NCH], F32)
    g1k = sb("g1k", [128, NCH], F32)
    g2k = sb("g2k", [128, NCH], F32)
    g1s = sb("g1s", [128, NCH], F32)
    g2tmp = sb("g2tmp", [128, NCH], F32)
    dscat = sb("dscat", [128, 2 * NCH], U32)
    ygat = sb("ygat", [128, 2 * NCH], U32)
    cnt8 = sb("cnt8", [NE, 1], F32)
    cntbf = sb("cntbf", [128, NE], BF16)
    allcnt = sb("allcnt", [NCORES, NE], F32)
    base_row = sb("base_row", [1, NE], F32)
    eoff512 = sb("eoff512_s", [128, NCH * NE], F32)
    dmp0 = sb("dmp0_s", [128, NCH], F32)
    dmp1 = sb("dmp1_s", [128, NCH], F32)
    b1_sb = sb("b1_sb", [128, NE * 32], F32)
    b2e = sb("b2e", [1, 1024], BF16)
    fence_v = sb("fence_v", [128, 1], F32)
    fence_a = sb("fence_a", [128, 1], F32)

    p1 = ExitStack()

    def sb1(name, shape, dt):
        return p1.enter_context(nc.sbuf_tensor(name, shape, dt))

    wqkv_sb = sb1("wqkv_sb", [128, 8 * 3072], BF16)
    wout_sb = sb1("wout_sb", [128, 8 * 1024], BF16)
    wg_sb = sb1("wg_sb", [128, 8 * NE], BF16)
    bqk = sb1("bqk", [128, 16], F32)
    bqks = sb1("bqks", [128, 16], F32)
    bv_row = sb1("bv_row", [1, 1024], BF16)
    bout_row = sb1("bout_row", [1, 1024], BF16)
    zrow = sb1("zrow", [128, 1024], BF16)
    xfp = [sb1(f"xfp{i}", [128, D], F32) for i in range(9)]
    bn12 = [sb1(f"bn12_{i}", [128, 12], F32) for i in range(2)]
    mv = [sb1(f"mv{i}", [128, 2], F32) for i in range(2)]
    rstd = [sb1(f"rstd{i}", [128, 1], F32) for i in range(2)]
    negms = [sb1(f"negms{i}", [128, 1], F32) for i in range(2)]
    xn_sb = [sb1(f"xn_sb{i}", [128, D], BF16) for i in range(2)]
    xnt = [sb1(f"xnt{i}", [128, 8 * 512], BF16) for i in range(2)]
    qf = sb1("qf", [128, 8 * 512], BF16)
    kf = sb1("kf", [128, 8 * 512], BF16)
    ef = [sb1(f"ef{i}", [128, 512], F32) for i in range(2)]
    rf = [sb1(f"rf{i}", [128, 512], F32) for i in range(2)]
    tf = [sb1(f"tf{i}", [128, 512], F32) for i in range(2)]
    vsb = [sb1(f"vsb{i}", [128, 16 * 65], BF16) for i in range(4)]
    ktok = [sb1(f"ktok{i}", [128, 1024], BF16) for i in range(4)]
    smt = [sb1(f"smt{i}", [128, 256], BF16) for i in range(2)]
    s0f = sb1("s0f", [128, 8 * 130], F32)
    s0b = sb1("s0b", [128, 8 * 130], BF16)
    dena = [sb1(f"dena{i}", [128, 1], F32) for i in range(4)]
    attn_sb = [sb1(f"attn_sb{i}", [128, 1024], BF16) for i in range(2)]
    attnt = [sb1(f"attnt{i}", [128, 1024], BF16) for i in range(2)]
    part_sb = [sb1(f"part_sb{i}", [128, D], F32) for i in range(2)]
    lgt = [sb1(f"lgt{i}", [128, 32], F32) for i in range(2)]
    mx8 = sb1("mx8", [128, NE], F32)
    idxu = sb1("idxu", [128, NE], U32)
    idxf = sb1("idxf", [128, NE], F32)
    p1.close()

    p2 = ExitStack()

    def sb2(name, shape, dt):
        return p2.enter_context(nc.sbuf_tensor(name, shape, dt))

    xnc = [sb2(f"xnc{i}", [128, D], BF16) for i in range(2)]
    w1_sb = sb2("w1_sb", [128, 8 * HID], BF16)
    w2_sb = sb2("w2_sb", [128, 32 * 1024], BF16)
    tok2 = [sb2(f"tok2_{i}", [128, D], BF16) for i in range(4)]
    xngt = [sb2(f"xngt{i}", [128, 8 * GROWS], BF16) for i in range(2)]
    ht_sb = sb2("ht_sb", [128, 32 * GROWS], BF16)
    y_sb = [sb2(f"y_sb{i}", [128, 1024], BF16) for i in range(2)]
    p2.close()

    p3 = ExitStack()

    def sb3(name, shape, dt):
        return p3.enter_context(nc.sbuf_tensor(name, shape, dt))

    ctot_row = sb3("ctot_row", [1, NCH * NE], BF16)
    ctot64 = sb3("ctot64", [NCH, NE], BF16)
    pfl64 = sb3("pfl64", [NCH, NE], F32)
    pfg64 = sb3("pfg64", [NCH, NE], F32)
    prefloc512 = sb3("prefloc512", [1, NCH * NE], F32)
    prefglob512 = sb3("prefglob512", [1, NCH * NE], F32)
    k8a = sb3("k8a", [128, NCH * NE], F32)
    k8b = sb3("k8b", [128, NCH * NE], F32)
    rp512 = sb3("rp512", [128, NCH * NE], F32)
    mk1 = sb3("mk1", [128, NCH * NE], F32)
    mk2 = sb3("mk2", [128, NCH * NE], F32)
    sc1 = sb3("sc1", [128, NCH], F32)
    sc2 = sb3("sc2", [128, NCH], F32)
    tc1 = sb3("tc1", [128, NCH], F32)
    tc2 = sb3("tc2", [128, NCH], F32)
    w1c = sb3("w1c", [128, NCH], F32)
    w2c = sb3("w2c", [128, NCH], F32)
    d1c = sb3("d1c", [128, NCH], F32)
    d2c = sb3("d2c", [128, NCH], F32)
    pt3 = [sb3(f"pt3_{i}", [128, D], F32) for i in range(4)]
    yg0 = [sb3(f"yg0_{i}", [128, D], BF16) for i in range(4)]
    yg1 = [sb3(f"yg1_{i}", [128, D], BF16) for i in range(4)]
    t3 = [sb3(f"t3_{i}", [128, D], F32) for i in range(4)]
    o3 = [sb3(f"o3_{i}", [128, D], F32) for i in range(4)]
    p3.close()

    pb = [psf(f"pb{i}") for i in range(8)]

    sems = {}
    slot_sems = {}
    for base, n in [("sLxA", 9), ("sTxnA", 2), ("sTkA", 4), ("sTatA", 2),
                    ("sPwA", 2), ("sXnwA", 2), ("sLxn2A", 2), ("sScatA", 2),
                    ("sRowA", 2), ("sTrsA", 2), ("sYwA", 2), ("sYgA", 4),
                    ("sLptA", 4), ("sOwA", 4)]:
        slot_sems[base] = [st.enter_context(nc.semaphore(f"{base}{i}")) for i in range(n)]
    sLxA = slot_sems["sLxA"]; sTxnA = slot_sems["sTxnA"]; sTkA = slot_sems["sTkA"]
    sTatA = slot_sems["sTatA"]; sPwA = slot_sems["sPwA"]; sXnwA = slot_sems["sXnwA"]
    sLxn2A = slot_sems["sLxn2A"]; sScatA = slot_sems["sScatA"]; sRowA = slot_sems["sRowA"]
    sTrsA = slot_sems["sTrsA"]; sYwA = slot_sems["sYwA"]; sYgA = slot_sems["sYgA"]
    sLptA = slot_sems["sLptA"]; sOwA = slot_sems["sOwA"]
    for name in ["sPreG", "sPreS", "sPreA", "sPreGm", "sMemB",
                 "sStat", "sBnA", "sRst",
                 "sQA", "sFeA", "sFeV", "sQB", "sVe", "sLgV",
                 "sS", "sSmV", "sNum", "sStV", "sDiv", "sPj", "sRes",
                 "sRout", "sKVp",
                 "sCntV1", "sCntP", "sCntV2", "sCCw", "sCC", "sCCl",
                 "sBase1", "sBase2",
                 "sCs", "sCsV", "sCtW", "sCt64", "sPfxL", "sPfxG", "sPfxV",
                 "sPfW", "sPref", "sRkA", "sRkB", "sDestV", "sGate",
                 "sW1", "sW2", "sH", "sGel", "sY", "sYe",
                 "sP3b", "sDbg", "sP1V", "sP1A", "sDivD", "sVeD", "sRcp"]:

        sems[name] = st.enter_context(nc.semaphore(name))
    S = type("S", (), sems)

    NPREG = 8 + 8 + 8 + 1 + 1
    NPRES = 11 + 16
    NPREA = NE * 32
    NMEMB = NBUF // 128 + 1
    NTR_G = 32
    NGG = NE * NGRP

    def wl_after_w1(e):
        return 16 * (NPREG + e * 41 + 8)

    def wl_after_w2(e):
        return 16 * (NPREG + e * 41 + 41)

    with nc.Block() as block:

        @block.sync
        def _(sync):
            for dst, src in [(maskut, maskut_in), (urank, urank_in), (ones128, ones128_in),
                             (ones1x128f, ones1x128f_in), (iota8, iota8_in),
                             (eoff512, eoff512_in), (dmp0, dmp0_in), (dmp1, dmp1_in),
                             (iota128, iota128_in), (prevmask, prevmask_in),
                             (epsz, epsz_in)]:
                sync.dma_start(out=dst[:], in_=src).then_inc(S.sPreS, 16)
            for ct in range(16):
                sync.dma_start(out=bqk[:, ct:ct + 1],
                               in_=bqkv_in[ct * 128:(ct + 1) * 128].rearrange("(a b) -> a b", b=1)
                               ).then_inc(S.sPreS, 16)

            def _pstores(sc_):
                for cc_ in range(4):
                    c_ = sc_ * 4 + cc_
                    sync.wait_ge(S.sRes, c_ + 1)
                    sync.dma_start(out=part_scr[c_ * 128:(c_ + 1) * 128, :],
                                   in_=part_sb[c_ % 2][:]).then_inc(sPwA[c_ % 2], 16)
                    if stage < 3:
                        sync.dma_start(out=out_dram[c_ * 128:(c_ + 1) * 128, :],
                                       in_=part_sb[c_ % 2][:]).then_inc(sOwA[c_ % 2], 16)

            for sc in range(n_sc if sublvl >= 2 else 0):
                for cc in range(4):
                    c = sc * 4 + cc
                    if c >= 9 and sublvl >= 7:
                        sync.wait_ge(S.sRes, c - 8)
                    sync.dma_start(out=xfp[c % 9][:], in_=x_in[c * 128:(c + 1) * 128, :]
                                   ).then_inc(sLxA[c % 9], 16)
                for cc in range(4):
                    c = sc * 4 + cc
                    sync.wait_ge(S.sStat, c + 1)
                    if sc >= 2 and cc == 0:
                        sync.wait_ge(S.sQB, 4 * (sc - 1))
                    for k in range(8):
                        sync.dma_start(
                            out=xnt[sc % 2][:, k * 512 + cc * 128: k * 512 + cc * 128 + 128],
                            in_=xn_sb[c % 2][:, k * 128:(k + 1) * 128],
                            transpose=True).then_inc(sTxnA[c % 2], 16)
                    sync.dma_start(out=xn_scr[c * 128:(c + 1) * 128, :], in_=xn_sb[c % 2][:]
                                   ).then_inc(sXnwA[c % 2], 16)
                if sublvl >= 7 and sc >= 1:
                    _pstores(sc - 1)
                if sublvl >= 5:
                    sync.wait_ge(S.sFeV, sc * 16 + 16)
                    for cc in range(4):
                        c = sc * 4 + cc
                        if c >= 4:
                            sync.wait_ge(S.sKVp, 8 * (c - 3))
                        for p in range(8):
                            sync.dma_start(out=ktok[cc][:, p * 128:(p + 1) * 128],
                                           in_=kf[:, p * 512 + cc * 128: p * 512 + cc * 128 + 128],
                                           transpose=True).then_inc(sTkA[cc], 16)
            if sublvl >= 7 and n_sc >= 1:
                _pstores(n_sc - 1)

            for c in (range(NCH) if stage >= 2 else []):
                if c == 0:
                    sync.wait_ge(sXnwA[0], 16 * 32)
                    sync.wait_ge(sXnwA[1], 16 * 32)
                    sync.wait_ge(S.sRes, NCH)
                    sync.wait_ge(S.sP1V, 1)
                    sync.wait_ge(S.sP1A, 1)
                if c >= 2:
                    sync.wait_ge(sScatA[c % 2], 32 * ((c - 2) // 2 + 1))
                sync.dma_start(out=xnc[c % 2][:], in_=xn_scr[c * 128:(c + 1) * 128, :]
                               ).then_inc(sLxn2A[c % 2], 16)

            for e in (range(NE) if stage >= 3 else []):
                for g in range(NGRP):
                    gg = e * NGRP + g
                    r0 = e * CAP_PAD + g * GROWS
                    if gg == 0:
                        sync.wait_ge(sScatA[0], 32 * 32)
                        sync.wait_ge(sScatA[1], 32 * 32)
                        sync.wait_ge(S.sP1V, 1)
                        sync.wait_ge(S.sP1A, 1)
                        sync.wait_ge(S.sPreGm, 2)
                        sync.wait_ge(sPwA[0], 16 * 32)
                        sync.wait_ge(sPwA[1], 16 * 32)
                    if gg >= 1:
                        sync.wait_ge(sTrsA[(gg - 1) % 2], 16 * NTR_G * ((gg - 1) // 2 + 1))
                    for i in range(4):
                        sync.dma_start(out=tok2[i][:],
                                       in_=xn_buf[r0 + i * 128: r0 + (i + 1) * 128, :]
                                       ).then_inc(sRowA[gg % 2], 16)
                    sync.wait_ge(sRowA[gg % 2], 16 * 4 * (gg // 2 + 1))
                    if gg >= 2:
                        sync.wait_ge(S.sH, 32 * (gg - 1))
                    for i in range(4):
                        for k in range(8):
                            sync.dma_start(
                                out=xngt[gg % 2][:, k * GROWS + i * 128: k * GROWS + i * 128 + 128],
                                in_=tok2[i][:, k * 128:(k + 1) * 128],
                                transpose=True).then_inc(sTrsA[gg % 2], 16)
                    if gg >= 1:
                        pg = gg - 1
                        pe_, pgr = pg // NGRP, pg % NGRP
                        pr0 = pe_ * CAP_PAD + pgr * GROWS
                        for sub in range(4):
                            ss = pg * 4 + sub
                            sync.wait_ge(S.sYe, ss + 1)
                            sync.dma_start(out=y_buf[pr0 + sub * 128: pr0 + (sub + 1) * 128, :],
                                           in_=y_sb[sub % 2][:]).then_inc(sYwA[sub % 2], 16)
            if stage >= 2:
                sync.wait_ge(S.sDestV, 1)
                sync.wait_ge(S.sGate, 2)
                sync.wait_ge(S.sCCl, 16)
                sync.dma_start(out=dbg_dscat, in_=dscat[:]).then_inc(S.sDbg, 16)
                sync.dma_start(out=dbg_ygat, in_=ygat[:]).then_inc(S.sDbg, 16)
                sync.dma_start(out=dbg_g1k, in_=g1k[:]).then_inc(S.sDbg, 16)
                sync.dma_start(out=dbg_g2k, in_=g2k[:]).then_inc(S.sDbg, 16)
                sync.dma_start(out=dbg_allcnt, in_=allcnt[:]).then_inc(S.sDbg, 16)
                sync.dma_start(out=dbg_thr, in_=cntacc[:]).then_inc(S.sDbg, 16)
            if stage >= 3:
                pg = NGG - 1
                pe_, pgr = pg // NGRP, pg % NGRP
                pr0 = pe_ * CAP_PAD + pgr * GROWS
                for sub in range(4):
                    ss = pg * 4 + sub
                    sync.wait_ge(S.sYe, ss + 1)
                    sync.dma_start(out=y_buf[pr0 + sub * 128: pr0 + (sub + 1) * 128, :],
                                   in_=y_sb[sub % 2][:]).then_inc(sYwA[sub % 2], 16)

            for c in (range(NCH) if stage >= 3 else []):
                if c == 0:
                    sync.wait_ge(sPwA[0], 16 * 32)
                    sync.wait_ge(sPwA[1], 16 * 32)
                    sync.wait_ge(S.sYe, 4 * NGG)
                if c >= 4:
                    sync.wait_ge(S.sP3b, c - 3)
                sync.dma_start(out=pt3[c % 4][:], in_=part_scr[c * 128:(c + 1) * 128, :]
                               ).then_inc(sLptA[c % 4], 16)
                if c >= 3:
                    cs = c - 3
                    sync.wait_ge(S.sP3b, cs + 1)
                    sync.dma_start(out=out_dram[cs * 128:(cs + 1) * 128, :], in_=o3[cs % 4][:]
                                   ).then_inc(sOwA[cs % 4], 16)
            if stage >= 3:
                for cs in range(NCH - 3, NCH):
                    sync.wait_ge(S.sP3b, cs + 1)
                    sync.dma_start(out=out_dram[cs * 128:(cs + 1) * 128, :], in_=o3[cs % 4][:]
                                   ).then_inc(sOwA[cs % 4], 16)
            if sublvl >= 7:
                if stage >= 3:
                    for i in range(4):
                        sync.wait_ge(sOwA[i], 16 * (NCH // 4))
                else:
                    sync.wait_ge(sOwA[0], 16 * 2 * n_sc)
                    sync.wait_ge(sOwA[1], 16 * 2 * n_sc)
                if stage >= 2:
                    sync.wait_ge(S.sDbg, 16 * 6)
            else:
                sync.wait_ge(S.sPreGm, 1)
                if os.environ.get("KK_NOMEMSET", "0") != "1":
                    sync.wait_ge(S.sMemB, 16 * NMEMB)
                sync.wait_ge(S.sPreG, 16 * NPREG)
                sync.wait_ge(S.sPreA, 16 * NPREA)
                if sublvl >= 2:
                    sync.wait_ge(sXnwA[0], 16 * 2 * n_sc)
                    sync.wait_ge(sXnwA[1], 16 * 2 * n_sc)
                if sublvl == 21:
                    sync.wait_ge(S.sQA, int(os.environ.get("KK_CT", "16")) * n_sc)
                if sublvl == 22:
                    sync.wait_ge(S.sFeA, 16 * n_sc)
                if sublvl >= 3 and sublvl not in (21, 22):
                    sync.wait_ge(S.sFeV, 16 * n_sc)
                if sublvl >= 4:
                    sync.wait_ge(S.sVe, 4 * n_sc)
                    sync.wait_ge(S.sLgV, n_sc)
                if sublvl >= 5:
                    sync.wait_ge(S.sDiv, 32 * n_sc)
                if sublvl >= 6:
                    sync.wait_ge(S.sRout, 4 * n_sc)

        @block.gpsimd
        def _(gpsimd):
            for k in range(8):
                gpsimd.dma_start(out=wqkv_sb[:, k * 3072:(k + 1) * 3072],
                                 in_=wqkv_in[k * 128:(k + 1) * 128, :]).then_inc(S.sPreG, 16)
            for k in range(8):
                gpsimd.dma_start(out=wout_sb[:, k * 1024:(k + 1) * 1024],
                                 in_=wout_in[k * 128:(k + 1) * 128, :]).then_inc(S.sPreG, 16)
            for k in range(8):
                gpsimd.dma_start(out=wg_sb[:, k * NE:(k + 1) * NE],
                                 in_=wg_in[k * 128:(k + 1) * 128, :]).then_inc(S.sPreG, 16)
            gpsimd.dma_start(out=bv_row[:], in_=bqkv_in[2048:3072].rearrange("(a b) -> a b", a=1)
                             ).then_inc(S.sPreG, 16)
            gpsimd.dma_start(out=bout_row[:], in_=bout_in.rearrange("(a b) -> a b", a=1)
                             ).then_inc(S.sPreG, 16)
            gpsimd.memset(zrow[:], 0.0)
            gpsimd.memset(s0f[:], 0.0)
            gpsimd.memset(s0b[:], 0.0).then_inc(S.sPreGm, 1)
            gpsimd.wait_ge(S.sPreGm, 1)
            for i in range(NBUF // 128):
                gpsimd.dma_start(out=xn_buf[i * 128:(i + 1) * 128, :], in_=zrow[:]
                                 ).then_inc(S.sMemB, 16)
            gpsimd.dma_start(out=y_buf[YZERO:YZERO + 1, :], in_=zrow[0:1, 0:1024]
                             ).then_inc(S.sMemB, 16)
            for e in range(NE):
                for ht in range(32):
                    gpsimd.dma_start(out=b1_sb[:, e * 32 + ht: e * 32 + ht + 1],
                                     in_=b1_in[e, ht * 128:(ht + 1) * 128].rearrange("(a b) -> a b", b=1)
                                     ).then_inc(S.sPreA, 16)

            if stage < 2:
                return
            gpsimd.wait_ge(S.sCntV2, 1)
            gpsimd.dma_start(out=cc_in.ap().rearrange("a b -> b a"), in_=cnt8[:]
                             ).then_inc(S.sCCw, 16)
            gpsimd.wait_ge(S.sCsV, 1)
            gpsimd.dma_start(out=ctot_dram.ap(), in_=ctot_row[:]).then_inc(S.sCtW, 16)
            gpsimd.wait_ge(S.sCtW, 16)
            gpsimd.dma_start(out=ctot64[:],
                             in_=ctot_dram.ap().rearrange("a (c e) -> (a c) e", e=NE)
                             ).then_inc(S.sCt64, 16)
            gpsimd.wait_ge(S.sCCw, 16)
            gpsimd.collective_compute(
                "AllGather", ALU.bypass,
                replica_groups=[list(range(NCORES))],
                ins=[cc_in.ap().opt()], outs=[cc_out.ap().opt()],
            ).then_inc(S.sCC)
            gpsimd.wait_ge(S.sCC, 1)
            gpsimd.dma_start(out=allcnt[:], in_=cc_out.ap()).then_inc(S.sCCl, 16)
            gpsimd.wait_ge(S.sPfxV, 1)
            gpsimd.dma_start(out=pfl_dram.ap(), in_=pfl64[:]).then_inc(S.sPfW, 16)
            gpsimd.dma_start(out=pfg_dram.ap(), in_=pfg64[:]).then_inc(S.sPfW, 16)
            gpsimd.wait_ge(S.sPfW, 32)
            gpsimd.dma_start(out=prefloc512[:],
                             in_=pfl_dram.ap().rearrange("(a c) e -> a (c e)", a=1)
                             ).then_inc(S.sPref, 16)
            gpsimd.dma_start(out=prefglob512[:],
                             in_=pfg_dram.ap().rearrange("(a c) e -> a (c e)", a=1)
                             ).then_inc(S.sPref, 16)

            gpsimd.wait_ge(S.sMemB, 16 * NMEMB)
            gpsimd.wait_ge(S.sDestV, 1)
            for c in range(NCH):
                gpsimd.wait_ge(sLxn2A[c % 2], 16 * (c // 2 + 1))
                for j in range(2):
                    gpsimd.indirect_dma_start(
                        out=xn_buf,
                        out_offset=IndirectOffsetOnAxis(ap=dscat[:, 2 * c + j: 2 * c + j + 1], axis=0),
                        in_=xnc[c % 2][:], in_offset=None,
                    ).then_inc(sScatA[c % 2], 16)

            for e in (range(NE) if stage >= 3 else []):
                if e == 0:
                    gpsimd.wait_ge(sScatA[0], 32 * 32)
                    gpsimd.wait_ge(sScatA[1], 32 * 32)
                    gpsimd.wait_ge(S.sP1V, 1)
                    gpsimd.wait_ge(S.sP1A, 1)
                    gpsimd.wait_ge(sPwA[0], 16 * 32)
                    gpsimd.wait_ge(sPwA[1], 16 * 32)
                    gpsimd.wait_ge(sXnwA[0], 16 * 32)
                    gpsimd.wait_ge(sXnwA[1], 16 * 32)
                    gpsimd.memset(xngt[0][:], 0.0)
                    gpsimd.memset(xngt[1][:], 0.0).then_inc(S.sPreGm, 1)
                else:
                    gpsimd.wait_ge(S.sH, 32 * NGRP * e)
                for k in range(8):
                    gpsimd.dma_start(out=w1_sb[:, k * HID:(k + 1) * HID],
                                     in_=w1_in[e, k * 128:(k + 1) * 128, :]).then_inc(S.sW1, 16)
                if e > 0:
                    gpsimd.wait_ge(S.sY, 4 * NGRP * e)
                for j in range(32):
                    gpsimd.dma_start(out=w2_sb[:, j * 1024:(j + 1) * 1024],
                                     in_=w2_in[e, j * 128:(j + 1) * 128, :]).then_inc(S.sW2, 16)
                gpsimd.dma_start(out=b2e[:],
                                 in_=b2_in[e, :].rearrange("(a b) -> a b", a=1)).then_inc(S.sW2, 16)

            if stage < 3:
                return
            gpsimd.wait_ge(sYwA[0], 16 * 2 * NGG)
            gpsimd.wait_ge(sYwA[1], 16 * 2 * NGG)
            for c in range(NCH):
                if c >= 4:
                    gpsimd.wait_ge(S.sP3b, c - 3)
                gpsimd.indirect_dma_start(
                    out=yg0[c % 4][:], out_offset=None, in_=y_buf,
                    in_offset=IndirectOffsetOnAxis(ap=ygat[:, 2 * c: 2 * c + 1], axis=0),
                ).then_inc(sYgA[c % 4], 16)
                gpsimd.indirect_dma_start(
                    out=yg1[c % 4][:], out_offset=None, in_=y_buf,
                    in_offset=IndirectOffsetOnAxis(ap=ygat[:, 2 * c + 1: 2 * c + 2], axis=0),
                ).then_inc(sYgA[c % 4], 16)

        @block.tensor
        def _(tensor):
            tensor.wait_ge(S.sPreG, 16 * NPREG)
            tensor.wait_ge(S.sPreS, 16 * NPRES)
            tensor.wait_ge(S.sPreGm, 1)

            for sc in range(n_sc if (sublvl >= 3 or sublvl in (21, 22)) else 0):
                tensor.wait_ge(sTxnA[0], 16 * 8 * 2 * (sc + 1))
                tensor.wait_ge(sTxnA[1], 16 * 8 * 2 * (sc + 1))
                _CT = int(os.environ.get("KK_CT", "16"))
                _K = int(os.environ.get("KK_K", "8"))
                for ct in range(_CT):
                    bank = ct % 2
                    gt = sc * 16 + ct
                    if gt >= 2 and sublvl != 21:
                        tensor.wait_ge(S.sFeA, gt - 1)
                    if ct < 2 and sc >= 1:
                        tensor.wait_ge(S.sSmV, 2 * 32 * sc)
                    _OPS = os.environ.get("KK_OPS", "orig")
                    for k in range(8):
                        i_ = tensor.matmul(pb[bank][:, 0:512],
                                           wqkv_sb[:, k * 3072 + ct * 128: k * 3072 + ct * 128 + 128],
                                           xnt[sc % 2][:, k * 512:(k + 1) * 512],
                                           start=(k == 0), stop=(k == 7))
                    i_.then_inc(S.sQA, 1)
                _SEQB = os.environ.get("KK_SEQB", "0") == "1"
                for cc in range(4 if sublvl >= 4 else 0):
                    c = sc * 4 + cc
                    if c >= 1:
                        tensor.wait_ge(S.sVe, c)
                    if sc >= 1 and cc == 0:
                        tensor.wait_ge(S.sRes, 4 * sc)
                        tensor.wait_ge(S.sLgV, sc)
                        tensor.wait_ge(S.sSmV, 2 * 32 * sc)
                        tensor.wait_ge(S.sStV, 32 * sc)
                    if _SEQB:
                        for cg in range(2):
                            for k in range(8):
                                lhs = xnt[sc % 2][:, k * 512 + cc * 128: k * 512 + cc * 128 + 128]
                                tensor.matmul(pb[2 + cg][:, 0:512], lhs,
                                              wqkv_sb[:, k * 3072 + 2048 + cg * 512: k * 3072 + 2048 + (cg + 1) * 512],
                                              start=(k == 0), stop=False)
                            tensor.matmul(pb[2 + cg][:, 0:512], ones128[0:1, :],
                                          bv_row[0:1, cg * 512:(cg + 1) * 512],
                                          start=False, stop=True)
                        for k in range(8):
                            lhs = xnt[sc % 2][:, k * 512 + cc * 128: k * 512 + cc * 128 + 128]
                            i_ = tensor.matmul(pb[4][:, cc * 8: cc * 8 + 8], lhs,
                                               wg_sb[:, k * NE:(k + 1) * NE],
                                               start=(cc == 0 and k == 0), stop=(cc == 3 and k == 7))
                        i_.then_inc(S.sQB, 1)
                    else:
                        for k in range(8):
                            lhs = xnt[sc % 2][:, k * 512 + cc * 128: k * 512 + cc * 128 + 128]
                            for cg in range(2):
                                tensor.matmul(pb[2 + cg][:, 0:512], lhs,
                                              wqkv_sb[:, k * 3072 + 2048 + cg * 512: k * 3072 + 2048 + (cg + 1) * 512],
                                              start=(k == 0), stop=False)
                            tensor.matmul(pb[4][:, cc * 8: cc * 8 + 8], lhs,
                                          wg_sb[:, k * NE:(k + 1) * NE],
                                          start=(cc == 0 and k == 0), stop=(cc == 3 and k == 7))
                        for cg in range(2):
                            i_ = tensor.matmul(pb[2 + cg][:, 0:512], ones128[0:1, :],
                                               bv_row[0:1, cg * 512:(cg + 1) * 512],
                                               start=False, stop=True)
                        i_.then_inc(S.sQB, 1)

                for cc in range(4 if sublvl >= 5 else 0):
                    c = sc * 4 + cc
                    pos = c % CPB
                    tensor.wait_ge(S.sFeV, sc * 16 + 16)
                    tensor.wait_ge(S.sVeD, c + 1)
                    tensor.wait_ge(sTkA[cc], 16 * 8 * (sc + 1))
                    if cc == 0:
                        tensor.wait_ge(S.sVe, 4 * sc + 4)
                        tensor.wait_ge(S.sLgV, sc + 1)
                    for p in range(8):
                        gp = c * 8 + p
                        sb0 = 0 if p % 2 == 0 else 2
                        if gp >= 2:
                            tensor.wait_ge(S.sSmV, 2 * (gp - 1))
                        colq = p * 512 + cc * 128
                        tensor.matmul(pb[sb0][:, 0:128], kf[0:64, colq:colq + 128],
                                      qf[0:64, colq:colq + 128], start=True, stop=True)
                        tensor.matmul(pb[sb0 + 1][:, 0:128], kf[64:128, colq:colq + 128],
                                      qf[64:128, colq:colq + 128], start=True, stop=True
                                      ).then_inc(S.sS, 1)
                        nbank = 6 + (p % 2)
                        if p == 0:
                            tensor.wait_ge(S.sRes, c)
                        if gp >= 2:
                            tensor.wait_ge(S.sDiv, gp - 1)
                        tensor.wait_ge(S.sSmV, 2 * (gp + 1))
                        if pos > 0:
                            tensor.wait_ge(S.sStV, (c - 1) * 8 + p + 1)
                        tensor.matmul(pb[nbank][:, 0:65], smt[p % 2][:, 0:128],
                                      vsb[cc][:, (2 * p) * 65:(2 * p) * 65 + 65],
                                      start=True, stop=False)
                        i_ = tensor.matmul(pb[nbank][:, 65:130], smt[p % 2][:, 128:256],
                                           vsb[cc][:, (2 * p + 1) * 65:(2 * p + 1) * 65 + 65],
                                           start=False, stop=(pos == 0))
                        if pos > 0:
                            i_ = tensor.matmul(pb[nbank][:, 0:130],
                                               qf[:, colq:colq + 128],
                                               s0b[:, p * 130:(p + 1) * 130],
                                               start=False, stop=True)
                        i_.then_inc(S.sNum, 1)
                        if os.environ.get("KK_ST7", "0") == "1":
                            if gp >= 1:
                                tensor.wait_ge(S.sStV, gp)
                            tensor.matmul(pb[7][:, 192:322],
                                          ktok[cc][:, p * 128:(p + 1) * 128],
                                          vsb[cc][:, (2 * p) * 65:(2 * p) * 65 + 130],
                                          start=True, stop=True).then_inc(S.sKVp, 1)
                        else:
                            kbank = 4 + (p % 2)
                            if gp >= 2:
                                tensor.wait_ge(S.sStV, gp - 1)
                            tensor.matmul(pb[kbank][:, 0:130],
                                          ktok[cc][:, p * 128:(p + 1) * 128],
                                          vsb[cc][:, (2 * p) * 65:(2 * p) * 65 + 130],
                                          start=True, stop=True).then_inc(S.sKVp, 1)
                    if sublvl < 7:
                        continue
                    tensor.wait_ge(sTatA[c % 2], 16 * 8 * (c // 2 + 1))
                    tensor.wait_ge(S.sDiv, 8 * (c + 1))
                    for k in range(8):
                        for cg in range(2):
                            tensor.matmul(pb[6 + cg][:, 0:512],
                                          attnt[c % 2][:, k * 128:(k + 1) * 128],
                                          wout_sb[:, k * 1024 + cg * 512: k * 1024 + (cg + 1) * 512],
                                          start=(k == 0), stop=False)
                    for cg in range(2):
                        i_ = tensor.matmul(pb[6 + cg][:, 0:512], ones128[0:1, :],
                                           bout_row[0:1, cg * 512:(cg + 1) * 512],
                                           start=False, stop=True)
                    i_.then_inc(S.sPj, 1)

            if stage < 2:
                return
            tensor.wait_ge(S.sRout, NCH)
            tensor.wait_ge(S.sCntV1, 1)
            tensor.wait_ge(S.sDiv, 8 * NCH)
            tensor.wait_ge(S.sRes, NCH)
            tensor.wait_ge(S.sLgV, NSC)
            tensor.wait_ge(S.sStV, 8 * NCH)
            tensor.wait_ge(sPwA[0], 16 * 32)
            tensor.wait_ge(sPwA[1], 16 * 32)
            tensor.wait_ge(sXnwA[0], 16 * 32)
            tensor.wait_ge(sXnwA[1], 16 * 32)
            tensor.matmul(pb[2][0:NE, 0:1], cntbf[:], ones128[:, 0:1], start=True, stop=True
                          ).then_inc(S.sCntP, 1)
            tensor.matmul(pb[6][0:1, 0:NCH * NE], ones128[:, 0:1], msk[:],
                          start=True, stop=True).then_inc(S.sCs, 1)
            tensor.wait_ge(S.sCCl, 16)
            tensor.matmul(pb[3][0:1, 0:NE], prevmask[:], allcnt[:], start=True, stop=True
                          ).then_inc(S.sBase1, 1)
            tensor.wait_ge(S.sCt64, 16)
            tensor.matmul(pb[4][0:NCH, 0:8], urank[0:NCH, 0:NCH], ctot64[:],
                          start=True, stop=True).then_inc(S.sPfxL, 1)
            tensor.wait_ge(S.sBase2, 1)
            tensor.matmul(pb[5][0:NCH, 0:8], urank[0:NCH, 0:NCH], ctot64[:],
                          start=True, stop=False)
            tensor.matmul(pb[5][0:NCH, 0:8], ones1x128f[0:1, 0:NCH], base_row[:],
                          start=False, stop=True).then_inc(S.sPfxG, 1)
            tensor.wait_ge(S.sPref, 32)
            tensor.matmul(pb[0][:, 0:NCH * NE], urank[:], msk[:], start=True, stop=False)
            tensor.matmul(pb[0][:, 0:NCH * NE], ones1x128f[:], prefloc512[:],
                          start=False, stop=True).then_inc(S.sRkA, 1)
            tensor.matmul(pb[1][:, 0:NCH * NE], urank[:], msk[:], start=True, stop=False)
            tensor.matmul(pb[1][:, 0:NCH * NE], ones1x128f[:], prefglob512[:],
                          start=False, stop=True).then_inc(S.sRkB, 1)

            if stage < 3:
                return
            tensor.wait_ge(S.sDestV, 1)
            tensor.wait_ge(S.sCntV2, 1)
            tensor.wait_ge(S.sBase2, 1)
            for e in range(NE):
                for g in range(NGRP):
                    gg = e * NGRP + g
                    tensor.wait_ge(sTrsA[gg % 2], 16 * NTR_G * (gg // 2 + 1))
                    if gg == 0:
                        tensor.wait_ge(S.sW1, 16 * 8)
                    for ht in range(32):
                        hh = gg * 32 + ht
                        bank = ht % 3
                        if hh >= 3:
                            tensor.wait_ge(S.sGel, hh - 2)
                        if g == 0 and ht == 0 and e > 0:
                            tensor.wait_ge(S.sW1, 16 * 8 * (e + 1))
                        for k in range(8):
                            i_ = tensor.matmul(pb[bank][:, 0:GROWS],
                                               w1_sb[:, k * HID + ht * 128: k * HID + ht * 128 + 128],
                                               xngt[gg % 2][:, k * GROWS:(k + 1) * GROWS],
                                               start=(k == 0), stop=(k == 7))
                        i_.then_inc(S.sH, 1)
                    tensor.wait_ge(S.sGel, 32 * (gg + 1))
                    if g == 0:
                        tensor.wait_ge(S.sW2, 16 * 33 * (e + 1))
                    for sub in range(4):
                        ss = gg * 4 + sub
                        ybk = 3 + (sub % 2) * 2
                        if ss >= 2:
                            tensor.wait_ge(S.sYe, ss - 1)
                        for ht in range(32):
                            for cg in range(2):
                                tensor.matmul(pb[ybk + cg][:, 0:512],
                                              ht_sb[:, ht * GROWS + sub * 128: ht * GROWS + sub * 128 + 128],
                                              w2_sb[:, ht * 1024 + cg * 512: ht * 1024 + (cg + 1) * 512],
                                              start=(ht == 0), stop=False)
                        for cg in range(2):
                            i_ = tensor.matmul(pb[ybk + cg][:, 0:512], ones128[0:1, :],
                                               b2e[0:1, cg * 512:(cg + 1) * 512],
                                               start=False, stop=True)
                        i_.then_inc(S.sY, 1)

        @block.vector
        def _(vector):
            vector.wait_ge(S.sPreS, 16 * NPRES)
            vector.wait_ge(S.sPreGm, 1)
            for i in range(4):
                vector.tensor_copy(
                    vsb[i][:].rearrange("p (h w) -> p h w", w=65)[:, :, 64:65],
                    ones128[:, 0:16].rearrange("p (h w) -> p h w", w=1))
            vector.tensor_scalar(bqks[:, 0:8], bqk[:, 0:8], QSCALE, None, ALU.mult)
            vector.tensor_scalar(bqks[:, 8:16], bqk[:, 8:16], 1.0, None, ALU.mult)
            vector.memset(cntacc[:], 0.0)
            vector.drain()

            def _stats(sc_):
                for cc_ in range(4):
                    c_ = sc_ * 4 + cc_
                    vector.wait_ge(sLxA[c_ % 9], 16 * (c_ // 9 + 1))
                    if c_ >= 2:
                        vector.wait_ge(S.sRst, c_ - 1)
                    vector.bn_stats(bn12[c_ % 2][:, 0:6], xfp[c_ % 9][:, 0:512])
                    vector.bn_stats(bn12[c_ % 2][:, 6:12], xfp[c_ % 9][:, 512:1024])
                    vector.drain()
                    vector.bn_aggr(mv[c_ % 2][:], bn12[c_ % 2][:]).then_inc(S.sBnA, 1)
                    vector.wait_ge(S.sRst, c_ + 1)
                    vector.tensor_scalar(negms[c_ % 2][:], mv[c_ % 2][:, 0:1], rstd[c_ % 2][:],
                                         -1.0, ALU.mult, ALU.mult)
                    vector.drain()
                    if c_ >= 2:
                        vector.wait_ge(sTxnA[c_ % 2], 16 * 8 * ((c_ - 2) // 2 + 1))
                        vector.wait_ge(sXnwA[c_ % 2], 16 * ((c_ - 2) // 2 + 1))
                    vector.tensor_scalar(xn_sb[c_ % 2][:], xfp[c_ % 9][:], rstd[c_ % 2][:],
                                         negms[c_ % 2][:], ALU.mult, ALU.add).then_inc(S.sStat, 1)

            if sublvl >= 2:
                _stats(0)
            for sc in range(n_sc if sublvl >= 2 else 0):
                for ct in range(16 if (sublvl >= 3 and sublvl != 21 and sublvl != 22) else 0):
                    gt = sc * 16 + ct
                    dst = qf if ct < 8 else kf
                    off = (ct % 8) * 512
                    if ct == 8 and sc >= 1 and sublvl >= 5:
                        for cc_ in range(4):
                            vector.wait_ge(sTkA[cc_], 16 * 8 * sc)
                    vector.wait_ge(S.sFeA, gt + 1)
                    vector.tensor_scalar(tf[ct % 2][:], ef[ct % 2][:], 1.0, 1e-4,
                                         ALU.min, ALU.add)
                    vector.drain()
                    vector.tensor_tensor(dst[:, off:off + 512], tf[ct % 2][:], rf[ct % 2][:],
                                         ALU.add).then_inc(S.sFeV, 1)
                if sublvl >= 4:
                    vector.wait_ge(S.sQB, 4 * (sc + 1))
                    vector.tensor_copy(lgt[sc % 2][:], pb[4][:, 0:32]).then_inc(S.sLgV, 1)
                    vector.drain()
                if sublvl >= 2 and sc + 1 < n_sc:
                    _stats(sc + 1)
                for cc in range(4 if sublvl >= 5 else 0):
                    c = sc * 4 + cc
                    pos = c % CPB
                    for p in range(8):
                        gp = c * 8 + p
                        sb0 = 0 if p % 2 == 0 else 2
                        vector.wait_ge(S.sS, gp + 1)
                        if gp >= 2:
                            vector.wait_ge(S.sNum, gp - 1)
                        vector.tensor_tensor(smt[p % 2][:, 0:128], pb[sb0][:, 0:128], maskut[:],
                                             ALU.mult).then_inc(S.sSmV, 1)
                        vector.tensor_tensor(smt[p % 2][:, 128:256], pb[sb0 + 1][:, 0:128], maskut[:],
                                             ALU.mult).then_inc(S.sSmV, 1)
                        vector.wait_ge(S.sKVp, gp + 1)
                        vector.wait_ge(S.sNum, gp + 1)
                        kbank = 4 + (p % 2)
                        kcol0 = 0
                        if os.environ.get("KK_ST7", "0") == "1":
                            kbank, kcol0 = 7, 192
                        if pos == 0:
                            vector.tensor_scalar(s0f[0:64, p * 130: p * 130 + 65],
                                                 pb[kbank][0:64, kcol0:kcol0 + 65],
                                                 1.0, None, ALU.mult)
                            vector.tensor_scalar(s0f[64:128, p * 130 + 65: p * 130 + 130],
                                                 pb[kbank][64:128, kcol0 + 65:kcol0 + 130],
                                                 1.0, None, ALU.mult)
                        else:
                            vector.tensor_tensor(s0f[0:64, p * 130: p * 130 + 65],
                                                 s0f[0:64, p * 130: p * 130 + 65],
                                                 pb[kbank][0:64, kcol0:kcol0 + 65], ALU.add)
                            vector.tensor_tensor(s0f[64:128, p * 130 + 65: p * 130 + 130],
                                                 s0f[64:128, p * 130 + 65: p * 130 + 130],
                                                 pb[kbank][64:128, kcol0 + 65:kcol0 + 130],
                                                 ALU.add)
                        vector.drain()
                        vector.tensor_copy(s0b[:, p * 130:(p + 1) * 130],
                                           s0f[:, p * 130:(p + 1) * 130]).then_inc(S.sStV, 1)
                        nbank = 6 + (p % 2)
                        if gp >= 2:
                            vector.wait_ge(S.sDiv, gp - 1)
                        vector.tensor_scalar(dena[2 * (p % 2)][:], pb[nbank][:, 64:65],
                                             EPS_DEN, None, ALU.add)
                        vector.tensor_scalar(dena[2 * (p % 2) + 1][:], pb[nbank][:, 129:130],
                                             EPS_DEN, None, ALU.add)
                        vector.drain()
                        vector.reciprocal(dena[2 * (p % 2)][:], dena[2 * (p % 2)][:]
                                          ).then_inc(S.sRcp, 1)
                        vector.reciprocal(dena[2 * (p % 2) + 1][:], dena[2 * (p % 2) + 1][:]
                                          ).then_inc(S.sRcp, 1)
                    if sublvl < 7:
                        continue
                    vector.wait_ge(S.sPj, c + 1)
                    if c >= 2:
                        vector.wait_ge(sPwA[c % 2], 16 * ((c - 2) // 2 + 1))
                        if stage < 3:
                            vector.wait_ge(sOwA[c % 2], 16 * ((c - 2) // 2 + 1))
                    vector.tensor_tensor(part_sb[c % 2][:, 0:512], pb[6][:, 0:512],
                                         xfp[c % 9][:, 0:512], ALU.add)
                    vector.tensor_tensor(part_sb[c % 2][:, 512:1024], pb[7][:, 0:512],
                                         xfp[c % 9][:, 512:1024], ALU.add)
                    vector.drain()
                    vector.memset(fence_v[:], 0.0).then_inc(S.sRes, 1)
                    if sublvl < 6:
                        continue
                    lgc = lgt[sc % 2][:, cc * 8: cc * 8 + 8]
                    vector.max(mx8[:], lgc)
                    vector.drain()
                    vector.max_index(idxu[:], mx8[:], lgc)
                    vector.drain()
                    vector.tensor_copy(idxf[:], idxu[:])
                    vector.tensor_tensor(dcol[:, c:c + 1], mx8[:, 0:1], mx8[:, 1:2], ALU.subtract)
                    vector.drain()
                    vector.tensor_scalar(m1s[:, c * NE:(c + 1) * NE], iota8[:], idxf[:, 0:1],
                                         None, ALU.is_equal)
                    vector.tensor_scalar(m2s[:, c * NE:(c + 1) * NE], iota8[:], idxf[:, 1:2],
                                         None, ALU.is_equal)
                    vector.drain()
                    vector.tensor_tensor(msk[:, c * NE:(c + 1) * NE], m1s[:, c * NE:(c + 1) * NE],
                                         m2s[:, c * NE:(c + 1) * NE], ALU.add)
                    vector.drain()
                    vector.tensor_tensor(cntacc[:], cntacc[:], msk[:, c * NE:(c + 1) * NE],
                                         ALU.add).then_inc(S.sRout, 1)
                    vector.drain()

            if stage < 2:
                return
            vector.tensor_copy(cntbf[:], cntacc[:]).then_inc(S.sCntV1, 1)
            vector.wait_ge(S.sCntP, 1)
            vector.tensor_copy(cnt8[:], pb[2][0:NE, 0:1]).then_inc(S.sCntV2, 1)
            vector.wait_ge(S.sCs, 1)
            vector.tensor_copy(ctot_row[:], pb[6][0:1, 0:NCH * NE]).then_inc(S.sCsV, 1)
            vector.wait_ge(S.sBase1, 1)
            vector.tensor_copy(base_row[:], pb[3][0:1, 0:NE]).then_inc(S.sBase2, 1)
            vector.wait_ge(S.sPfxL, 1)
            vector.tensor_copy(pfl64[:], pb[4][0:NCH, 0:8])
            vector.wait_ge(S.sPfxG, 1)
            vector.tensor_copy(pfg64[:], pb[5][0:NCH, 0:8]).then_inc(S.sPfxV, 1)
            vector.wait_ge(S.sRkA, 1)
            vector.wait_ge(S.sRkB, 1)
            vector.tensor_scalar(k8a[:], pb[0][:, 0:NCH * NE], float(CAP_PAD), None, ALU.is_lt)
            vector.tensor_scalar(k8b[:], pb[1][:, 0:NCH * NE], float(CAP), None, ALU.is_lt)
            vector.tensor_tensor(rp512[:], pb[0][:, 0:NCH * NE], eoff512[:], ALU.add)
            vector.drain()
            vector.tensor_tensor(k8a[:], k8a[:], k8b[:], ALU.mult)
            vector.drain()
            vector.tensor_tensor(mk1[:], m1s[:], k8a[:], ALU.mult)
            vector.tensor_tensor(mk2[:], m2s[:], k8a[:], ALU.mult)
            vector.drain()
            vector.tensor_tensor(k8a[:], mk1[:], rp512[:], ALU.mult)
            vector.tensor_tensor(k8b[:], mk2[:], rp512[:], ALU.mult)
            vector.drain()
            vector.reduce_sum(kk1[:].rearrange("p (c j) -> p c j", j=1),
                              mk1[:].rearrange("p (c e) -> p c e", e=NE), axis=AX.X)
            vector.reduce_sum(kk2[:].rearrange("p (c j) -> p c j", j=1),
                              mk2[:].rearrange("p (c e) -> p c e", e=NE), axis=AX.X)
            vector.reduce_sum(sc1[:].rearrange("p (c j) -> p c j", j=1),
                              k8a[:].rearrange("p (c e) -> p c e", e=NE), axis=AX.X)
            vector.reduce_sum(sc2[:].rearrange("p (c j) -> p c j", j=1),
                              k8b[:].rearrange("p (c e) -> p c e", e=NE), axis=AX.X)
            vector.drain()
            vector.tensor_scalar(tc1[:], kk1[:], -1.0, 1.0, ALU.mult, ALU.add)
            vector.tensor_scalar(tc2[:], kk2[:], -1.0, 1.0, ALU.mult, ALU.add)
            vector.drain()
            vector.tensor_tensor(w1c[:], tc1[:], dmp0[:], ALU.mult)
            vector.tensor_tensor(w2c[:], tc2[:], dmp1[:], ALU.mult)
            vector.drain()
            vector.tensor_tensor(d1c[:], sc1[:], w1c[:], ALU.add)
            vector.tensor_tensor(d2c[:], sc2[:], w2c[:], ALU.add)
            vector.drain()
            vector.tensor_copy(dscat[:].rearrange("p (c j) -> p c j", j=2)[:, :, 0:1],
                               d1c[:].rearrange("p (c j) -> p c j", j=1))
            vector.tensor_copy(dscat[:].rearrange("p (c j) -> p c j", j=2)[:, :, 1:2],
                               d2c[:].rearrange("p (c j) -> p c j", j=1))
            vector.tensor_scalar(w1c[:], tc1[:], float(YZERO), None, ALU.mult)
            vector.tensor_scalar(w2c[:], tc2[:], float(YZERO), None, ALU.mult)
            vector.drain()
            vector.tensor_tensor(d1c[:], sc1[:], w1c[:], ALU.add)
            vector.tensor_tensor(d2c[:], sc2[:], w2c[:], ALU.add)
            vector.drain()
            vector.tensor_copy(ygat[:].rearrange("p (c j) -> p c j", j=2)[:, :, 0:1],
                               d1c[:].rearrange("p (c j) -> p c j", j=1))
            vector.tensor_copy(ygat[:].rearrange("p (c j) -> p c j", j=2)[:, :, 1:2],
                               d2c[:].rearrange("p (c j) -> p c j", j=1)).then_inc(S.sDestV, 1)
            vector.drain()
            vector.wait_ge(S.sGate, 1)
            vector.tensor_tensor(g1k[:], g1s[:], kk1[:], ALU.mult)
            vector.tensor_scalar(g2tmp[:], g1s[:], -1.0, 1.0, ALU.mult, ALU.add)
            vector.drain()
            vector.tensor_tensor(g2k[:], g2tmp[:], kk2[:], ALU.mult).then_inc(S.sGate, 1)
            vector.drain()
            vector.memset(fence_v[:], 0.0).then_inc(S.sP1V, 1)

            for c in (range(NCH) if stage >= 3 else []):
                vector.wait_ge(sLptA[c % 4], 16 * (c // 4 + 1))
                vector.wait_ge(sYgA[c % 4], 32 * (c // 4 + 1))
                if c >= 4:
                    vector.wait_ge(sOwA[c % 4], 16 * (c // 4))
                vector.scalar_tensor_tensor(t3[c % 4][:], yg0[c % 4][:], g1k[:, c:c + 1],
                                            pt3[c % 4][:], ALU.mult, ALU.add)
                vector.drain()
                vector.scalar_tensor_tensor(o3[c % 4][:], yg1[c % 4][:], g2k[:, c:c + 1],
                                            t3[c % 4][:], ALU.mult, ALU.add).then_inc(S.sP3b, 1)

        @block.scalar
        def _(scalar):
            scalar.wait_ge(S.sPreS, 16 * NPRES)
            scalar.wait_ge(S.sPreG, 16 * NPREG)

            def _rstd(sc_):
                for cc_ in range(4):
                    c_ = sc_ * 4 + cc_
                    scalar.wait_ge(S.sBnA, c_ + 1)
                    if c_ >= 2:
                        scalar.wait_ge(S.sStat, c_ - 1)
                    scalar.activation(rstd[c_ % 2][:], mv[c_ % 2][:, 1:2], AF.Ln,
                                      bias=epsz[:, 0:1], scale=1.0)
                    scalar.drain()
                    scalar.activation(rstd[c_ % 2][:], rstd[c_ % 2][:], AF.Exp,
                                      bias=epsz[:, 1:2], scale=-0.5).then_inc(S.sRst, 1)

            if sublvl >= 2:
                _rstd(0)
            for sc in range(n_sc if sublvl >= 2 else 0):
                for ct in range(16 if (sublvl >= 3 or sublvl == 22) else 0):
                    gt = sc * 16 + ct
                    bank = ct % 2
                    scl = QSCALE if ct < 8 else 1.0
                    scalar.wait_ge(S.sQA, gt + 1)
                    if gt >= 2 and sublvl != 22:
                        scalar.wait_ge(S.sFeV, gt - 1)
                    scalar.activation(ef[ct % 2][:], pb[bank][:, 0:512], AF.Exp,
                                      bias=bqks[:, ct:ct + 1], scale=scl)
                    scalar.activation(rf[ct % 2][:], pb[bank][:, 0:512], AF.Relu,
                                      bias=bqks[:, ct:ct + 1], scale=scl).then_inc(S.sFeA, 1)
                for cc in range(4 if sublvl >= 4 else 0):
                    c = sc * 4 + cc
                    scalar.wait_ge(S.sQB, c + 1)
                    for h in range(16):
                        ins = pb[2 + h // 8][:, (h % 8) * 64:(h % 8) * 64 + 64]
                        i_ = scalar.activation(vsb[cc][:, h * 65: h * 65 + 64], ins, AF.Copy)
                    i_.then_inc(S.sVe, 1)
                    scalar.drain()
                    scalar.activation(fence_a[:], bqks[:, 0:1], AF.Copy).then_inc(S.sVeD, 1)
                if sublvl >= 2 and sc + 1 < n_sc:
                    _rstd(sc + 1)
                if sublvl >= 7:
                    for cc in range(4):
                        c = sc * 4 + cc
                        if c >= 2:
                            scalar.wait_ge(sTatA[c % 2], 16 * 8 * ((c - 2) // 2 + 1))
                        for p in range(8):
                            gp = c * 8 + p
                            nbank = 6 + (p % 2)
                            scalar.wait_ge(S.sRcp, 2 * (gp + 1))
                            scalar.activation(attn_sb[c % 2][:, (2 * p) * 64:(2 * p) * 64 + 64],
                                              pb[nbank][:, 0:64], AF.Copy,
                                              scale=dena[2 * (p % 2)][:])
                            scalar.activation(attn_sb[c % 2][:, (2 * p + 1) * 64:(2 * p + 1) * 64 + 64],
                                              pb[nbank][:, 65:129], AF.Copy,
                                              scale=dena[2 * (p % 2) + 1][:]).then_inc(S.sDiv, 1)
                        scalar.drain()
                        scalar.activation(fence_a[:], bqks[:, 0:1], AF.Copy).then_inc(S.sDivD, 1)
                        scalar.wait_ge(S.sDivD, c + 1)
                        if c >= 1:
                            scalar.wait_ge(S.sPj, c)
                        for k in range(8):
                            scalar.dma_start(out=attnt[c % 2][:, k * 128:(k + 1) * 128],
                                             in_=attn_sb[c % 2][:, k * 128:(k + 1) * 128],
                                             transpose=True).then_inc(sTatA[c % 2], 16)

            if stage < 2:
                return
            scalar.wait_ge(S.sRout, NCH)
            scalar.activation(g1s[:], dcol[:], AF.Sigmoid).then_inc(S.sGate, 1)
            scalar.drain()
            scalar.activation(fence_a[:], g1s[:, 0:1], AF.Copy).then_inc(S.sP1A, 1)

            if stage < 3:
                return
            scalar.wait_ge(S.sPreA, 16 * NPREA)
            for e in range(NE):
                for g in range(NGRP):
                    gg = e * NGRP + g
                    for ht in range(32):
                        hh = gg * 32 + ht
                        bank = ht % 3
                        scalar.wait_ge(S.sH, hh + 1)
                        scalar.activation(ht_sb[:, ht * GROWS:(ht + 1) * GROWS],
                                          pb[bank][:, 0:GROWS], AF.Gelu_apprx_tanh,
                                          bias=b1_sb[:, e * 32 + ht: e * 32 + ht + 1],
                                          scale=1.0).then_inc(S.sGel, 1)
                    for sub in range(4):
                        ss = gg * 4 + sub
                        ybk = 3 + (sub % 2) * 2
                        scalar.wait_ge(S.sY, ss + 1)
                        if ss >= 2:
                            scalar.wait_ge(sYwA[sub % 2], 16 * ((ss - 2) // 2 + 1))
                        scalar.activation(y_sb[sub % 2][:, 0:512], pb[ybk][:, 0:512], AF.Copy)
                        scalar.activation(y_sb[sub % 2][:, 512:1024], pb[ybk + 1][:, 0:512],
                                          AF.Copy).then_inc(S.sYe, 1)

    st.close()
    return nc


_NC_CACHE = {}


def _consts():
    i = np.arange(128)
    maskut = (i[:, None] <= i[None, :]).astype(ml_dtypes.bfloat16)
    urank = (i[:, None] < i[None, :]).astype(ml_dtypes.bfloat16)
    ones128 = np.ones((128, 128), ml_dtypes.bfloat16)
    ones1x128f = np.ones((1, 128), np.float32)
    iota8 = np.broadcast_to(np.arange(NE, dtype=np.float32), (128, NE)).copy()
    eoff = np.broadcast_to((np.arange(NE) * CAP_PAD).astype(np.float32), (128, NE)).copy()
    eoff512 = np.tile(eoff[0], (128, NCH)).astype(np.float32)
    cidx = np.arange(NCH, dtype=np.float32)[None, :]
    pidx = np.arange(128, dtype=np.float32)[:, None]
    dmp0 = (DUMP0 + 256.0 * cidx + 2.0 * pidx + 0.0).astype(np.float32)
    dmp1 = (DUMP0 + 256.0 * cidx + 2.0 * pidx + 1.0).astype(np.float32)
    iota128 = np.arange(128, dtype=np.float32).reshape(128, 1)
    epsz = np.zeros((128, 2), np.float32); epsz[:, 0] = EPS_LN
    return maskut, urank, ones128, ones1x128f, iota8, eoff, eoff512, dmp0, dmp1, iota128, epsz


def kernel(L0_attn_x, L0_attn_W_qkv, L0_attn_b_qkv, L0_attn_W_out, L0_attn_b_out,
           L0_ffn_Wg, L0_ffn_W1, L0_ffn_b1, L0_ffn_W2, L0_ffn_b2):
    xf = np.ascontiguousarray(np.asarray(L0_attn_x, np.float32).reshape(NTOT, D))
    (maskut, urank, ones128, ones1x128f, iota8, eoff, eoff512, dmp0, dmp1,
     iota128, epsz) = _consts()

    if "nc" not in _NC_CACHE:
        _NC_CACHE["nc"] = build_program(3, NSC, 7)
    nc = _NC_CACHE["nc"]

    shared = dict(
        wqkv=np.ascontiguousarray(np.asarray(L0_attn_W_qkv, np.float32)),
        bqkv=np.ascontiguousarray(np.asarray(L0_attn_b_qkv, np.float32)),
        wout=np.ascontiguousarray(np.asarray(L0_attn_W_out, np.float32)),
        bout=np.ascontiguousarray(np.asarray(L0_attn_b_out, np.float32)),
        wg=np.ascontiguousarray(np.asarray(L0_ffn_Wg, np.float32)),
        w1=np.ascontiguousarray(np.asarray(L0_ffn_W1, np.float32)),
        b1=np.ascontiguousarray(np.asarray(L0_ffn_b1, np.float32)),
        w2=np.ascontiguousarray(np.asarray(L0_ffn_W2, np.float32)),
        b2=np.ascontiguousarray(np.asarray(L0_ffn_b2, np.float32)),
        maskut=maskut, urank=urank, ones128=ones128, ones1x128f=ones1x128f,
        iota8=iota8, eoff=eoff, eoff512=eoff512, dmp0=dmp0, dmp1=dmp1,
        iota128=iota128, epsz=epsz,
    )
    in_maps = []
    for c in range(NCORES):
        m = dict(shared)
        m["x"] = np.ascontiguousarray(xf[c * TOK:(c + 1) * TOK])
        m["prevmask"] = (np.arange(NCORES) < c).astype(np.float32).reshape(NCORES, 1)
        in_maps.append(m)

    res = run_bass_kernel_spmd(nc, in_maps, core_ids=list(range(NCORES)))
    out = np.concatenate([res.results[c]["out"] for c in range(NCORES)], axis=0)
    return out.reshape(B, L, D)


if __name__ == "__main__":
    import time
    t0 = time.time()
    nc = build_program()
    print(f"build ok in {time.time()-t0:.1f}s; instructions: "
          f"{sum(1 for _ in nc.all_instructions())}")
